# version 04
# baseline (speedup 1.0000x reference)
import contextlib
import numpy as np
import ml_dtypes
import concourse.bass as bass
import concourse.mybir as mybir
from concourse.bass_utils import run_bass_kernel_spmd

F32 = mybir.dt.float32
BF16 = mybir.dt.bfloat16
I32 = mybir.dt.int32
OP = mybir.AluOpType
ACT = mybir.ActivationFunctionType
AX = mybir.AxisListType.X

NCORES = 8
NSEQ = 2
S = 2048
D = 2048
H = 16
DKV = 128
NTT = 16
NDC = 16
NQC = 4
SCL = float(1.0 / np.sqrt(np.float32(128.0)))
ROPE_THETA = 10000.0

_DBG = set()
_debug_capture = None


class Sem:
    def __init__(self, h):
        self.h = h
        self.v = 0

    def nxt(self, k=1):
        self.v += k
        return self.v


class Prog:
    ENGS = ("sync", "gpsimd", "vector", "scalar", "tensor")

    def __init__(self):
        self.s = {e: [] for e in self.ENGS}

    def op(self, eng, meth, *a, inc=None, inck=1, **kw):
        if (inc is not None and eng in ("vector", "scalar")
                and meth not in ("nop", "wait_ge", "drain")):
            self.s[eng].append((meth, a, kw, None, 0))
            self.s[eng].append(("drain", (), {}, None, 0))
            self.s[eng].append(("nop", (), {}, inc, inck))
        else:
            self.s[eng].append((meth, a, kw, inc, inck))

    def wait(self, eng, sem, v):
        if v > 0:
            self.s[eng].append(("wait_ge", (sem.h, v), {}, None, 0))

    def replay(self, eng_name, e):
        for meth, a, kw, inc, inck in self.s[eng_name]:
            inst = getattr(e, meth)(*a, **kw)
            if inc is not None:
                inst.then_inc(inc.h, inck)


def _bf(x):
    return np.ascontiguousarray(np.asarray(x).astype(ml_dtypes.bfloat16))


def _f32(x):
    return np.ascontiguousarray(np.asarray(x).astype(np.float32))


def _host_constants():
    c = {}
    c["ident_f"] = _f32(np.eye(128))
    c["ident_b"] = _bf(np.eye(128))
    r = np.arange(128)
    c["tri_b"] = _bf((r[None, :] >= r[:, None]).astype(np.float32))
    c["onescol_b"] = _bf(np.ones((128, 1)))
    c["onescol_f"] = _f32(np.ones((128, 1)))
    c["onesrow_f"] = _f32(np.ones((1, 128)))
    c["iota128col"] = _f32(((np.arange(128) % 64) // 16).reshape(128, 1))
    pos = np.arange(S, dtype=np.float64)
    freq = ROPE_THETA ** (-(np.arange(64, dtype=np.float64)) / 64.0)
    ang = pos[None, :] * freq[:, None]
    cosv = np.cos(ang)
    sinv = np.sin(ang)
    ctab = np.empty((128, S), dtype=np.float64)
    stab = np.empty((128, S), dtype=np.float64)
    ctab[0::2] = cosv
    ctab[1::2] = cosv
    stab[0::2] = sinv
    stab[1::2] = -sinv
    c["ropeC"] = _f32(ctab)
    c["ropeS"] = _f32(stab)
    cv = np.zeros((128, 4))
    cv[:, 0] = 1e-5
    cv[:, 1] = 1e-30
    cv[:, 2] = float(np.log(1.0 / np.sqrt(np.float32(128.0))) + np.log(1.03))
    cv[:, 3] = 1.0
    c["cvec"] = _f32(cv)
    return c


def _host_prep(inputs):
    g = {}
    g["wqT"] = _f32(inputs["L0_attn_Wq"].T.reshape(D, H, DKV).transpose(1, 0, 2))
    g["wkT"] = _f32(inputs["L0_attn_Wk"].T.reshape(D, H, DKV).transpose(1, 0, 2))
    g["wvT"] = _f32(inputs["L0_attn_Wv"].T.reshape(D, H, DKV).transpose(1, 0, 2))
    g["woT"] = _f32(np.asarray(inputs["L0_attn_Wo"]).T)
    g["w1T"] = _f32(inputs["L0_ffn_w1"].T.reshape(D, 16, 128).transpose(1, 0, 2))
    g["w3T"] = _f32(inputs["L0_ffn_w3"].T.reshape(D, 16, 128).transpose(1, 0, 2))
    g["w2T"] = _f32(np.asarray(inputs["L0_ffn_w2"]).T)
    g["wr_sb"] = _f32(np.asarray(inputs["L0_ffn_wr"]).T.reshape(16, 128, 8)
                      .transpose(1, 0, 2).reshape(128, 128))
    for p, W in (("q", "L0_attn_Aq"), ("k", "L0_attn_Ak"),
                 ("v", "L0_attn_Av"), ("o", "L0_attn_Ao")):
        acat = np.asarray(inputs[W]).transpose(2, 0, 1).reshape(D, 64)
        g["acat_" + p] = _bf(acat.reshape(16, 128, 64).transpose(1, 0, 2)
                             .reshape(128, 1024))
    bq = np.asarray(inputs["L0_attn_Bq"]).transpose(0, 2, 1).reshape(64, D)
    bk = np.asarray(inputs["L0_attn_Bk"]).transpose(0, 2, 1).reshape(64, D)
    bv = np.asarray(inputs["L0_attn_Bv"]).transpose(0, 2, 1).reshape(64, D)
    bo = np.asarray(inputs["L0_attn_Bo"]).transpose(0, 2, 1).reshape(64, D)
    g["bcat1"] = _bf(np.concatenate([bq, bk], axis=0))
    g["bcat2"] = _bf(np.concatenate([bo, bv], axis=0))
    g["starts"] = np.ascontiguousarray(
        np.asarray(inputs["L0_attn_lora_start_indices"]).astype(np.int32).reshape(1, 4))
    g.update(_host_constants())
    return g


def build_nc():
    nc = bass.Bass("TRN2", target_bir_lowering=False)
    P = Prog()

    x_d = nc.dram_tensor("x", [NSEQ * S, D], F32, kind="ExternalInput")
    out_d = nc.dram_tensor("out", [NSEQ * S, D], F32, kind="ExternalOutput")
    ins = {}
    for name, shp, dt in [
        ("wqT", [H, D, DKV], F32), ("wkT", [H, D, DKV], F32),
        ("wvT", [H, D, DKV], F32), ("woT", [D, D], F32),
        ("w1T", [16, D, 128], F32), ("w3T", [16, D, 128], F32),
        ("w2T", [D, D], F32), ("wr_sb", [128, 128], F32),
        ("acat_q", [128, 1024], BF16), ("acat_k", [128, 1024], BF16),
        ("acat_v", [128, 1024], BF16), ("acat_o", [128, 1024], BF16),
        ("bcat1", [128, D], BF16), ("bcat2", [128, D], BF16),
        ("starts", [1, 4], I32),
        ("ident_f", [128, 128], F32), ("ident_b", [128, 128], BF16),
        ("tri_b", [128, 128], BF16),
        ("onescol_b", [128, 1], BF16), ("onescol_f", [128, 1], F32),
        ("onesrow_f", [1, 128], F32), ("iota128col", [128, 1], F32),
        ("ropeC", [128, S], F32), ("ropeS", [128, S], F32),
        ("bval", [1, 2], F32), ("c0m", [8, 1], F32), ("cvec", [128, 4], F32),
    ]:
        ins[name] = nc.dram_tensor(name, shp, dt, kind="ExternalInput")
    attT_d = nc.dram_tensor("attT", [H, 128, S], BF16,
                            kind="ExternalOutput" if "attT" in _DBG else "Internal")
    cc_in = nc.dram_tensor("cc_in", [8, D], F32)
    cc_out = nc.dram_tensor("cc_out", [8, D], F32)
    dbg = {}
    if "xnT" in _DBG:
        dbg["xnT"] = nc.dram_tensor("dbg_xnT", [128, NDC * S], BF16, kind="ExternalOutput")
    if "gates" in _DBG:
        dbg["logits"] = nc.dram_tensor("dbg_logits", [128, 128], F32, kind="ExternalOutput")
        dbg["gates"] = nc.dram_tensor("dbg_gates", [128, 128], BF16, kind="ExternalOutput")
    if "qkv" in _DBG:
        for n in ("qraw", "QT", "KT", "V"):
            dbg[n] = nc.dram_tensor("dbg_" + n, [128, S], BF16, kind="ExternalOutput")
        dbg["negB"] = nc.dram_tensor("dbg_negB", [128, 1], F32, kind="ExternalOutput")
    if "low" in _DBG:
        dbg["low1"] = nc.dram_tensor("dbg_low1", [128, S], BF16, kind="ExternalOutput")
        dbg["low2"] = nc.dram_tensor("dbg_low2", [128, S], BF16, kind="ExternalOutput")
    if "h2" in _DBG:
        dbg["h1"] = nc.dram_tensor("dbg_h1", [128, 16 * 1024], BF16, kind="ExternalOutput")
        dbg["h2"] = nc.dram_tensor("dbg_h2", [128, 16 * 1024], BF16, kind="ExternalOutput")
    if "G" in _DBG:
        dbg["G"] = nc.dram_tensor("dbg_G", [8, D], F32, kind="ExternalOutput")
    if "S" in _DBG:
        dbg["S"] = nc.dram_tensor("dbg_S", [8, D], F32, kind="ExternalOutput")
    if "snap" in _DBG:
        dbg["snap"] = nc.dram_tensor("dbg_snap", [128, 64], F32, kind="ExternalOutput")

    es = contextlib.ExitStack()
    with es:
        sems = {}
        for n in ("d_pre", "d_x0", "d_x1", "d_w0", "d_w1", "d_wo", "d_at0",
                  "d_at1", "d_as0", "d_as1", "d_out0", "d_out1", "d_cc", "d_pt",
                  "c_mm", "c_sc", "c_ve", "c_cc",
                  "f_x0", "f_x1", "f_w0", "f_w1", "f_at0", "f_at1"):
            sems[n] = Sem(es.enter_context(nc.semaphore(n)))
        d_pre = sems["d_pre"]
        d_x = [sems["d_x0"], sems["d_x1"]]
        d_w = [sems["d_w0"], sems["d_w1"]]
        d_wo = sems["d_wo"]
        d_at = [sems["d_at0"], sems["d_at1"]]
        d_as = [sems["d_as0"], sems["d_as1"]]
        d_out = [sems["d_out0"], sems["d_out1"]]
        d_cc = sems["d_cc"]
        d_pt = sems["d_pt"]
        c_mm = sems["c_mm"]
        c_sc = sems["c_sc"]
        c_ve = sems["c_ve"]
        c_cc = sems["c_cc"]
        f_x = [sems["f_x0"], sems["f_x1"]]
        f_w = [sems["f_w0"], sems["f_w1"]]
        f_at = [sems["f_at0"], sems["f_at1"]]

        sb = lambda n, shp, dt: es.enter_context(nc.sbuf_tensor("s_" + n, shp, dt))
        ropeC = sb("ropeC", [128, S], F32)
        ropeS = sb("ropeS", [128, S], F32)
        identf = sb("identf", [128, 128], F32)
        identb = sb("identb", [128, 128], BF16)
        tri = sb("tri", [128, 128], BF16)
        onescol_b = sb("onescol_b", [128, 1], BF16)
        onescol_f = sb("onescol_f", [128, 1], F32)
        onesrow_f = sb("onesrow_f", [1, 128], F32)
        iota128 = sb("iota128", [128, 1], F32)
        wr_sb = sb("wr_sb", [128, 128], F32)
        acat = {p: sb("acat_" + p, [128, 1024], BF16) for p in "qkvo"}
        bcat1 = sb("bcat1", [128, D], BF16)
        bcat2 = sb("bcat2", [128, D], BF16)
        starts_sb = sb("starts_sb", [1, 4], I32)
        startsf = sb("startsf", [1, 4], F32)
        bval_sb = sb("bval_sb", [1, 2], F32)
        c0m_sb = sb("c0m_sb", [8, 1], F32)
        m64col = sb("m64col", [128, 1], F32)
        LOW1 = sb("LOW1", [128, S], BF16)
        LOW2 = sb("LOW2", [128, S], BF16)
        gates = sb("gates", [128, 128], BF16)
        logit_sb = sb("logit_sb", [128, 128], F32)
        Gsb = sb("Gsb", [8, D], F32)
        nrm = sb("nrm", [1, 8], F32)
        st1 = sb("st1", [128, 1], F32)
        st2 = sb("st2", [128, 1], F32)
        st3 = sb("st3", [128, 1], F32)
        st4 = sb("st4", [128, 1], F32)
        negB = sb("negB", [128, 1], F32)
        gsc = sb("gsc", [128, 96], F32)
        cst = sb("cst", [128, 4], F32)
        snap = sb("snap", [128, 64], F32) if "snap" in _DBG else None
        xnT = sb("xnT", [128, NDC * S], BF16)

        scA = es.enter_context(nc.psum_tensor("scA", [128, 1024], F32))
        scB = es.enter_context(nc.psum_tensor("scB", [128, 1024], F32))
        pb47 = [es.enter_context(nc.psum_tensor("pb%d" % i, [128, 512], F32))
                for i in range(4, 8)]

        def pbank(i):
            if i < 2:
                return scA[:, i * 512:(i + 1) * 512]
            if i < 4:
                return scB[:, (i - 2) * 512:(i - 1) * 512]
            return pb47[i - 4][:, :]
        pb = [pbank(i) for i in range(8)]
        _uid = [0]

        def usb(stack, n, shp, dt):
            _uid[0] += 1
            return stack.enter_context(
                nc.sbuf_tensor("u%d_%s" % (_uid[0], n), shp, dt))

        bank_state = {}
        buf_state = {}

        def bank_acq(i, eng="tensor"):
            if i in bank_state:
                sem, v = bank_state[i]
                P.wait(eng, sem, v)

        def bank_rel(i, sem, v):
            bank_state[i] = (sem, v)

        def buf_acq(name, eng):
            if name in buf_state:
                sem, v = buf_state[name]
                P.wait(eng, sem, v)

        def buf_rel(name, sem, v):
            buf_state[name] = (sem, v)

        def AP(t, off, dims):
            return bass.AP(t, off, dims)

        def phase_fence(engs=("gpsimd", "sync"), extra=()):
            for e in engs:
                P.wait(e, c_mm, c_mm.v)
                P.wait(e, c_ve, c_ve.v)
                P.wait(e, c_sc, c_sc.v)
                for sem, v in extra:
                    P.wait(e, sem, v)

        def vdrain():
            P.op("vector", "drain")

        def sdrain():
            P.op("scalar", "drain")

        n_pre = 0
        for name, dst in [("ropeC", ropeC), ("ropeS", ropeS), ("ident_f", identf),
                          ("ident_b", identb), ("tri_b", tri),
                          ("onescol_b", onescol_b), ("onescol_f", onescol_f),
                          ("onesrow_f", onesrow_f), ("iota128col", iota128),
                          ("wr_sb", wr_sb), ("bcat1", bcat1), ("bcat2", bcat2),
                          ("starts", starts_sb), ("bval", bval_sb), ("c0m", c0m_sb),
                          ("cvec", cst)]:
            P.op("sync", "dma_start", out=dst[:, :], in_=ins[name].ap(), inc=d_pre, inck=16)
            n_pre += 1
        for p in "qkvo":
            P.op("sync", "dma_start", out=acat[p][:, :], in_=ins["acat_" + p].ap(),
                 inc=d_pre, inck=16)
            n_pre += 1
        pre_v = d_pre.nxt(16 * n_pre)
        for e in ("vector", "scalar", "tensor", "gpsimd"):
            P.wait(e, d_pre, pre_v)

        P.op("vector", "tensor_copy", out=startsf[:, :], in_=starts_sb[:, :], inc=c_ve)
        v_startsf = c_ve.nxt()

        rdy = {}
        NH_LOADS = [0, 0]
        x_loads = [0, 0]
        x_free_v = [None, None]
        xC_free_v = [None, None]
        at_loads = [0, 0]
        at_free_v = [None, None]
        as_st = [0, 0]
        out_st = [0, 0]
        w_free_v = [None, None]

        for s in range(NSEQ):
            row0 = s * S
            esA = contextlib.ExitStack()
            with esA:
                xsb = [usb(esA, "xsb%d" % i, [128, D], F32)
                       for i in range(2)]
                sqx = usb(esA, "sqx", [128, D], F32)
                dsq = usb(esA, "dsq", [128, D], F32)
                xnt = [usb(esA, "xnt%d" % i, [128, D], F32)
                       for i in range(2)]
                xnf = [usb(esA, "xnf%d" % i, [128, 512], F32)
                       for i in range(2)]
                sumsq = usb(esA, "sumsq", [128, 1], F32)
                sumx = usb(esA, "sumx", [128, 1], F32)
                meanc = usb(esA, "meanc", [128, 1], F32)
                m2c = usb(esA, "m2c", [128, 1], F32)
                varc = usb(esA, "varc", [128, 1], F32)
                rstd = usb(esA, "rstd", [128, 1], F32)

                phase_fence(engs=("sync",),
                            extra=((d_out[0], 16 * out_st[0]),
                                   (d_out[1], 16 * out_st[1])))
                buf_acq("xnT", "vector")
                buf_acq("gates", "vector")
                buf_acq("logit_sb", "scalar")
                xnt_freeat = [0, 0]
                xnf_free_mm = [0, 0]
                xnf_free_ve = [0, 0]

                for tt in range(NTT):
                    p = tt % 2
                    if x_free_v[p] is not None:
                        P.wait("sync", c_ve if x_free_v[p][0] == "ve" else c_mm,
                               x_free_v[p][1])
                    P.op("sync", "dma_start", out=xsb[p][:, :],
                         in_=x_d.ap()[row0 + tt * 128: row0 + (tt + 1) * 128, :],
                         inc=d_x[p], inck=16)
                    x_loads[p] += 1
                    xv = 16 * x_loads[p]
                    P.wait("vector", d_x[p], xv)
                    P.op("vector", "reduce_sum", sumx[:, :], xsb[p][:, :], axis=AX)
                    vdrain()
                    P.op("vector", "tensor_scalar", out=meanc[:, :], in0=sumx[:, :],
                         scalar1=1.0 / D, scalar2=None, op0=OP.mult)
                    vdrain()
                    P.op("vector", "tensor_scalar", out=sqx[:, :], in0=xsb[p][:, :],
                         scalar1=meanc[:, :], scalar2=None, op0=OP.subtract,
                         inc=c_ve)
                    x_free_v[p] = ("ve", c_ve.nxt())
                    P.op("vector", "tensor_tensor", out=dsq[:, :], in0=sqx[:, :],
                         in1=sqx[:, :], op=OP.mult)
                    P.op("vector", "reduce_sum", sumsq[:, :], dsq[:, :], axis=AX)
                    vdrain()
                    P.op("vector", "tensor_scalar", out=varc[:, :], in0=sumsq[:, :],
                         scalar1=1.0 / D, scalar2=None, op0=OP.mult, inc=c_ve)
                    v_var = c_ve.nxt()
                    P.wait("scalar", c_ve, v_var)
                    P.op("scalar", "activation", st1[:, :], varc[:, :], ACT.Ln,
                         bias=cst[:, 0:1])
                    sdrain()
                    P.op("scalar", "activation", rstd[:, :], st1[:, :], ACT.Exp,
                         scale=-0.5, inc=c_sc)
                    v_rstd = c_sc.nxt()
                    P.wait("vector", c_sc, v_rstd)
                    P.wait("vector", c_mm, xnt_freeat[p])
                    P.op("vector", "tensor_scalar", out=xnt[p][:, :], in0=sqx[:, :],
                         scalar1=rstd[:, :], scalar2=None,
                         op0=OP.mult, inc=c_ve)
                    v_xn = c_ve.nxt()
                    if "snap" in _DBG and s == 0 and tt == 0:
                        P.op("vector", "tensor_copy", out=snap[:, 48:52],
                             in_=xnt[p][:, 0:4])
                        P.op("vector", "tensor_copy", out=snap[:, 52:53],
                             in_=rstd[:, :])
                        P.op("vector", "tensor_copy", out=snap[:, 53:54],
                             in_=varc[:, :])
                        P.op("vector", "tensor_copy", out=snap[:, 54:55],
                             in_=meanc[:, :])
                        P.op("vector", "tensor_copy", out=snap[:, 55:56],
                             in_=sumx[:, :])
                        P.op("vector", "tensor_copy", out=snap[:, 60:64],
                             in_=xsb[p][:, 0:4])
                    P.wait("tensor", c_ve, v_xn)
                    for g in range(4):
                        gp = g % 2
                        bank_acq(gp)
                        for k in range(4):
                            dc = 4 * g + k
                            P.op("tensor", "transpose", pb[gp][:, k * 128:(k + 1) * 128],
                                 xnt[p][:, dc * 128:(dc + 1) * 128], identf[:, :],
                                 inc=(c_mm if k == 3 else None))
                        v_tr = c_mm.nxt()
                        if g == 3:
                            xnt_freeat[p] = v_tr
                        P.wait("scalar", c_mm, v_tr)
                        P.wait("scalar", c_mm, xnf_free_mm[gp])
                        P.wait("scalar", c_ve, xnf_free_ve[gp])
                        P.op("scalar", "copy", xnf[gp][:, :], pb[gp][:, :], inc=c_sc)
                        v_ev = c_sc.nxt()
                        if "snap" in _DBG and s == 0 and tt == 0 and g == 0:
                            P.op("scalar", "copy", snap[:, 56:60], xnf[gp][:, 0:4])
                        bank_rel(gp, c_sc, v_ev)
                        P.wait("tensor", c_sc, v_ev)
                        if g == 0:
                            bank_acq(2)
                        for k in range(4):
                            dc = 4 * g + k
                            P.op("tensor", "matmul", pb[2][:, 0:8],
                                 xnf[gp][:, k * 128:(k + 1) * 128],
                                 wr_sb[:, dc * 8:(dc + 1) * 8],
                                 start=(dc == 0), stop=(dc == 15),
                                 inc=(c_mm if k == 3 else None))
                        v_rt = c_mm.nxt()
                        xnf_free_mm[gp] = v_rt
                        P.wait("vector", c_sc, v_ev)
                        P.op("vector", "tensor_copy",
                             out=AP(xnT, (4 * g) * S + tt * 128,
                                    [[NDC * S, 128], [S, 4], [1, 128]]),
                             in_=AP(xnf[gp], 0, [[512, 128], [128, 4], [1, 128]]),
                             inc=c_ve)
                        xnf_free_ve[gp] = c_ve.nxt()
                    P.wait("scalar", c_mm, c_mm.v)
                    P.op("scalar", "copy", logit_sb[:, tt * 8:(tt + 1) * 8],
                         pb[2][:, 0:8], inc=c_sc)
                    bank_rel(2, c_sc, c_sc.nxt())


                rdy[("xnT", s)] = c_ve.v
                v_logits = c_sc.v

                m1n = gsc[:, 0:16]
                m2n = gsc[:, 16:32]
                eb = usb(esA, "eb", [128, 128], F32)
                zb2 = usb(esA, "zb2", [128, 128], F32)
                indb = usb(esA, "indb", [128, 128], F32)
                l2b = usb(esA, "l2b", [128, 128], F32)
                rg = gsc[:, 48:64]
                P.wait("vector", c_sc, v_logits)
                for t2 in range(16):
                    P.op("vector", "reduce_max", m1n[:, t2:t2 + 1],
                         logit_sb[:, t2 * 8:(t2 + 1) * 8], axis=AX, negate=True)
                vdrain()
                for t2 in range(16):
                    P.op("vector", "tensor_scalar", out=zb2[:, t2 * 8:(t2 + 1) * 8],
                         in0=logit_sb[:, t2 * 8:(t2 + 1) * 8],
                         scalar1=m1n[:, t2:t2 + 1], scalar2=None, op0=OP.add)
                vdrain()
                P.op("vector", "tensor_single_scalar", indb[:, :], zb2[:, :], 0.0,
                     OP.is_equal)
                vdrain()
                P.op("vector", "scalar_tensor_tensor", out=l2b[:, :], in0=indb[:, :],
                     scalar=-1e30, in1=logit_sb[:, :], op0=OP.mult, op1=OP.add)
                vdrain()
                for t2 in range(16):
                    P.op("vector", "reduce_max", m2n[:, t2:t2 + 1],
                         l2b[:, t2 * 8:(t2 + 1) * 8], axis=AX, negate=True)
                vdrain()
                P.op("vector", "nop", inc=c_ve)
                v_g1 = c_ve.nxt()
                P.wait("scalar", c_ve, v_g1)
                P.op("scalar", "activation", eb[:, :], zb2[:, :], ACT.Exp,
                     inc=c_sc)
                v_g2 = c_sc.nxt()
                P.wait("vector", c_sc, v_g2)
                P.op("vector", "tensor_tensor", out=gsc[:, 64:80], in0=m1n,
                     in1=m2n, op=OP.subtract, inc=c_ve)
                v_g3 = c_ve.nxt()
                P.wait("scalar", c_ve, v_g3)
                P.op("scalar", "activation", gsc[:, 80:96], gsc[:, 64:80], ACT.Exp,
                     inc=c_sc)
                v_g4 = c_sc.nxt()
                P.wait("vector", c_sc, v_g4)
                P.op("vector", "tensor_scalar", out=gsc[:, 64:80],
                     in0=gsc[:, 80:96], scalar1=1.0, scalar2=None, op0=OP.add)
                vdrain()
                P.op("vector", "reciprocal", rg, gsc[:, 64:80])
                for t2 in range(16):
                    P.op("vector", "tensor_scalar", out=l2b[:, t2 * 8:(t2 + 1) * 8],
                         in0=logit_sb[:, t2 * 8:(t2 + 1) * 8],
                         scalar1=m2n[:, t2:t2 + 1], scalar2=None, op0=OP.add)
                vdrain()
                P.op("vector", "tensor_single_scalar", indb[:, :], l2b[:, :], 0.0,
                     OP.is_ge)
                vdrain()
                P.op("vector", "tensor_tensor", out=eb[:, :], in0=eb[:, :],
                     in1=indb[:, :], op=OP.mult)
                vdrain()
                for t2 in range(16):
                    P.op("vector", "tensor_scalar", out=gates[:, t2 * 8:(t2 + 1) * 8],
                         in0=eb[:, t2 * 8:(t2 + 1) * 8],
                         scalar1=rg[:, t2:t2 + 1], scalar2=None, op0=OP.mult,
                         inc=(c_ve if t2 == 15 else None))
                rdy[("gates", s)] = c_ve.nxt()
                if "gates" in _DBG and s == 0:
                    P.wait("sync", c_ve, rdy[("gates", s)])
                    P.op("sync", "dma_start", out=dbg["logits"].ap()[:, :],
                         in_=logit_sb[:, :], inc=d_pre, inck=16)
                    P.op("sync", "dma_start", out=dbg["gates"].ap()[:, :],
                         in_=gates[:, :], inc=d_pre, inck=16)
                    d_pre.nxt(32)
                if "xnT" in _DBG and s == 0:
                    P.wait("sync", c_ve, rdy[("xnT", s)])
                    P.op("sync", "dma_start", out=dbg["xnT"].ap()[:, :],
                         in_=xnT[:, :], inc=d_pre, inck=16)
                    d_pre.nxt(16)

            P.wait("vector", c_ve, v_startsf)
            buf_acq("LOW1", "vector")
            buf_acq("LOW2", "vector")
            P.op("vector", "tensor_scalar", out=gsc[0:1, 0:4],
                 in0=startsf[:, :], scalar1=bval_sb[0:1, s:s + 1], scalar2=None,
                 op0=OP.is_le)
            vdrain()
            P.op("vector", "reduce_sum", gsc[0:1, 4:5], gsc[0:1, 0:4], axis=AX)
            vdrain()
            P.op("vector", "tensor_scalar", out=gsc[0:1, 5:6], in0=gsc[0:1, 4:5],
                 scalar1=-1.0, scalar2=0.0, op0=OP.add, op1=OP.max)
            vdrain()
            P.op("vector", "tensor_scalar", out=gsc[0:1, 6:7], in0=gsc[0:1, 5:6],
                 scalar1=3.0, scalar2=None, op0=OP.min, inc=c_ve)
            v_ids = c_ve.nxt()
            P.wait("tensor", c_ve, v_ids)
            bank_acq(3)
            P.op("tensor", "matmul", pb[3][:, 0:1], onesrow_f[:, :], gsc[0:1, 6:7],
                 start=True, stop=True, inc=c_mm)
            v_idb = c_mm.nxt()
            P.wait("vector", c_mm, v_idb)
            P.op("vector", "tensor_copy", out=st2[:, :], in_=pb[3][:, 0:1], inc=c_ve)
            bank_rel(3, c_ve, c_ve.nxt())
            vdrain()
            P.op("vector", "tensor_scalar", out=m64col[:, :], in0=iota128[:, :],
                 scalar1=st2[:, :], scalar2=None, op0=OP.is_equal, inc=c_ve)
            v_m64 = c_ve.nxt()
            xnT_rdy = rdy[("xnT", s)]
            P.wait("tensor", c_ve, xnT_rdy)
            for pn, dstT, rlo in (("q", LOW1, 0), ("k", LOW1, 64), ("v", LOW2, 64)):
                for qc in range(NQC):
                    bi = 4 + (qc % 2)
                    bank = pb[bi]
                    bank_acq(bi)
                    for dc in range(NDC):
                        tp = (0, 64) if rlo == 64 else None
                        P.op("tensor", "matmul", bank[rlo:rlo + 64, 0:512],
                             acat[pn][:, dc * 64:(dc + 1) * 64],
                             xnT[:, dc * S + qc * 512: dc * S + qc * 512 + 512],
                             start=(dc == 0), stop=(dc == 15),
                             tile_position=tp,
                             inc=(c_mm if dc == 15 else None))
                    v_low = c_mm.nxt()
                    P.wait("vector", c_mm, v_low)
                    P.wait("vector", c_ve, v_m64)
                    P.op("vector", "tensor_scalar",
                         out=dstT[rlo:rlo + 64, qc * 512:(qc + 1) * 512],
                         in0=bank[rlo:rlo + 64, 0:512],
                         scalar1=m64col[rlo:rlo + 64, :], scalar2=None,
                         op0=OP.mult, inc=c_ve)
                    bank_rel(bi, c_ve, c_ve.nxt())
            rdy[("lows", s)] = c_ve.v
            if "low" in _DBG and s == 0:
                P.wait("sync", c_ve, rdy[("lows", s)])
                P.op("sync", "dma_start", out=dbg["low1"].ap()[:, :], in_=LOW1[:, :],
                     inc=d_pre, inck=16)
                d_pre.nxt(16)
            esB = contextlib.ExitStack()
            with esB:
                wsl = {pn: [usb(esB, "wsl_%s%d" % (pn, i), [128, D], BF16)
                    for i in range(2)] for pn in "qkv"}
                qraw = usb(esB, "qraw", [128, S], F32)
                kraw = usb(esB, "kraw", [128, S], F32)
                QT = usb(esB, "QT", [128, S], BF16)
                KT = usb(esB, "KT", [128, S], BF16)
                Vt = usb(esB, "Vt", [128, S], BF16)
                u1 = usb(esB, "u1", [128, S], F32)
                u2 = usb(esB, "u2", [128, S], F32)
                lowo_f = usb(esB, "lowo_f", [64, S], F32)
                stage = [usb(esB, "stage%d" % i, [128, 512], BF16) for i in range(2)]
                pt = [usb(esB, "pt%d" % i, [128, 1024], BF16) for i in range(2)]
                rb_sb = usb(esB, "rb_sb", [128, 512], F32)
                rcp_sb = usb(esB, "rcp_sb", [1, 512], F32)
                sq2 = [usb(esB, "sq2_%d" % i, [128, 512], F32) for i in range(2)]

                P.op("vector", "memset", lowo_f[:, :], 0.0, inc=c_ve)
                rdy[("lowo0", s)] = c_ve.nxt()
                phase_fence()
                wnames = {"q": "wqT", "k": "wkT", "v": "wvT"}
                rope_mask = []
                for i in range(16):
                    rope_mask += [2 * i + 1, 2 * i]

                pend = []

                def flush_pend():
                    while pend:
                        pend.pop(0)()

                for h in range(H):
                    hp = h % 2
                    if w_free_v[hp] is not None:
                        P.wait("gpsimd", c_mm, w_free_v[hp])
                    for pn in "qkv":
                        P.op("gpsimd", "dma_start", out=wsl[pn][hp][:, :],
                             in_=AP(ins[wnames[pn]], h * D * DKV,
                                    [[DKV, 128], [128 * DKV, 16], [1, 128]]),
                             inc=d_w[hp], inck=16)
                        NH_LOADS[hp] += 1
                    wv_ready = 16 * NH_LOADS[hp]

                    def proj(pn, b0, lorlo, bc, lowbuf):
                        P.wait("tensor", d_w[hp], wv_ready)
                        P.wait("tensor", c_ve, rdy[("lows", s)])
                        for qc in range(NQC):
                            bank_acq(b0 + qc)
                        for dc in range(NDC):
                            for qc in range(NQC):
                                P.op("tensor", "matmul", pb[b0 + qc],
                                     wsl[pn][hp][:, dc * 128:(dc + 1) * 128],
                                     xnT[:, dc * S + qc * 512: dc * S + qc * 512 + 512],
                                     start=(dc == 0), stop=False)
                        tp = (64, 0) if lorlo == 64 else None
                        for qc in range(NQC):
                            P.op("tensor", "matmul", pb[b0 + qc],
                                 bc[lorlo:lorlo + 64, h * 128:(h + 1) * 128],
                                 lowbuf[lorlo:lorlo + 64, qc * 512:(qc + 1) * 512],
                                 start=False, stop=True, tile_position=tp,
                                 inc=(c_mm if qc == 3 else None))
                        return c_mm.nxt()

                    def evac_norms(rawbuf, b0, ncol0, v_p):
                        for qc in range(NQC):
                            P.wait("scalar", c_mm, v_p)
                            buf_acq(rawbuf, "scalar")
                            P.op("scalar", "copy",
                                 locals_raw[rawbuf][:, qc * 512:(qc + 1) * 512],
                                 pb[b0 + qc])
                            buf_acq("sq%d" % (qc % 2), "scalar")
                            P.op("scalar", "activation", sq2[qc % 2][:, :],
                                 pb[b0 + qc], ACT.Square, inc=c_sc)
                            v_ev = c_sc.nxt()
                            P.wait("tensor", c_sc, v_ev)
                            P.op("tensor", "matmul", pb[b0 + qc][0:1, 0:512],
                                 onescol_f[:, :], sq2[qc % 2][:, :], start=True,
                                 stop=True, inc=c_mm)
                            v_nm = c_mm.nxt()
                            buf_rel("sq%d" % (qc % 2), c_mm, v_nm)
                            P.wait("vector", c_mm, v_nm)
                            P.op("vector", "reduce_max",
                                 nrm[0:1, ncol0 + qc:ncol0 + qc + 1],
                                 pb[b0 + qc][0:1, 0:512], axis=AX, inc=c_ve)
                            bank_rel(b0 + qc, c_ve, c_ve.nxt())
                        return c_sc.v

                    locals_raw = {"qraw": qraw, "kraw": kraw}

                    v_q = proj("q", 0, 0, bcat1, LOW1)
                    flush_pend()
                    v_qev = evac_norms("qraw", 0, 0, v_q)
                    v_k = proj("k", 4, 64, bcat1, LOW1)
                    v_kev = evac_norms("kraw", 4, 4, v_k)

                    v_rope = {}
                    for rawbuf, dstT, key, v_prod in ((qraw, QT, "q", v_qev),
                                                      (kraw, KT, "k", v_kev)):
                        P.wait("vector", c_sc, v_prod)
                        buf_acq("QT" if key == "q" else "KT", "vector")
                        P.op("vector", "tensor_tensor", out=u1[:, :], in0=rawbuf[:, :],
                             in1=ropeS[:, :], op=OP.mult)
                        P.op("vector", "stream_shuffle", out=u2[:, :], in_=u1[:, :],
                             mask=rope_mask)
                        P.op("vector", "tensor_tensor", out=u1[:, :], in0=rawbuf[:, :],
                             in1=ropeC[:, :], op=OP.mult)
                        P.op("vector", "tensor_tensor", out=dstT[:, :], in0=u2[:, :],
                             in1=u1[:, :], op=OP.add, inc=c_ve)
                        v_rope[key] = c_ve.nxt()
                    buf_rel("qraw", c_ve, v_rope["q"])
                    buf_rel("kraw", c_ve, v_rope["k"])

                    vdrain()
                    P.op("vector", "reduce_max", st1[0:1, 0:1], nrm[0:1, 0:4], axis=AX)
                    P.op("vector", "reduce_max", st2[0:1, 0:1], nrm[0:1, 4:8],
                         axis=AX, inc=c_ve)
                    v_nmx = c_ve.nxt()
                    P.wait("scalar", c_ve, v_nmx)
                    P.op("scalar", "activation", st3[0:1, 0:1], st1[0:1, 0:1],
                         ACT.Ln, bias=cst[0:1, 1:2])
                    P.op("scalar", "activation", st4[0:1, 0:1], st2[0:1, 0:1],
                         ACT.Ln, bias=cst[0:1, 1:2], inc=c_sc)
                    v_ln = c_sc.nxt()
                    P.wait("vector", c_sc, v_ln)
                    P.op("vector", "tensor_tensor", out=st3[0:1, 0:1],
                         in0=st3[0:1, 0:1], in1=st4[0:1, 0:1], op=OP.add, inc=c_ve)
                    v_add2 = c_ve.nxt()
                    P.wait("scalar", c_ve, v_add2)
                    sdrain()
                    P.op("scalar", "activation", st4[0:1, 0:1], st3[0:1, 0:1],
                         ACT.Exp, scale=0.5, bias=cst[0:1, 2:3], inc=c_sc)
                    v_B = c_sc.nxt()
                    P.wait("vector", c_sc, v_B)
                    vdrain()
                    P.op("vector", "tensor_scalar", out=st3[0:1, 0:1],
                         in0=st4[0:1, 0:1], scalar1=-1.0, scalar2=None, op0=OP.mult,
                         inc=c_ve)
                    v_nB = c_ve.nxt()

                    v_v = proj("v", 0, 64, bcat2, LOW2)
                    P.wait("scalar", c_ve, v_rope["q"])
                    for qc in range(NQC):
                        P.wait("scalar", c_mm, v_v)
                        P.op("scalar", "copy", qraw[:, qc * 512:(qc + 1) * 512],
                             pb[qc], inc=c_sc)
                        bank_rel(qc, c_sc, c_sc.nxt())
                    v_vev = c_sc.v
                    buf_acq("Vt", "vector")
                    for g in range(4):
                        bi = 4 + (g % 2)
                        P.wait("tensor", c_sc, v_vev)
                        bank_acq(bi)
                        for k in range(4):
                            tc = 4 * g + k
                            P.op("tensor", "transpose",
                                 pb[bi][:, k * 128:(k + 1) * 128],
                                 qraw[:, tc * 128:(tc + 1) * 128], identf[:, :],
                                 inc=(c_mm if k == 3 else None))
                        v_tr = c_mm.nxt()
                        P.wait("vector", c_mm, v_tr)
                        P.op("vector", "tensor_copy",
                             out=Vt[:, g * 512:(g + 1) * 512],
                             in_=pb[bi][:, 0:512],
                             inc=c_ve)
                        bank_rel(bi, c_ve, c_ve.nxt())
                    v_vt = c_ve.v
                    buf_rel("qraw", c_mm, c_mm.v)
                    w_free_v[hp] = c_mm.v

                    P.wait("tensor", c_ve, v_nB)
                    bank_acq(6)
                    P.op("tensor", "matmul", pb[6][:, 0:1], onesrow_f[:, :],
                         st3[0:1, 0:1], start=True, stop=True, inc=c_mm)
                    v_nBb = c_mm.nxt()
                    P.wait("scalar", c_mm, v_nBb)
                    P.op("scalar", "copy", negB[:, :], pb[6][:, 0:1], inc=c_sc)
                    v_negB = c_sc.nxt()
                    bank_rel(6, c_sc, v_negB)

                    for qc in range(NQC):
                        jmax = 4 * qc + 4
                        ob_i = 4 + (qc % 2)
                        ob = pb[ob_i]
                        dbk = pb[6]
                        npair = jmax // 2
                        for p_ in range(npair):
                            j0, j1 = 2 * p_, 2 * p_ + 1
                            ti = p_ % 2
                            T = scA if ti == 0 else scB
                            bank_acq(2 * ti)
                            bank_acq(2 * ti + 1)
                            P.wait("tensor", c_ve, v_rope["k"])
                            P.op("tensor", "matmul", T[:, 0:512],
                                 KT[:, j0 * 128:(j0 + 1) * 128],
                                 QT[:, qc * 512:(qc + 1) * 512],
                                 start=True, stop=True)
                            P.op("tensor", "matmul", T[:, 512:1024],
                                 KT[:, j1 * 128:(j1 + 1) * 128],
                                 QT[:, qc * 512:(qc + 1) * 512],
                                 start=True, stop=True, inc=c_mm)
                            v_mm1 = c_mm.nxt()
                            if p_ == 0:
                                flush_pend()
                                bank_acq(ob_i)
                                bank_acq(6)
                            P.wait("scalar", c_mm, v_mm1)
                            P.wait("scalar", c_sc, v_negB)
                            buf_acq("pt%d" % ti, "scalar")
                            P.op("scalar", "activation", pt[ti][:, :],
                                 T[:, 0:1024], ACT.Exp, bias=negB[:, :],
                                 scale=SCL, inc=c_sc)
                            v_exp = c_sc.nxt()
                            bank_rel(2 * ti, c_sc, v_exp)
                            bank_rel(2 * ti + 1, c_sc, v_exp)
                            v_msk = 0
                            for j in (j0, j1):
                                if j >= 4 * qc:
                                    m = j - 4 * qc
                                    half = (j % 2) * 512
                                    P.wait("vector", c_sc, v_exp)
                                    if m > 0:
                                        P.op("vector", "memset",
                                             pt[ti][:, half:half + m * 128], 0.0)
                                    P.op("vector", "tensor_tensor",
                                         out=pt[ti][:, half + m * 128:half + (m + 1) * 128],
                                         in0=pt[ti][:, half + m * 128:half + (m + 1) * 128],
                                         in1=tri[:, :], op=OP.mult, inc=c_ve)
                                    v_msk = c_ve.nxt()
                            P.wait("tensor", c_sc, v_exp)
                            if v_msk:
                                P.wait("tensor", c_ve, v_msk)
                            P.wait("tensor", c_ve, v_vt)
                            P.op("tensor", "matmul", ob,
                                 Vt[:, j0 * 128:(j0 + 1) * 128], pt[ti][:, 0:512],
                                 start=(j0 == 0), stop=False)
                            P.op("tensor", "matmul", ob,
                                 Vt[:, j1 * 128:(j1 + 1) * 128], pt[ti][:, 512:1024],
                                 start=False, stop=(j1 == jmax - 1))
                            P.op("tensor", "matmul", dbk[0:1, 0:512],
                                 onescol_b[:, :], pt[ti][:, 0:512],
                                 start=(j0 == 0), stop=False)
                            P.op("tensor", "matmul", dbk[0:1, 0:512],
                                 onescol_b[:, :], pt[ti][:, 512:1024],
                                 start=False, stop=(j1 == jmax - 1), inc=c_mm)
                            v_j = c_mm.nxt()
                            buf_rel("pt%d" % ti, c_mm, v_j)
                        v_pv = c_mm.v
                        sp = qc % 2
                        P.wait("vector", c_mm, v_pv)
                        buf_acq("rcp_sb", "vector")
                        P.op("vector", "reciprocal", rcp_sb[:, :],
                             dbk[0:1, 0:512], inc=c_ve)
                        v_rc = c_ve.nxt()
                        bank_rel(6, c_ve, v_rc)

                        def make_tail(qc=qc, sp=sp, ob=ob, ob_i=ob_i, v_rc=v_rc,
                                      h=h, s=s):
                            def tail():
                                P.wait("tensor", c_ve, v_rc)
                                bank_acq(7)
                                P.op("tensor", "matmul", pb[7], onesrow_f[:, :],
                                     rcp_sb[:, :], start=True, stop=True, inc=c_mm)
                                v_rbmm = c_mm.nxt()
                                buf_rel("rcp_sb", c_mm, v_rbmm)
                                P.wait("scalar", c_mm, v_rbmm)
                                buf_acq("rb_sb", "scalar")
                                P.op("scalar", "copy", rb_sb[:, :], pb[7], inc=c_sc)
                                v_rbev = c_sc.nxt()
                                bank_rel(7, c_sc, v_rbev)
                                P.wait("vector", c_sc, v_rbev)
                                P.wait("vector", d_as[sp], 16 * as_st[sp])
                                buf_acq("stage%d" % sp, "vector")
                                P.op("vector", "tensor_tensor", out=stage[sp][:, :],
                                     in0=ob, in1=rb_sb[:, :], op=OP.mult,
                                     inc=c_ve)
                                v_st = c_ve.nxt()
                                bank_rel(ob_i, c_ve, v_st)
                                buf_rel("rb_sb", c_ve, v_st)
                                P.wait("sync", c_ve, v_st)
                                P.op("sync", "dma_start",
                                     out=AP(attT_d, h * 128 * S + qc * 512,
                                            [[S, 128], [1, 512]]),
                                     in_=stage[sp][:, :], inc=d_as[sp], inck=16)
                                as_st[sp] += 1
                                P.wait("tensor", c_ve, v_st)
                                bank_acq(6)
                                P.op("tensor", "matmul", pb[6][0:64, 0:512],
                                     acat["o"][:, h * 64:(h + 1) * 64],
                                     stage[sp][:, :],
                                     start=True, stop=True, inc=c_mm)
                                v_ol = c_mm.nxt()
                                buf_rel("stage%d" % sp, c_mm, v_ol)
                                P.wait("vector", c_mm, v_ol)
                                P.wait("vector", c_ve, rdy[("lowo0", s)])
                                P.op("vector", "tensor_tensor",
                                     out=lowo_f[:, qc * 512:(qc + 1) * 512],
                                     in0=lowo_f[:, qc * 512:(qc + 1) * 512],
                                     in1=pb[6][0:64, 0:512], op=OP.add, inc=c_ve)
                                v_lo = c_ve.nxt()
                                bank_rel(6, c_ve, v_lo)
                            return tail
                        pend.append(make_tail())
                    buf_rel("QT", c_mm, c_mm.v)
                    buf_rel("KT", c_mm, c_mm.v)
                    buf_rel("Vt", c_mm, c_mm.v)
                flush_pend()
                P.wait("vector", c_mm, c_mm.v)
                P.op("vector", "tensor_scalar", out=LOW2[0:64, :],
                     in0=lowo_f[:, :], scalar1=m64col[0:64, :], scalar2=None,
                     op0=OP.mult, inc=c_ve)
                rdy[("lowo_bf", s)] = c_ve.nxt()
                buf_rel("lowo_f", c_ve, rdy[("lowo_bf", s)])
                if "low" in _DBG and s == 0:
                    P.wait("sync", c_ve, rdy[("lowo_bf", s)])
                    P.op("sync", "dma_start", out=dbg["low2"].ap()[:, :],
                         in_=LOW2[:, :], inc=d_pre, inck=16)
                    d_pre.nxt(16)

            esF = contextlib.ExitStack()
            with esF:
                w1sl = [usb(esF, "w1sl%d" % i, [128, D], BF16)
                        for i in range(2)]
                h1b = usb(esF, "h1b", [128, 16 * 1024], BF16)
                h2b = usb(esF, "h2b", [128, 16 * 1024], BF16)
                h2tok = usb(esF, "h2tok", [128, 2048], BF16)
                phase_fence(extra=((d_as[0], 16 * as_st[0]), (d_as[1], 16 * as_st[1])))

                for hf in range(2):
                    tok0 = hf * 1024
                    for ic in range(16):
                        icp = ic % 2
                        if w_free_v[icp] is not None:
                            P.wait("gpsimd", c_mm, w_free_v[icp])
                        P.op("gpsimd", "dma_start", out=w1sl[icp][:, :],
                             in_=AP(ins["w1T"], ic * D * 128,
                                    [[128, 128], [128 * 128, 16], [1, 128]]),
                             inc=d_w[icp], inck=16)
                        NH_LOADS[icp] += 1
                        wv_r = 16 * NH_LOADS[icp]
                        P.wait("tensor", d_w[icp], wv_r)
                        bank_acq(6)
                        bank_acq(7)
                        for dc in range(NDC):
                            for qcc in range(2):
                                col0 = tok0 + qcc * 512
                                P.op("tensor", "matmul", pb[6 + qcc],
                                     w1sl[icp][:, dc * 128:(dc + 1) * 128],
                                     xnT[:, dc * S + col0: dc * S + col0 + 512],
                                     start=(dc == 0), stop=(dc == 15),
                                     inc=(c_mm if (dc == 15 and qcc == 1) else None))
                        v_h = c_mm.nxt()
                        for qcc in range(2):
                            P.wait("scalar", c_mm, v_h)
                            P.op("scalar", "activation",
                                 h1b[:, ic * 1024 + qcc * 512: ic * 1024 + qcc * 512 + 512],
                                 pb[6 + qcc], ACT.Silu, inc=c_sc)
                            bank_rel(6 + qcc, c_sc, c_sc.nxt())
                        w_free_v[icp] = c_mm.v
                    v_h1_done = c_sc.v
                    for ic2 in range(16):
                        icp = ic2 % 2
                        if w_free_v[icp] is not None:
                            P.wait("gpsimd", c_mm, w_free_v[icp])
                        P.op("gpsimd", "dma_start", out=w1sl[icp][:, :],
                             in_=AP(ins["w3T"], ic2 * D * 128,
                                    [[128, 128], [128 * 128, 16], [1, 128]]),
                             inc=d_w[icp], inck=16)
                        NH_LOADS[icp] += 1
                        wv_r = 16 * NH_LOADS[icp]
                        P.wait("tensor", d_w[icp], wv_r)
                        P.wait("tensor", c_sc, v_h1_done)
                        bank_acq(6)
                        bank_acq(7)
                        for i1 in range(NDC):
                            for qcc in range(2):
                                P.op("tensor", "matmul", pb[6 + qcc],
                                     w1sl[icp][:, i1 * 128:(i1 + 1) * 128],
                                     h1b[:, i1 * 1024 + qcc * 512: i1 * 1024 + qcc * 512 + 512],
                                     start=(i1 == 0), stop=(i1 == 15),
                                     inc=(c_mm if (i1 == 15 and qcc == 1) else None))
                        v_h = c_mm.nxt()
                        for qcc in range(2):
                            P.wait("scalar", c_mm, v_h)
                            P.op("scalar", "activation",
                                 h2b[:, ic2 * 1024 + qcc * 512: ic2 * 1024 + qcc * 512 + 512],
                                 pb[6 + qcc], ACT.Silu, inc=c_sc)
                            bank_rel(6 + qcc, c_sc, c_sc.nxt())
                        w_free_v[icp] = c_mm.v
                    v_h2_done = c_sc.v
                    if "h2" in _DBG and s == 0 and hf == 0:
                        P.wait("sync", c_sc, v_h2_done)
                        P.op("sync", "dma_start", out=dbg["h1"].ap()[:, :],
                             in_=h1b[:, :], inc=d_pre, inck=16)
                        P.op("sync", "dma_start", out=dbg["h2"].ap()[:, :],
                             in_=h2b[:, :], inc=d_pre, inck=16)
                        d_pre.nxt(32)
                    P.wait("tensor", c_sc, v_h2_done)
                    P.wait("tensor", c_ve, rdy[("gates", s)])
                    for tc in range(8):
                        for g in range(4):
                            bi = g % 2
                            bank_acq(bi)
                            buf_acq("h2tok_g%d" % g, "vector")
                            for k in range(4):
                                i2 = 4 * g + k
                                P.op("tensor", "transpose",
                                     pb[bi][:, k * 64:(k + 1) * 64].bitcast(BF16),
                                     h2b[:, i2 * 1024 + tc * 128: i2 * 1024 + tc * 128 + 128],
                                     identb[:, :],
                                     inc=(c_mm if k == 3 else None))
                            v_tr = c_mm.nxt()
                            P.wait("vector", c_mm, v_tr)
                            P.op("vector", "tensor_copy",
                                 out=h2tok[:, g * 512:(g + 1) * 512],
                                 in_=pb[bi][:, 0:256].bitcast(BF16),
                                 inc=c_ve)
                            bank_rel(bi, c_ve, c_ve.nxt())
                        v_ht = c_ve.v
                        P.wait("tensor", c_ve, v_ht)
                        tglob = hf * 8 + tc
                        if hf == 0 and tc == 0:
                            for gq in range(4):
                                bank_acq(2 + gq)
                        for gq in range(4):
                            P.op("tensor", "matmul", pb[2 + gq][0:8, 0:512],
                                 gates[:, tglob * 8:(tglob + 1) * 8],
                                 h2tok[:, gq * 512:(gq + 1) * 512],
                                 start=(hf == 0 and tc == 0),
                                 stop=(hf == 1 and tc == 7),
                                 inc=(c_mm if gq == 3 else None))
                        v_gmm = c_mm.nxt()
                        for g in range(4):
                            buf_rel("h2tok_g%d" % g, c_mm, v_gmm)
                v_gmm_all = c_mm.v
                P.wait("vector", c_mm, v_gmm_all)
                for gq in range(4):
                    if s == 0:
                        P.op("vector", "tensor_copy",
                             out=Gsb[0:8, gq * 512:(gq + 1) * 512],
                             in_=pb[2 + gq][0:8, 0:512],
                             inc=(c_ve if gq == 3 else None))
                    else:
                        P.op("vector", "tensor_tensor",
                             out=Gsb[0:8, gq * 512:(gq + 1) * 512],
                             in0=Gsb[0:8, gq * 512:(gq + 1) * 512],
                             in1=pb[2 + gq][0:8, 0:512], op=OP.add,
                             inc=(c_ve if gq == 3 else None))
                v_gsb = c_ve.nxt()
                rdy[("Gsb", s)] = v_gsb
                for gq in range(4):
                    bank_rel(2 + gq, c_ve, v_gsb)
                buf_rel("xnT", c_mm, c_mm.v)
                buf_rel("gates", c_mm, c_mm.v)
                buf_rel("logit_sb", c_ve, rdy[("gates", s)])

            esC = contextlib.ExitStack()
            with esC:
                wo_sb = usb(esC, "wo_sb", [128, NDC * D], BF16)
                attld = [usb(esC, "attld%d" % i, [128, 2048], BF16)
                         for i in range(2)]
                xsbC = [usb(esC, "xsbC%d" % i, [128, D], F32)
                        for i in range(2)]
                outb = [usb(esC, "outb%d" % i, [128, 512], F32)
                        for i in range(2)]
                phase_fence()
                for dcg in range(4):
                    P.op("gpsimd", "dma_start",
                         out=AP(wo_sb, dcg * 4 * D, [[NDC * D, 128], [D, 4], [1, D]]),
                         in_=AP(ins["woT"], dcg * 4 * 128 * D,
                                [[D, 128], [128 * D, 4], [1, D]]),
                         inc=d_wo, inck=16)
                v_wo = d_wo.nxt(64)
                P.wait("tensor", d_wo, v_wo)
                P.wait("sync", d_as[0], 16 * as_st[0])
                P.wait("sync", d_as[1], 16 * as_st[1])
                for tt in range(NTT):
                    tp_ = tt % 2
                    if at_free_v[tp_] is not None:
                        P.wait("sync", c_mm, at_free_v[tp_])
                    P.op("sync", "dma_start",
                         out=attld[tp_][:, :],
                         in_=AP(attT_d, tt * 128,
                                [[S, 128], [128 * S, 16], [1, 128]]),
                         inc=d_at[tp_], inck=16)
                    at_loads[tp_] += 1
                    v_at = 16 * at_loads[tp_]
                    if xC_free_v[tp_] is not None:
                        P.wait("sync", c_ve, xC_free_v[tp_])
                    P.op("sync", "dma_start", out=xsbC[tp_][:, :],
                         in_=x_d.ap()[row0 + tt * 128: row0 + (tt + 1) * 128, :],
                         inc=d_x[tp_], inck=16)
                    x_loads[tp_] += 1
                    v_x = 16 * x_loads[tp_]
                    bgrp_i = [4 * tp_ + i for i in range(4)]
                    P.wait("tensor", d_at[tp_], v_at)
                    P.wait("tensor", c_ve, rdy[("lowo_bf", s)])
                    for bi in bgrp_i:
                        bank_acq(bi)
                    for dc in range(NDC):
                        for oci, bi in enumerate(bgrp_i):
                            P.op("tensor", "matmul", pb[bi][:, 0:512],
                                 attld[tp_][:, dc * 128:(dc + 1) * 128],
                                 wo_sb[:, dc * D + oci * 512: dc * D + oci * 512 + 512],
                                 start=(dc == 0), stop=False)
                    for oci, bi in enumerate(bgrp_i):
                        P.op("tensor", "matmul", pb[bi][:, 0:512],
                             LOW2[0:64, tt * 128:(tt + 1) * 128],
                             bcat2[0:64, oci * 512:(oci + 1) * 512],
                             start=False, stop=True,
                             inc=(c_mm if oci == 3 else None))
                    v_ao = c_mm.nxt()
                    at_free_v[tp_] = v_ao
                    P.wait("vector", c_mm, v_ao)
                    P.wait("vector", d_x[tp_], v_x)
                    for oci, bi in enumerate(bgrp_i):
                        po = oci % 2
                        P.wait("vector", d_out[po], 16 * out_st[po])
                        P.op("vector", "tensor_tensor",
                             out=outb[po][:, :],
                             in0=pb[bi][:, 0:512],
                             in1=xsbC[tp_][:, oci * 512:(oci + 1) * 512],
                             op=OP.add, inc=c_ve)
                        v_epi = c_ve.nxt()
                        bank_rel(bi, c_ve, v_epi)
                        P.wait("sync", c_ve, v_epi)
                        P.op("sync", "dma_start",
                             out=out_d.ap()[row0 + tt * 128: row0 + (tt + 1) * 128,
                                            oci * 512:(oci + 1) * 512],
                             in_=outb[po][:, :], inc=d_out[po], inck=16)
                        out_st[po] += 1
                    v_epi = c_ve.v
                    rdy[("epi", s, tt)] = v_epi
                    xC_free_v[tp_] = v_epi
                buf_rel("LOW1", c_mm, c_mm.v)
                buf_rel("LOW2", c_mm, c_mm.v)

        esS = contextlib.ExitStack()
        with esS:
            GT_sb = usb(esS, "GT_sb", [128, 128], BF16)
            GT_lo = usb(esS, "GT_lo", [128, 128], BF16)
            w2f = [usb(esS, "w2f%d" % i, [128, D], F32) for i in range(2)]
            w2lo = [usb(esS, "w2lo%d" % i, [128, D], BF16) for i in range(2)]
            w2sl = [usb(esS, "w2sl%d" % i, [128, D], BF16)
                    for i in range(2)]
            patch = usb(esS, "patch", [8, D], F32)
            Ssum = usb(esS, "Ssum", [8, D], F32)
            phase_fence(extra=((d_out[0], 16 * out_st[0]),
                               (d_out[1], 16 * out_st[1])))
            P.wait("tensor", c_ve, rdy[("Gsb", 1)])
            for g in range(4):
                bi = 4 + (g % 2)
                bank_acq(bi)
                for k in range(4):
                    ic = 4 * g + k
                    P.op("tensor", "transpose", pb[bi][:, k * 8:(k + 1) * 8],
                         Gsb[0:8, ic * 128:(ic + 1) * 128], identf[0:8, 0:8],
                         inc=(c_mm if k == 3 else None))
                v_tr = c_mm.nxt()
                P.wait("vector", c_mm, v_tr)
                P.op("vector", "tensor_copy",
                     out=GT_sb[:, (4 * g) * 8:(4 * g + 4) * 8],
                     in_=pb[bi][:, 0:32])
                vdrain()
                P.op("vector", "tensor_tensor",
                     out=GT_lo[:, (4 * g) * 8:(4 * g + 4) * 8],
                     in0=pb[bi][:, 0:32],
                     in1=GT_sb[:, (4 * g) * 8:(4 * g + 4) * 8],
                     op=OP.subtract,
                     inc=c_ve)
                bank_rel(bi, c_ve, c_ve.nxt())
            v_gt = c_ve.v
            for ic in range(16):
                icp = ic % 2
                if w_free_v[icp] is not None:
                    P.wait("gpsimd", c_mm, w_free_v[icp])
                P.wait("gpsimd", c_ve, rdy.get(("w2lo", ic - 2), 0))
                P.op("gpsimd", "dma_start", out=w2sl[icp][:, :],
                     in_=ins["w2T"].ap()[ic * 128:(ic + 1) * 128, :],
                     inc=d_w[icp], inck=16)
                P.op("gpsimd", "dma_start", out=w2f[icp][:, :],
                     in_=ins["w2T"].ap()[ic * 128:(ic + 1) * 128, :],
                     inc=d_w[icp], inck=16)
                NH_LOADS[icp] += 2
                wv_r = 16 * NH_LOADS[icp]
                P.wait("vector", d_w[icp], wv_r)
                P.op("vector", "tensor_tensor", out=w2lo[icp][:, :],
                     in0=w2f[icp][:, :], in1=w2sl[icp][:, :], op=OP.subtract,
                     inc=c_ve)
                rdy[("w2lo", ic)] = c_ve.nxt()
                P.wait("tensor", d_w[icp], wv_r)
                P.wait("tensor", c_ve, rdy[("w2lo", ic)])
                P.wait("tensor", c_ve, v_gt)
                if ic == 0:
                    for oc in range(4):
                        bank_acq(oc)
                for oc in range(4):
                    P.op("tensor", "matmul", pb[oc][0:8, 0:512],
                         GT_sb[:, ic * 8:(ic + 1) * 8],
                         w2sl[icp][:, oc * 512:(oc + 1) * 512],
                         start=(ic == 0), stop=False)
                    P.op("tensor", "matmul", pb[oc][0:8, 0:512],
                         GT_lo[:, ic * 8:(ic + 1) * 8],
                         w2sl[icp][:, oc * 512:(oc + 1) * 512],
                         start=False, stop=False)
                    P.op("tensor", "matmul", pb[oc][0:8, 0:512],
                         GT_sb[:, ic * 8:(ic + 1) * 8],
                         w2lo[icp][:, oc * 512:(oc + 1) * 512],
                         start=False, stop=(ic == 15),
                         inc=(c_mm if oc == 3 else None))
                c_mm.nxt()
                w_free_v[icp] = c_mm.v
            v_smm = c_mm.v
            P.wait("vector", c_mm, v_smm)
            for oc in range(4):
                P.op("vector", "tensor_copy",
                     out=Ssum[0:8, oc * 512:(oc + 1) * 512],
                     in_=pb[oc][0:8, 0:512], inc=(c_ve if oc == 3 else None))
            v_S = c_ve.nxt()
            if "G" in _DBG:
                P.wait("sync", c_ve, v_S)
                P.op("sync", "dma_start", out=dbg["G"].ap()[:, :], in_=Gsb[0:8, :],
                     inc=d_pre, inck=16)
                d_pre.nxt(16)
            P.wait("gpsimd", c_ve, v_S)
            P.op("gpsimd", "dma_start", out=cc_in.ap()[:, :], in_=Ssum[0:8, :],
                 inc=d_cc, inck=16)
            v_cc1 = d_cc.nxt(16)
            P.wait("gpsimd", d_cc, v_cc1)
            P.op("gpsimd", "collective_compute", "AllReduce", OP.add,
                 replica_groups=[list(range(NCORES))],
                 ins=[cc_in.ap().opt()], outs=[cc_out.ap().opt()], inc=c_cc)
            v_ccd = c_cc.nxt()
            P.wait("gpsimd", c_cc, v_ccd)
            P.op("gpsimd", "dma_start", out=Ssum[0:8, :], in_=cc_out.ap()[:, :],
                 inc=d_cc, inck=16)
            v_cc2 = d_cc.nxt(16)
            P.wait("vector", d_cc, v_cc2)
            P.op("vector", "tensor_scalar", out=Ssum[0:8, :], in0=Ssum[0:8, :],
                 scalar1=c0m_sb[:, :], scalar2=None, op0=OP.mult, inc=c_ve)
            v_Sm = c_ve.nxt()
            if "S" in _DBG:
                P.wait("sync", c_ve, v_Sm)
                P.op("sync", "dma_start", out=dbg["S"].ap()[:, :], in_=Ssum[0:8, :],
                     inc=d_pre, inck=16)
                d_pre.nxt(16)
            P.wait("sync", d_out[0], 16 * out_st[0])
            P.wait("sync", d_out[1], 16 * out_st[1])
            P.op("sync", "dma_start", out=patch[:, :], in_=out_d.ap()[0:8, :],
                 inc=d_pt, inck=16)
            v_cc3 = d_pt.nxt(16)
            P.wait("vector", d_pt, v_cc3)
            P.wait("vector", c_ve, v_Sm)
            P.op("vector", "tensor_tensor", out=patch[:, :], in0=patch[:, :],
                 in1=Ssum[0:8, :], op=OP.add, inc=c_ve)
            v_pt = c_ve.nxt()
            P.wait("sync", c_ve, v_pt)
            P.op("sync", "dma_start", out=out_d.ap()[0:8, :], in_=patch[:, :],
                 inc=d_cc, inck=16)
            v_cc4 = d_cc.nxt(16)
            if "snap" in _DBG:
                P.op("sync", "dma_start", out=dbg["snap"].ap()[:, :],
                     in_=snap[:, :], inc=d_pre, inck=16)
                d_pre.nxt(16)
            P.wait("sync", d_cc, v_cc4)
            P.wait("sync", d_pre, d_pre.v)

        block = es.enter_context(nc.Block())
        block.sync(lambda e: P.replay("sync", e))
        block.gpsimd(lambda e: P.replay("gpsimd", e))
        block.vector(lambda e: P.replay("vector", e))
        block.scalar(lambda e: P.replay("scalar", e))
        block.tensor(lambda e: P.replay("tensor", e))
    return nc


_nc_cache = {}


def kernel(**inputs):
    if "nc" not in _nc_cache:
        _nc_cache["nc"] = build_nc()
    nc = _nc_cache["nc"]
    g = _host_prep(inputs)
    x = np.ascontiguousarray(np.asarray(inputs["L0_attn_x"], dtype=np.float32))
    in_maps = []
    for c in range(NCORES):
        m = dict(g)
        m["x"] = np.ascontiguousarray(x[2 * c:2 * c + 2].reshape(NSEQ * S, D))
        m["bval"] = _f32(np.array([[2 * c, 2 * c + 1]]))
        m["c0m"] = _f32(np.full((8, 1), 1.0 if c == 0 else 0.0))
        in_maps.append(m)
    res = run_bass_kernel_spmd(nc, in_maps, core_ids=list(range(NCORES)))
    if _debug_capture is not None:
        _debug_capture.append(res)
    out = np.stack([np.asarray(res.results[c]["out"]).reshape(NSEQ, S, D)
                    for c in range(NCORES)])
    return np.ascontiguousarray(out.reshape(16, S, D).astype(np.float32))


# version 06
# speedup vs baseline: 1.1388x; 1.1388x over previous
import contextlib
import numpy as np
import ml_dtypes
import concourse.bass as bass
import concourse.mybir as mybir
from concourse.bass_utils import run_bass_kernel_spmd

F32 = mybir.dt.float32
BF16 = mybir.dt.bfloat16
I32 = mybir.dt.int32
OP = mybir.AluOpType
ACT = mybir.ActivationFunctionType
AX = mybir.AxisListType.X

NCORES = 8
NSEQ = 2
S = 2048
D = 2048
H = 16
DKV = 128
NTT = 16
NDC = 16
NQC = 4
SCL = float(1.0 / np.sqrt(np.float32(128.0)))
ROPE_THETA = 10000.0

_DBG = set()
_debug_capture = None


class Sem:
    def __init__(self, h):
        self.h = h
        self.v = 0

    def nxt(self, k=1):
        self.v += k
        return self.v


class Prog:
    ENGS = ("sync", "gpsimd", "vector", "scalar", "tensor")

    def __init__(self):
        self.s = {e: [] for e in self.ENGS}

    def op(self, eng, meth, *a, inc=None, inck=1, **kw):
        if (inc is not None and eng in ("vector", "scalar")
                and meth not in ("nop", "wait_ge", "drain")):
            self.s[eng].append((meth, a, kw, None, 0))
            self.s[eng].append(("drain", (), {}, None, 0))
            self.s[eng].append(("nop", (), {}, inc, inck))
        else:
            self.s[eng].append((meth, a, kw, inc, inck))

    def wait(self, eng, sem, v):
        if v > 0:
            self.s[eng].append(("wait_ge", (sem.h, v), {}, None, 0))

    def replay(self, eng_name, e):
        for meth, a, kw, inc, inck in self.s[eng_name]:
            inst = getattr(e, meth)(*a, **kw)
            if inc is not None:
                inst.then_inc(inc.h, inck)


def _bf(x):
    return np.ascontiguousarray(np.asarray(x).astype(ml_dtypes.bfloat16))


def _f32(x):
    return np.ascontiguousarray(np.asarray(x).astype(np.float32))


def _host_constants():
    c = {}
    c["ident_f"] = _f32(np.eye(128))
    c["ident_b"] = _bf(np.eye(128))
    r = np.arange(128)
    c["tri_b"] = _bf((r[None, :] >= r[:, None]).astype(np.float32))
    c["onescol_b"] = _bf(np.ones((128, 1)))
    c["onescol_f"] = _f32(np.ones((128, 1)))
    c["onesrow_f"] = _f32(np.ones((1, 128)))
    c["iota128col"] = _f32(((np.arange(128) % 64) // 16).reshape(128, 1))
    pos = np.arange(S, dtype=np.float64)
    freq = ROPE_THETA ** (-(np.arange(64, dtype=np.float64)) / 64.0)
    ang = pos[None, :] * freq[:, None]
    cosv = np.cos(ang)
    sinv = np.sin(ang)
    ctab = np.empty((128, S), dtype=np.float64)
    stab = np.empty((128, S), dtype=np.float64)
    ctab[0::2] = cosv
    ctab[1::2] = cosv
    stab[0::2] = sinv
    stab[1::2] = -sinv
    c["ropeC"] = _f32(ctab)
    c["ropeS"] = _f32(stab)
    cv = np.zeros((128, 4))
    cv[:, 0] = 1e-5
    cv[:, 1] = 1e-30
    cv[:, 2] = float(np.log(1.0 / np.sqrt(np.float32(128.0))) + np.log(1.03))
    cv[:, 3] = 1.0
    c["cvec"] = _f32(cv)
    return c


def _host_prep(inputs):
    g = {}
    g["wqT"] = _f32(inputs["L0_attn_Wq"].T.reshape(D, H, DKV).transpose(1, 0, 2))
    g["wkT"] = _f32(inputs["L0_attn_Wk"].T.reshape(D, H, DKV).transpose(1, 0, 2))
    g["wvT"] = _f32(inputs["L0_attn_Wv"].T.reshape(D, H, DKV).transpose(1, 0, 2))
    g["woT"] = _f32(np.asarray(inputs["L0_attn_Wo"]).T)
    g["w1T"] = _f32(inputs["L0_ffn_w1"].T.reshape(D, 16, 128).transpose(1, 0, 2))
    g["w3T"] = _f32(inputs["L0_ffn_w3"].T.reshape(D, 16, 128).transpose(1, 0, 2))
    g["w2T"] = _f32(np.asarray(inputs["L0_ffn_w2"]).T)
    g["wr_sb"] = _f32(np.asarray(inputs["L0_ffn_wr"]).T.reshape(16, 128, 8)
                      .transpose(1, 0, 2).reshape(128, 128))
    for p, W in (("q", "L0_attn_Aq"), ("k", "L0_attn_Ak"),
                 ("v", "L0_attn_Av"), ("o", "L0_attn_Ao")):
        acat = np.asarray(inputs[W]).transpose(2, 0, 1).reshape(D, 64)
        g["acat_" + p] = _bf(acat.reshape(16, 128, 64).transpose(1, 0, 2)
                             .reshape(128, 1024))
    bq = np.asarray(inputs["L0_attn_Bq"]).transpose(0, 2, 1).reshape(64, D)
    bk = np.asarray(inputs["L0_attn_Bk"]).transpose(0, 2, 1).reshape(64, D)
    bv = np.asarray(inputs["L0_attn_Bv"]).transpose(0, 2, 1).reshape(64, D)
    bo = np.asarray(inputs["L0_attn_Bo"]).transpose(0, 2, 1).reshape(64, D)
    g["bcat1"] = _bf(np.concatenate([bq, bk], axis=0))
    g["bcat2"] = _bf(np.concatenate([bo, bv], axis=0))
    g["starts"] = np.ascontiguousarray(
        np.asarray(inputs["L0_attn_lora_start_indices"]).astype(np.int32).reshape(1, 4))
    g.update(_host_constants())
    return g


def build_nc():
    nc = bass.Bass("TRN2", target_bir_lowering=False)
    P = Prog()

    x_d = nc.dram_tensor("x", [NSEQ * S, D], F32, kind="ExternalInput")
    out_d = nc.dram_tensor("out", [NSEQ * S, D], F32, kind="ExternalOutput")
    ins = {}
    for name, shp, dt in [
        ("wqT", [H, D, DKV], F32), ("wkT", [H, D, DKV], F32),
        ("wvT", [H, D, DKV], F32), ("woT", [D, D], F32),
        ("w1T", [16, D, 128], F32), ("w3T", [16, D, 128], F32),
        ("w2T", [D, D], F32), ("wr_sb", [128, 128], F32),
        ("acat_q", [128, 1024], BF16), ("acat_k", [128, 1024], BF16),
        ("acat_v", [128, 1024], BF16), ("acat_o", [128, 1024], BF16),
        ("bcat1", [128, D], BF16), ("bcat2", [128, D], BF16),
        ("starts", [1, 4], I32),
        ("ident_f", [128, 128], F32), ("ident_b", [128, 128], BF16),
        ("tri_b", [128, 128], BF16),
        ("onescol_b", [128, 1], BF16), ("onescol_f", [128, 1], F32),
        ("onesrow_f", [1, 128], F32), ("iota128col", [128, 1], F32),
        ("ropeC", [128, S], F32), ("ropeS", [128, S], F32),
        ("bval", [1, 2], F32), ("c0m", [8, 1], F32), ("cvec", [128, 4], F32),
    ]:
        ins[name] = nc.dram_tensor(name, shp, dt, kind="ExternalInput")
    attT_d = nc.dram_tensor("attT", [H, 128, S], BF16,
                            kind="ExternalOutput" if "attT" in _DBG else "Internal")
    cc_in = nc.dram_tensor("cc_in", [8, D], F32)
    cc_out = nc.dram_tensor("cc_out", [8, D], F32)
    dbg = {}
    if "xnT" in _DBG:
        dbg["xnT"] = nc.dram_tensor("dbg_xnT", [128, NDC * S], BF16, kind="ExternalOutput")
    if "gates" in _DBG:
        dbg["logits"] = nc.dram_tensor("dbg_logits", [128, 128], F32, kind="ExternalOutput")
        dbg["gates"] = nc.dram_tensor("dbg_gates", [128, 128], BF16, kind="ExternalOutput")
    if "qkv" in _DBG:
        for n in ("qraw", "QT", "KT", "V"):
            dbg[n] = nc.dram_tensor("dbg_" + n, [128, S], BF16, kind="ExternalOutput")
        dbg["negB"] = nc.dram_tensor("dbg_negB", [128, 1], F32, kind="ExternalOutput")
    if "low" in _DBG:
        dbg["low1"] = nc.dram_tensor("dbg_low1", [128, S], BF16, kind="ExternalOutput")
        dbg["low2"] = nc.dram_tensor("dbg_low2", [128, S], BF16, kind="ExternalOutput")
    if "h2" in _DBG:
        dbg["h1"] = nc.dram_tensor("dbg_h1", [128, 16 * 1024], BF16, kind="ExternalOutput")
        dbg["h2"] = nc.dram_tensor("dbg_h2", [128, 16 * 1024], BF16, kind="ExternalOutput")
    if "G" in _DBG:
        dbg["G"] = nc.dram_tensor("dbg_G", [8, D], F32, kind="ExternalOutput")
    if "S" in _DBG:
        dbg["S"] = nc.dram_tensor("dbg_S", [8, D], F32, kind="ExternalOutput")
    if "snap" in _DBG:
        dbg["snap"] = nc.dram_tensor("dbg_snap", [128, 64], F32, kind="ExternalOutput")

    es = contextlib.ExitStack()
    with es:
        sems = {}
        for n in ("d_pre", "d_x0", "d_x1", "d_w0", "d_w1", "d_w2", "d_wo", "d_at0",
                  "d_at1", "d_as0", "d_as1", "d_out0", "d_out1", "d_cc", "d_pt",
                  "c_mm", "c_sc", "c_ve", "c_cc",
                  "f_x0", "f_x1", "f_w0", "f_w1", "f_at0", "f_at1"):
            sems[n] = Sem(es.enter_context(nc.semaphore(n)))
        d_pre = sems["d_pre"]
        d_x = [sems["d_x0"], sems["d_x1"]]
        d_w = [sems["d_w0"], sems["d_w1"], sems["d_w2"]]
        d_wo = sems["d_wo"]
        d_at = [sems["d_at0"], sems["d_at1"]]
        d_as = [sems["d_as0"], sems["d_as1"]]
        d_out = [sems["d_out0"], sems["d_out1"]]
        d_cc = sems["d_cc"]
        d_pt = sems["d_pt"]
        c_mm = sems["c_mm"]
        c_sc = sems["c_sc"]
        c_ve = sems["c_ve"]
        c_cc = sems["c_cc"]
        f_x = [sems["f_x0"], sems["f_x1"]]
        f_w = [sems["f_w0"], sems["f_w1"]]
        f_at = [sems["f_at0"], sems["f_at1"]]

        sb = lambda n, shp, dt: es.enter_context(nc.sbuf_tensor("s_" + n, shp, dt))
        ropeC = sb("ropeC", [128, S], F32)
        ropeS = sb("ropeS", [128, S], F32)
        identf = sb("identf", [128, 128], F32)
        identb = sb("identb", [128, 128], BF16)
        tri = sb("tri", [128, 128], BF16)
        onescol_b = sb("onescol_b", [128, 1], BF16)
        onescol_f = sb("onescol_f", [128, 1], F32)
        onesrow_f = sb("onesrow_f", [1, 128], F32)
        iota128 = sb("iota128", [128, 1], F32)
        wr_sb = sb("wr_sb", [128, 128], F32)
        acat = {p: sb("acat_" + p, [128, 1024], BF16) for p in "qkvo"}
        bcat1 = sb("bcat1", [128, D], BF16)
        bcat2 = sb("bcat2", [128, D], BF16)
        starts_sb = sb("starts_sb", [1, 4], I32)
        startsf = sb("startsf", [1, 4], F32)
        bval_sb = sb("bval_sb", [1, 2], F32)
        c0m_sb = sb("c0m_sb", [8, 1], F32)
        m64col = sb("m64col", [128, 1], F32)
        LOW1 = sb("LOW1", [128, S], BF16)
        LOW2 = sb("LOW2", [128, S], BF16)
        gates = sb("gates", [128, 128], BF16)
        logit_sb = sb("logit_sb", [128, 128], F32)
        Gsb = sb("Gsb", [8, D], F32)
        nrm = sb("nrm", [1, 8], F32)
        st1 = sb("st1", [128, 1], F32)
        st2 = sb("st2", [128, 1], F32)
        st3 = sb("st3", [128, 1], F32)
        st4 = sb("st4", [128, 1], F32)
        negB = sb("negB", [128, 1], F32)
        gsc = sb("gsc", [128, 96], F32)
        cst = sb("cst", [128, 4], F32)
        snap = sb("snap", [128, 64], F32) if "snap" in _DBG else None
        xnT = sb("xnT", [128, NDC * S], BF16)

        scA = es.enter_context(nc.psum_tensor("scA", [128, 1024], F32))
        scB = es.enter_context(nc.psum_tensor("scB", [128, 1024], F32))
        pb47 = [es.enter_context(nc.psum_tensor("pb%d" % i, [128, 512], F32))
                for i in range(4, 8)]

        def pbank(i):
            if i < 2:
                return scA[:, i * 512:(i + 1) * 512]
            if i < 4:
                return scB[:, (i - 2) * 512:(i - 1) * 512]
            return pb47[i - 4][:, :]
        pb = [pbank(i) for i in range(8)]
        _uid = [0]

        def usb(stack, n, shp, dt):
            _uid[0] += 1
            return stack.enter_context(
                nc.sbuf_tensor("u%d_%s" % (_uid[0], n), shp, dt))

        bank_state = {}
        buf_state = {}

        def bank_acq(i, eng="tensor"):
            if i in bank_state:
                sem, v = bank_state[i]
                P.wait(eng, sem, v)

        def bank_rel(i, sem, v):
            bank_state[i] = (sem, v)

        def buf_acq(name, eng):
            if name in buf_state:
                sem, v = buf_state[name]
                P.wait(eng, sem, v)

        def buf_rel(name, sem, v):
            buf_state[name] = (sem, v)

        def AP(t, off, dims):
            return bass.AP(t, off, dims)

        def phase_fence(engs=("gpsimd", "sync"), extra=()):
            for e in engs:
                P.wait(e, c_mm, c_mm.v)
                P.wait(e, c_ve, c_ve.v)
                P.wait(e, c_sc, c_sc.v)
                for sem, v in extra:
                    P.wait(e, sem, v)

        def vdrain():
            P.op("vector", "drain")

        def sdrain():
            P.op("scalar", "drain")

        n_pre = 0
        for name, dst in [("ropeC", ropeC), ("ropeS", ropeS), ("ident_f", identf),
                          ("ident_b", identb), ("tri_b", tri),
                          ("onescol_b", onescol_b), ("onescol_f", onescol_f),
                          ("onesrow_f", onesrow_f), ("iota128col", iota128),
                          ("wr_sb", wr_sb), ("bcat1", bcat1), ("bcat2", bcat2),
                          ("starts", starts_sb), ("bval", bval_sb), ("c0m", c0m_sb),
                          ("cvec", cst)]:
            P.op("sync", "dma_start", out=dst[:, :], in_=ins[name].ap(), inc=d_pre, inck=16)
            n_pre += 1
        for p in "qkvo":
            P.op("sync", "dma_start", out=acat[p][:, :], in_=ins["acat_" + p].ap(),
                 inc=d_pre, inck=16)
            n_pre += 1
        pre_v = d_pre.nxt(16 * n_pre)
        for e in ("vector", "scalar", "tensor", "gpsimd"):
            P.wait(e, d_pre, pre_v)

        P.op("vector", "tensor_copy", out=startsf[:, :], in_=starts_sb[:, :], inc=c_ve)
        v_startsf = c_ve.nxt()

        rdy = {}
        NH_LOADS = [0, 0, 0]
        x_loads = [0, 0]
        x_free_v = [None, None]
        xC_free_v = [None, None]
        at_loads = [0, 0]
        at_free_v = [None, None]
        as_st = [0, 0]
        out_st = [0, 0]
        w_free_v = [None, None, None]

        for s in range(NSEQ):
            row0 = s * S
            esA = contextlib.ExitStack()
            with esA:
                xsb = [usb(esA, "xsb%d" % i, [128, D], F32)
                       for i in range(2)]
                sqx = usb(esA, "sqx", [128, D], F32)
                dsq = usb(esA, "dsq", [128, D], F32)
                xnt = [usb(esA, "xnt%d" % i, [128, D], F32)
                       for i in range(2)]
                xnf = [usb(esA, "xnf%d" % i, [128, 512], F32)
                       for i in range(2)]
                sumsq = usb(esA, "sumsq", [128, 1], F32)
                sumx = usb(esA, "sumx", [128, 1], F32)
                meanc = usb(esA, "meanc", [128, 1], F32)
                m2c = usb(esA, "m2c", [128, 1], F32)
                varc = usb(esA, "varc", [128, 1], F32)
                rstd = usb(esA, "rstd", [128, 1], F32)

                phase_fence(engs=("sync",),
                            extra=((d_out[0], 16 * out_st[0]),
                                   (d_out[1], 16 * out_st[1])))
                buf_acq("xnT", "vector")
                buf_acq("gates", "vector")
                buf_acq("logit_sb", "scalar")
                xnt_freeat = [0, 0]
                xnf_free_mm = [0, 0]
                xnf_free_ve = [0, 0]

                for tt in range(NTT):
                    p = tt % 2
                    if x_free_v[p] is not None:
                        P.wait("sync", c_ve if x_free_v[p][0] == "ve" else c_mm,
                               x_free_v[p][1])
                    P.op("sync", "dma_start", out=xsb[p][:, :],
                         in_=x_d.ap()[row0 + tt * 128: row0 + (tt + 1) * 128, :],
                         inc=d_x[p], inck=16)
                    x_loads[p] += 1
                    xv = 16 * x_loads[p]
                    P.wait("vector", d_x[p], xv)
                    P.op("vector", "reduce_sum", sumx[:, :], xsb[p][:, :], axis=AX)
                    vdrain()
                    P.op("vector", "tensor_scalar", out=meanc[:, :], in0=sumx[:, :],
                         scalar1=1.0 / D, scalar2=None, op0=OP.mult)
                    vdrain()
                    P.op("vector", "tensor_scalar", out=sqx[:, :], in0=xsb[p][:, :],
                         scalar1=meanc[:, :], scalar2=None, op0=OP.subtract,
                         inc=c_ve)
                    x_free_v[p] = ("ve", c_ve.nxt())
                    P.op("vector", "tensor_tensor", out=dsq[:, :], in0=sqx[:, :],
                         in1=sqx[:, :], op=OP.mult)
                    P.op("vector", "reduce_sum", sumsq[:, :], dsq[:, :], axis=AX)
                    vdrain()
                    P.op("vector", "tensor_scalar", out=varc[:, :], in0=sumsq[:, :],
                         scalar1=1.0 / D, scalar2=None, op0=OP.mult, inc=c_ve)
                    v_var = c_ve.nxt()
                    P.wait("scalar", c_ve, v_var)
                    P.op("scalar", "activation", st1[:, :], varc[:, :], ACT.Ln,
                         bias=cst[:, 0:1])
                    sdrain()
                    P.op("scalar", "activation", rstd[:, :], st1[:, :], ACT.Exp,
                         scale=-0.5, inc=c_sc)
                    v_rstd = c_sc.nxt()
                    P.wait("vector", c_sc, v_rstd)
                    P.wait("vector", c_mm, xnt_freeat[p])
                    P.op("vector", "tensor_scalar", out=xnt[p][:, :], in0=sqx[:, :],
                         scalar1=rstd[:, :], scalar2=None,
                         op0=OP.mult, inc=c_ve)
                    v_xn = c_ve.nxt()
                    if "snap" in _DBG and s == 0 and tt == 0:
                        P.op("vector", "tensor_copy", out=snap[:, 48:52],
                             in_=xnt[p][:, 0:4])
                        P.op("vector", "tensor_copy", out=snap[:, 52:53],
                             in_=rstd[:, :])
                        P.op("vector", "tensor_copy", out=snap[:, 53:54],
                             in_=varc[:, :])
                        P.op("vector", "tensor_copy", out=snap[:, 54:55],
                             in_=meanc[:, :])
                        P.op("vector", "tensor_copy", out=snap[:, 55:56],
                             in_=sumx[:, :])
                        P.op("vector", "tensor_copy", out=snap[:, 60:64],
                             in_=xsb[p][:, 0:4])
                    P.wait("tensor", c_ve, v_xn)
                    for g in range(4):
                        gp = g % 2
                        bank_acq(gp)
                        for k in range(4):
                            dc = 4 * g + k
                            P.op("tensor", "transpose", pb[gp][:, k * 128:(k + 1) * 128],
                                 xnt[p][:, dc * 128:(dc + 1) * 128], identf[:, :],
                                 inc=(c_mm if k == 3 else None))
                        v_tr = c_mm.nxt()
                        if g == 3:
                            xnt_freeat[p] = v_tr
                        P.wait("scalar", c_mm, v_tr)
                        P.wait("scalar", c_mm, xnf_free_mm[gp])
                        P.wait("scalar", c_ve, xnf_free_ve[gp])
                        P.op("scalar", "copy", xnf[gp][:, :], pb[gp][:, :], inc=c_sc)
                        v_ev = c_sc.nxt()
                        if "snap" in _DBG and s == 0 and tt == 0 and g == 0:
                            P.op("scalar", "copy", snap[:, 56:60], xnf[gp][:, 0:4])
                        bank_rel(gp, c_sc, v_ev)
                        P.wait("tensor", c_sc, v_ev)
                        if g == 0:
                            bank_acq(2)
                        for k in range(4):
                            dc = 4 * g + k
                            P.op("tensor", "matmul", pb[2][:, 0:8],
                                 xnf[gp][:, k * 128:(k + 1) * 128],
                                 wr_sb[:, dc * 8:(dc + 1) * 8],
                                 start=(dc == 0), stop=(dc == 15),
                                 inc=(c_mm if k == 3 else None))
                        v_rt = c_mm.nxt()
                        xnf_free_mm[gp] = v_rt
                        P.wait("vector", c_sc, v_ev)
                        P.op("vector", "tensor_copy",
                             out=AP(xnT, (4 * g) * S + tt * 128,
                                    [[NDC * S, 128], [S, 4], [1, 128]]),
                             in_=AP(xnf[gp], 0, [[512, 128], [128, 4], [1, 128]]),
                             inc=c_ve)
                        xnf_free_ve[gp] = c_ve.nxt()
                    P.wait("scalar", c_mm, c_mm.v)
                    P.op("scalar", "copy", logit_sb[:, tt * 8:(tt + 1) * 8],
                         pb[2][:, 0:8], inc=c_sc)
                    bank_rel(2, c_sc, c_sc.nxt())


                rdy[("xnT", s)] = c_ve.v
                v_logits = c_sc.v

                m1n = gsc[:, 0:16]
                m2n = gsc[:, 16:32]
                eb = usb(esA, "eb", [128, 128], F32)
                zb2 = usb(esA, "zb2", [128, 128], F32)
                indb = usb(esA, "indb", [128, 128], F32)
                l2b = usb(esA, "l2b", [128, 128], F32)
                rg = gsc[:, 48:64]
                P.wait("vector", c_sc, v_logits)
                for t2 in range(16):
                    P.op("vector", "reduce_max", m1n[:, t2:t2 + 1],
                         logit_sb[:, t2 * 8:(t2 + 1) * 8], axis=AX, negate=True)
                vdrain()
                for t2 in range(16):
                    P.op("vector", "tensor_scalar", out=zb2[:, t2 * 8:(t2 + 1) * 8],
                         in0=logit_sb[:, t2 * 8:(t2 + 1) * 8],
                         scalar1=m1n[:, t2:t2 + 1], scalar2=None, op0=OP.add)
                vdrain()
                P.op("vector", "tensor_single_scalar", indb[:, :], zb2[:, :], 0.0,
                     OP.is_equal)
                vdrain()
                P.op("vector", "scalar_tensor_tensor", out=l2b[:, :], in0=indb[:, :],
                     scalar=-1e30, in1=logit_sb[:, :], op0=OP.mult, op1=OP.add)
                vdrain()
                for t2 in range(16):
                    P.op("vector", "reduce_max", m2n[:, t2:t2 + 1],
                         l2b[:, t2 * 8:(t2 + 1) * 8], axis=AX, negate=True)
                vdrain()
                P.op("vector", "nop", inc=c_ve)
                v_g1 = c_ve.nxt()
                P.wait("scalar", c_ve, v_g1)
                P.op("scalar", "activation", eb[:, :], zb2[:, :], ACT.Exp,
                     inc=c_sc)
                v_g2 = c_sc.nxt()
                P.wait("vector", c_sc, v_g2)
                P.op("vector", "tensor_tensor", out=gsc[:, 64:80], in0=m1n,
                     in1=m2n, op=OP.subtract, inc=c_ve)
                v_g3 = c_ve.nxt()
                P.wait("scalar", c_ve, v_g3)
                P.op("scalar", "activation", gsc[:, 80:96], gsc[:, 64:80], ACT.Exp,
                     inc=c_sc)
                v_g4 = c_sc.nxt()
                P.wait("vector", c_sc, v_g4)
                P.op("vector", "tensor_scalar", out=gsc[:, 64:80],
                     in0=gsc[:, 80:96], scalar1=1.0, scalar2=None, op0=OP.add)
                vdrain()
                P.op("vector", "reciprocal", rg, gsc[:, 64:80])
                for t2 in range(16):
                    P.op("vector", "tensor_scalar", out=l2b[:, t2 * 8:(t2 + 1) * 8],
                         in0=logit_sb[:, t2 * 8:(t2 + 1) * 8],
                         scalar1=m2n[:, t2:t2 + 1], scalar2=None, op0=OP.add)
                vdrain()
                P.op("vector", "tensor_single_scalar", indb[:, :], l2b[:, :], 0.0,
                     OP.is_ge)
                vdrain()
                P.op("vector", "tensor_tensor", out=eb[:, :], in0=eb[:, :],
                     in1=indb[:, :], op=OP.mult)
                vdrain()
                for t2 in range(16):
                    P.op("vector", "tensor_scalar", out=gates[:, t2 * 8:(t2 + 1) * 8],
                         in0=eb[:, t2 * 8:(t2 + 1) * 8],
                         scalar1=rg[:, t2:t2 + 1], scalar2=None, op0=OP.mult,
                         inc=(c_ve if t2 == 15 else None))
                rdy[("gates", s)] = c_ve.nxt()
                if "gates" in _DBG and s == 0:
                    P.wait("sync", c_ve, rdy[("gates", s)])
                    P.op("sync", "dma_start", out=dbg["logits"].ap()[:, :],
                         in_=logit_sb[:, :], inc=d_pre, inck=16)
                    P.op("sync", "dma_start", out=dbg["gates"].ap()[:, :],
                         in_=gates[:, :], inc=d_pre, inck=16)
                    d_pre.nxt(32)
                if "xnT" in _DBG and s == 0:
                    P.wait("sync", c_ve, rdy[("xnT", s)])
                    P.op("sync", "dma_start", out=dbg["xnT"].ap()[:, :],
                         in_=xnT[:, :], inc=d_pre, inck=16)
                    d_pre.nxt(16)

            P.wait("vector", c_ve, v_startsf)
            buf_acq("LOW1", "vector")
            buf_acq("LOW2", "vector")
            P.op("vector", "tensor_scalar", out=gsc[0:1, 0:4],
                 in0=startsf[:, :], scalar1=bval_sb[0:1, s:s + 1], scalar2=None,
                 op0=OP.is_le)
            vdrain()
            P.op("vector", "reduce_sum", gsc[0:1, 4:5], gsc[0:1, 0:4], axis=AX)
            vdrain()
            P.op("vector", "tensor_scalar", out=gsc[0:1, 5:6], in0=gsc[0:1, 4:5],
                 scalar1=-1.0, scalar2=0.0, op0=OP.add, op1=OP.max)
            vdrain()
            P.op("vector", "tensor_scalar", out=gsc[0:1, 6:7], in0=gsc[0:1, 5:6],
                 scalar1=3.0, scalar2=None, op0=OP.min, inc=c_ve)
            v_ids = c_ve.nxt()
            P.wait("tensor", c_ve, v_ids)
            bank_acq(3)
            P.op("tensor", "matmul", pb[3][:, 0:1], onesrow_f[:, :], gsc[0:1, 6:7],
                 start=True, stop=True, inc=c_mm)
            v_idb = c_mm.nxt()
            P.wait("vector", c_mm, v_idb)
            P.op("vector", "tensor_copy", out=st2[:, :], in_=pb[3][:, 0:1], inc=c_ve)
            bank_rel(3, c_ve, c_ve.nxt())
            vdrain()
            P.op("vector", "tensor_scalar", out=m64col[:, :], in0=iota128[:, :],
                 scalar1=st2[:, :], scalar2=None, op0=OP.is_equal, inc=c_ve)
            v_m64 = c_ve.nxt()
            xnT_rdy = rdy[("xnT", s)]
            P.wait("tensor", c_ve, xnT_rdy)
            for pn, dstT, rlo in (("q", LOW1, 0), ("k", LOW1, 64), ("v", LOW2, 64)):
                for qc in range(NQC):
                    bi = 4 + (qc % 2)
                    bank = pb[bi]
                    bank_acq(bi)
                    for dc in range(NDC):
                        tp = (0, 64) if rlo == 64 else None
                        P.op("tensor", "matmul", bank[rlo:rlo + 64, 0:512],
                             acat[pn][:, dc * 64:(dc + 1) * 64],
                             xnT[:, dc * S + qc * 512: dc * S + qc * 512 + 512],
                             start=(dc == 0), stop=(dc == 15),
                             tile_position=tp,
                             inc=(c_mm if dc == 15 else None))
                    v_low = c_mm.nxt()
                    P.wait("vector", c_mm, v_low)
                    P.wait("vector", c_ve, v_m64)
                    P.op("vector", "tensor_scalar",
                         out=dstT[rlo:rlo + 64, qc * 512:(qc + 1) * 512],
                         in0=bank[rlo:rlo + 64, 0:512],
                         scalar1=m64col[rlo:rlo + 64, :], scalar2=None,
                         op0=OP.mult, inc=c_ve)
                    bank_rel(bi, c_ve, c_ve.nxt())
            rdy[("lows", s)] = c_ve.v
            if "low" in _DBG and s == 0:
                P.wait("sync", c_ve, rdy[("lows", s)])
                P.op("sync", "dma_start", out=dbg["low1"].ap()[:, :], in_=LOW1[:, :],
                     inc=d_pre, inck=16)
                d_pre.nxt(16)
            esB = contextlib.ExitStack()
            with esB:
                wsl = {pn: [usb(esB, "wsl_%s%d" % (pn, i), [128, D], BF16)
                    for i in range(2)] for pn in "qkv"}
                qraw = usb(esB, "qraw", [128, S], F32)
                kraw = usb(esB, "kraw", [128, S], F32)
                QT = usb(esB, "QT", [128, S], BF16)
                KT = usb(esB, "KT", [128, S], BF16)
                Vt = usb(esB, "Vt", [128, S], BF16)
                u1 = usb(esB, "u1", [128, S], F32)
                u2 = usb(esB, "u2", [128, S], F32)
                lowo_f = usb(esB, "lowo_f", [64, S], F32)
                stage = [usb(esB, "stage%d" % i, [128, 512], BF16) for i in range(2)]
                pt = [usb(esB, "pt%d" % i, [128, 1024], BF16) for i in range(2)]
                rb_sb = usb(esB, "rb_sb", [128, 512], F32)
                rcp_sb = usb(esB, "rcp_sb", [1, 512], F32)
                rscr = usb(esB, "rscr", [1, 512], F32)
                sq2 = [usb(esB, "sq2_%d" % i, [128, 512], F32) for i in range(2)]

                P.op("vector", "memset", lowo_f[:, :], 0.0, inc=c_ve)
                rdy[("lowo0", s)] = c_ve.nxt()
                phase_fence()
                wnames = {"q": "wqT", "k": "wkT", "v": "wvT"}
                rope_mask = []
                for i in range(16):
                    rope_mask += [2 * i + 1, 2 * i]

                pend = []

                def flush_pend():
                    while pend:
                        pend.pop(0)()

                for h in range(H):
                    hp = h % 2
                    if w_free_v[hp] is not None:
                        P.wait("gpsimd", c_mm, w_free_v[hp])
                    for pn in "qkv":
                        P.op("gpsimd", "dma_start", out=wsl[pn][hp][:, :],
                             in_=AP(ins[wnames[pn]], h * D * DKV,
                                    [[DKV, 128], [128 * DKV, 16], [1, 128]]),
                             inc=d_w[hp], inck=16)
                        NH_LOADS[hp] += 1
                    wv_ready = 16 * NH_LOADS[hp]

                    def proj(pn, b0, lorlo, bc, lowbuf):
                        P.wait("tensor", d_w[hp], wv_ready)
                        P.wait("tensor", c_ve, rdy[("lows", s)])
                        for qc in range(NQC):
                            bank_acq(b0 + qc)
                        for dc in range(NDC):
                            for qc in range(NQC):
                                P.op("tensor", "matmul", pb[b0 + qc],
                                     wsl[pn][hp][:, dc * 128:(dc + 1) * 128],
                                     xnT[:, dc * S + qc * 512: dc * S + qc * 512 + 512],
                                     start=(dc == 0), stop=False)
                        tp = (64, 0) if lorlo == 64 else None
                        for qc in range(NQC):
                            P.op("tensor", "matmul", pb[b0 + qc],
                                 bc[lorlo:lorlo + 64, h * 128:(h + 1) * 128],
                                 lowbuf[lorlo:lorlo + 64, qc * 512:(qc + 1) * 512],
                                 start=False, stop=True, tile_position=tp,
                                 inc=(c_mm if qc == 3 else None))
                        return c_mm.nxt()

                    def evac_norms(rawbuf, b0, ncol0, v_p):
                        for qc in range(NQC):
                            P.wait("scalar", c_mm, v_p)
                            buf_acq(rawbuf, "scalar")
                            P.op("scalar", "copy",
                                 locals_raw[rawbuf][:, qc * 512:(qc + 1) * 512],
                                 pb[b0 + qc])
                            buf_acq("sq%d" % (qc % 2), "scalar")
                            P.op("scalar", "activation", sq2[qc % 2][:, :],
                                 pb[b0 + qc], ACT.Square, inc=c_sc)
                            v_ev = c_sc.nxt()
                            P.wait("tensor", c_sc, v_ev)
                            P.op("tensor", "matmul", pb[b0 + qc][0:1, 0:512],
                                 onescol_f[:, :], sq2[qc % 2][:, :], start=True,
                                 stop=True, inc=c_mm)
                            v_nm = c_mm.nxt()
                            buf_rel("sq%d" % (qc % 2), c_mm, v_nm)
                            P.wait("vector", c_mm, v_nm)
                            P.op("vector", "reduce_max",
                                 nrm[0:1, ncol0 + qc:ncol0 + qc + 1],
                                 pb[b0 + qc][0:1, 0:512], axis=AX, inc=c_ve)
                            bank_rel(b0 + qc, c_ve, c_ve.nxt())
                        return c_sc.v

                    locals_raw = {"qraw": qraw, "kraw": kraw}

                    v_q = proj("q", 0, 0, bcat1, LOW1)
                    flush_pend()
                    v_qev = evac_norms("qraw", 0, 0, v_q)
                    v_k = proj("k", 4, 64, bcat1, LOW1)
                    v_kev = evac_norms("kraw", 4, 4, v_k)

                    v_rope = {}
                    for rawbuf, dstT, key, v_prod in ((qraw, QT, "q", v_qev),
                                                      (kraw, KT, "k", v_kev)):
                        P.wait("vector", c_sc, v_prod)
                        buf_acq("QT" if key == "q" else "KT", "vector")
                        P.op("vector", "tensor_tensor", out=u1[:, :], in0=rawbuf[:, :],
                             in1=ropeS[:, :], op=OP.mult)
                        P.op("vector", "stream_shuffle", out=u2[:, :], in_=u1[:, :],
                             mask=rope_mask)
                        P.op("vector", "tensor_tensor", out=u1[:, :], in0=rawbuf[:, :],
                             in1=ropeC[:, :], op=OP.mult)
                        P.op("vector", "tensor_tensor", out=dstT[:, :], in0=u2[:, :],
                             in1=u1[:, :], op=OP.add, inc=c_ve)
                        v_rope[key] = c_ve.nxt()
                    buf_rel("qraw", c_ve, v_rope["q"])
                    buf_rel("kraw", c_ve, v_rope["k"])

                    vdrain()
                    P.op("vector", "reduce_max", st1[0:1, 0:1], nrm[0:1, 0:4], axis=AX)
                    P.op("vector", "reduce_max", st2[0:1, 0:1], nrm[0:1, 4:8],
                         axis=AX, inc=c_ve)
                    v_nmx = c_ve.nxt()
                    P.wait("scalar", c_ve, v_nmx)
                    P.op("scalar", "activation", st3[0:1, 0:1], st1[0:1, 0:1],
                         ACT.Ln, bias=cst[0:1, 1:2])
                    P.op("scalar", "activation", st4[0:1, 0:1], st2[0:1, 0:1],
                         ACT.Ln, bias=cst[0:1, 1:2], inc=c_sc)
                    v_ln = c_sc.nxt()
                    P.wait("vector", c_sc, v_ln)
                    P.op("vector", "tensor_tensor", out=st3[0:1, 0:1],
                         in0=st3[0:1, 0:1], in1=st4[0:1, 0:1], op=OP.add, inc=c_ve)
                    v_add2 = c_ve.nxt()
                    P.wait("scalar", c_ve, v_add2)
                    sdrain()
                    P.op("scalar", "activation", st4[0:1, 0:1], st3[0:1, 0:1],
                         ACT.Exp, scale=0.5, bias=cst[0:1, 2:3], inc=c_sc)
                    v_B = c_sc.nxt()
                    P.wait("vector", c_sc, v_B)
                    vdrain()
                    P.op("vector", "tensor_scalar", out=st3[0:1, 0:1],
                         in0=st4[0:1, 0:1], scalar1=-1.0, scalar2=None, op0=OP.mult,
                         inc=c_ve)
                    v_nB = c_ve.nxt()

                    v_v = proj("v", 0, 64, bcat2, LOW2)
                    P.wait("scalar", c_ve, v_rope["q"])
                    for qc in range(NQC):
                        P.wait("scalar", c_mm, v_v)
                        P.op("scalar", "copy", qraw[:, qc * 512:(qc + 1) * 512],
                             pb[qc], inc=c_sc)
                        bank_rel(qc, c_sc, c_sc.nxt())
                    v_vev = c_sc.v
                    buf_acq("Vt", "vector")
                    for g in range(4):
                        bi = 4 + (g % 2)
                        P.wait("tensor", c_sc, v_vev)
                        bank_acq(bi)
                        for k in range(4):
                            tc = 4 * g + k
                            P.op("tensor", "transpose",
                                 pb[bi][:, k * 128:(k + 1) * 128],
                                 qraw[:, tc * 128:(tc + 1) * 128], identf[:, :],
                                 inc=(c_mm if k == 3 else None))
                        v_tr = c_mm.nxt()
                        P.wait("vector", c_mm, v_tr)
                        P.op("vector", "tensor_copy",
                             out=Vt[:, g * 512:(g + 1) * 512],
                             in_=pb[bi][:, 0:512],
                             inc=c_ve)
                        bank_rel(bi, c_ve, c_ve.nxt())
                    v_vt = c_ve.v
                    buf_rel("qraw", c_mm, c_mm.v)
                    w_free_v[hp] = c_mm.v

                    P.wait("tensor", c_ve, v_nB)
                    bank_acq(6)
                    P.op("tensor", "matmul", pb[6][:, 0:1], onesrow_f[:, :],
                         st3[0:1, 0:1], start=True, stop=True, inc=c_mm)
                    v_nBb = c_mm.nxt()
                    P.wait("scalar", c_mm, v_nBb)
                    P.op("scalar", "copy", negB[:, :], pb[6][:, 0:1], inc=c_sc)
                    v_negB = c_sc.nxt()
                    bank_rel(6, c_sc, v_negB)

                    for qc in range(NQC):
                        jmax = 4 * qc + 4
                        ob_i = 4 + (qc % 2)
                        ob = pb[ob_i]
                        dbk = pb[6]
                        npair = jmax // 2

                        def emit_mm1(p_):
                            j0, j1 = 2 * p_, 2 * p_ + 1
                            ti = p_ % 2
                            T = scA if ti == 0 else scB
                            bank_acq(2 * ti)
                            bank_acq(2 * ti + 1)
                            P.wait("tensor", c_ve, v_rope["k"])
                            P.op("tensor", "matmul", T[:, 0:512],
                                 KT[:, j0 * 128:(j0 + 1) * 128],
                                 QT[:, qc * 512:(qc + 1) * 512],
                                 start=True, stop=True)
                            P.op("tensor", "matmul", T[:, 512:1024],
                                 KT[:, j1 * 128:(j1 + 1) * 128],
                                 QT[:, qc * 512:(qc + 1) * 512],
                                 start=True, stop=True, inc=c_mm)
                            v_mm1 = c_mm.nxt()
                            P.wait("scalar", c_mm, v_mm1)
                            P.wait("scalar", c_sc, v_negB)
                            buf_acq("pt%d" % ti, "scalar")
                            P.op("scalar", "activation", pt[ti][:, :],
                                 T[:, 0:1024], ACT.Exp, bias=negB[:, :],
                                 scale=SCL, inc=c_sc)
                            v_exp = c_sc.nxt()
                            bank_rel(2 * ti, c_sc, v_exp)
                            bank_rel(2 * ti + 1, c_sc, v_exp)
                            v_msk = 0
                            for j in (j0, j1):
                                if j >= 4 * qc:
                                    m = j - 4 * qc
                                    half = (j % 2) * 512
                                    P.wait("vector", c_sc, v_exp)
                                    if m > 0:
                                        P.op("vector", "memset",
                                             pt[ti][:, half:half + m * 128], 0.0)
                                    P.op("vector", "tensor_tensor",
                                         out=pt[ti][:, half + m * 128:half + (m + 1) * 128],
                                         in0=pt[ti][:, half + m * 128:half + (m + 1) * 128],
                                         in1=tri[:, :], op=OP.mult, inc=c_ve)
                                    v_msk = c_ve.nxt()
                            return v_exp, v_msk

                        def emit_mm23(p_, v_exp, v_msk):
                            j0, j1 = 2 * p_, 2 * p_ + 1
                            ti = p_ % 2
                            P.wait("tensor", c_sc, v_exp)
                            if v_msk:
                                P.wait("tensor", c_ve, v_msk)
                            P.wait("tensor", c_ve, v_vt)
                            P.op("tensor", "matmul", ob,
                                 Vt[:, j0 * 128:(j0 + 1) * 128], pt[ti][:, 0:512],
                                 start=(j0 == 0), stop=False)
                            P.op("tensor", "matmul", ob,
                                 Vt[:, j1 * 128:(j1 + 1) * 128], pt[ti][:, 512:1024],
                                 start=False, stop=(j1 == jmax - 1))
                            P.op("tensor", "matmul", dbk[0:1, 0:512],
                                 onescol_b[:, :], pt[ti][:, 0:512],
                                 start=(j0 == 0), stop=False)
                            P.op("tensor", "matmul", dbk[0:1, 0:512],
                                 onescol_b[:, :], pt[ti][:, 512:1024],
                                 start=False, stop=(j1 == jmax - 1), inc=c_mm)
                            buf_rel("pt%d" % ti, c_mm, c_mm.nxt())

                        prev = None
                        for p_ in range(npair):
                            e_m = emit_mm1(p_)
                            if p_ == 0:
                                flush_pend()
                                bank_acq(ob_i)
                                bank_acq(6)
                            if prev is not None:
                                emit_mm23(p_ - 1, *prev)
                            prev = e_m
                        emit_mm23(npair - 1, *prev)
                        v_pv = c_mm.v
                        sp = qc % 2
                        P.wait("vector", c_mm, v_pv)
                        buf_acq("rcp_sb", "vector")
                        P.op("vector", "reciprocal", rcp_sb[:, :],
                             dbk[0:1, 0:512], inc=c_ve)
                        v_rc = c_ve.nxt()
                        bank_rel(6, c_ve, v_rc)

                        def make_tail(qc=qc, sp=sp, ob=ob, ob_i=ob_i, v_rc=v_rc,
                                      h=h, s=s):
                            def tail():
                                P.wait("tensor", c_ve, v_rc)
                                bank_acq(7)
                                P.op("tensor", "matmul", pb[7], onesrow_f[:, :],
                                     rcp_sb[:, :], start=True, stop=True, inc=c_mm)
                                v_rbmm = c_mm.nxt()
                                buf_rel("rcp_sb", c_mm, v_rbmm)
                                P.wait("scalar", c_mm, v_rbmm)
                                buf_acq("rb_sb", "scalar")
                                P.op("scalar", "copy", rb_sb[:, :], pb[7], inc=c_sc)
                                v_rbev = c_sc.nxt()
                                bank_rel(7, c_sc, v_rbev)
                                P.wait("vector", c_sc, v_rbev)
                                P.wait("vector", d_as[sp], 16 * as_st[sp])
                                buf_acq("stage%d" % sp, "vector")
                                P.op("vector", "tensor_tensor", out=stage[sp][:, :],
                                     in0=ob, in1=rb_sb[:, :], op=OP.mult,
                                     inc=c_ve)
                                v_st = c_ve.nxt()
                                bank_rel(ob_i, c_ve, v_st)
                                buf_rel("rb_sb", c_ve, v_st)
                                P.wait("sync", c_ve, v_st)
                                P.op("sync", "dma_start",
                                     out=AP(attT_d, h * 128 * S + qc * 512,
                                            [[S, 128], [1, 512]]),
                                     in_=stage[sp][:, :], inc=d_as[sp], inck=16)
                                as_st[sp] += 1
                                P.wait("tensor", c_ve, v_st)
                                bank_acq(6)
                                P.op("tensor", "matmul", pb[6][0:64, 0:512],
                                     acat["o"][:, h * 64:(h + 1) * 64],
                                     stage[sp][:, :],
                                     start=True, stop=True, inc=c_mm)
                                v_ol = c_mm.nxt()
                                buf_rel("stage%d" % sp, c_mm, v_ol)
                                P.wait("vector", c_mm, v_ol)
                                P.wait("vector", c_ve, rdy[("lowo0", s)])
                                P.op("vector", "tensor_tensor",
                                     out=lowo_f[:, qc * 512:(qc + 1) * 512],
                                     in0=lowo_f[:, qc * 512:(qc + 1) * 512],
                                     in1=pb[6][0:64, 0:512], op=OP.add, inc=c_ve)
                                v_lo = c_ve.nxt()
                                bank_rel(6, c_ve, v_lo)
                            return tail
                        pend.append(make_tail())
                    buf_rel("QT", c_mm, c_mm.v)
                    buf_rel("KT", c_mm, c_mm.v)
                    buf_rel("Vt", c_mm, c_mm.v)
                flush_pend()
                P.wait("vector", c_mm, c_mm.v)
                P.op("vector", "tensor_scalar", out=LOW2[0:64, :],
                     in0=lowo_f[:, :], scalar1=m64col[0:64, :], scalar2=None,
                     op0=OP.mult, inc=c_ve)
                rdy[("lowo_bf", s)] = c_ve.nxt()
                buf_rel("lowo_f", c_ve, rdy[("lowo_bf", s)])
                if "low" in _DBG and s == 0:
                    P.wait("sync", c_ve, rdy[("lowo_bf", s)])
                    P.op("sync", "dma_start", out=dbg["low2"].ap()[:, :],
                         in_=LOW2[:, :], inc=d_pre, inck=16)
                    d_pre.nxt(16)

            esF = contextlib.ExitStack()
            with esF:
                w1sl = [usb(esF, "w1sl%d" % i, [128, D], BF16)
                        for i in range(3)]
                h1b = usb(esF, "h1b", [128, 16 * 1024], BF16)
                h2b = usb(esF, "h2b", [128, 16 * 1024], BF16)
                h2tok = usb(esF, "h2tok", [128, 2048], BF16)
                phase_fence(extra=((d_as[0], 16 * as_st[0]), (d_as[1], 16 * as_st[1])))

                for hf in range(2):
                    tok0 = hf * 1024
                    for ic in range(16):
                        icp = ic % 3
                        if w_free_v[icp] is not None:
                            P.wait("gpsimd", c_mm, w_free_v[icp])
                        P.op("gpsimd", "dma_start", out=w1sl[icp][:, :],
                             in_=AP(ins["w1T"], ic * D * 128,
                                    [[128, 128], [128 * 128, 16], [1, 128]]),
                             inc=d_w[icp], inck=16)
                        NH_LOADS[icp] += 1
                        wv_r = 16 * NH_LOADS[icp]
                        P.wait("tensor", d_w[icp], wv_r)
                        bank_acq(6)
                        bank_acq(7)
                        for dc in range(NDC):
                            for qcc in range(2):
                                col0 = tok0 + qcc * 512
                                P.op("tensor", "matmul", pb[6 + qcc],
                                     w1sl[icp][:, dc * 128:(dc + 1) * 128],
                                     xnT[:, dc * S + col0: dc * S + col0 + 512],
                                     start=(dc == 0), stop=(dc == 15),
                                     inc=(c_mm if (dc == 15 and qcc == 1) else None))
                        v_h = c_mm.nxt()
                        for qcc in range(2):
                            P.wait("scalar", c_mm, v_h)
                            P.op("scalar", "activation",
                                 h1b[:, ic * 1024 + qcc * 512: ic * 1024 + qcc * 512 + 512],
                                 pb[6 + qcc], ACT.Silu, inc=c_sc)
                            bank_rel(6 + qcc, c_sc, c_sc.nxt())
                        w_free_v[icp] = c_mm.v
                    v_h1_done = c_sc.v
                    for ic2 in range(16):
                        icp = ic2 % 3
                        if w_free_v[icp] is not None:
                            P.wait("gpsimd", c_mm, w_free_v[icp])
                        P.op("gpsimd", "dma_start", out=w1sl[icp][:, :],
                             in_=AP(ins["w3T"], ic2 * D * 128,
                                    [[128, 128], [128 * 128, 16], [1, 128]]),
                             inc=d_w[icp], inck=16)
                        NH_LOADS[icp] += 1
                        wv_r = 16 * NH_LOADS[icp]
                        P.wait("tensor", d_w[icp], wv_r)
                        P.wait("tensor", c_sc, v_h1_done)
                        bank_acq(6)
                        bank_acq(7)
                        for i1 in range(NDC):
                            for qcc in range(2):
                                P.op("tensor", "matmul", pb[6 + qcc],
                                     w1sl[icp][:, i1 * 128:(i1 + 1) * 128],
                                     h1b[:, i1 * 1024 + qcc * 512: i1 * 1024 + qcc * 512 + 512],
                                     start=(i1 == 0), stop=(i1 == 15),
                                     inc=(c_mm if (i1 == 15 and qcc == 1) else None))
                        v_h = c_mm.nxt()
                        for qcc in range(2):
                            P.wait("scalar", c_mm, v_h)
                            P.op("scalar", "activation",
                                 h2b[:, ic2 * 1024 + qcc * 512: ic2 * 1024 + qcc * 512 + 512],
                                 pb[6 + qcc], ACT.Silu, inc=c_sc)
                            bank_rel(6 + qcc, c_sc, c_sc.nxt())
                        w_free_v[icp] = c_mm.v
                    v_h2_done = c_sc.v
                    if "h2" in _DBG and s == 0 and hf == 0:
                        P.wait("sync", c_sc, v_h2_done)
                        P.op("sync", "dma_start", out=dbg["h1"].ap()[:, :],
                             in_=h1b[:, :], inc=d_pre, inck=16)
                        P.op("sync", "dma_start", out=dbg["h2"].ap()[:, :],
                             in_=h2b[:, :], inc=d_pre, inck=16)
                        d_pre.nxt(32)
                    P.wait("tensor", c_sc, v_h2_done)
                    P.wait("tensor", c_ve, rdy[("gates", s)])
                    for tc in range(8):
                        for g in range(4):
                            bi = g % 2
                            bank_acq(bi)
                            buf_acq("h2tok_g%d" % g, "vector")
                            for k in range(4):
                                i2 = 4 * g + k
                                P.op("tensor", "transpose",
                                     pb[bi][:, k * 64:(k + 1) * 64].bitcast(BF16),
                                     h2b[:, i2 * 1024 + tc * 128: i2 * 1024 + tc * 128 + 128],
                                     identb[:, :],
                                     inc=(c_mm if k == 3 else None))
                            v_tr = c_mm.nxt()
                            P.wait("vector", c_mm, v_tr)
                            P.op("vector", "tensor_copy",
                                 out=h2tok[:, g * 512:(g + 1) * 512],
                                 in_=pb[bi][:, 0:256].bitcast(BF16),
                                 inc=c_ve)
                            bank_rel(bi, c_ve, c_ve.nxt())
                        v_ht = c_ve.v
                        P.wait("tensor", c_ve, v_ht)
                        tglob = hf * 8 + tc
                        if hf == 0 and tc == 0:
                            for gq in range(4):
                                bank_acq(2 + gq)
                        for gq in range(4):
                            P.op("tensor", "matmul", pb[2 + gq][0:8, 0:512],
                                 gates[:, tglob * 8:(tglob + 1) * 8],
                                 h2tok[:, gq * 512:(gq + 1) * 512],
                                 start=(hf == 0 and tc == 0),
                                 stop=(hf == 1 and tc == 7),
                                 inc=(c_mm if gq == 3 else None))
                        v_gmm = c_mm.nxt()
                        for g in range(4):
                            buf_rel("h2tok_g%d" % g, c_mm, v_gmm)
                v_gmm_all = c_mm.v
                P.wait("vector", c_mm, v_gmm_all)
                for gq in range(4):
                    if s == 0:
                        P.op("vector", "tensor_copy",
                             out=Gsb[0:8, gq * 512:(gq + 1) * 512],
                             in_=pb[2 + gq][0:8, 0:512],
                             inc=(c_ve if gq == 3 else None))
                    else:
                        P.op("vector", "tensor_tensor",
                             out=Gsb[0:8, gq * 512:(gq + 1) * 512],
                             in0=Gsb[0:8, gq * 512:(gq + 1) * 512],
                             in1=pb[2 + gq][0:8, 0:512], op=OP.add,
                             inc=(c_ve if gq == 3 else None))
                v_gsb = c_ve.nxt()
                rdy[("Gsb", s)] = v_gsb
                for gq in range(4):
                    bank_rel(2 + gq, c_ve, v_gsb)
                buf_rel("xnT", c_mm, c_mm.v)
                buf_rel("gates", c_mm, c_mm.v)
                buf_rel("logit_sb", c_ve, rdy[("gates", s)])

            esC = contextlib.ExitStack()
            with esC:
                wo_sb = usb(esC, "wo_sb", [128, NDC * D], BF16)
                attld = [usb(esC, "attld%d" % i, [128, 2048], BF16)
                         for i in range(2)]
                xsbC = [usb(esC, "xsbC%d" % i, [128, D], F32)
                        for i in range(2)]
                outb = [usb(esC, "outb%d" % i, [128, 512], F32)
                        for i in range(2)]
                phase_fence()
                for dcg in range(4):
                    P.op("gpsimd", "dma_start",
                         out=AP(wo_sb, dcg * 4 * D, [[NDC * D, 128], [D, 4], [1, D]]),
                         in_=AP(ins["woT"], dcg * 4 * 128 * D,
                                [[D, 128], [128 * D, 4], [1, D]]),
                         inc=d_wo, inck=16)
                v_wo = d_wo.nxt(64)
                P.wait("tensor", d_wo, v_wo)
                P.wait("sync", d_as[0], 16 * as_st[0])
                P.wait("sync", d_as[1], 16 * as_st[1])
                for tt in range(NTT):
                    tp_ = tt % 2
                    if at_free_v[tp_] is not None:
                        P.wait("sync", c_mm, at_free_v[tp_])
                    P.op("sync", "dma_start",
                         out=attld[tp_][:, :],
                         in_=AP(attT_d, tt * 128,
                                [[S, 128], [128 * S, 16], [1, 128]]),
                         inc=d_at[tp_], inck=16)
                    at_loads[tp_] += 1
                    v_at = 16 * at_loads[tp_]
                    if xC_free_v[tp_] is not None:
                        P.wait("sync", c_ve, xC_free_v[tp_])
                    P.op("sync", "dma_start", out=xsbC[tp_][:, :],
                         in_=x_d.ap()[row0 + tt * 128: row0 + (tt + 1) * 128, :],
                         inc=d_x[tp_], inck=16)
                    x_loads[tp_] += 1
                    v_x = 16 * x_loads[tp_]
                    bgrp_i = [4 * tp_ + i for i in range(4)]
                    P.wait("tensor", d_at[tp_], v_at)
                    P.wait("tensor", c_ve, rdy[("lowo_bf", s)])
                    for bi in bgrp_i:
                        bank_acq(bi)
                    for dc in range(NDC):
                        for oci, bi in enumerate(bgrp_i):
                            P.op("tensor", "matmul", pb[bi][:, 0:512],
                                 attld[tp_][:, dc * 128:(dc + 1) * 128],
                                 wo_sb[:, dc * D + oci * 512: dc * D + oci * 512 + 512],
                                 start=(dc == 0), stop=False)
                    for oci, bi in enumerate(bgrp_i):
                        P.op("tensor", "matmul", pb[bi][:, 0:512],
                             LOW2[0:64, tt * 128:(tt + 1) * 128],
                             bcat2[0:64, oci * 512:(oci + 1) * 512],
                             start=False, stop=True,
                             inc=(c_mm if oci == 3 else None))
                    v_ao = c_mm.nxt()
                    at_free_v[tp_] = v_ao
                    P.wait("vector", c_mm, v_ao)
                    P.wait("vector", d_x[tp_], v_x)
                    for oci, bi in enumerate(bgrp_i):
                        po = oci % 2
                        P.wait("vector", d_out[po], 16 * out_st[po])
                        P.op("vector", "tensor_tensor",
                             out=outb[po][:, :],
                             in0=pb[bi][:, 0:512],
                             in1=xsbC[tp_][:, oci * 512:(oci + 1) * 512],
                             op=OP.add, inc=c_ve)
                        v_epi = c_ve.nxt()
                        bank_rel(bi, c_ve, v_epi)
                        P.wait("sync", c_ve, v_epi)
                        P.op("sync", "dma_start",
                             out=out_d.ap()[row0 + tt * 128: row0 + (tt + 1) * 128,
                                            oci * 512:(oci + 1) * 512],
                             in_=outb[po][:, :], inc=d_out[po], inck=16)
                        out_st[po] += 1
                    v_epi = c_ve.v
                    rdy[("epi", s, tt)] = v_epi
                    xC_free_v[tp_] = v_epi
                buf_rel("LOW1", c_mm, c_mm.v)
                buf_rel("LOW2", c_mm, c_mm.v)

        esS = contextlib.ExitStack()
        with esS:
            GT_sb = usb(esS, "GT_sb", [128, 128], BF16)
            GT_lo = usb(esS, "GT_lo", [128, 128], BF16)
            w2f = [usb(esS, "w2f%d" % i, [128, D], F32) for i in range(2)]
            w2lo = [usb(esS, "w2lo%d" % i, [128, D], BF16) for i in range(2)]
            w2sl = [usb(esS, "w2sl%d" % i, [128, D], BF16)
                    for i in range(2)]
            patch = usb(esS, "patch", [8, D], F32)
            Ssum = usb(esS, "Ssum", [8, D], F32)
            phase_fence(extra=((d_out[0], 16 * out_st[0]),
                               (d_out[1], 16 * out_st[1])))
            P.wait("tensor", c_ve, rdy[("Gsb", 1)])
            for g in range(4):
                bi = 4 + (g % 2)
                bank_acq(bi)
                for k in range(4):
                    ic = 4 * g + k
                    P.op("tensor", "transpose", pb[bi][:, k * 8:(k + 1) * 8],
                         Gsb[0:8, ic * 128:(ic + 1) * 128], identf[0:8, 0:8],
                         inc=(c_mm if k == 3 else None))
                v_tr = c_mm.nxt()
                P.wait("vector", c_mm, v_tr)
                P.op("vector", "tensor_copy",
                     out=GT_sb[:, (4 * g) * 8:(4 * g + 4) * 8],
                     in_=pb[bi][:, 0:32])
                vdrain()
                P.op("vector", "tensor_tensor",
                     out=GT_lo[:, (4 * g) * 8:(4 * g + 4) * 8],
                     in0=pb[bi][:, 0:32],
                     in1=GT_sb[:, (4 * g) * 8:(4 * g + 4) * 8],
                     op=OP.subtract,
                     inc=c_ve)
                bank_rel(bi, c_ve, c_ve.nxt())
            v_gt = c_ve.v
            for ic in range(16):
                icp = ic % 2
                if w_free_v[icp] is not None:
                    P.wait("gpsimd", c_mm, w_free_v[icp])
                P.wait("gpsimd", c_ve, rdy.get(("w2lo", ic - 2), 0))
                P.op("gpsimd", "dma_start", out=w2sl[icp][:, :],
                     in_=ins["w2T"].ap()[ic * 128:(ic + 1) * 128, :],
                     inc=d_w[icp], inck=16)
                P.op("gpsimd", "dma_start", out=w2f[icp][:, :],
                     in_=ins["w2T"].ap()[ic * 128:(ic + 1) * 128, :],
                     inc=d_w[icp], inck=16)
                NH_LOADS[icp] += 2
                wv_r = 16 * NH_LOADS[icp]
                P.wait("vector", d_w[icp], wv_r)
                P.op("vector", "tensor_tensor", out=w2lo[icp][:, :],
                     in0=w2f[icp][:, :], in1=w2sl[icp][:, :], op=OP.subtract,
                     inc=c_ve)
                rdy[("w2lo", ic)] = c_ve.nxt()
                P.wait("tensor", d_w[icp], wv_r)
                P.wait("tensor", c_ve, rdy[("w2lo", ic)])
                P.wait("tensor", c_ve, v_gt)
                if ic == 0:
                    for oc in range(4):
                        bank_acq(oc)
                for oc in range(4):
                    P.op("tensor", "matmul", pb[oc][0:8, 0:512],
                         GT_sb[:, ic * 8:(ic + 1) * 8],
                         w2sl[icp][:, oc * 512:(oc + 1) * 512],
                         start=(ic == 0), stop=False)
                    P.op("tensor", "matmul", pb[oc][0:8, 0:512],
                         GT_lo[:, ic * 8:(ic + 1) * 8],
                         w2sl[icp][:, oc * 512:(oc + 1) * 512],
                         start=False, stop=False)
                    P.op("tensor", "matmul", pb[oc][0:8, 0:512],
                         GT_sb[:, ic * 8:(ic + 1) * 8],
                         w2lo[icp][:, oc * 512:(oc + 1) * 512],
                         start=False, stop=(ic == 15),
                         inc=(c_mm if oc == 3 else None))
                c_mm.nxt()
                w_free_v[icp] = c_mm.v
            v_smm = c_mm.v
            P.wait("vector", c_mm, v_smm)
            for oc in range(4):
                P.op("vector", "tensor_copy",
                     out=Ssum[0:8, oc * 512:(oc + 1) * 512],
                     in_=pb[oc][0:8, 0:512], inc=(c_ve if oc == 3 else None))
            v_S = c_ve.nxt()
            if "G" in _DBG:
                P.wait("sync", c_ve, v_S)
                P.op("sync", "dma_start", out=dbg["G"].ap()[:, :], in_=Gsb[0:8, :],
                     inc=d_pre, inck=16)
                d_pre.nxt(16)
            P.wait("gpsimd", c_ve, v_S)
            P.op("gpsimd", "dma_start", out=cc_in.ap()[:, :], in_=Ssum[0:8, :],
                 inc=d_cc, inck=16)
            v_cc1 = d_cc.nxt(16)
            P.wait("gpsimd", d_cc, v_cc1)
            P.op("gpsimd", "collective_compute", "AllReduce", OP.add,
                 replica_groups=[list(range(NCORES))],
                 ins=[cc_in.ap().opt()], outs=[cc_out.ap().opt()], inc=c_cc)
            v_ccd = c_cc.nxt()
            P.wait("gpsimd", c_cc, v_ccd)
            P.op("gpsimd", "dma_start", out=Ssum[0:8, :], in_=cc_out.ap()[:, :],
                 inc=d_cc, inck=16)
            v_cc2 = d_cc.nxt(16)
            P.wait("vector", d_cc, v_cc2)
            P.op("vector", "tensor_scalar", out=Ssum[0:8, :], in0=Ssum[0:8, :],
                 scalar1=c0m_sb[:, :], scalar2=None, op0=OP.mult, inc=c_ve)
            v_Sm = c_ve.nxt()
            if "S" in _DBG:
                P.wait("sync", c_ve, v_Sm)
                P.op("sync", "dma_start", out=dbg["S"].ap()[:, :], in_=Ssum[0:8, :],
                     inc=d_pre, inck=16)
                d_pre.nxt(16)
            P.wait("sync", d_out[0], 16 * out_st[0])
            P.wait("sync", d_out[1], 16 * out_st[1])
            P.op("sync", "dma_start", out=patch[:, :], in_=out_d.ap()[0:8, :],
                 inc=d_pt, inck=16)
            v_cc3 = d_pt.nxt(16)
            P.wait("vector", d_pt, v_cc3)
            P.wait("vector", c_ve, v_Sm)
            P.op("vector", "tensor_tensor", out=patch[:, :], in0=patch[:, :],
                 in1=Ssum[0:8, :], op=OP.add, inc=c_ve)
            v_pt = c_ve.nxt()
            P.wait("sync", c_ve, v_pt)
            P.op("sync", "dma_start", out=out_d.ap()[0:8, :], in_=patch[:, :],
                 inc=d_cc, inck=16)
            v_cc4 = d_cc.nxt(16)
            if "snap" in _DBG:
                P.op("sync", "dma_start", out=dbg["snap"].ap()[:, :],
                     in_=snap[:, :], inc=d_pre, inck=16)
                d_pre.nxt(16)
            P.wait("sync", d_cc, v_cc4)
            P.wait("sync", d_pre, d_pre.v)

        block = es.enter_context(nc.Block())
        block.sync(lambda e: P.replay("sync", e))
        block.gpsimd(lambda e: P.replay("gpsimd", e))
        block.vector(lambda e: P.replay("vector", e))
        block.scalar(lambda e: P.replay("scalar", e))
        block.tensor(lambda e: P.replay("tensor", e))
    return nc


_nc_cache = {}


def kernel(**inputs):
    if "nc" not in _nc_cache:
        _nc_cache["nc"] = build_nc()
    nc = _nc_cache["nc"]
    g = _host_prep(inputs)
    x = np.ascontiguousarray(np.asarray(inputs["L0_attn_x"], dtype=np.float32))
    in_maps = []
    for c in range(NCORES):
        m = dict(g)
        m["x"] = np.ascontiguousarray(x[2 * c:2 * c + 2].reshape(NSEQ * S, D))
        m["bval"] = _f32(np.array([[2 * c, 2 * c + 1]]))
        m["c0m"] = _f32(np.full((8, 1), 1.0 if c == 0 else 0.0))
        in_maps.append(m)
    res = run_bass_kernel_spmd(nc, in_maps, core_ids=list(range(NCORES)))
    if _debug_capture is not None:
        _debug_capture.append(res)
    out = np.stack([np.asarray(res.results[c]["out"]).reshape(NSEQ, S, D)
                    for c in range(NCORES)])
    return np.ascontiguousarray(out.reshape(16, S, D).astype(np.float32))


# version 07
# speedup vs baseline: 1.1565x; 1.0155x over previous
import contextlib
import numpy as np
import ml_dtypes
import concourse.bass as bass
import concourse.mybir as mybir
from concourse.bass_utils import run_bass_kernel_spmd

F32 = mybir.dt.float32
BF16 = mybir.dt.bfloat16
I32 = mybir.dt.int32
OP = mybir.AluOpType
ACT = mybir.ActivationFunctionType
AX = mybir.AxisListType.X

NCORES = 8
NSEQ = 2
S = 2048
D = 2048
H = 16
DKV = 128
NTT = 16
NDC = 16
NQC = 4
SCL = float(1.0 / np.sqrt(np.float32(128.0)))
ROPE_THETA = 10000.0

_DBG = set()
_debug_capture = None


class Sem:
    def __init__(self, h):
        self.h = h
        self.v = 0

    def nxt(self, k=1):
        self.v += k
        return self.v


class Prog:
    ENGS = ("sync", "gpsimd", "vector", "scalar", "tensor")

    def __init__(self):
        self.s = {e: [] for e in self.ENGS}

    def op(self, eng, meth, *a, inc=None, inck=1, **kw):
        if (inc is not None and eng in ("vector", "scalar")
                and meth not in ("nop", "wait_ge", "drain")):
            self.s[eng].append((meth, a, kw, None, 0))
            self.s[eng].append(("drain", (), {}, None, 0))
            self.s[eng].append(("nop", (), {}, inc, inck))
        else:
            self.s[eng].append((meth, a, kw, inc, inck))

    def wait(self, eng, sem, v):
        if v > 0:
            self.s[eng].append(("wait_ge", (sem.h, v), {}, None, 0))

    def replay(self, eng_name, e):
        for meth, a, kw, inc, inck in self.s[eng_name]:
            inst = getattr(e, meth)(*a, **kw)
            if inc is not None:
                inst.then_inc(inc.h, inck)


def _bf(x):
    return np.ascontiguousarray(np.asarray(x).astype(ml_dtypes.bfloat16))


def _f32(x):
    return np.ascontiguousarray(np.asarray(x).astype(np.float32))


def _host_constants():
    c = {}
    c["ident_f"] = _f32(np.eye(128))
    c["ident_b"] = _bf(np.eye(128))
    r = np.arange(128)
    c["tri_b"] = _bf((r[None, :] >= r[:, None]).astype(np.float32))
    c["onescol_b"] = _bf(np.ones((128, 1)))
    c["onescol_f"] = _f32(np.ones((128, 1)))
    c["onesrow_f"] = _f32(np.ones((1, 128)))
    c["iota128col"] = _f32(((np.arange(128) % 64) // 16).reshape(128, 1))
    pos = np.arange(S, dtype=np.float64)
    freq = ROPE_THETA ** (-(np.arange(64, dtype=np.float64)) / 64.0)
    ang = pos[None, :] * freq[:, None]
    cosv = np.cos(ang)
    sinv = np.sin(ang)
    ctab = np.empty((128, S), dtype=np.float64)
    stab = np.empty((128, S), dtype=np.float64)
    ctab[0::2] = cosv
    ctab[1::2] = cosv
    stab[0::2] = sinv
    stab[1::2] = -sinv
    c["ropeC"] = _f32(ctab)
    c["ropeS"] = _f32(stab)
    cv = np.zeros((128, 4))
    cv[:, 0] = 1e-5
    cv[:, 1] = 1e-30
    cv[:, 2] = float(np.log(1.0 / np.sqrt(np.float32(128.0))) + np.log(1.03))
    cv[:, 3] = 1.0
    c["cvec"] = _f32(cv)
    return c


def _host_prep(inputs):
    g = {}
    def _tile_heads(WT):
        return _f32(WT.reshape(16, 128, H, DKV).transpose(2, 1, 0, 3)
                    .reshape(H, 128, D))

    g["wqT"] = _tile_heads(np.asarray(inputs["L0_attn_Wq"]).T)
    g["wkT"] = _tile_heads(np.asarray(inputs["L0_attn_Wk"]).T)
    g["wvT"] = _tile_heads(np.asarray(inputs["L0_attn_Wv"]).T)
    g["woT"] = _f32(np.asarray(inputs["L0_attn_Wo"]).T
                    .reshape(16, 128, D).transpose(1, 0, 2).reshape(128, 16 * D))
    g["w1T"] = _tile_heads(np.asarray(inputs["L0_ffn_w1"]).T)
    g["w3T"] = _tile_heads(np.asarray(inputs["L0_ffn_w3"]).T)
    g["w2T"] = _f32(np.asarray(inputs["L0_ffn_w2"]).T)
    g["wr_sb"] = _f32(np.asarray(inputs["L0_ffn_wr"]).T.reshape(16, 128, 8)
                      .transpose(1, 0, 2).reshape(128, 128))
    for p, W in (("q", "L0_attn_Aq"), ("k", "L0_attn_Ak"),
                 ("v", "L0_attn_Av"), ("o", "L0_attn_Ao")):
        acat = np.asarray(inputs[W]).transpose(2, 0, 1).reshape(D, 64)
        g["acat_" + p] = _bf(acat.reshape(16, 128, 64).transpose(1, 0, 2)
                             .reshape(128, 1024))
    bq = np.asarray(inputs["L0_attn_Bq"]).transpose(0, 2, 1).reshape(64, D)
    bk = np.asarray(inputs["L0_attn_Bk"]).transpose(0, 2, 1).reshape(64, D)
    bv = np.asarray(inputs["L0_attn_Bv"]).transpose(0, 2, 1).reshape(64, D)
    bo = np.asarray(inputs["L0_attn_Bo"]).transpose(0, 2, 1).reshape(64, D)
    g["bcat1"] = _bf(np.concatenate([bq, bk], axis=0))
    g["bcat2"] = _bf(np.concatenate([bo, bv], axis=0))
    g["starts"] = np.ascontiguousarray(
        np.asarray(inputs["L0_attn_lora_start_indices"]).astype(np.int32).reshape(1, 4))
    g.update(_host_constants())
    return g


def build_nc():
    nc = bass.Bass("TRN2", target_bir_lowering=False)
    P = Prog()

    x_d = nc.dram_tensor("x", [NSEQ * S, D], F32, kind="ExternalInput")
    out_d = nc.dram_tensor("out", [NSEQ * S, D], F32, kind="ExternalOutput")
    ins = {}
    for name, shp, dt in [
        ("wqT", [H, 128, D], F32), ("wkT", [H, 128, D], F32),
        ("wvT", [H, 128, D], F32), ("woT", [128, 16 * D], F32),
        ("w1T", [16, 128, D], F32), ("w3T", [16, 128, D], F32),
        ("w2T", [D, D], F32), ("wr_sb", [128, 128], F32),
        ("acat_q", [128, 1024], BF16), ("acat_k", [128, 1024], BF16),
        ("acat_v", [128, 1024], BF16), ("acat_o", [128, 1024], BF16),
        ("bcat1", [128, D], BF16), ("bcat2", [128, D], BF16),
        ("starts", [1, 4], I32),
        ("ident_f", [128, 128], F32), ("ident_b", [128, 128], BF16),
        ("tri_b", [128, 128], BF16),
        ("onescol_b", [128, 1], BF16), ("onescol_f", [128, 1], F32),
        ("onesrow_f", [1, 128], F32), ("iota128col", [128, 1], F32),
        ("ropeC", [128, S], F32), ("ropeS", [128, S], F32),
        ("bval", [1, 2], F32), ("c0m", [8, 1], F32), ("cvec", [128, 4], F32),
    ]:
        ins[name] = nc.dram_tensor(name, shp, dt, kind="ExternalInput")
    attT_d = nc.dram_tensor("attT", [H, 128, S], BF16,
                            kind="ExternalOutput" if "attT" in _DBG else "Internal")
    cc_in = nc.dram_tensor("cc_in", [8, D], F32)
    cc_out = nc.dram_tensor("cc_out", [8, D], F32)
    dbg = {}
    if "xnT" in _DBG:
        dbg["xnT"] = nc.dram_tensor("dbg_xnT", [128, NDC * S], BF16, kind="ExternalOutput")
    if "gates" in _DBG:
        dbg["logits"] = nc.dram_tensor("dbg_logits", [128, 128], F32, kind="ExternalOutput")
        dbg["gates"] = nc.dram_tensor("dbg_gates", [128, 128], BF16, kind="ExternalOutput")
    if "qkv" in _DBG:
        for n in ("qraw", "QT", "KT", "V"):
            dbg[n] = nc.dram_tensor("dbg_" + n, [128, S], BF16, kind="ExternalOutput")
        dbg["negB"] = nc.dram_tensor("dbg_negB", [128, 1], F32, kind="ExternalOutput")
    if "low" in _DBG:
        dbg["low1"] = nc.dram_tensor("dbg_low1", [128, S], BF16, kind="ExternalOutput")
        dbg["low2"] = nc.dram_tensor("dbg_low2", [128, S], BF16, kind="ExternalOutput")
    if "h2" in _DBG:
        dbg["h1"] = nc.dram_tensor("dbg_h1", [128, 16 * 1024], BF16, kind="ExternalOutput")
        dbg["h2"] = nc.dram_tensor("dbg_h2", [128, 16 * 1024], BF16, kind="ExternalOutput")
    if "G" in _DBG:
        dbg["G"] = nc.dram_tensor("dbg_G", [8, D], F32, kind="ExternalOutput")
    if "S" in _DBG:
        dbg["S"] = nc.dram_tensor("dbg_S", [8, D], F32, kind="ExternalOutput")
    if "snap" in _DBG:
        dbg["snap"] = nc.dram_tensor("dbg_snap", [128, 64], F32, kind="ExternalOutput")

    es = contextlib.ExitStack()
    with es:
        sems = {}
        for n in ("d_pre", "d_x0", "d_x1", "d_w0", "d_w1", "d_w2", "d_wo", "d_at0",
                  "d_at1", "d_as0", "d_as1", "d_out0", "d_out1", "d_cc", "d_pt",
                  "c_mm", "c_sc", "c_ve", "c_cc",
                  "f_x0", "f_x1", "f_w0", "f_w1", "f_at0", "f_at1"):
            sems[n] = Sem(es.enter_context(nc.semaphore(n)))
        d_pre = sems["d_pre"]
        d_x = [sems["d_x0"], sems["d_x1"]]
        d_w = [sems["d_w0"], sems["d_w1"], sems["d_w2"]]
        d_wo = sems["d_wo"]
        d_at = [sems["d_at0"], sems["d_at1"]]
        d_as = [sems["d_as0"], sems["d_as1"]]
        d_out = [sems["d_out0"], sems["d_out1"]]
        d_cc = sems["d_cc"]
        d_pt = sems["d_pt"]
        c_mm = sems["c_mm"]
        c_sc = sems["c_sc"]
        c_ve = sems["c_ve"]
        c_cc = sems["c_cc"]
        f_x = [sems["f_x0"], sems["f_x1"]]
        f_w = [sems["f_w0"], sems["f_w1"]]
        f_at = [sems["f_at0"], sems["f_at1"]]

        sb = lambda n, shp, dt: es.enter_context(nc.sbuf_tensor("s_" + n, shp, dt))
        ropeC = sb("ropeC", [128, S], F32)
        ropeS = sb("ropeS", [128, S], F32)
        identf = sb("identf", [128, 128], F32)
        identb = sb("identb", [128, 128], BF16)
        tri = sb("tri", [128, 128], BF16)
        onescol_b = sb("onescol_b", [128, 1], BF16)
        onescol_f = sb("onescol_f", [128, 1], F32)
        onesrow_f = sb("onesrow_f", [1, 128], F32)
        iota128 = sb("iota128", [128, 1], F32)
        wr_sb = sb("wr_sb", [128, 128], F32)
        acat = {p: sb("acat_" + p, [128, 1024], BF16) for p in "qkvo"}
        bcat1 = sb("bcat1", [128, D], BF16)
        bcat2 = sb("bcat2", [128, D], BF16)
        starts_sb = sb("starts_sb", [1, 4], I32)
        startsf = sb("startsf", [1, 4], F32)
        bval_sb = sb("bval_sb", [1, 2], F32)
        c0m_sb = sb("c0m_sb", [8, 1], F32)
        m64col = sb("m64col", [128, 1], F32)
        LOW1 = sb("LOW1", [128, S], BF16)
        LOW2 = sb("LOW2", [128, S], BF16)
        gates = sb("gates", [128, 128], BF16)
        logit_sb = sb("logit_sb", [128, 128], F32)
        Gsb = sb("Gsb", [8, D], F32)
        nrm = sb("nrm", [1, 8], F32)
        st1 = sb("st1", [128, 1], F32)
        st2 = sb("st2", [128, 1], F32)
        st3 = sb("st3", [128, 1], F32)
        st4 = sb("st4", [128, 1], F32)
        negB = sb("negB", [128, 1], F32)
        gsc = sb("gsc", [128, 96], F32)
        cst = sb("cst", [128, 4], F32)
        snap = sb("snap", [128, 64], F32) if "snap" in _DBG else None
        xnT = sb("xnT", [128, NDC * S], BF16)

        scA = es.enter_context(nc.psum_tensor("scA", [128, 1024], F32))
        scB = es.enter_context(nc.psum_tensor("scB", [128, 1024], F32))
        pb47 = [es.enter_context(nc.psum_tensor("pb%d" % i, [128, 512], F32))
                for i in range(4, 8)]

        def pbank(i):
            if i < 2:
                return scA[:, i * 512:(i + 1) * 512]
            if i < 4:
                return scB[:, (i - 2) * 512:(i - 1) * 512]
            return pb47[i - 4][:, :]
        pb = [pbank(i) for i in range(8)]
        _uid = [0]

        def usb(stack, n, shp, dt):
            _uid[0] += 1
            return stack.enter_context(
                nc.sbuf_tensor("u%d_%s" % (_uid[0], n), shp, dt))

        bank_state = {}
        buf_state = {}

        def bank_acq(i, eng="tensor"):
            if i in bank_state:
                sem, v = bank_state[i]
                P.wait(eng, sem, v)

        def bank_rel(i, sem, v):
            bank_state[i] = (sem, v)

        def buf_acq(name, eng):
            if name in buf_state:
                sem, v = buf_state[name]
                P.wait(eng, sem, v)

        def buf_rel(name, sem, v):
            buf_state[name] = (sem, v)

        def AP(t, off, dims):
            return bass.AP(t, off, dims)

        def phase_fence(engs=("gpsimd", "sync"), extra=()):
            for e in engs:
                P.wait(e, c_mm, c_mm.v)
                P.wait(e, c_ve, c_ve.v)
                P.wait(e, c_sc, c_sc.v)
                for sem, v in extra:
                    P.wait(e, sem, v)

        def vdrain():
            P.op("vector", "drain")

        def sdrain():
            P.op("scalar", "drain")

        n_pre = 0
        for name, dst in [("ropeC", ropeC), ("ropeS", ropeS), ("ident_f", identf),
                          ("ident_b", identb), ("tri_b", tri),
                          ("onescol_b", onescol_b), ("onescol_f", onescol_f),
                          ("onesrow_f", onesrow_f), ("iota128col", iota128),
                          ("wr_sb", wr_sb), ("bcat1", bcat1), ("bcat2", bcat2),
                          ("starts", starts_sb), ("bval", bval_sb), ("c0m", c0m_sb),
                          ("cvec", cst)]:
            P.op("sync", "dma_start", out=dst[:, :], in_=ins[name].ap(), inc=d_pre, inck=16)
            n_pre += 1
        for p in "qkvo":
            P.op("sync", "dma_start", out=acat[p][:, :], in_=ins["acat_" + p].ap(),
                 inc=d_pre, inck=16)
            n_pre += 1
        pre_v = d_pre.nxt(16 * n_pre)
        for e in ("vector", "scalar", "tensor", "gpsimd"):
            P.wait(e, d_pre, pre_v)

        P.op("vector", "tensor_copy", out=startsf[:, :], in_=starts_sb[:, :], inc=c_ve)
        v_startsf = c_ve.nxt()

        rdy = {}
        NH_LOADS = [0, 0, 0]
        x_loads = [0, 0]
        x_free_v = [None, None]
        xC_free_v = [None, None]
        at_loads = [0, 0]
        at_free_v = [None, None]
        as_st = [0, 0]
        out_st = [0, 0]
        w_free_v = [None, None, None]

        for s in range(NSEQ):
            row0 = s * S
            esA = contextlib.ExitStack()
            with esA:
                xsb = [usb(esA, "xsb%d" % i, [128, D], F32)
                       for i in range(2)]
                sqx = usb(esA, "sqx", [128, D], F32)
                dsq = usb(esA, "dsq", [128, D], F32)
                xnt = [usb(esA, "xnt%d" % i, [128, D], F32)
                       for i in range(2)]
                xnf = [usb(esA, "xnf%d" % i, [128, 512], F32)
                       for i in range(2)]
                sumsq = usb(esA, "sumsq", [128, 1], F32)
                sumx = usb(esA, "sumx", [128, 1], F32)
                meanc = usb(esA, "meanc", [128, 1], F32)
                m2c = usb(esA, "m2c", [128, 1], F32)
                varc = usb(esA, "varc", [128, 1], F32)
                rstd = usb(esA, "rstd", [128, 1], F32)

                phase_fence(engs=("sync",),
                            extra=((d_out[0], 16 * out_st[0]),
                                   (d_out[1], 16 * out_st[1])))
                buf_acq("xnT", "vector")
                buf_acq("gates", "vector")
                buf_acq("logit_sb", "scalar")
                xnt_freeat = [0, 0]
                xnf_free_mm = [0, 0]
                xnf_free_ve = [0, 0]

                for tt in range(NTT):
                    p = tt % 2
                    if x_free_v[p] is not None:
                        P.wait("sync", c_ve if x_free_v[p][0] == "ve" else c_mm,
                               x_free_v[p][1])
                    P.op("sync", "dma_start", out=xsb[p][:, :],
                         in_=x_d.ap()[row0 + tt * 128: row0 + (tt + 1) * 128, :],
                         inc=d_x[p], inck=16)
                    x_loads[p] += 1
                    xv = 16 * x_loads[p]
                    P.wait("vector", d_x[p], xv)
                    P.op("vector", "reduce_sum", sumx[:, :], xsb[p][:, :], axis=AX)
                    vdrain()
                    P.op("vector", "tensor_scalar", out=meanc[:, :], in0=sumx[:, :],
                         scalar1=1.0 / D, scalar2=None, op0=OP.mult)
                    vdrain()
                    P.op("vector", "tensor_scalar", out=sqx[:, :], in0=xsb[p][:, :],
                         scalar1=meanc[:, :], scalar2=None, op0=OP.subtract,
                         inc=c_ve)
                    x_free_v[p] = ("ve", c_ve.nxt())
                    P.op("vector", "tensor_tensor", out=dsq[:, :], in0=sqx[:, :],
                         in1=sqx[:, :], op=OP.mult)
                    P.op("vector", "reduce_sum", sumsq[:, :], dsq[:, :], axis=AX)
                    vdrain()
                    P.op("vector", "tensor_scalar", out=varc[:, :], in0=sumsq[:, :],
                         scalar1=1.0 / D, scalar2=None, op0=OP.mult, inc=c_ve)
                    v_var = c_ve.nxt()
                    P.wait("scalar", c_ve, v_var)
                    P.op("scalar", "activation", st1[:, :], varc[:, :], ACT.Ln,
                         bias=cst[:, 0:1])
                    sdrain()
                    P.op("scalar", "activation", rstd[:, :], st1[:, :], ACT.Exp,
                         scale=-0.5, inc=c_sc)
                    v_rstd = c_sc.nxt()
                    P.wait("vector", c_sc, v_rstd)
                    P.wait("vector", c_mm, xnt_freeat[p])
                    P.op("vector", "tensor_scalar", out=xnt[p][:, :], in0=sqx[:, :],
                         scalar1=rstd[:, :], scalar2=None,
                         op0=OP.mult, inc=c_ve)
                    v_xn = c_ve.nxt()
                    if "snap" in _DBG and s == 0 and tt == 0:
                        P.op("vector", "tensor_copy", out=snap[:, 48:52],
                             in_=xnt[p][:, 0:4])
                        P.op("vector", "tensor_copy", out=snap[:, 52:53],
                             in_=rstd[:, :])
                        P.op("vector", "tensor_copy", out=snap[:, 53:54],
                             in_=varc[:, :])
                        P.op("vector", "tensor_copy", out=snap[:, 54:55],
                             in_=meanc[:, :])
                        P.op("vector", "tensor_copy", out=snap[:, 55:56],
                             in_=sumx[:, :])
                        P.op("vector", "tensor_copy", out=snap[:, 60:64],
                             in_=xsb[p][:, 0:4])
                    P.wait("tensor", c_ve, v_xn)
                    for g in range(4):
                        gp = g % 2
                        bank_acq(gp)
                        for k in range(4):
                            dc = 4 * g + k
                            P.op("tensor", "transpose", pb[gp][:, k * 128:(k + 1) * 128],
                                 xnt[p][:, dc * 128:(dc + 1) * 128], identf[:, :],
                                 inc=(c_mm if k == 3 else None))
                        v_tr = c_mm.nxt()
                        if g == 3:
                            xnt_freeat[p] = v_tr
                        P.wait("scalar", c_mm, v_tr)
                        P.wait("scalar", c_mm, xnf_free_mm[gp])
                        P.wait("scalar", c_ve, xnf_free_ve[gp])
                        P.op("scalar", "copy", xnf[gp][:, :], pb[gp][:, :], inc=c_sc)
                        v_ev = c_sc.nxt()
                        if "snap" in _DBG and s == 0 and tt == 0 and g == 0:
                            P.op("scalar", "copy", snap[:, 56:60], xnf[gp][:, 0:4])
                        bank_rel(gp, c_sc, v_ev)
                        P.wait("tensor", c_sc, v_ev)
                        if g == 0:
                            bank_acq(2)
                        for k in range(4):
                            dc = 4 * g + k
                            P.op("tensor", "matmul", pb[2][:, 0:8],
                                 xnf[gp][:, k * 128:(k + 1) * 128],
                                 wr_sb[:, dc * 8:(dc + 1) * 8],
                                 start=(dc == 0), stop=(dc == 15),
                                 inc=(c_mm if k == 3 else None))
                        v_rt = c_mm.nxt()
                        xnf_free_mm[gp] = v_rt
                        P.wait("vector", c_sc, v_ev)
                        P.op("vector", "tensor_copy",
                             out=AP(xnT, (4 * g) * S + tt * 128,
                                    [[NDC * S, 128], [S, 4], [1, 128]]),
                             in_=AP(xnf[gp], 0, [[512, 128], [128, 4], [1, 128]]),
                             inc=c_ve)
                        xnf_free_ve[gp] = c_ve.nxt()
                    P.wait("scalar", c_mm, c_mm.v)
                    P.op("scalar", "copy", logit_sb[:, tt * 8:(tt + 1) * 8],
                         pb[2][:, 0:8], inc=c_sc)
                    bank_rel(2, c_sc, c_sc.nxt())


                rdy[("xnT", s)] = c_ve.v
                v_logits = c_sc.v

                m1n = gsc[:, 0:16]
                m2n = gsc[:, 16:32]
                eb = usb(esA, "eb", [128, 128], F32)
                zb2 = usb(esA, "zb2", [128, 128], F32)
                indb = usb(esA, "indb", [128, 128], F32)
                l2b = usb(esA, "l2b", [128, 128], F32)
                rg = gsc[:, 48:64]
                P.wait("vector", c_sc, v_logits)
                for t2 in range(16):
                    P.op("vector", "reduce_max", m1n[:, t2:t2 + 1],
                         logit_sb[:, t2 * 8:(t2 + 1) * 8], axis=AX, negate=True)
                vdrain()
                for t2 in range(16):
                    P.op("vector", "tensor_scalar", out=zb2[:, t2 * 8:(t2 + 1) * 8],
                         in0=logit_sb[:, t2 * 8:(t2 + 1) * 8],
                         scalar1=m1n[:, t2:t2 + 1], scalar2=None, op0=OP.add)
                vdrain()
                P.op("vector", "tensor_single_scalar", indb[:, :], zb2[:, :], 0.0,
                     OP.is_equal)
                vdrain()
                P.op("vector", "scalar_tensor_tensor", out=l2b[:, :], in0=indb[:, :],
                     scalar=-1e30, in1=logit_sb[:, :], op0=OP.mult, op1=OP.add)
                vdrain()
                for t2 in range(16):
                    P.op("vector", "reduce_max", m2n[:, t2:t2 + 1],
                         l2b[:, t2 * 8:(t2 + 1) * 8], axis=AX, negate=True)
                vdrain()
                P.op("vector", "nop", inc=c_ve)
                v_g1 = c_ve.nxt()
                P.wait("scalar", c_ve, v_g1)
                P.op("scalar", "activation", eb[:, :], zb2[:, :], ACT.Exp,
                     inc=c_sc)
                v_g2 = c_sc.nxt()
                P.wait("vector", c_sc, v_g2)
                P.op("vector", "tensor_tensor", out=gsc[:, 64:80], in0=m1n,
                     in1=m2n, op=OP.subtract, inc=c_ve)
                v_g3 = c_ve.nxt()
                P.wait("scalar", c_ve, v_g3)
                P.op("scalar", "activation", gsc[:, 80:96], gsc[:, 64:80], ACT.Exp,
                     inc=c_sc)
                v_g4 = c_sc.nxt()
                P.wait("vector", c_sc, v_g4)
                P.op("vector", "tensor_scalar", out=gsc[:, 64:80],
                     in0=gsc[:, 80:96], scalar1=1.0, scalar2=None, op0=OP.add)
                vdrain()
                P.op("vector", "reciprocal", rg, gsc[:, 64:80])
                for t2 in range(16):
                    P.op("vector", "tensor_scalar", out=l2b[:, t2 * 8:(t2 + 1) * 8],
                         in0=logit_sb[:, t2 * 8:(t2 + 1) * 8],
                         scalar1=m2n[:, t2:t2 + 1], scalar2=None, op0=OP.add)
                vdrain()
                P.op("vector", "tensor_single_scalar", indb[:, :], l2b[:, :], 0.0,
                     OP.is_ge)
                vdrain()
                P.op("vector", "tensor_tensor", out=eb[:, :], in0=eb[:, :],
                     in1=indb[:, :], op=OP.mult)
                vdrain()
                for t2 in range(16):
                    P.op("vector", "tensor_scalar", out=gates[:, t2 * 8:(t2 + 1) * 8],
                         in0=eb[:, t2 * 8:(t2 + 1) * 8],
                         scalar1=rg[:, t2:t2 + 1], scalar2=None, op0=OP.mult,
                         inc=(c_ve if t2 == 15 else None))
                rdy[("gates", s)] = c_ve.nxt()
                if "gates" in _DBG and s == 0:
                    P.wait("sync", c_ve, rdy[("gates", s)])
                    P.op("sync", "dma_start", out=dbg["logits"].ap()[:, :],
                         in_=logit_sb[:, :], inc=d_pre, inck=16)
                    P.op("sync", "dma_start", out=dbg["gates"].ap()[:, :],
                         in_=gates[:, :], inc=d_pre, inck=16)
                    d_pre.nxt(32)
                if "xnT" in _DBG and s == 0:
                    P.wait("sync", c_ve, rdy[("xnT", s)])
                    P.op("sync", "dma_start", out=dbg["xnT"].ap()[:, :],
                         in_=xnT[:, :], inc=d_pre, inck=16)
                    d_pre.nxt(16)

            P.wait("vector", c_ve, v_startsf)
            buf_acq("LOW1", "vector")
            buf_acq("LOW2", "vector")
            P.op("vector", "tensor_scalar", out=gsc[0:1, 0:4],
                 in0=startsf[:, :], scalar1=bval_sb[0:1, s:s + 1], scalar2=None,
                 op0=OP.is_le)
            vdrain()
            P.op("vector", "reduce_sum", gsc[0:1, 4:5], gsc[0:1, 0:4], axis=AX)
            vdrain()
            P.op("vector", "tensor_scalar", out=gsc[0:1, 5:6], in0=gsc[0:1, 4:5],
                 scalar1=-1.0, scalar2=0.0, op0=OP.add, op1=OP.max)
            vdrain()
            P.op("vector", "tensor_scalar", out=gsc[0:1, 6:7], in0=gsc[0:1, 5:6],
                 scalar1=3.0, scalar2=None, op0=OP.min, inc=c_ve)
            v_ids = c_ve.nxt()
            P.wait("tensor", c_ve, v_ids)
            bank_acq(3)
            P.op("tensor", "matmul", pb[3][:, 0:1], onesrow_f[:, :], gsc[0:1, 6:7],
                 start=True, stop=True, inc=c_mm)
            v_idb = c_mm.nxt()
            P.wait("vector", c_mm, v_idb)
            P.op("vector", "tensor_copy", out=st2[:, :], in_=pb[3][:, 0:1], inc=c_ve)
            bank_rel(3, c_ve, c_ve.nxt())
            vdrain()
            P.op("vector", "tensor_scalar", out=m64col[:, :], in0=iota128[:, :],
                 scalar1=st2[:, :], scalar2=None, op0=OP.is_equal, inc=c_ve)
            v_m64 = c_ve.nxt()
            xnT_rdy = rdy[("xnT", s)]
            P.wait("tensor", c_ve, xnT_rdy)
            for pn, dstT, rlo in (("q", LOW1, 0), ("k", LOW1, 64), ("v", LOW2, 64)):
                for qc in range(NQC):
                    bi = 4 + (qc % 2)
                    bank = pb[bi]
                    bank_acq(bi)
                    for dc in range(NDC):
                        tp = (0, 64) if rlo == 64 else None
                        P.op("tensor", "matmul", bank[rlo:rlo + 64, 0:512],
                             acat[pn][:, dc * 64:(dc + 1) * 64],
                             xnT[:, dc * S + qc * 512: dc * S + qc * 512 + 512],
                             start=(dc == 0), stop=(dc == 15),
                             tile_position=tp,
                             inc=(c_mm if dc == 15 else None))
                    v_low = c_mm.nxt()
                    P.wait("vector", c_mm, v_low)
                    P.wait("vector", c_ve, v_m64)
                    P.op("vector", "tensor_scalar",
                         out=dstT[rlo:rlo + 64, qc * 512:(qc + 1) * 512],
                         in0=bank[rlo:rlo + 64, 0:512],
                         scalar1=m64col[rlo:rlo + 64, :], scalar2=None,
                         op0=OP.mult, inc=c_ve)
                    bank_rel(bi, c_ve, c_ve.nxt())
            rdy[("lows", s)] = c_ve.v
            if "low" in _DBG and s == 0:
                P.wait("sync", c_ve, rdy[("lows", s)])
                P.op("sync", "dma_start", out=dbg["low1"].ap()[:, :], in_=LOW1[:, :],
                     inc=d_pre, inck=16)
                d_pre.nxt(16)
            esB = contextlib.ExitStack()
            with esB:
                wsl = {pn: [usb(esB, "wsl_%s%d" % (pn, i), [128, D], BF16)
                    for i in range(2)] for pn in "qkv"}
                qraw = usb(esB, "qraw", [128, S], F32)
                kraw = usb(esB, "kraw", [128, S], F32)
                QT = usb(esB, "QT", [128, S], BF16)
                KT = usb(esB, "KT", [128, S], BF16)
                Vt = usb(esB, "Vt", [128, S], BF16)
                u1 = usb(esB, "u1", [128, S], F32)
                u2 = usb(esB, "u2", [128, S], F32)
                lowo_f = usb(esB, "lowo_f", [64, S], F32)
                stage = [usb(esB, "stage%d" % i, [128, 512], BF16) for i in range(2)]
                pt = [usb(esB, "pt%d" % i, [128, 1024], BF16) for i in range(2)]
                rb_sb = usb(esB, "rb_sb", [128, 512], F32)
                rcp_sb = usb(esB, "rcp_sb", [1, 512], F32)
                rscr = usb(esB, "rscr", [1, 512], F32)
                sq2 = [usb(esB, "sq2_%d" % i, [128, 512], F32) for i in range(2)]

                P.op("vector", "memset", lowo_f[:, :], 0.0, inc=c_ve)
                rdy[("lowo0", s)] = c_ve.nxt()
                phase_fence()
                wnames = {"q": "wqT", "k": "wkT", "v": "wvT"}
                rope_mask = []
                for i in range(16):
                    rope_mask += [2 * i + 1, 2 * i]

                pend = []

                def flush_pend():
                    while pend:
                        pend.pop(0)()

                for h in range(H):
                    hp = h % 2
                    if w_free_v[hp] is not None:
                        P.wait("gpsimd", c_mm, w_free_v[hp])
                    for pn in "qkv":
                        P.op("gpsimd", "dma_start", out=wsl[pn][hp][:, :],
                             in_=AP(ins[wnames[pn]], h * 128 * D,
                                    [[D, 128], [1, D]]),
                             inc=d_w[hp], inck=16)
                        NH_LOADS[hp] += 1
                    wv_ready = 16 * NH_LOADS[hp]

                    def proj(pn, b0, lorlo, bc, lowbuf):
                        P.wait("tensor", d_w[hp], wv_ready)
                        P.wait("tensor", c_ve, rdy[("lows", s)])
                        for qc in range(NQC):
                            bank_acq(b0 + qc)
                        for dc in range(NDC):
                            for qc in range(NQC):
                                P.op("tensor", "matmul", pb[b0 + qc],
                                     wsl[pn][hp][:, dc * 128:(dc + 1) * 128],
                                     xnT[:, dc * S + qc * 512: dc * S + qc * 512 + 512],
                                     start=(dc == 0), stop=False)
                        tp = (64, 0) if lorlo == 64 else None
                        for qc in range(NQC):
                            P.op("tensor", "matmul", pb[b0 + qc],
                                 bc[lorlo:lorlo + 64, h * 128:(h + 1) * 128],
                                 lowbuf[lorlo:lorlo + 64, qc * 512:(qc + 1) * 512],
                                 start=False, stop=True, tile_position=tp,
                                 inc=(c_mm if qc == 3 else None))
                        return c_mm.nxt()

                    def evac_norms(rawbuf, b0, ncol0, v_p):
                        for qc in range(NQC):
                            P.wait("scalar", c_mm, v_p)
                            buf_acq(rawbuf, "scalar")
                            P.op("scalar", "copy",
                                 locals_raw[rawbuf][:, qc * 512:(qc + 1) * 512],
                                 pb[b0 + qc])
                            buf_acq("sq%d" % (qc % 2), "scalar")
                            P.op("scalar", "activation", sq2[qc % 2][:, :],
                                 pb[b0 + qc], ACT.Square, inc=c_sc)
                            v_ev = c_sc.nxt()
                            P.wait("tensor", c_sc, v_ev)
                            P.op("tensor", "matmul", pb[b0 + qc][0:1, 0:512],
                                 onescol_f[:, :], sq2[qc % 2][:, :], start=True,
                                 stop=True, inc=c_mm)
                            v_nm = c_mm.nxt()
                            buf_rel("sq%d" % (qc % 2), c_mm, v_nm)
                            P.wait("vector", c_mm, v_nm)
                            P.op("vector", "reduce_max",
                                 nrm[0:1, ncol0 + qc:ncol0 + qc + 1],
                                 pb[b0 + qc][0:1, 0:512], axis=AX, inc=c_ve)
                            bank_rel(b0 + qc, c_ve, c_ve.nxt())
                        return c_sc.v

                    locals_raw = {"qraw": qraw, "kraw": kraw}

                    v_q = proj("q", 0, 0, bcat1, LOW1)
                    flush_pend()
                    v_qev = evac_norms("qraw", 0, 0, v_q)
                    v_k = proj("k", 4, 64, bcat1, LOW1)
                    v_kev = evac_norms("kraw", 4, 4, v_k)

                    v_rope = {}
                    for rawbuf, dstT, key, v_prod in ((qraw, QT, "q", v_qev),
                                                      (kraw, KT, "k", v_kev)):
                        P.wait("vector", c_sc, v_prod)
                        buf_acq("QT" if key == "q" else "KT", "vector")
                        P.op("vector", "tensor_tensor", out=u1[:, :], in0=rawbuf[:, :],
                             in1=ropeS[:, :], op=OP.mult)
                        P.op("vector", "stream_shuffle", out=u2[:, :], in_=u1[:, :],
                             mask=rope_mask)
                        P.op("vector", "tensor_tensor", out=u1[:, :], in0=rawbuf[:, :],
                             in1=ropeC[:, :], op=OP.mult)
                        P.op("vector", "tensor_tensor", out=dstT[:, :], in0=u2[:, :],
                             in1=u1[:, :], op=OP.add, inc=c_ve)
                        v_rope[key] = c_ve.nxt()
                    buf_rel("qraw", c_ve, v_rope["q"])
                    buf_rel("kraw", c_ve, v_rope["k"])

                    vdrain()
                    P.op("vector", "reduce_max", st1[0:1, 0:1], nrm[0:1, 0:4], axis=AX)
                    P.op("vector", "reduce_max", st2[0:1, 0:1], nrm[0:1, 4:8],
                         axis=AX, inc=c_ve)
                    v_nmx = c_ve.nxt()
                    P.wait("scalar", c_ve, v_nmx)
                    P.op("scalar", "activation", st3[0:1, 0:1], st1[0:1, 0:1],
                         ACT.Ln, bias=cst[0:1, 1:2])
                    P.op("scalar", "activation", st4[0:1, 0:1], st2[0:1, 0:1],
                         ACT.Ln, bias=cst[0:1, 1:2], inc=c_sc)
                    v_ln = c_sc.nxt()
                    P.wait("vector", c_sc, v_ln)
                    P.op("vector", "tensor_tensor", out=st3[0:1, 0:1],
                         in0=st3[0:1, 0:1], in1=st4[0:1, 0:1], op=OP.add, inc=c_ve)
                    v_add2 = c_ve.nxt()
                    P.wait("scalar", c_ve, v_add2)
                    sdrain()
                    P.op("scalar", "activation", st4[0:1, 0:1], st3[0:1, 0:1],
                         ACT.Exp, scale=0.5, bias=cst[0:1, 2:3], inc=c_sc)
                    v_B = c_sc.nxt()
                    P.wait("vector", c_sc, v_B)
                    vdrain()
                    P.op("vector", "tensor_scalar", out=st3[0:1, 0:1],
                         in0=st4[0:1, 0:1], scalar1=-1.0, scalar2=None, op0=OP.mult,
                         inc=c_ve)
                    v_nB = c_ve.nxt()

                    v_v = proj("v", 0, 64, bcat2, LOW2)
                    P.wait("scalar", c_ve, v_rope["q"])
                    for qc in range(NQC):
                        P.wait("scalar", c_mm, v_v)
                        P.op("scalar", "copy", qraw[:, qc * 512:(qc + 1) * 512],
                             pb[qc], inc=c_sc)
                        bank_rel(qc, c_sc, c_sc.nxt())
                    v_vev = c_sc.v
                    buf_acq("Vt", "vector")
                    for g in range(4):
                        bi = 4 + (g % 2)
                        P.wait("tensor", c_sc, v_vev)
                        bank_acq(bi)
                        for k in range(4):
                            tc = 4 * g + k
                            P.op("tensor", "transpose",
                                 pb[bi][:, k * 128:(k + 1) * 128],
                                 qraw[:, tc * 128:(tc + 1) * 128], identf[:, :],
                                 inc=(c_mm if k == 3 else None))
                        v_tr = c_mm.nxt()
                        P.wait("vector", c_mm, v_tr)
                        P.op("vector", "tensor_copy",
                             out=Vt[:, g * 512:(g + 1) * 512],
                             in_=pb[bi][:, 0:512],
                             inc=c_ve)
                        bank_rel(bi, c_ve, c_ve.nxt())
                    v_vt = c_ve.v
                    buf_rel("qraw", c_mm, c_mm.v)
                    w_free_v[hp] = c_mm.v

                    P.wait("tensor", c_ve, v_nB)
                    bank_acq(6)
                    P.op("tensor", "matmul", pb[6][:, 0:1], onesrow_f[:, :],
                         st3[0:1, 0:1], start=True, stop=True, inc=c_mm)
                    v_nBb = c_mm.nxt()
                    P.wait("scalar", c_mm, v_nBb)
                    P.op("scalar", "copy", negB[:, :], pb[6][:, 0:1], inc=c_sc)
                    v_negB = c_sc.nxt()
                    bank_rel(6, c_sc, v_negB)

                    for qc in range(NQC):
                        jmax = 4 * qc + 4
                        ob_i = 4 + (qc % 2)
                        ob = pb[ob_i]
                        dbk = pb[6]
                        npair = jmax // 2

                        def emit_mm1(p_):
                            j0, j1 = 2 * p_, 2 * p_ + 1
                            ti = p_ % 2
                            T = scA if ti == 0 else scB
                            bank_acq(2 * ti)
                            bank_acq(2 * ti + 1)
                            P.wait("tensor", c_ve, v_rope["k"])
                            P.op("tensor", "matmul", T[:, 0:512],
                                 KT[:, j0 * 128:(j0 + 1) * 128],
                                 QT[:, qc * 512:(qc + 1) * 512],
                                 start=True, stop=True)
                            P.op("tensor", "matmul", T[:, 512:1024],
                                 KT[:, j1 * 128:(j1 + 1) * 128],
                                 QT[:, qc * 512:(qc + 1) * 512],
                                 start=True, stop=True, inc=c_mm)
                            v_mm1 = c_mm.nxt()
                            P.wait("scalar", c_mm, v_mm1)
                            P.wait("scalar", c_sc, v_negB)
                            buf_acq("pt%d" % ti, "scalar")
                            P.op("scalar", "activation", pt[ti][:, :],
                                 T[:, 0:1024], ACT.Exp, bias=negB[:, :],
                                 scale=SCL, inc=c_sc)
                            v_exp = c_sc.nxt()
                            bank_rel(2 * ti, c_sc, v_exp)
                            bank_rel(2 * ti + 1, c_sc, v_exp)
                            v_msk = 0
                            for j in (j0, j1):
                                if j >= 4 * qc:
                                    m = j - 4 * qc
                                    half = (j % 2) * 512
                                    P.wait("vector", c_sc, v_exp)
                                    if m > 0:
                                        P.op("vector", "memset",
                                             pt[ti][:, half:half + m * 128], 0.0)
                                    P.op("vector", "tensor_tensor",
                                         out=pt[ti][:, half + m * 128:half + (m + 1) * 128],
                                         in0=pt[ti][:, half + m * 128:half + (m + 1) * 128],
                                         in1=tri[:, :], op=OP.mult, inc=c_ve)
                                    v_msk = c_ve.nxt()
                            return v_exp, v_msk

                        def emit_mm23(p_, v_exp, v_msk):
                            j0, j1 = 2 * p_, 2 * p_ + 1
                            ti = p_ % 2
                            P.wait("tensor", c_sc, v_exp)
                            if v_msk:
                                P.wait("tensor", c_ve, v_msk)
                            P.wait("tensor", c_ve, v_vt)
                            P.op("tensor", "matmul", ob,
                                 Vt[:, j0 * 128:(j0 + 1) * 128], pt[ti][:, 0:512],
                                 start=(j0 == 0), stop=False)
                            P.op("tensor", "matmul", ob,
                                 Vt[:, j1 * 128:(j1 + 1) * 128], pt[ti][:, 512:1024],
                                 start=False, stop=(j1 == jmax - 1))
                            P.op("tensor", "matmul", dbk[0:1, 0:512],
                                 onescol_b[:, :], pt[ti][:, 0:512],
                                 start=(j0 == 0), stop=False)
                            P.op("tensor", "matmul", dbk[0:1, 0:512],
                                 onescol_b[:, :], pt[ti][:, 512:1024],
                                 start=False, stop=(j1 == jmax - 1), inc=c_mm)
                            buf_rel("pt%d" % ti, c_mm, c_mm.nxt())

                        prev = None
                        for p_ in range(npair):
                            e_m = emit_mm1(p_)
                            if p_ == 0:
                                flush_pend()
                                bank_acq(ob_i)
                                bank_acq(6)
                            if prev is not None:
                                emit_mm23(p_ - 1, *prev)
                            prev = e_m
                        emit_mm23(npair - 1, *prev)
                        v_pv = c_mm.v
                        sp = qc % 2
                        P.wait("vector", c_mm, v_pv)
                        buf_acq("rcp_sb", "vector")
                        P.op("vector", "reciprocal", rcp_sb[:, :],
                             dbk[0:1, 0:512], inc=c_ve)
                        v_rc = c_ve.nxt()
                        bank_rel(6, c_ve, v_rc)

                        def make_tail(qc=qc, sp=sp, ob=ob, ob_i=ob_i, v_rc=v_rc,
                                      h=h, s=s):
                            def tail():
                                P.wait("tensor", c_ve, v_rc)
                                bank_acq(7)
                                P.op("tensor", "matmul", pb[7], onesrow_f[:, :],
                                     rcp_sb[:, :], start=True, stop=True, inc=c_mm)
                                v_rbmm = c_mm.nxt()
                                buf_rel("rcp_sb", c_mm, v_rbmm)
                                P.wait("scalar", c_mm, v_rbmm)
                                buf_acq("rb_sb", "scalar")
                                P.op("scalar", "copy", rb_sb[:, :], pb[7], inc=c_sc)
                                v_rbev = c_sc.nxt()
                                bank_rel(7, c_sc, v_rbev)
                                P.wait("vector", c_sc, v_rbev)
                                P.wait("vector", d_as[sp], 16 * as_st[sp])
                                buf_acq("stage%d" % sp, "vector")
                                P.op("vector", "tensor_tensor", out=stage[sp][:, :],
                                     in0=ob, in1=rb_sb[:, :], op=OP.mult,
                                     inc=c_ve)
                                v_st = c_ve.nxt()
                                bank_rel(ob_i, c_ve, v_st)
                                buf_rel("rb_sb", c_ve, v_st)
                                P.wait("sync", c_ve, v_st)
                                P.op("sync", "dma_start",
                                     out=AP(attT_d, h * 128 * S + qc * 512,
                                            [[S, 128], [1, 512]]),
                                     in_=stage[sp][:, :], inc=d_as[sp], inck=16)
                                as_st[sp] += 1
                                P.wait("tensor", c_ve, v_st)
                                bank_acq(6)
                                P.op("tensor", "matmul", pb[6][0:64, 0:512],
                                     acat["o"][:, h * 64:(h + 1) * 64],
                                     stage[sp][:, :],
                                     start=True, stop=True, inc=c_mm)
                                v_ol = c_mm.nxt()
                                buf_rel("stage%d" % sp, c_mm, v_ol)
                                P.wait("vector", c_mm, v_ol)
                                P.wait("vector", c_ve, rdy[("lowo0", s)])
                                P.op("vector", "tensor_tensor",
                                     out=lowo_f[:, qc * 512:(qc + 1) * 512],
                                     in0=lowo_f[:, qc * 512:(qc + 1) * 512],
                                     in1=pb[6][0:64, 0:512], op=OP.add, inc=c_ve)
                                v_lo = c_ve.nxt()
                                bank_rel(6, c_ve, v_lo)
                            return tail
                        pend.append(make_tail())
                    buf_rel("QT", c_mm, c_mm.v)
                    buf_rel("KT", c_mm, c_mm.v)
                    buf_rel("Vt", c_mm, c_mm.v)
                flush_pend()
                P.wait("vector", c_mm, c_mm.v)
                P.op("vector", "tensor_scalar", out=LOW2[0:64, :],
                     in0=lowo_f[:, :], scalar1=m64col[0:64, :], scalar2=None,
                     op0=OP.mult, inc=c_ve)
                rdy[("lowo_bf", s)] = c_ve.nxt()
                buf_rel("lowo_f", c_ve, rdy[("lowo_bf", s)])
                if "low" in _DBG and s == 0:
                    P.wait("sync", c_ve, rdy[("lowo_bf", s)])
                    P.op("sync", "dma_start", out=dbg["low2"].ap()[:, :],
                         in_=LOW2[:, :], inc=d_pre, inck=16)
                    d_pre.nxt(16)

            esF = contextlib.ExitStack()
            with esF:
                w1sl = [usb(esF, "w1sl%d" % i, [128, D], BF16)
                        for i in range(3)]
                h1b = usb(esF, "h1b", [128, 16 * 1024], BF16)
                h2b = usb(esF, "h2b", [128, 16 * 1024], BF16)
                h2tok = usb(esF, "h2tok", [128, 2048], BF16)
                phase_fence(extra=((d_as[0], 16 * as_st[0]), (d_as[1], 16 * as_st[1])))

                for hf in range(2):
                    tok0 = hf * 1024
                    for ic in range(16):
                        icp = ic % 3
                        if w_free_v[icp] is not None:
                            P.wait("gpsimd", c_mm, w_free_v[icp])
                        P.op("gpsimd", "dma_start", out=w1sl[icp][:, :],
                             in_=AP(ins["w1T"], ic * 128 * D,
                                    [[D, 128], [1, D]]),
                             inc=d_w[icp], inck=16)
                        NH_LOADS[icp] += 1
                        wv_r = 16 * NH_LOADS[icp]
                        P.wait("tensor", d_w[icp], wv_r)
                        bank_acq(6)
                        bank_acq(7)
                        for dc in range(NDC):
                            for qcc in range(2):
                                col0 = tok0 + qcc * 512
                                P.op("tensor", "matmul", pb[6 + qcc],
                                     w1sl[icp][:, dc * 128:(dc + 1) * 128],
                                     xnT[:, dc * S + col0: dc * S + col0 + 512],
                                     start=(dc == 0), stop=(dc == 15),
                                     inc=(c_mm if (dc == 15 and qcc == 1) else None))
                        v_h = c_mm.nxt()
                        for qcc in range(2):
                            P.wait("scalar", c_mm, v_h)
                            P.op("scalar", "activation",
                                 h1b[:, ic * 1024 + qcc * 512: ic * 1024 + qcc * 512 + 512],
                                 pb[6 + qcc], ACT.Silu, inc=c_sc)
                            bank_rel(6 + qcc, c_sc, c_sc.nxt())
                        w_free_v[icp] = c_mm.v
                    v_h1_done = c_sc.v
                    for ic2 in range(16):
                        icp = ic2 % 3
                        if w_free_v[icp] is not None:
                            P.wait("gpsimd", c_mm, w_free_v[icp])
                        P.op("gpsimd", "dma_start", out=w1sl[icp][:, :],
                             in_=AP(ins["w3T"], ic2 * 128 * D,
                                    [[D, 128], [1, D]]),
                             inc=d_w[icp], inck=16)
                        NH_LOADS[icp] += 1
                        wv_r = 16 * NH_LOADS[icp]
                        P.wait("tensor", d_w[icp], wv_r)
                        P.wait("tensor", c_sc, v_h1_done)
                        bank_acq(6)
                        bank_acq(7)
                        for i1 in range(NDC):
                            for qcc in range(2):
                                P.op("tensor", "matmul", pb[6 + qcc],
                                     w1sl[icp][:, i1 * 128:(i1 + 1) * 128],
                                     h1b[:, i1 * 1024 + qcc * 512: i1 * 1024 + qcc * 512 + 512],
                                     start=(i1 == 0), stop=(i1 == 15),
                                     inc=(c_mm if (i1 == 15 and qcc == 1) else None))
                        v_h = c_mm.nxt()
                        for qcc in range(2):
                            P.wait("scalar", c_mm, v_h)
                            P.op("scalar", "activation",
                                 h2b[:, ic2 * 1024 + qcc * 512: ic2 * 1024 + qcc * 512 + 512],
                                 pb[6 + qcc], ACT.Silu, inc=c_sc)
                            bank_rel(6 + qcc, c_sc, c_sc.nxt())
                        w_free_v[icp] = c_mm.v
                    v_h2_done = c_sc.v
                    if "h2" in _DBG and s == 0 and hf == 0:
                        P.wait("sync", c_sc, v_h2_done)
                        P.op("sync", "dma_start", out=dbg["h1"].ap()[:, :],
                             in_=h1b[:, :], inc=d_pre, inck=16)
                        P.op("sync", "dma_start", out=dbg["h2"].ap()[:, :],
                             in_=h2b[:, :], inc=d_pre, inck=16)
                        d_pre.nxt(32)
                    P.wait("tensor", c_sc, v_h2_done)
                    P.wait("tensor", c_ve, rdy[("gates", s)])
                    for tc in range(8):
                        for g in range(4):
                            bi = g % 2
                            bank_acq(bi)
                            buf_acq("h2tok_g%d" % g, "vector")
                            for k in range(4):
                                i2 = 4 * g + k
                                P.op("tensor", "transpose",
                                     pb[bi][:, k * 64:(k + 1) * 64].bitcast(BF16),
                                     h2b[:, i2 * 1024 + tc * 128: i2 * 1024 + tc * 128 + 128],
                                     identb[:, :],
                                     inc=(c_mm if k == 3 else None))
                            v_tr = c_mm.nxt()
                            P.wait("vector", c_mm, v_tr)
                            P.op("vector", "tensor_copy",
                                 out=h2tok[:, g * 512:(g + 1) * 512],
                                 in_=pb[bi][:, 0:256].bitcast(BF16),
                                 inc=c_ve)
                            bank_rel(bi, c_ve, c_ve.nxt())
                        v_ht = c_ve.v
                        P.wait("tensor", c_ve, v_ht)
                        tglob = hf * 8 + tc
                        if hf == 0 and tc == 0:
                            for gq in range(4):
                                bank_acq(2 + gq)
                        for gq in range(4):
                            P.op("tensor", "matmul", pb[2 + gq][0:8, 0:512],
                                 gates[:, tglob * 8:(tglob + 1) * 8],
                                 h2tok[:, gq * 512:(gq + 1) * 512],
                                 start=(hf == 0 and tc == 0),
                                 stop=(hf == 1 and tc == 7),
                                 inc=(c_mm if gq == 3 else None))
                        v_gmm = c_mm.nxt()
                        for g in range(4):
                            buf_rel("h2tok_g%d" % g, c_mm, v_gmm)
                v_gmm_all = c_mm.v
                P.wait("vector", c_mm, v_gmm_all)
                for gq in range(4):
                    if s == 0:
                        P.op("vector", "tensor_copy",
                             out=Gsb[0:8, gq * 512:(gq + 1) * 512],
                             in_=pb[2 + gq][0:8, 0:512],
                             inc=(c_ve if gq == 3 else None))
                    else:
                        P.op("vector", "tensor_tensor",
                             out=Gsb[0:8, gq * 512:(gq + 1) * 512],
                             in0=Gsb[0:8, gq * 512:(gq + 1) * 512],
                             in1=pb[2 + gq][0:8, 0:512], op=OP.add,
                             inc=(c_ve if gq == 3 else None))
                v_gsb = c_ve.nxt()
                rdy[("Gsb", s)] = v_gsb
                for gq in range(4):
                    bank_rel(2 + gq, c_ve, v_gsb)
                buf_rel("xnT", c_mm, c_mm.v)
                buf_rel("gates", c_mm, c_mm.v)
                buf_rel("logit_sb", c_ve, rdy[("gates", s)])

            esC = contextlib.ExitStack()
            with esC:
                wo_sb = usb(esC, "wo_sb", [128, NDC * D], BF16)
                attld = [usb(esC, "attld%d" % i, [128, 2048], BF16)
                         for i in range(2)]
                xsbC = [usb(esC, "xsbC%d" % i, [128, D], F32)
                        for i in range(2)]
                outb = [usb(esC, "outb%d" % i, [128, 512], F32)
                        for i in range(2)]
                phase_fence()
                for dcg in range(4):
                    P.op("gpsimd", "dma_start",
                         out=AP(wo_sb, dcg * 4 * D, [[NDC * D, 128], [1, 4 * D]]),
                         in_=AP(ins["woT"], dcg * 4 * D,
                                [[16 * D, 128], [1, 4 * D]]),
                         inc=d_wo, inck=16)
                v_wo = d_wo.nxt(64)
                P.wait("tensor", d_wo, v_wo)
                P.wait("sync", d_as[0], 16 * as_st[0])
                P.wait("sync", d_as[1], 16 * as_st[1])
                for tt in range(NTT):
                    tp_ = tt % 2
                    if at_free_v[tp_] is not None:
                        P.wait("sync", c_mm, at_free_v[tp_])
                    P.op("sync", "dma_start",
                         out=attld[tp_][:, :],
                         in_=AP(attT_d, tt * 128,
                                [[S, 128], [128 * S, 16], [1, 128]]),
                         inc=d_at[tp_], inck=16)
                    at_loads[tp_] += 1
                    v_at = 16 * at_loads[tp_]
                    if xC_free_v[tp_] is not None:
                        P.wait("sync", c_ve, xC_free_v[tp_])
                    P.op("sync", "dma_start", out=xsbC[tp_][:, :],
                         in_=x_d.ap()[row0 + tt * 128: row0 + (tt + 1) * 128, :],
                         inc=d_x[tp_], inck=16)
                    x_loads[tp_] += 1
                    v_x = 16 * x_loads[tp_]
                    bgrp_i = [4 * tp_ + i for i in range(4)]
                    P.wait("tensor", d_at[tp_], v_at)
                    P.wait("tensor", c_ve, rdy[("lowo_bf", s)])
                    for bi in bgrp_i:
                        bank_acq(bi)
                    for dc in range(NDC):
                        for oci, bi in enumerate(bgrp_i):
                            P.op("tensor", "matmul", pb[bi][:, 0:512],
                                 attld[tp_][:, dc * 128:(dc + 1) * 128],
                                 wo_sb[:, dc * D + oci * 512: dc * D + oci * 512 + 512],
                                 start=(dc == 0), stop=False)
                    for oci, bi in enumerate(bgrp_i):
                        P.op("tensor", "matmul", pb[bi][:, 0:512],
                             LOW2[0:64, tt * 128:(tt + 1) * 128],
                             bcat2[0:64, oci * 512:(oci + 1) * 512],
                             start=False, stop=True,
                             inc=(c_mm if oci == 3 else None))
                    v_ao = c_mm.nxt()
                    at_free_v[tp_] = v_ao
                    P.wait("vector", c_mm, v_ao)
                    P.wait("vector", d_x[tp_], v_x)
                    for oci, bi in enumerate(bgrp_i):
                        po = oci % 2
                        P.wait("vector", d_out[po], 16 * out_st[po])
                        P.op("vector", "tensor_tensor",
                             out=outb[po][:, :],
                             in0=pb[bi][:, 0:512],
                             in1=xsbC[tp_][:, oci * 512:(oci + 1) * 512],
                             op=OP.add, inc=c_ve)
                        v_epi = c_ve.nxt()
                        bank_rel(bi, c_ve, v_epi)
                        P.wait("sync", c_ve, v_epi)
                        P.op("sync", "dma_start",
                             out=out_d.ap()[row0 + tt * 128: row0 + (tt + 1) * 128,
                                            oci * 512:(oci + 1) * 512],
                             in_=outb[po][:, :], inc=d_out[po], inck=16)
                        out_st[po] += 1
                    v_epi = c_ve.v
                    rdy[("epi", s, tt)] = v_epi
                    xC_free_v[tp_] = v_epi
                buf_rel("LOW1", c_mm, c_mm.v)
                buf_rel("LOW2", c_mm, c_mm.v)

        esS = contextlib.ExitStack()
        with esS:
            GT_sb = usb(esS, "GT_sb", [128, 128], BF16)
            GT_lo = usb(esS, "GT_lo", [128, 128], BF16)
            w2f = [usb(esS, "w2f%d" % i, [128, D], F32) for i in range(2)]
            w2lo = [usb(esS, "w2lo%d" % i, [128, D], BF16) for i in range(2)]
            w2sl = [usb(esS, "w2sl%d" % i, [128, D], BF16)
                    for i in range(2)]
            patch = usb(esS, "patch", [8, D], F32)
            Ssum = usb(esS, "Ssum", [8, D], F32)
            phase_fence(extra=((d_out[0], 16 * out_st[0]),
                               (d_out[1], 16 * out_st[1])))
            P.wait("tensor", c_ve, rdy[("Gsb", 1)])
            for g in range(4):
                bi = 4 + (g % 2)
                bank_acq(bi)
                for k in range(4):
                    ic = 4 * g + k
                    P.op("tensor", "transpose", pb[bi][:, k * 8:(k + 1) * 8],
                         Gsb[0:8, ic * 128:(ic + 1) * 128], identf[0:8, 0:8],
                         inc=(c_mm if k == 3 else None))
                v_tr = c_mm.nxt()
                P.wait("vector", c_mm, v_tr)
                P.op("vector", "tensor_copy",
                     out=GT_sb[:, (4 * g) * 8:(4 * g + 4) * 8],
                     in_=pb[bi][:, 0:32])
                vdrain()
                P.op("vector", "tensor_tensor",
                     out=GT_lo[:, (4 * g) * 8:(4 * g + 4) * 8],
                     in0=pb[bi][:, 0:32],
                     in1=GT_sb[:, (4 * g) * 8:(4 * g + 4) * 8],
                     op=OP.subtract,
                     inc=c_ve)
                bank_rel(bi, c_ve, c_ve.nxt())
            v_gt = c_ve.v
            for ic in range(16):
                icp = ic % 2
                if w_free_v[icp] is not None:
                    P.wait("gpsimd", c_mm, w_free_v[icp])
                P.wait("gpsimd", c_ve, rdy.get(("w2lo", ic - 2), 0))
                P.op("gpsimd", "dma_start", out=w2sl[icp][:, :],
                     in_=ins["w2T"].ap()[ic * 128:(ic + 1) * 128, :],
                     inc=d_w[icp], inck=16)
                P.op("gpsimd", "dma_start", out=w2f[icp][:, :],
                     in_=ins["w2T"].ap()[ic * 128:(ic + 1) * 128, :],
                     inc=d_w[icp], inck=16)
                NH_LOADS[icp] += 2
                wv_r = 16 * NH_LOADS[icp]
                P.wait("vector", d_w[icp], wv_r)
                P.op("vector", "tensor_tensor", out=w2lo[icp][:, :],
                     in0=w2f[icp][:, :], in1=w2sl[icp][:, :], op=OP.subtract,
                     inc=c_ve)
                rdy[("w2lo", ic)] = c_ve.nxt()
                P.wait("tensor", d_w[icp], wv_r)
                P.wait("tensor", c_ve, rdy[("w2lo", ic)])
                P.wait("tensor", c_ve, v_gt)
                if ic == 0:
                    for oc in range(4):
                        bank_acq(oc)
                for oc in range(4):
                    P.op("tensor", "matmul", pb[oc][0:8, 0:512],
                         GT_sb[:, ic * 8:(ic + 1) * 8],
                         w2sl[icp][:, oc * 512:(oc + 1) * 512],
                         start=(ic == 0), stop=False)
                    P.op("tensor", "matmul", pb[oc][0:8, 0:512],
                         GT_lo[:, ic * 8:(ic + 1) * 8],
                         w2sl[icp][:, oc * 512:(oc + 1) * 512],
                         start=False, stop=False)
                    P.op("tensor", "matmul", pb[oc][0:8, 0:512],
                         GT_sb[:, ic * 8:(ic + 1) * 8],
                         w2lo[icp][:, oc * 512:(oc + 1) * 512],
                         start=False, stop=(ic == 15),
                         inc=(c_mm if oc == 3 else None))
                c_mm.nxt()
                w_free_v[icp] = c_mm.v
            v_smm = c_mm.v
            P.wait("vector", c_mm, v_smm)
            for oc in range(4):
                P.op("vector", "tensor_copy",
                     out=Ssum[0:8, oc * 512:(oc + 1) * 512],
                     in_=pb[oc][0:8, 0:512], inc=(c_ve if oc == 3 else None))
            v_S = c_ve.nxt()
            if "G" in _DBG:
                P.wait("sync", c_ve, v_S)
                P.op("sync", "dma_start", out=dbg["G"].ap()[:, :], in_=Gsb[0:8, :],
                     inc=d_pre, inck=16)
                d_pre.nxt(16)
            P.wait("gpsimd", c_ve, v_S)
            P.op("gpsimd", "dma_start", out=cc_in.ap()[:, :], in_=Ssum[0:8, :],
                 inc=d_cc, inck=16)
            v_cc1 = d_cc.nxt(16)
            P.wait("gpsimd", d_cc, v_cc1)
            P.op("gpsimd", "collective_compute", "AllReduce", OP.add,
                 replica_groups=[list(range(NCORES))],
                 ins=[cc_in.ap().opt()], outs=[cc_out.ap().opt()], inc=c_cc)
            v_ccd = c_cc.nxt()
            P.wait("gpsimd", c_cc, v_ccd)
            P.op("gpsimd", "dma_start", out=Ssum[0:8, :], in_=cc_out.ap()[:, :],
                 inc=d_cc, inck=16)
            v_cc2 = d_cc.nxt(16)
            P.wait("vector", d_cc, v_cc2)
            P.op("vector", "tensor_scalar", out=Ssum[0:8, :], in0=Ssum[0:8, :],
                 scalar1=c0m_sb[:, :], scalar2=None, op0=OP.mult, inc=c_ve)
            v_Sm = c_ve.nxt()
            if "S" in _DBG:
                P.wait("sync", c_ve, v_Sm)
                P.op("sync", "dma_start", out=dbg["S"].ap()[:, :], in_=Ssum[0:8, :],
                     inc=d_pre, inck=16)
                d_pre.nxt(16)
            P.wait("sync", d_out[0], 16 * out_st[0])
            P.wait("sync", d_out[1], 16 * out_st[1])
            P.op("sync", "dma_start", out=patch[:, :], in_=out_d.ap()[0:8, :],
                 inc=d_pt, inck=16)
            v_cc3 = d_pt.nxt(16)
            P.wait("vector", d_pt, v_cc3)
            P.wait("vector", c_ve, v_Sm)
            P.op("vector", "tensor_tensor", out=patch[:, :], in0=patch[:, :],
                 in1=Ssum[0:8, :], op=OP.add, inc=c_ve)
            v_pt = c_ve.nxt()
            P.wait("sync", c_ve, v_pt)
            P.op("sync", "dma_start", out=out_d.ap()[0:8, :], in_=patch[:, :],
                 inc=d_cc, inck=16)
            v_cc4 = d_cc.nxt(16)
            if "snap" in _DBG:
                P.op("sync", "dma_start", out=dbg["snap"].ap()[:, :],
                     in_=snap[:, :], inc=d_pre, inck=16)
                d_pre.nxt(16)
            P.wait("sync", d_cc, v_cc4)
            P.wait("sync", d_pre, d_pre.v)

        block = es.enter_context(nc.Block())
        block.sync(lambda e: P.replay("sync", e))
        block.gpsimd(lambda e: P.replay("gpsimd", e))
        block.vector(lambda e: P.replay("vector", e))
        block.scalar(lambda e: P.replay("scalar", e))
        block.tensor(lambda e: P.replay("tensor", e))
    return nc


_nc_cache = {}


def kernel(**inputs):
    if "nc" not in _nc_cache:
        _nc_cache["nc"] = build_nc()
    nc = _nc_cache["nc"]
    g = _host_prep(inputs)
    x = np.ascontiguousarray(np.asarray(inputs["L0_attn_x"], dtype=np.float32))
    in_maps = []
    for c in range(NCORES):
        m = dict(g)
        m["x"] = np.ascontiguousarray(x[2 * c:2 * c + 2].reshape(NSEQ * S, D))
        m["bval"] = _f32(np.array([[2 * c, 2 * c + 1]]))
        m["c0m"] = _f32(np.full((8, 1), 1.0 if c == 0 else 0.0))
        in_maps.append(m)
    res = run_bass_kernel_spmd(nc, in_maps, core_ids=list(range(NCORES)))
    if _debug_capture is not None:
        _debug_capture.append(res)
    out = np.stack([np.asarray(res.results[c]["out"]).reshape(NSEQ, S, D)
                    for c in range(NCORES)])
    return np.ascontiguousarray(out.reshape(16, S, D).astype(np.float32))


# version 15
# speedup vs baseline: 1.1946x; 1.0330x over previous
import contextlib
import numpy as np
import ml_dtypes
import concourse.bass as bass
import concourse.mybir as mybir
from concourse.bass_utils import run_bass_kernel_spmd

F32 = mybir.dt.float32
BF16 = mybir.dt.bfloat16
I32 = mybir.dt.int32
OP = mybir.AluOpType
ACT = mybir.ActivationFunctionType
AX = mybir.AxisListType.X

NCORES = 8
NSEQ = 2
S = 2048
D = 2048
H = 16
DKV = 128
NTT = 16
NDC = 16
NQC = 4
SCL = float(1.0 / np.sqrt(np.float32(128.0)))
ROPE_THETA = 10000.0

_DBG = set()
_debug_capture = None


class Sem:
    def __init__(self, h):
        self.h = h
        self.v = 0

    def nxt(self, k=1):
        self.v += k
        return self.v


class Prog:
    ENGS = ("sync", "gpsimd", "vector", "scalar", "tensor")

    def __init__(self):
        self.s = {e: [] for e in self.ENGS}

    def op(self, eng, meth, *a, inc=None, inck=1, **kw):
        if (inc is not None and eng in ("vector", "scalar")
                and meth not in ("nop", "wait_ge", "drain")):
            self.s[eng].append((meth, a, kw, None, 0))
            self.s[eng].append(("drain", (), {}, None, 0))
            self.s[eng].append(("nop", (), {}, inc, inck))
        else:
            self.s[eng].append((meth, a, kw, inc, inck))

    def wait(self, eng, sem, v):
        if v > 0:
            self.s[eng].append(("wait_ge", (sem.h, v), {}, None, 0))

    def replay(self, eng_name, e):
        for meth, a, kw, inc, inck in self.s[eng_name]:
            inst = getattr(e, meth)(*a, **kw)
            if inc is not None:
                inst.then_inc(inc.h, inck)


def _bf(x):
    return np.ascontiguousarray(np.asarray(x).astype(ml_dtypes.bfloat16))


def _f32(x):
    return np.ascontiguousarray(np.asarray(x).astype(np.float32))


def _host_constants():
    c = {}
    c["ident_f"] = _f32(np.eye(128))
    c["ident_b"] = _bf(np.eye(128))
    r = np.arange(128)
    c["tri_b"] = _bf((r[None, :] >= r[:, None]).astype(np.float32))
    c["onescol_b"] = _bf(np.ones((128, 1)))
    c["onescol_f"] = _f32(np.ones((128, 1)))
    c["onesrow_f"] = _f32(np.ones((1, 128)))
    c["iota128col"] = _f32(((np.arange(128) % 64) // 16).reshape(128, 1))
    pos = np.arange(S, dtype=np.float64)
    freq = ROPE_THETA ** (-(np.arange(64, dtype=np.float64)) / 64.0)
    ang = pos[None, :] * freq[:, None]
    cosv = np.cos(ang)
    sinv = np.sin(ang)
    ctab = np.empty((128, S), dtype=np.float64)
    stab = np.empty((128, S), dtype=np.float64)
    ctab[0::2] = cosv
    ctab[1::2] = cosv
    stab[0::2] = sinv
    stab[1::2] = -sinv
    c["ropeC"] = _f32(ctab)
    c["ropeS"] = _f32(stab)
    cv = np.zeros((128, 4))
    cv[:, 0] = 1e-5
    cv[:, 1] = 1e-30
    cv[:, 2] = float(np.log(1.0 / np.sqrt(np.float32(128.0))) + np.log(1.03))
    cv[:, 3] = 1.0
    c["cvec"] = _f32(cv)
    return c


def _host_prep(inputs):
    g = {}
    def _tile_heads(WT):
        return _f32(WT.reshape(16, 128, H, DKV).transpose(2, 1, 0, 3)
                    .reshape(H, 128, D))

    g["wqT"] = _tile_heads(np.asarray(inputs["L0_attn_Wq"]).T)
    g["wkT"] = _tile_heads(np.asarray(inputs["L0_attn_Wk"]).T)
    g["wvT"] = _tile_heads(np.asarray(inputs["L0_attn_Wv"]).T)
    g["woT"] = _f32(np.asarray(inputs["L0_attn_Wo"]).T
                    .reshape(16, 128, D).transpose(1, 0, 2).reshape(128, 16 * D))
    g["w1T"] = _tile_heads(np.asarray(inputs["L0_ffn_w1"]).T)
    g["w3T"] = _tile_heads(np.asarray(inputs["L0_ffn_w3"]).T)
    g["w2T"] = _f32(np.asarray(inputs["L0_ffn_w2"]).T)
    g["wr_sb"] = _f32(np.asarray(inputs["L0_ffn_wr"]).T.reshape(16, 128, 8)
                      .transpose(1, 0, 2).reshape(128, 128))
    for p, W in (("q", "L0_attn_Aq"), ("k", "L0_attn_Ak"),
                 ("v", "L0_attn_Av"), ("o", "L0_attn_Ao")):
        acat = np.asarray(inputs[W]).transpose(2, 0, 1).reshape(D, 64)
        g["acat_" + p] = _bf(acat.reshape(16, 128, 64).transpose(1, 0, 2)
                             .reshape(128, 1024))
    bq = np.asarray(inputs["L0_attn_Bq"]).transpose(0, 2, 1).reshape(64, D)
    bk = np.asarray(inputs["L0_attn_Bk"]).transpose(0, 2, 1).reshape(64, D)
    bv = np.asarray(inputs["L0_attn_Bv"]).transpose(0, 2, 1).reshape(64, D)
    bo = np.asarray(inputs["L0_attn_Bo"]).transpose(0, 2, 1).reshape(64, D)
    g["bcat1"] = _bf(np.concatenate([bq, bk], axis=0))
    g["bcat2"] = _bf(np.concatenate([bo, bv], axis=0))
    g["starts"] = np.ascontiguousarray(
        np.asarray(inputs["L0_attn_lora_start_indices"]).astype(np.int32).reshape(1, 4))
    g.update(_host_constants())
    return g


def build_nc():
    nc = bass.Bass("TRN2", target_bir_lowering=False)
    P = Prog()

    x_d = nc.dram_tensor("x", [NSEQ * S, D], F32, kind="ExternalInput")
    out_d = nc.dram_tensor("out", [NSEQ * S, D], F32, kind="ExternalOutput")
    ins = {}
    for name, shp, dt in [
        ("wqT", [H, 128, D], F32), ("wkT", [H, 128, D], F32),
        ("wvT", [H, 128, D], F32), ("woT", [128, 16 * D], F32),
        ("w1T", [16, 128, D], F32), ("w3T", [16, 128, D], F32),
        ("w2T", [D, D], F32), ("wr_sb", [128, 128], F32),
        ("acat_q", [128, 1024], BF16), ("acat_k", [128, 1024], BF16),
        ("acat_v", [128, 1024], BF16), ("acat_o", [128, 1024], BF16),
        ("bcat1", [128, D], BF16), ("bcat2", [128, D], BF16),
        ("starts", [1, 4], I32),
        ("ident_f", [128, 128], F32), ("ident_b", [128, 128], BF16),
        ("tri_b", [128, 128], BF16),
        ("onescol_b", [128, 1], BF16), ("onescol_f", [128, 1], F32),
        ("onesrow_f", [1, 128], F32), ("iota128col", [128, 1], F32),
        ("ropeC", [128, S], F32), ("ropeS", [128, S], F32),
        ("bval", [1, 2], F32), ("c0m", [8, 1], F32), ("cvec", [128, 4], F32),
    ]:
        ins[name] = nc.dram_tensor(name, shp, dt, kind="ExternalInput")
    attT_d = nc.dram_tensor("attT", [H, 128, S], BF16,
                            kind="ExternalOutput" if "attT" in _DBG else "Internal")
    cc_in = nc.dram_tensor("cc_in", [8, D], F32)
    cc_out = nc.dram_tensor("cc_out", [8, D], F32)
    dbg = {}
    if "xnT" in _DBG:
        dbg["xnT"] = nc.dram_tensor("dbg_xnT", [128, NDC * S], BF16, kind="ExternalOutput")
    if "gates" in _DBG:
        dbg["logits"] = nc.dram_tensor("dbg_logits", [128, 128], F32, kind="ExternalOutput")
        dbg["gates"] = nc.dram_tensor("dbg_gates", [128, 128], BF16, kind="ExternalOutput")
    if "qkv" in _DBG:
        for n in ("qraw", "QT", "KT", "V"):
            dbg[n] = nc.dram_tensor("dbg_" + n, [128, S], BF16, kind="ExternalOutput")
        dbg["negB"] = nc.dram_tensor("dbg_negB", [128, 1], F32, kind="ExternalOutput")
    if "low" in _DBG:
        dbg["low1"] = nc.dram_tensor("dbg_low1", [128, S], BF16, kind="ExternalOutput")
        dbg["low2"] = nc.dram_tensor("dbg_low2", [128, S], BF16, kind="ExternalOutput")
    if "h2" in _DBG:
        dbg["h1"] = nc.dram_tensor("dbg_h1", [128, 16 * 1024], BF16, kind="ExternalOutput")
        dbg["h2"] = nc.dram_tensor("dbg_h2", [128, 16 * 1024], BF16, kind="ExternalOutput")
    if "G" in _DBG:
        dbg["G"] = nc.dram_tensor("dbg_G", [8, D], F32, kind="ExternalOutput")
    if "S" in _DBG:
        dbg["S"] = nc.dram_tensor("dbg_S", [8, D], F32, kind="ExternalOutput")
    if "snap" in _DBG:
        dbg["snap"] = nc.dram_tensor("dbg_snap", [128, 64], F32, kind="ExternalOutput")

    es = contextlib.ExitStack()
    with es:
        sems = {}
        for n in ("d_pre", "d_x0", "d_x1", "d_w0", "d_w1", "d_w2", "d_wo", "d_at0",
                  "d_at1", "d_as0", "d_as1", "d_out0", "d_out1", "d_cc", "d_pt",
                  "c_mm", "c_sc", "c_ve", "c_cc",
                  "f_x0", "f_x1", "f_w0", "f_w1", "f_at0", "f_at1"):
            sems[n] = Sem(es.enter_context(nc.semaphore(n)))
        d_pre = sems["d_pre"]
        d_x = [sems["d_x0"], sems["d_x1"]]
        d_w = [sems["d_w0"], sems["d_w1"], sems["d_w2"]]
        d_wo = sems["d_wo"]
        d_at = [sems["d_at0"], sems["d_at1"]]
        d_as = [sems["d_as0"], sems["d_as1"]]
        d_out = [sems["d_out0"], sems["d_out1"]]
        d_cc = sems["d_cc"]
        d_pt = sems["d_pt"]
        c_mm = sems["c_mm"]
        c_sc = sems["c_sc"]
        c_ve = sems["c_ve"]
        c_cc = sems["c_cc"]
        f_x = [sems["f_x0"], sems["f_x1"]]
        f_w = [sems["f_w0"], sems["f_w1"]]
        f_at = [sems["f_at0"], sems["f_at1"]]

        sb = lambda n, shp, dt: es.enter_context(nc.sbuf_tensor("s_" + n, shp, dt))
        identf = sb("identf", [128, 128], F32)
        identb = sb("identb", [128, 128], BF16)
        tri = sb("tri", [128, 128], BF16)
        onescol_b = sb("onescol_b", [128, 1], BF16)
        onescol_f = sb("onescol_f", [128, 1], F32)
        onesrow_f = sb("onesrow_f", [1, 128], F32)
        iota128 = sb("iota128", [128, 1], F32)
        wr_sb = sb("wr_sb", [128, 128], F32)
        acat = {p: sb("acat_" + p, [128, 1024], BF16) for p in "qkvo"}
        bcat1 = sb("bcat1", [128, D], BF16)
        bcat2 = sb("bcat2", [128, D], BF16)
        starts_sb = sb("starts_sb", [1, 4], I32)
        startsf = sb("startsf", [1, 4], F32)
        bval_sb = sb("bval_sb", [1, 2], F32)
        c0m_sb = sb("c0m_sb", [8, 1], F32)
        m64col = sb("m64col", [128, 1], F32)
        LOW1 = sb("LOW1", [128, S], BF16)
        LOW2 = sb("LOW2", [128, S], BF16)
        gates = sb("gates", [128, 128], BF16)
        logit_sb = sb("logit_sb", [128, 128], F32)
        Gsb = sb("Gsb", [8, D], F32)
        nrm = sb("nrm", [1, 8], F32)
        st1 = sb("st1", [128, 1], F32)
        st2 = sb("st2", [128, 1], F32)
        st3 = sb("st3", [128, 1], F32)
        st4 = sb("st4", [128, 1], F32)
        negB = sb("negB", [128, 1], F32)
        gsc = sb("gsc", [128, 96], F32)
        cst = sb("cst", [128, 4], F32)
        snap = sb("snap", [128, 64], F32) if "snap" in _DBG else None
        xnT = sb("xnT", [128, NDC * S], BF16)

        scA = es.enter_context(nc.psum_tensor("scA", [128, 1024], F32))
        scB = es.enter_context(nc.psum_tensor("scB", [128, 1024], F32))
        pb47 = [es.enter_context(nc.psum_tensor("pb%d" % i, [128, 512], F32))
                for i in range(4, 8)]

        def pbank(i):
            if i < 2:
                return scA[:, i * 512:(i + 1) * 512]
            if i < 4:
                return scB[:, (i - 2) * 512:(i - 1) * 512]
            return pb47[i - 4][:, :]
        pb = [pbank(i) for i in range(8)]
        _uid = [0]

        def usb(stack, n, shp, dt):
            _uid[0] += 1
            return stack.enter_context(
                nc.sbuf_tensor("u%d_%s" % (_uid[0], n), shp, dt))

        bank_state = {}
        buf_state = {}

        def bank_acq(i, eng="tensor"):
            if i in bank_state:
                sem, v = bank_state[i]
                P.wait(eng, sem, v)

        def bank_rel(i, sem, v):
            bank_state[i] = (sem, v)

        def buf_acq(name, eng):
            if name in buf_state:
                sem, v = buf_state[name]
                P.wait(eng, sem, v)

        def buf_rel(name, sem, v):
            buf_state[name] = (sem, v)

        def AP(t, off, dims):
            return bass.AP(t, off, dims)

        def phase_fence(engs=("gpsimd", "sync"), extra=()):
            for e in engs:
                P.wait(e, c_mm, c_mm.v)
                P.wait(e, c_ve, c_ve.v)
                P.wait(e, c_sc, c_sc.v)
                for sem, v in extra:
                    P.wait(e, sem, v)

        def vdrain():
            P.op("vector", "drain")

        def sdrain():
            P.op("scalar", "drain")

        n_pre = 0
        for name, dst in [("ident_f", identf),
                          ("ident_b", identb), ("tri_b", tri),
                          ("onescol_b", onescol_b), ("onescol_f", onescol_f),
                          ("onesrow_f", onesrow_f), ("iota128col", iota128),
                          ("wr_sb", wr_sb), ("bcat1", bcat1), ("bcat2", bcat2),
                          ("starts", starts_sb), ("bval", bval_sb), ("c0m", c0m_sb),
                          ("cvec", cst)]:
            P.op("sync", "dma_start", out=dst[:, :], in_=ins[name].ap(), inc=d_pre, inck=16)
            n_pre += 1
        for p in "qkvo":
            P.op("sync", "dma_start", out=acat[p][:, :], in_=ins["acat_" + p].ap(),
                 inc=d_pre, inck=16)
            n_pre += 1
        pre_v = d_pre.nxt(16 * n_pre)
        for e in ("vector", "scalar", "tensor", "gpsimd"):
            P.wait(e, d_pre, pre_v)

        P.op("vector", "tensor_copy", out=startsf[:, :], in_=starts_sb[:, :], inc=c_ve)
        v_startsf = c_ve.nxt()

        rdy = {}
        NH_LOADS = [0, 0, 0]
        x_loads = [0, 0]
        x_free_v = [None, None]
        xC_free_v = [None, None]
        at_loads = [0, 0]
        at_free_v = [None, None]
        as_st = [0, 0]
        out_st = [0, 0]
        w_free_v = [None, None, None]

        for s in range(NSEQ):
            row0 = s * S
            esA = contextlib.ExitStack()
            with esA:
                xsb = [usb(esA, "xsb%d" % i, [128, D], F32)
                       for i in range(2)]
                sqx = usb(esA, "sqx", [128, D], F32)
                dsq = usb(esA, "dsq", [128, D], F32)
                xnt = [usb(esA, "xnt%d" % i, [128, D], F32)
                       for i in range(2)]
                xnf = [usb(esA, "xnf%d" % i, [128, 512], F32)
                       for i in range(2)]
                sumsq = usb(esA, "sumsq", [128, 1], F32)
                sumx = usb(esA, "sumx", [128, 1], F32)
                meanc = usb(esA, "meanc", [128, 1], F32)
                m2c = usb(esA, "m2c", [128, 1], F32)
                varc = usb(esA, "varc", [128, 1], F32)
                rstd = usb(esA, "rstd", [128, 1], F32)

                phase_fence(engs=("sync",),
                            extra=((d_out[0], 16 * out_st[0]),
                                   (d_out[1], 16 * out_st[1])))
                buf_acq("xnT", "vector")
                buf_acq("gates", "vector")
                buf_acq("logit_sb", "scalar")
                xnt_freeat = [0, 0]
                xnf_free_mm = [0, 0]
                xnf_free_ve = [0, 0]

                for tt in range(NTT):
                    p = tt % 2
                    if x_free_v[p] is not None:
                        P.wait("sync", c_ve if x_free_v[p][0] == "ve" else c_mm,
                               x_free_v[p][1])
                    P.op("sync", "dma_start", out=xsb[p][:, :],
                         in_=x_d.ap()[row0 + tt * 128: row0 + (tt + 1) * 128, :],
                         inc=d_x[p], inck=16)
                    x_loads[p] += 1
                    xv = 16 * x_loads[p]
                    P.wait("vector", d_x[p], xv)
                    P.op("vector", "reduce_sum", sumx[:, :], xsb[p][:, :], axis=AX)
                    vdrain()
                    P.op("vector", "tensor_scalar", out=meanc[:, :], in0=sumx[:, :],
                         scalar1=1.0 / D, scalar2=None, op0=OP.mult)
                    vdrain()
                    P.op("vector", "tensor_scalar", out=sqx[:, :], in0=xsb[p][:, :],
                         scalar1=meanc[:, :], scalar2=None, op0=OP.subtract,
                         inc=c_ve)
                    x_free_v[p] = ("ve", c_ve.nxt())
                    P.op("vector", "tensor_tensor", out=dsq[:, :], in0=sqx[:, :],
                         in1=sqx[:, :], op=OP.mult)
                    P.op("vector", "reduce_sum", sumsq[:, :], dsq[:, :], axis=AX)
                    vdrain()
                    P.op("vector", "tensor_scalar", out=varc[:, :], in0=sumsq[:, :],
                         scalar1=1.0 / D, scalar2=None, op0=OP.mult, inc=c_ve)
                    v_var = c_ve.nxt()
                    P.wait("scalar", c_ve, v_var)
                    P.op("scalar", "activation", st1[:, :], varc[:, :], ACT.Ln,
                         bias=cst[:, 0:1])
                    sdrain()
                    P.op("scalar", "activation", rstd[:, :], st1[:, :], ACT.Exp,
                         scale=-0.5, inc=c_sc)
                    v_rstd = c_sc.nxt()
                    P.wait("vector", c_sc, v_rstd)
                    P.wait("vector", c_mm, xnt_freeat[p])
                    P.op("vector", "tensor_scalar", out=xnt[p][:, :], in0=sqx[:, :],
                         scalar1=rstd[:, :], scalar2=None,
                         op0=OP.mult, inc=c_ve)
                    v_xn = c_ve.nxt()
                    if "snap" in _DBG and s == 0 and tt == 0:
                        P.op("vector", "tensor_copy", out=snap[:, 48:52],
                             in_=xnt[p][:, 0:4])
                        P.op("vector", "tensor_copy", out=snap[:, 52:53],
                             in_=rstd[:, :])
                        P.op("vector", "tensor_copy", out=snap[:, 53:54],
                             in_=varc[:, :])
                        P.op("vector", "tensor_copy", out=snap[:, 54:55],
                             in_=meanc[:, :])
                        P.op("vector", "tensor_copy", out=snap[:, 55:56],
                             in_=sumx[:, :])
                        P.op("vector", "tensor_copy", out=snap[:, 60:64],
                             in_=xsb[p][:, 0:4])
                    P.wait("tensor", c_ve, v_xn)
                    for g in range(4):
                        gp = g % 2
                        bank_acq(gp)
                        for k in range(4):
                            dc = 4 * g + k
                            P.op("tensor", "transpose", pb[gp][:, k * 128:(k + 1) * 128],
                                 xnt[p][:, dc * 128:(dc + 1) * 128], identf[:, :],
                                 inc=(c_mm if k == 3 else None))
                        v_tr = c_mm.nxt()
                        if g == 3:
                            xnt_freeat[p] = v_tr
                        P.wait("scalar", c_mm, v_tr)
                        P.wait("scalar", c_mm, xnf_free_mm[gp])
                        P.wait("scalar", c_ve, xnf_free_ve[gp])
                        P.op("scalar", "copy", xnf[gp][:, :], pb[gp][:, :], inc=c_sc)
                        v_ev = c_sc.nxt()
                        if "snap" in _DBG and s == 0 and tt == 0 and g == 0:
                            P.op("scalar", "copy", snap[:, 56:60], xnf[gp][:, 0:4])
                        bank_rel(gp, c_sc, v_ev)
                        P.wait("tensor", c_sc, v_ev)
                        if g == 0:
                            bank_acq(2)
                        for k in range(4):
                            dc = 4 * g + k
                            P.op("tensor", "matmul", pb[2][:, 0:8],
                                 xnf[gp][:, k * 128:(k + 1) * 128],
                                 wr_sb[:, dc * 8:(dc + 1) * 8],
                                 start=(dc == 0), stop=(dc == 15),
                                 inc=(c_mm if k == 3 else None))
                        v_rt = c_mm.nxt()
                        xnf_free_mm[gp] = v_rt
                        P.wait("vector", c_sc, v_ev)
                        P.op("vector", "tensor_copy",
                             out=AP(xnT, (4 * g) * S + tt * 128,
                                    [[NDC * S, 128], [S, 4], [1, 128]]),
                             in_=AP(xnf[gp], 0, [[512, 128], [128, 4], [1, 128]]),
                             inc=c_ve)
                        xnf_free_ve[gp] = c_ve.nxt()
                    P.wait("scalar", c_mm, c_mm.v)
                    P.op("scalar", "copy", logit_sb[:, tt * 8:(tt + 1) * 8],
                         pb[2][:, 0:8], inc=c_sc)
                    bank_rel(2, c_sc, c_sc.nxt())


                rdy[("xnT", s)] = c_ve.v
                v_logits = c_sc.v

                m1n = gsc[:, 0:16]
                m2n = gsc[:, 16:32]
                eb = usb(esA, "eb", [128, 128], F32)
                zb2 = usb(esA, "zb2", [128, 128], F32)
                indb = usb(esA, "indb", [128, 128], F32)
                l2b = usb(esA, "l2b", [128, 128], F32)
                rg = gsc[:, 48:64]
                P.wait("vector", c_sc, v_logits)
                for t2 in range(16):
                    P.op("vector", "reduce_max", m1n[:, t2:t2 + 1],
                         logit_sb[:, t2 * 8:(t2 + 1) * 8], axis=AX, negate=True)
                vdrain()
                for t2 in range(16):
                    P.op("vector", "tensor_scalar", out=zb2[:, t2 * 8:(t2 + 1) * 8],
                         in0=logit_sb[:, t2 * 8:(t2 + 1) * 8],
                         scalar1=m1n[:, t2:t2 + 1], scalar2=None, op0=OP.add)
                vdrain()
                P.op("vector", "tensor_single_scalar", indb[:, :], zb2[:, :], 0.0,
                     OP.is_equal)
                vdrain()
                P.op("vector", "scalar_tensor_tensor", out=l2b[:, :], in0=indb[:, :],
                     scalar=-1e30, in1=logit_sb[:, :], op0=OP.mult, op1=OP.add)
                vdrain()
                for t2 in range(16):
                    P.op("vector", "reduce_max", m2n[:, t2:t2 + 1],
                         l2b[:, t2 * 8:(t2 + 1) * 8], axis=AX, negate=True)
                vdrain()
                P.op("vector", "nop", inc=c_ve)
                v_g1 = c_ve.nxt()
                P.wait("scalar", c_ve, v_g1)
                P.op("scalar", "activation", eb[:, :], zb2[:, :], ACT.Exp,
                     inc=c_sc)
                v_g2 = c_sc.nxt()
                P.wait("vector", c_sc, v_g2)
                P.op("vector", "tensor_tensor", out=gsc[:, 64:80], in0=m1n,
                     in1=m2n, op=OP.subtract, inc=c_ve)
                v_g3 = c_ve.nxt()
                P.wait("scalar", c_ve, v_g3)
                P.op("scalar", "activation", gsc[:, 80:96], gsc[:, 64:80], ACT.Exp,
                     inc=c_sc)
                v_g4 = c_sc.nxt()
                P.wait("vector", c_sc, v_g4)
                P.op("vector", "tensor_scalar", out=gsc[:, 64:80],
                     in0=gsc[:, 80:96], scalar1=1.0, scalar2=None, op0=OP.add)
                vdrain()
                P.op("vector", "reciprocal", rg, gsc[:, 64:80])
                for t2 in range(16):
                    P.op("vector", "tensor_scalar", out=l2b[:, t2 * 8:(t2 + 1) * 8],
                         in0=logit_sb[:, t2 * 8:(t2 + 1) * 8],
                         scalar1=m2n[:, t2:t2 + 1], scalar2=None, op0=OP.add)
                vdrain()
                P.op("vector", "tensor_single_scalar", indb[:, :], l2b[:, :], 0.0,
                     OP.is_ge)
                vdrain()
                P.op("vector", "tensor_tensor", out=eb[:, :], in0=eb[:, :],
                     in1=indb[:, :], op=OP.mult)
                vdrain()
                for t2 in range(16):
                    P.op("vector", "tensor_scalar", out=gates[:, t2 * 8:(t2 + 1) * 8],
                         in0=eb[:, t2 * 8:(t2 + 1) * 8],
                         scalar1=rg[:, t2:t2 + 1], scalar2=None, op0=OP.mult,
                         inc=(c_ve if t2 == 15 else None))
                rdy[("gates", s)] = c_ve.nxt()
                if "gates" in _DBG and s == 0:
                    P.wait("sync", c_ve, rdy[("gates", s)])
                    P.op("sync", "dma_start", out=dbg["logits"].ap()[:, :],
                         in_=logit_sb[:, :], inc=d_pre, inck=16)
                    P.op("sync", "dma_start", out=dbg["gates"].ap()[:, :],
                         in_=gates[:, :], inc=d_pre, inck=16)
                    d_pre.nxt(32)
                if "xnT" in _DBG and s == 0:
                    P.wait("sync", c_ve, rdy[("xnT", s)])
                    P.op("sync", "dma_start", out=dbg["xnT"].ap()[:, :],
                         in_=xnT[:, :], inc=d_pre, inck=16)
                    d_pre.nxt(16)

            P.wait("vector", c_ve, v_startsf)
            buf_acq("LOW1", "vector")
            buf_acq("LOW2", "vector")
            P.op("vector", "tensor_scalar", out=gsc[0:1, 0:4],
                 in0=startsf[:, :], scalar1=bval_sb[0:1, s:s + 1], scalar2=None,
                 op0=OP.is_le)
            vdrain()
            P.op("vector", "reduce_sum", gsc[0:1, 4:5], gsc[0:1, 0:4], axis=AX)
            vdrain()
            P.op("vector", "tensor_scalar", out=gsc[0:1, 5:6], in0=gsc[0:1, 4:5],
                 scalar1=-1.0, scalar2=0.0, op0=OP.add, op1=OP.max)
            vdrain()
            P.op("vector", "tensor_scalar", out=gsc[0:1, 6:7], in0=gsc[0:1, 5:6],
                 scalar1=3.0, scalar2=None, op0=OP.min, inc=c_ve)
            v_ids = c_ve.nxt()
            P.wait("tensor", c_ve, v_ids)
            bank_acq(3)
            P.op("tensor", "matmul", pb[3][:, 0:1], onesrow_f[:, :], gsc[0:1, 6:7],
                 start=True, stop=True, inc=c_mm)
            v_idb = c_mm.nxt()
            P.wait("vector", c_mm, v_idb)
            P.op("vector", "tensor_copy", out=st2[:, :], in_=pb[3][:, 0:1], inc=c_ve)
            bank_rel(3, c_ve, c_ve.nxt())
            vdrain()
            P.op("vector", "tensor_scalar", out=m64col[:, :], in0=iota128[:, :],
                 scalar1=st2[:, :], scalar2=None, op0=OP.is_equal, inc=c_ve)
            v_m64 = c_ve.nxt()
            xnT_rdy = rdy[("xnT", s)]
            P.wait("tensor", c_ve, xnT_rdy)
            for pn, dstT, rlo in (("q", LOW1, 0), ("k", LOW1, 64), ("v", LOW2, 64)):
                for qc in range(NQC):
                    bi = 4 + (qc % 2)
                    bank = pb[bi]
                    bank_acq(bi)
                    for dc in range(NDC):
                        tp = (0, 64) if rlo == 64 else None
                        P.op("tensor", "matmul", bank[rlo:rlo + 64, 0:512],
                             acat[pn][:, dc * 64:(dc + 1) * 64],
                             xnT[:, dc * S + qc * 512: dc * S + qc * 512 + 512],
                             start=(dc == 0), stop=(dc == 15),
                             tile_position=tp,
                             inc=(c_mm if dc == 15 else None))
                    v_low = c_mm.nxt()
                    P.wait("vector", c_mm, v_low)
                    P.wait("vector", c_ve, v_m64)
                    P.op("vector", "tensor_scalar",
                         out=dstT[rlo:rlo + 64, qc * 512:(qc + 1) * 512],
                         in0=bank[rlo:rlo + 64, 0:512],
                         scalar1=m64col[rlo:rlo + 64, :], scalar2=None,
                         op0=OP.mult, inc=c_ve)
                    bank_rel(bi, c_ve, c_ve.nxt())
            rdy[("lows", s)] = c_ve.v
            if "low" in _DBG and s == 0:
                P.wait("sync", c_ve, rdy[("lows", s)])
                P.op("sync", "dma_start", out=dbg["low1"].ap()[:, :], in_=LOW1[:, :],
                     inc=d_pre, inck=16)
                d_pre.nxt(16)
            esB = contextlib.ExitStack()
            with esB:
                wsl = {pn: [usb(esB, "wsl_%s%d" % (pn, i), [128, D], BF16)
                    for i in range(2)] for pn in "qkv"}
                ropeC = usb(esB, "ropeC", [128, S], F32)
                ropeS = usb(esB, "ropeS", [128, S], F32)
                qraw = usb(esB, "qraw", [128, S], F32)
                kraw = usb(esB, "kraw", [128, S], F32)
                QT = usb(esB, "QT", [128, S], BF16)
                KT = usb(esB, "KT", [128, S], BF16)
                Vt = usb(esB, "Vt", [128, S], BF16)
                u1 = usb(esB, "u1", [128, S], F32)
                u2 = usb(esB, "u2", [128, S], F32)
                lowo_f = usb(esB, "lowo_f", [64, S], F32)
                stage = [usb(esB, "stage%d" % i, [128, 512], BF16) for i in range(2)]
                pt = [usb(esB, "pt%d" % i, [128, 1024], BF16) for i in range(2)]
                rb_sb = usb(esB, "rb_sb", [128, 512], F32)
                rcp_sb = usb(esB, "rcp_sb", [1, 512], F32)
                sq2 = [usb(esB, "sq2_%d" % i, [128, 512], F32) for i in range(2)]

                P.op("vector", "memset", lowo_f[:, :], 0.0, inc=c_ve)
                rdy[("lowo0", s)] = c_ve.nxt()
                phase_fence()
                P.op("sync", "dma_start", out=ropeC[:, :], in_=ins["ropeC"].ap(),
                     inc=d_pre, inck=16)
                P.op("sync", "dma_start", out=ropeS[:, :], in_=ins["ropeS"].ap(),
                     inc=d_pre, inck=16)
                v_rope_tabs = d_pre.nxt(32)
                P.wait("vector", d_pre, v_rope_tabs)
                wnames = {"q": "wqT", "k": "wkT", "v": "wvT"}
                rope_mask = []
                for i in range(16):
                    rope_mask += [2 * i + 1, 2 * i]

                pend = []

                def flush_pend():
                    while pend:
                        pend.pop(0)()

                for h in range(H):
                    hp = h % 2
                    if w_free_v[hp] is not None:
                        P.wait("gpsimd", c_mm, w_free_v[hp])
                    for pn in "qkv":
                        P.op("gpsimd", "dma_start", out=wsl[pn][hp][:, :],
                             in_=AP(ins[wnames[pn]], h * 128 * D,
                                    [[D, 128], [1, D]]),
                             inc=d_w[hp], inck=16)
                        NH_LOADS[hp] += 1
                    wv_ready = 16 * NH_LOADS[hp]

                    def proj(pn, b0, lorlo, bc, lowbuf):
                        P.wait("tensor", d_w[hp], wv_ready)
                        P.wait("tensor", c_ve, rdy[("lows", s)])
                        for qc in range(NQC):
                            bank_acq(b0 + qc)
                        for dc in range(NDC):
                            for qc in range(NQC):
                                P.op("tensor", "matmul", pb[b0 + qc],
                                     wsl[pn][hp][:, dc * 128:(dc + 1) * 128],
                                     xnT[:, dc * S + qc * 512: dc * S + qc * 512 + 512],
                                     start=(dc == 0), stop=False)
                        tp = (64, 0) if lorlo == 64 else None
                        for qc in range(NQC):
                            P.op("tensor", "matmul", pb[b0 + qc],
                                 bc[lorlo:lorlo + 64, h * 128:(h + 1) * 128],
                                 lowbuf[lorlo:lorlo + 64, qc * 512:(qc + 1) * 512],
                                 start=False, stop=True, tile_position=tp,
                                 inc=(c_mm if qc == 3 else None))
                        return c_mm.nxt()

                    def evac_norms(rawbuf, b0, ncol0, v_p):
                        for qc in range(NQC):
                            P.wait("scalar", c_mm, v_p)
                            buf_acq(rawbuf, "scalar")
                            P.op("scalar", "copy",
                                 locals_raw[rawbuf][:, qc * 512:(qc + 1) * 512],
                                 pb[b0 + qc])
                            buf_acq("sq%d" % (qc % 2), "scalar")
                            P.op("scalar", "activation", sq2[qc % 2][:, :],
                                 pb[b0 + qc], ACT.Square, inc=c_sc)
                            v_ev = c_sc.nxt()
                            P.wait("tensor", c_sc, v_ev)
                            P.op("tensor", "matmul", pb[b0 + qc][0:1, 0:512],
                                 onescol_f[:, :], sq2[qc % 2][:, :], start=True,
                                 stop=True, inc=c_mm)
                            v_nm = c_mm.nxt()
                            buf_rel("sq%d" % (qc % 2), c_mm, v_nm)
                            P.wait("vector", c_mm, v_nm)
                            P.op("vector", "reduce_max",
                                 nrm[0:1, ncol0 + qc:ncol0 + qc + 1],
                                 pb[b0 + qc][0:1, 0:512], axis=AX, inc=c_ve)
                            bank_rel(b0 + qc, c_ve, c_ve.nxt())
                        return c_sc.v

                    locals_raw = {"qraw": qraw, "kraw": kraw}

                    v_q = proj("q", 0, 0, bcat1, LOW1)
                    flush_pend()
                    v_qev = evac_norms("qraw", 0, 0, v_q)
                    v_k = proj("k", 4, 64, bcat1, LOW1)
                    v_kev = evac_norms("kraw", 4, 4, v_k)

                    v_rope = {}
                    for rawbuf, dstT, key, v_prod in ((qraw, QT, "q", v_qev),
                                                      (kraw, KT, "k", v_kev)):
                        P.wait("vector", c_sc, v_prod)
                        buf_acq("QT" if key == "q" else "KT", "vector")
                        P.op("vector", "tensor_tensor", out=u1[:, :], in0=rawbuf[:, :],
                             in1=ropeS[:, :], op=OP.mult)
                        P.op("vector", "stream_shuffle", out=u2[:, :], in_=u1[:, :],
                             mask=rope_mask)
                        P.op("vector", "tensor_tensor", out=u1[:, :], in0=rawbuf[:, :],
                             in1=ropeC[:, :], op=OP.mult)
                        P.op("vector", "tensor_tensor", out=dstT[:, :], in0=u2[:, :],
                             in1=u1[:, :], op=OP.add, inc=c_ve)
                        v_rope[key] = c_ve.nxt()
                    buf_rel("qraw", c_ve, v_rope["q"])
                    buf_rel("kraw", c_ve, v_rope["k"])

                    vdrain()
                    P.op("vector", "reduce_max", st1[0:1, 0:1], nrm[0:1, 0:4], axis=AX)
                    P.op("vector", "reduce_max", st2[0:1, 0:1], nrm[0:1, 4:8],
                         axis=AX, inc=c_ve)
                    v_nmx = c_ve.nxt()
                    P.wait("scalar", c_ve, v_nmx)
                    P.op("scalar", "activation", st3[0:1, 0:1], st1[0:1, 0:1],
                         ACT.Ln, bias=cst[0:1, 1:2])
                    P.op("scalar", "activation", st4[0:1, 0:1], st2[0:1, 0:1],
                         ACT.Ln, bias=cst[0:1, 1:2], inc=c_sc)
                    v_ln = c_sc.nxt()
                    P.wait("vector", c_sc, v_ln)
                    P.op("vector", "tensor_tensor", out=st3[0:1, 0:1],
                         in0=st3[0:1, 0:1], in1=st4[0:1, 0:1], op=OP.add, inc=c_ve)
                    v_add2 = c_ve.nxt()
                    P.wait("scalar", c_ve, v_add2)
                    sdrain()
                    P.op("scalar", "activation", st4[0:1, 0:1], st3[0:1, 0:1],
                         ACT.Exp, scale=0.5, bias=cst[0:1, 2:3], inc=c_sc)
                    v_B = c_sc.nxt()
                    P.wait("vector", c_sc, v_B)
                    vdrain()
                    P.op("vector", "tensor_scalar", out=st3[0:1, 0:1],
                         in0=st4[0:1, 0:1], scalar1=-1.0, scalar2=None, op0=OP.mult,
                         inc=c_ve)
                    v_nB = c_ve.nxt()

                    v_v = proj("v", 0, 64, bcat2, LOW2)
                    buf_acq("qraw", "scalar")
                    for qc in range(NQC):
                        P.wait("scalar", c_mm, v_v)
                        P.op("scalar", "copy", qraw[:, qc * 512:(qc + 1) * 512],
                             pb[qc], inc=c_sc)
                        bank_rel(qc, c_sc, c_sc.nxt())
                    v_vev = c_sc.v
                    buf_acq("Vt", "vector")
                    for g in range(4):
                        bi = 4 + (g % 2)
                        P.wait("tensor", c_sc, v_vev)
                        bank_acq(bi)
                        for k in range(4):
                            tc = 4 * g + k
                            P.op("tensor", "transpose",
                                 pb[bi][:, k * 128:(k + 1) * 128],
                                 qraw[:, tc * 128:(tc + 1) * 128], identf[:, :],
                                 inc=(c_mm if k == 3 else None))
                        v_tr = c_mm.nxt()
                        P.wait("vector", c_mm, v_tr)
                        P.op("vector", "tensor_copy",
                             out=Vt[:, g * 512:(g + 1) * 512],
                             in_=pb[bi][:, 0:512],
                             inc=c_ve)
                        bank_rel(bi, c_ve, c_ve.nxt())
                    v_vt = c_ve.v
                    buf_rel("qraw", c_mm, c_mm.v)
                    w_free_v[hp] = c_mm.v

                    P.wait("tensor", c_ve, v_nB)
                    bank_acq(6)
                    P.op("tensor", "matmul", pb[6][:, 0:1], onesrow_f[:, :],
                         st3[0:1, 0:1], start=True, stop=True, inc=c_mm)
                    v_nBb = c_mm.nxt()
                    P.wait("scalar", c_mm, v_nBb)
                    P.op("scalar", "copy", negB[:, :], pb[6][:, 0:1], inc=c_sc)
                    v_negB = c_sc.nxt()
                    bank_rel(6, c_sc, v_negB)

                    for qc in range(NQC):
                        jmax = 4 * qc + 4
                        ob_i = 4 + (qc % 2)
                        ob = pb[ob_i]
                        dbk = pb[6]
                        npair = jmax // 2

                        def emit_mm1(p_):
                            j0, j1 = 2 * p_, 2 * p_ + 1
                            ti = p_ % 2
                            T = scA if ti == 0 else scB
                            bank_acq(2 * ti)
                            bank_acq(2 * ti + 1)
                            P.wait("tensor", c_ve, v_rope["k"])
                            P.op("tensor", "matmul", T[:, 0:512],
                                 KT[:, j0 * 128:(j0 + 1) * 128],
                                 QT[:, qc * 512:(qc + 1) * 512],
                                 start=True, stop=True)
                            P.op("tensor", "matmul", T[:, 512:1024],
                                 KT[:, j1 * 128:(j1 + 1) * 128],
                                 QT[:, qc * 512:(qc + 1) * 512],
                                 start=True, stop=True, inc=c_mm)
                            v_mm1 = c_mm.nxt()
                            P.wait("scalar", c_mm, v_mm1)
                            P.wait("scalar", c_sc, v_negB)
                            buf_acq("pt%d" % ti, "scalar")
                            P.op("scalar", "activation", pt[ti][:, :],
                                 T[:, 0:1024], ACT.Exp, bias=negB[:, :],
                                 scale=SCL, inc=c_sc)
                            v_exp = c_sc.nxt()
                            bank_rel(2 * ti, c_sc, v_exp)
                            bank_rel(2 * ti + 1, c_sc, v_exp)
                            v_msk = 0
                            for j in (j0, j1):
                                if j >= 4 * qc:
                                    m = j - 4 * qc
                                    half = (j % 2) * 512
                                    P.wait("vector", c_sc, v_exp)
                                    if m > 0:
                                        P.op("vector", "memset",
                                             pt[ti][:, half:half + m * 128], 0.0)
                                    P.op("vector", "tensor_tensor",
                                         out=pt[ti][:, half + m * 128:half + (m + 1) * 128],
                                         in0=pt[ti][:, half + m * 128:half + (m + 1) * 128],
                                         in1=tri[:, :], op=OP.mult, inc=c_ve)
                                    v_msk = c_ve.nxt()
                            return v_exp, v_msk

                        def emit_mm23(p_, v_exp, v_msk):
                            j0, j1 = 2 * p_, 2 * p_ + 1
                            ti = p_ % 2
                            P.wait("tensor", c_sc, v_exp)
                            if v_msk:
                                P.wait("tensor", c_ve, v_msk)
                            P.wait("tensor", c_ve, v_vt)
                            P.op("tensor", "matmul", ob,
                                 Vt[:, j0 * 128:(j0 + 1) * 128], pt[ti][:, 0:512],
                                 start=(j0 == 0), stop=False)
                            P.op("tensor", "matmul", ob,
                                 Vt[:, j1 * 128:(j1 + 1) * 128], pt[ti][:, 512:1024],
                                 start=False, stop=(j1 == jmax - 1))
                            P.op("tensor", "matmul", dbk[0:1, 0:512],
                                 onescol_b[:, :], pt[ti][:, 0:512],
                                 start=(j0 == 0), stop=False)
                            P.op("tensor", "matmul", dbk[0:1, 0:512],
                                 onescol_b[:, :], pt[ti][:, 512:1024],
                                 start=False, stop=(j1 == jmax - 1), inc=c_mm)
                            buf_rel("pt%d" % ti, c_mm, c_mm.nxt())

                        prev = None
                        for p_ in range(npair):
                            e_m = emit_mm1(p_)
                            if p_ == 0:
                                flush_pend()
                                bank_acq(ob_i)
                                bank_acq(6)
                            if prev is not None:
                                emit_mm23(p_ - 1, *prev)
                            prev = e_m
                        emit_mm23(npair - 1, *prev)
                        v_pv = c_mm.v
                        sp = qc % 2
                        P.wait("vector", c_mm, v_pv)
                        buf_acq("rcp_sb", "vector")
                        P.op("vector", "reciprocal", rcp_sb[:, :],
                             dbk[0:1, 0:512], inc=c_ve)
                        v_rc = c_ve.nxt()
                        bank_rel(6, c_ve, v_rc)

                        def make_tail(qc=qc, sp=sp, ob=ob, ob_i=ob_i, v_rc=v_rc,
                                      h=h, s=s):
                            def tail():
                                P.wait("tensor", c_ve, v_rc)
                                bank_acq(7)
                                P.op("tensor", "matmul", pb[7], onesrow_f[:, :],
                                     rcp_sb[:, :], start=True, stop=True, inc=c_mm)
                                v_rbmm = c_mm.nxt()
                                buf_rel("rcp_sb", c_mm, v_rbmm)
                                P.wait("scalar", c_mm, v_rbmm)
                                buf_acq("rb_sb", "scalar")
                                P.op("scalar", "copy", rb_sb[:, :], pb[7], inc=c_sc)
                                v_rbev = c_sc.nxt()
                                bank_rel(7, c_sc, v_rbev)
                                P.wait("vector", c_sc, v_rbev)
                                P.wait("vector", d_as[sp], 16 * as_st[sp])
                                buf_acq("stage%d" % sp, "vector")
                                P.op("vector", "tensor_tensor", out=stage[sp][:, :],
                                     in0=ob, in1=rb_sb[:, :], op=OP.mult,
                                     inc=c_ve)
                                v_st = c_ve.nxt()
                                bank_rel(ob_i, c_ve, v_st)
                                buf_rel("rb_sb", c_ve, v_st)
                                P.wait("sync", c_ve, v_st)
                                P.op("sync", "dma_start",
                                     out=AP(attT_d, (4 * qc) * 128 * S + h * 128,
                                            [[S, 128], [128 * S, 4], [1, 128]]),
                                     in_=stage[sp][:, :], inc=d_as[sp], inck=16)
                                as_st[sp] += 1
                                P.wait("tensor", c_ve, v_st)
                                bank_acq(6)
                                P.op("tensor", "matmul", pb[6][0:64, 0:512],
                                     acat["o"][:, h * 64:(h + 1) * 64],
                                     stage[sp][:, :],
                                     start=True, stop=True, inc=c_mm)
                                v_ol = c_mm.nxt()
                                buf_rel("stage%d" % sp, c_mm, v_ol)
                                P.wait("vector", c_mm, v_ol)
                                P.wait("vector", c_ve, rdy[("lowo0", s)])
                                P.op("vector", "tensor_tensor",
                                     out=lowo_f[:, qc * 512:(qc + 1) * 512],
                                     in0=lowo_f[:, qc * 512:(qc + 1) * 512],
                                     in1=pb[6][0:64, 0:512], op=OP.add, inc=c_ve)
                                v_lo = c_ve.nxt()
                                bank_rel(6, c_ve, v_lo)
                            return tail
                        pend.append(make_tail())
                    buf_rel("QT", c_mm, c_mm.v)
                    buf_rel("KT", c_mm, c_mm.v)
                    buf_rel("Vt", c_mm, c_mm.v)
                flush_pend()
                P.wait("vector", c_mm, c_mm.v)
                P.op("vector", "tensor_scalar", out=LOW2[0:64, :],
                     in0=lowo_f[:, :], scalar1=m64col[0:64, :], scalar2=None,
                     op0=OP.mult, inc=c_ve)
                rdy[("lowo_bf", s)] = c_ve.nxt()
                buf_rel("lowo_f", c_ve, rdy[("lowo_bf", s)])
                if "low" in _DBG and s == 0:
                    P.wait("sync", c_ve, rdy[("lowo_bf", s)])
                    P.op("sync", "dma_start", out=dbg["low2"].ap()[:, :],
                         in_=LOW2[:, :], inc=d_pre, inck=16)
                    d_pre.nxt(16)

            esF = contextlib.ExitStack()
            with esF:
                w1sl = [usb(esF, "w1sl%d" % i, [128, D], BF16)
                        for i in range(2)]
                w1f = [usb(esF, "w1f%d" % i, [128, D], F32) for i in range(2)]
                h1b = usb(esF, "h1b", [128, 16 * 1024], BF16)
                h2b = usb(esF, "h2b", [128, 16 * 1024], BF16)
                h2tok = usb(esF, "h2tok", [128, 2048], BF16)
                phase_fence(extra=((d_as[0], 16 * as_st[0]), (d_as[1], 16 * as_st[1])))

                for hf in range(2):
                    tok0 = hf * 1024
                    for ic in range(16):
                        icp = ic % 2
                        P.wait("sync", c_ve, rdy.get(("wcast", icp), 0))
                        P.op("sync", "dma_start", out=w1f[icp][:, :],
                             in_=AP(ins["w1T"], ic * 128 * D,
                                    [[D, 128], [1, D]]),
                             inc=d_w[icp], inck=16)
                        NH_LOADS[icp] += 1
                        P.wait("vector", d_w[icp], 16 * NH_LOADS[icp])
                        if w_free_v[icp] is not None:
                            P.wait("vector", c_mm, w_free_v[icp])
                        P.op("vector", "tensor_copy", out=w1sl[icp][:, :],
                             in_=w1f[icp][:, :], inc=c_ve)
                        rdy[("wcast", icp)] = c_ve.nxt()
                        wv_r = rdy[("wcast", icp)]
                        P.wait("tensor", c_ve, wv_r)
                        bank_acq(6)
                        bank_acq(7)
                        for dc in range(NDC):
                            for qcc in range(2):
                                col0 = tok0 + qcc * 512
                                P.op("tensor", "matmul", pb[6 + qcc],
                                     w1sl[icp][:, dc * 128:(dc + 1) * 128],
                                     xnT[:, dc * S + col0: dc * S + col0 + 512],
                                     start=(dc == 0), stop=(dc == 15),
                                     inc=(c_mm if (dc == 15 and qcc == 1) else None))
                        v_h = c_mm.nxt()
                        for qcc in range(2):
                            P.wait("scalar", c_mm, v_h)
                            P.op("scalar", "activation",
                                 h1b[:, ic * 1024 + qcc * 512: ic * 1024 + qcc * 512 + 512],
                                 pb[6 + qcc], ACT.Silu, inc=c_sc)
                            bank_rel(6 + qcc, c_sc, c_sc.nxt())
                        w_free_v[icp] = c_mm.v
                    v_h1_done = c_sc.v
                    for ic2 in range(16):
                        icp = ic2 % 2
                        P.wait("sync", c_ve, rdy.get(("wcast", icp), 0))
                        P.op("sync", "dma_start", out=w1f[icp][:, :],
                             in_=AP(ins["w3T"], ic2 * 128 * D,
                                    [[D, 128], [1, D]]),
                             inc=d_w[icp], inck=16)
                        NH_LOADS[icp] += 1
                        P.wait("vector", d_w[icp], 16 * NH_LOADS[icp])
                        if w_free_v[icp] is not None:
                            P.wait("vector", c_mm, w_free_v[icp])
                        P.op("vector", "tensor_copy", out=w1sl[icp][:, :],
                             in_=w1f[icp][:, :], inc=c_ve)
                        rdy[("wcast", icp)] = c_ve.nxt()
                        wv_r = rdy[("wcast", icp)]
                        P.wait("tensor", c_ve, wv_r)
                        P.wait("tensor", c_sc, v_h1_done)
                        bank_acq(6)
                        bank_acq(7)
                        for i1 in range(NDC):
                            for qcc in range(2):
                                P.op("tensor", "matmul", pb[6 + qcc],
                                     w1sl[icp][:, i1 * 128:(i1 + 1) * 128],
                                     h1b[:, i1 * 1024 + qcc * 512: i1 * 1024 + qcc * 512 + 512],
                                     start=(i1 == 0), stop=(i1 == 15),
                                     inc=(c_mm if (i1 == 15 and qcc == 1) else None))
                        v_h = c_mm.nxt()
                        for qcc in range(2):
                            P.wait("scalar", c_mm, v_h)
                            P.op("scalar", "activation",
                                 h2b[:, ic2 * 1024 + qcc * 512: ic2 * 1024 + qcc * 512 + 512],
                                 pb[6 + qcc], ACT.Silu, inc=c_sc)
                            bank_rel(6 + qcc, c_sc, c_sc.nxt())
                        w_free_v[icp] = c_mm.v
                    v_h2_done = c_sc.v
                    if "h2" in _DBG and s == 0 and hf == 0:
                        P.wait("sync", c_sc, v_h2_done)
                        P.op("sync", "dma_start", out=dbg["h1"].ap()[:, :],
                             in_=h1b[:, :], inc=d_pre, inck=16)
                        P.op("sync", "dma_start", out=dbg["h2"].ap()[:, :],
                             in_=h2b[:, :], inc=d_pre, inck=16)
                        d_pre.nxt(32)
                    P.wait("tensor", c_sc, v_h2_done)
                    P.wait("tensor", c_ve, rdy[("gates", s)])
                    for tc in range(8):
                        for g in range(4):
                            bi = g % 2
                            bank_acq(bi)
                            buf_acq("h2tok_g%d" % g, "vector")
                            for k in range(4):
                                i2 = 4 * g + k
                                P.op("tensor", "transpose",
                                     pb[bi][:, k * 64:(k + 1) * 64].bitcast(BF16),
                                     h2b[:, i2 * 1024 + tc * 128: i2 * 1024 + tc * 128 + 128],
                                     identb[:, :],
                                     inc=(c_mm if k == 3 else None))
                            v_tr = c_mm.nxt()
                            P.wait("vector", c_mm, v_tr)
                            P.op("vector", "tensor_copy",
                                 out=h2tok[:, g * 512:(g + 1) * 512],
                                 in_=pb[bi][:, 0:256].bitcast(BF16),
                                 inc=c_ve)
                            bank_rel(bi, c_ve, c_ve.nxt())
                        v_ht = c_ve.v
                        P.wait("tensor", c_ve, v_ht)
                        tglob = hf * 8 + tc
                        if hf == 0 and tc == 0:
                            for gq in range(4):
                                bank_acq(2 + gq)
                        for gq in range(4):
                            P.op("tensor", "matmul", pb[2 + gq][0:8, 0:512],
                                 gates[:, tglob * 8:(tglob + 1) * 8],
                                 h2tok[:, gq * 512:(gq + 1) * 512],
                                 start=(hf == 0 and tc == 0),
                                 stop=(hf == 1 and tc == 7),
                                 inc=(c_mm if gq == 3 else None))
                        v_gmm = c_mm.nxt()
                        for g in range(4):
                            buf_rel("h2tok_g%d" % g, c_mm, v_gmm)
                v_gmm_all = c_mm.v
                P.wait("vector", c_mm, v_gmm_all)
                for gq in range(4):
                    if s == 0:
                        P.op("vector", "tensor_copy",
                             out=Gsb[0:8, gq * 512:(gq + 1) * 512],
                             in_=pb[2 + gq][0:8, 0:512],
                             inc=(c_ve if gq == 3 else None))
                    else:
                        P.op("vector", "tensor_tensor",
                             out=Gsb[0:8, gq * 512:(gq + 1) * 512],
                             in0=Gsb[0:8, gq * 512:(gq + 1) * 512],
                             in1=pb[2 + gq][0:8, 0:512], op=OP.add,
                             inc=(c_ve if gq == 3 else None))
                v_gsb = c_ve.nxt()
                rdy[("Gsb", s)] = v_gsb
                for gq in range(4):
                    bank_rel(2 + gq, c_ve, v_gsb)
                buf_rel("xnT", c_mm, c_mm.v)
                buf_rel("gates", c_mm, c_mm.v)
                buf_rel("logit_sb", c_ve, rdy[("gates", s)])

            esC = contextlib.ExitStack()
            with esC:
                wo_sb = usb(esC, "wo_sb", [128, NDC * D], BF16)
                attld = [usb(esC, "attld%d" % i, [128, 2048], BF16)
                         for i in range(2)]
                xsbC = [usb(esC, "xsbC%d" % i, [128, D], F32)
                        for i in range(2)]
                outb = [usb(esC, "outb%d" % i, [128, 512], F32)
                        for i in range(2)]
                wof = [usb(esC, "wof%d" % i, [128, D], F32) for i in range(2)]
                phase_fence()
                wo_cast = [0, 0]
                for dcg in range(16):
                    wp = dcg % 2
                    if wo_cast[wp]:
                        P.wait("sync", c_ve, wo_cast[wp])
                    P.op("sync", "dma_start",
                         out=wof[wp][:, :],
                         in_=AP(ins["woT"], dcg * D,
                                [[16 * D, 128], [1, D]]),
                         inc=d_wo, inck=16)
                    v_ld = d_wo.nxt(16)
                    P.wait("vector", d_wo, v_ld)
                    P.op("vector", "tensor_copy",
                         out=AP(wo_sb, dcg * D, [[NDC * D, 128], [1, D]]),
                         in_=wof[wp][:, :], inc=c_ve)
                    wo_cast[wp] = c_ve.nxt()
                v_wo = c_ve.v
                P.wait("tensor", c_ve, v_wo)
                P.wait("gpsimd", d_as[0], 16 * as_st[0])
                P.wait("gpsimd", d_as[1], 16 * as_st[1])
                for tt in range(NTT):
                    tp_ = tt % 2
                    if at_free_v[tp_] is not None:
                        P.wait("gpsimd", c_mm, at_free_v[tp_])
                    P.op("gpsimd", "dma_start",
                         out=attld[tp_][:, :],
                         in_=AP(attT_d, tt * 128 * S,
                                [[S, 128], [1, S]]),
                         inc=d_at[tp_], inck=16)
                    at_loads[tp_] += 1
                    v_at = 16 * at_loads[tp_]
                    if xC_free_v[tp_] is not None:
                        P.wait("gpsimd", c_ve, xC_free_v[tp_])
                    P.op("gpsimd", "dma_start", out=xsbC[tp_][:, :],
                         in_=x_d.ap()[row0 + tt * 128: row0 + (tt + 1) * 128, :],
                         inc=d_x[tp_], inck=16)
                    x_loads[tp_] += 1
                    v_x = 16 * x_loads[tp_]
                    bgrp_i = [4 * tp_ + i for i in range(4)]
                    P.wait("tensor", d_at[tp_], v_at)
                    P.wait("tensor", c_ve, rdy[("lowo_bf", s)])
                    for bi in bgrp_i:
                        bank_acq(bi)
                    for dc in range(NDC):
                        for oci, bi in enumerate(bgrp_i):
                            P.op("tensor", "matmul", pb[bi][:, 0:512],
                                 attld[tp_][:, dc * 128:(dc + 1) * 128],
                                 wo_sb[:, dc * D + oci * 512: dc * D + oci * 512 + 512],
                                 start=(dc == 0), stop=False)
                    for oci, bi in enumerate(bgrp_i):
                        P.op("tensor", "matmul", pb[bi][:, 0:512],
                             LOW2[0:64, tt * 128:(tt + 1) * 128],
                             bcat2[0:64, oci * 512:(oci + 1) * 512],
                             start=False, stop=True,
                             inc=(c_mm if oci == 3 else None))
                    v_ao = c_mm.nxt()
                    at_free_v[tp_] = v_ao
                    P.wait("vector", c_mm, v_ao)
                    P.wait("vector", d_x[tp_], v_x)
                    for oci, bi in enumerate(bgrp_i):
                        po = oci % 2
                        P.wait("vector", d_out[po], 16 * out_st[po])
                        P.op("vector", "tensor_tensor",
                             out=outb[po][:, :],
                             in0=pb[bi][:, 0:512],
                             in1=xsbC[tp_][:, oci * 512:(oci + 1) * 512],
                             op=OP.add, inc=c_ve)
                        v_epi = c_ve.nxt()
                        bank_rel(bi, c_ve, v_epi)
                        P.wait("sync", c_ve, v_epi)
                        P.op("sync", "dma_start",
                             out=out_d.ap()[row0 + tt * 128: row0 + (tt + 1) * 128,
                                            oci * 512:(oci + 1) * 512],
                             in_=outb[po][:, :], inc=d_out[po], inck=16)
                        out_st[po] += 1
                    v_epi = c_ve.v
                    rdy[("epi", s, tt)] = v_epi
                    xC_free_v[tp_] = v_epi
                buf_rel("LOW1", c_mm, c_mm.v)
                buf_rel("LOW2", c_mm, c_mm.v)

        esS = contextlib.ExitStack()
        with esS:
            GT_sb = usb(esS, "GT_sb", [128, 128], BF16)
            GT_lo = usb(esS, "GT_lo", [128, 128], BF16)
            w2f = [usb(esS, "w2f%d" % i, [128, D], F32) for i in range(2)]
            w2lo = [usb(esS, "w2lo%d" % i, [128, D], BF16) for i in range(2)]
            w2sl = [usb(esS, "w2sl%d" % i, [128, D], BF16)
                    for i in range(2)]
            patch = usb(esS, "patch", [8, D], F32)
            Ssum = usb(esS, "Ssum", [8, D], F32)
            phase_fence(extra=((d_out[0], 16 * out_st[0]),
                               (d_out[1], 16 * out_st[1])))
            P.wait("tensor", c_ve, rdy[("Gsb", 1)])
            for g in range(4):
                bi = 4 + (g % 2)
                bank_acq(bi)
                for k in range(4):
                    ic = 4 * g + k
                    P.op("tensor", "transpose", pb[bi][:, k * 8:(k + 1) * 8],
                         Gsb[0:8, ic * 128:(ic + 1) * 128], identf[0:8, 0:8],
                         inc=(c_mm if k == 3 else None))
                v_tr = c_mm.nxt()
                P.wait("vector", c_mm, v_tr)
                P.op("vector", "tensor_copy",
                     out=GT_sb[:, (4 * g) * 8:(4 * g + 4) * 8],
                     in_=pb[bi][:, 0:32])
                vdrain()
                P.op("vector", "tensor_tensor",
                     out=GT_lo[:, (4 * g) * 8:(4 * g + 4) * 8],
                     in0=pb[bi][:, 0:32],
                     in1=GT_sb[:, (4 * g) * 8:(4 * g + 4) * 8],
                     op=OP.subtract,
                     inc=c_ve)
                bank_rel(bi, c_ve, c_ve.nxt())
            v_gt = c_ve.v
            for ic in range(16):
                icp = ic % 2
                if w_free_v[icp] is not None:
                    P.wait("sync", c_mm, w_free_v[icp])
                P.wait("sync", c_ve, rdy.get(("w2lo", ic - 2), 0))
                P.op("sync", "dma_start", out=w2f[icp][:, :],
                     in_=ins["w2T"].ap()[ic * 128:(ic + 1) * 128, :],
                     inc=d_w[icp], inck=16)
                NH_LOADS[icp] += 1
                wv_r = 16 * NH_LOADS[icp]
                P.wait("vector", d_w[icp], wv_r)
                P.op("vector", "tensor_copy", out=w2sl[icp][:, :],
                     in_=w2f[icp][:, :])
                vdrain()
                P.op("vector", "tensor_tensor", out=w2lo[icp][:, :],
                     in0=w2f[icp][:, :], in1=w2sl[icp][:, :], op=OP.subtract,
                     inc=c_ve)
                rdy[("w2lo", ic)] = c_ve.nxt()
                P.wait("tensor", d_w[icp], wv_r)
                P.wait("tensor", c_ve, rdy[("w2lo", ic)])
                P.wait("tensor", c_ve, v_gt)
                if ic == 0:
                    for oc in range(4):
                        bank_acq(oc)
                for oc in range(4):
                    P.op("tensor", "matmul", pb[oc][0:8, 0:512],
                         GT_sb[:, ic * 8:(ic + 1) * 8],
                         w2sl[icp][:, oc * 512:(oc + 1) * 512],
                         start=(ic == 0), stop=False)
                    P.op("tensor", "matmul", pb[oc][0:8, 0:512],
                         GT_lo[:, ic * 8:(ic + 1) * 8],
                         w2sl[icp][:, oc * 512:(oc + 1) * 512],
                         start=False, stop=False)
                    P.op("tensor", "matmul", pb[oc][0:8, 0:512],
                         GT_sb[:, ic * 8:(ic + 1) * 8],
                         w2lo[icp][:, oc * 512:(oc + 1) * 512],
                         start=False, stop=(ic == 15),
                         inc=(c_mm if oc == 3 else None))
                c_mm.nxt()
                w_free_v[icp] = c_mm.v
            v_smm = c_mm.v
            P.wait("vector", c_mm, v_smm)
            for oc in range(4):
                P.op("vector", "tensor_copy",
                     out=Ssum[0:8, oc * 512:(oc + 1) * 512],
                     in_=pb[oc][0:8, 0:512], inc=(c_ve if oc == 3 else None))
            v_S = c_ve.nxt()
            if "G" in _DBG:
                P.wait("sync", c_ve, v_S)
                P.op("sync", "dma_start", out=dbg["G"].ap()[:, :], in_=Gsb[0:8, :],
                     inc=d_pre, inck=16)
                d_pre.nxt(16)
            P.wait("gpsimd", c_ve, v_S)
            P.op("gpsimd", "dma_start", out=cc_in.ap()[:, :], in_=Ssum[0:8, :],
                 inc=d_cc, inck=16)
            v_cc1 = d_cc.nxt(16)
            P.wait("gpsimd", d_cc, v_cc1)
            P.op("gpsimd", "collective_compute", "AllReduce", OP.add,
                 replica_groups=[list(range(NCORES))],
                 ins=[cc_in.ap().opt()], outs=[cc_out.ap().opt()], inc=c_cc)
            v_ccd = c_cc.nxt()
            P.wait("gpsimd", c_cc, v_ccd)
            P.op("gpsimd", "dma_start", out=Ssum[0:8, :], in_=cc_out.ap()[:, :],
                 inc=d_cc, inck=16)
            v_cc2 = d_cc.nxt(16)
            P.wait("vector", d_cc, v_cc2)
            P.op("vector", "tensor_scalar", out=Ssum[0:8, :], in0=Ssum[0:8, :],
                 scalar1=c0m_sb[:, :], scalar2=None, op0=OP.mult, inc=c_ve)
            v_Sm = c_ve.nxt()
            if "S" in _DBG:
                P.wait("sync", c_ve, v_Sm)
                P.op("sync", "dma_start", out=dbg["S"].ap()[:, :], in_=Ssum[0:8, :],
                     inc=d_pre, inck=16)
                d_pre.nxt(16)
            P.wait("sync", d_out[0], 16 * out_st[0])
            P.wait("sync", d_out[1], 16 * out_st[1])
            P.op("sync", "dma_start", out=patch[:, :], in_=out_d.ap()[0:8, :],
                 inc=d_pt, inck=16)
            v_cc3 = d_pt.nxt(16)
            P.wait("vector", d_pt, v_cc3)
            P.wait("vector", c_ve, v_Sm)
            P.op("vector", "tensor_tensor", out=patch[:, :], in0=patch[:, :],
                 in1=Ssum[0:8, :], op=OP.add, inc=c_ve)
            v_pt = c_ve.nxt()
            P.wait("sync", c_ve, v_pt)
            P.op("sync", "dma_start", out=out_d.ap()[0:8, :], in_=patch[:, :],
                 inc=d_cc, inck=16)
            v_cc4 = d_cc.nxt(16)
            if "snap" in _DBG:
                P.op("sync", "dma_start", out=dbg["snap"].ap()[:, :],
                     in_=snap[:, :], inc=d_pre, inck=16)
                d_pre.nxt(16)
            P.wait("sync", d_cc, v_cc4)
            P.wait("sync", d_pre, d_pre.v)

        block = es.enter_context(nc.Block())
        block.sync(lambda e: P.replay("sync", e))
        block.gpsimd(lambda e: P.replay("gpsimd", e))
        block.vector(lambda e: P.replay("vector", e))
        block.scalar(lambda e: P.replay("scalar", e))
        block.tensor(lambda e: P.replay("tensor", e))
    return nc


_nc_cache = {}


def kernel(**inputs):
    if "nc" not in _nc_cache:
        _nc_cache["nc"] = build_nc()
    nc = _nc_cache["nc"]
    g = _host_prep(inputs)
    x = np.ascontiguousarray(np.asarray(inputs["L0_attn_x"], dtype=np.float32))
    in_maps = []
    for c in range(NCORES):
        m = dict(g)
        m["x"] = np.ascontiguousarray(x[2 * c:2 * c + 2].reshape(NSEQ * S, D))
        m["bval"] = _f32(np.array([[2 * c, 2 * c + 1]]))
        m["c0m"] = _f32(np.full((8, 1), 1.0 if c == 0 else 0.0))
        in_maps.append(m)
    res = run_bass_kernel_spmd(nc, in_maps, core_ids=list(range(NCORES)))
    if _debug_capture is not None:
        _debug_capture.append(res)
    out = np.stack([np.asarray(res.results[c]["out"]).reshape(NSEQ, S, D)
                    for c in range(NCORES)])
    return np.ascontiguousarray(out.reshape(16, S, D).astype(np.float32))


# version 17
# speedup vs baseline: 1.2253x; 1.0257x over previous
import contextlib
import numpy as np
import ml_dtypes
import concourse.bass as bass
import concourse.mybir as mybir
from concourse.bass_utils import run_bass_kernel_spmd

F32 = mybir.dt.float32
BF16 = mybir.dt.bfloat16
I32 = mybir.dt.int32
OP = mybir.AluOpType
ACT = mybir.ActivationFunctionType
AX = mybir.AxisListType.X

NCORES = 8
NSEQ = 2
S = 2048
D = 2048
H = 16
DKV = 128
NTT = 16
NDC = 16
NQC = 4
SCL = float(1.0 / np.sqrt(np.float32(128.0)))
ROPE_THETA = 10000.0

_DBG = set()
_debug_capture = None


class Sem:
    def __init__(self, h):
        self.h = h
        self.v = 0

    def nxt(self, k=1):
        self.v += k
        return self.v


class Prog:
    ENGS = ("sync", "gpsimd", "vector", "scalar", "tensor")

    def __init__(self):
        self.s = {e: [] for e in self.ENGS}

    def op(self, eng, meth, *a, inc=None, inck=1, **kw):
        if (inc is not None and eng in ("vector", "scalar")
                and meth not in ("nop", "wait_ge", "drain")):
            self.s[eng].append((meth, a, kw, None, 0))
            self.s[eng].append(("drain", (), {}, None, 0))
            self.s[eng].append(("nop", (), {}, inc, inck))
        else:
            self.s[eng].append((meth, a, kw, inc, inck))

    def wait(self, eng, sem, v):
        if v > 0:
            self.s[eng].append(("wait_ge", (sem.h, v), {}, None, 0))

    def replay(self, eng_name, e):
        for meth, a, kw, inc, inck in self.s[eng_name]:
            inst = getattr(e, meth)(*a, **kw)
            if inc is not None:
                inst.then_inc(inc.h, inck)


def _bf(x):
    return np.ascontiguousarray(np.asarray(x).astype(ml_dtypes.bfloat16))


def _f32(x):
    return np.ascontiguousarray(np.asarray(x).astype(np.float32))


def _host_constants():
    c = {}
    c["ident_f"] = _f32(np.eye(128))
    c["ident_b"] = _bf(np.eye(128))
    r = np.arange(128)
    c["tri_b"] = _bf((r[None, :] >= r[:, None]).astype(np.float32))
    c["onescol_b"] = _bf(np.ones((128, 1)))
    c["onescol_f"] = _f32(np.ones((128, 1)))
    c["onesrow_f"] = _f32(np.ones((1, 128)))
    c["iota128col"] = _f32(((np.arange(128) % 64) // 16).reshape(128, 1))
    pos = np.arange(S, dtype=np.float64)
    freq = ROPE_THETA ** (-(np.arange(64, dtype=np.float64)) / 64.0)
    ang = pos[None, :] * freq[:, None]
    cosv = np.cos(ang)
    sinv = np.sin(ang)
    ctab = np.empty((128, S), dtype=np.float64)
    stab = np.empty((128, S), dtype=np.float64)
    ctab[0::2] = cosv
    ctab[1::2] = cosv
    stab[0::2] = sinv
    stab[1::2] = -sinv
    c["ropeC"] = _f32(ctab)
    c["ropeS"] = _f32(stab)
    cv = np.zeros((128, 4))
    cv[:, 0] = 1e-5
    cv[:, 1] = 1e-30
    cv[:, 2] = float(np.log(1.0 / np.sqrt(np.float32(128.0))) + np.log(1.03))
    cv[:, 3] = 1.0
    c["cvec"] = _f32(cv)
    return c


def _host_prep(inputs):
    g = {}
    def _tile_heads(WT):
        return _f32(WT.reshape(16, 128, H, DKV).transpose(2, 1, 0, 3)
                    .reshape(H, 128, D))

    g["wqT"] = _tile_heads(np.asarray(inputs["L0_attn_Wq"]).T)
    g["wkT"] = _tile_heads(np.asarray(inputs["L0_attn_Wk"]).T)
    g["wvT"] = _tile_heads(np.asarray(inputs["L0_attn_Wv"]).T)
    g["woT"] = _f32(np.asarray(inputs["L0_attn_Wo"]).T
                    .reshape(16, 128, D).transpose(1, 0, 2).reshape(128, 16 * D))
    g["w1T"] = _tile_heads(np.asarray(inputs["L0_ffn_w1"]).T)
    g["w3T"] = _tile_heads(np.asarray(inputs["L0_ffn_w3"]).T)
    g["w2T"] = _f32(np.asarray(inputs["L0_ffn_w2"]).T)
    g["wr_sb"] = _f32(np.asarray(inputs["L0_ffn_wr"]).T.reshape(16, 128, 8)
                      .transpose(1, 0, 2).reshape(128, 128))
    for p, W in (("q", "L0_attn_Aq"), ("k", "L0_attn_Ak"),
                 ("v", "L0_attn_Av"), ("o", "L0_attn_Ao")):
        acat = np.asarray(inputs[W]).transpose(2, 0, 1).reshape(D, 64)
        g["acat_" + p] = _bf(acat.reshape(16, 128, 64).transpose(1, 0, 2)
                             .reshape(128, 1024))
    bq = np.asarray(inputs["L0_attn_Bq"]).transpose(0, 2, 1).reshape(64, D)
    bk = np.asarray(inputs["L0_attn_Bk"]).transpose(0, 2, 1).reshape(64, D)
    bv = np.asarray(inputs["L0_attn_Bv"]).transpose(0, 2, 1).reshape(64, D)
    bo = np.asarray(inputs["L0_attn_Bo"]).transpose(0, 2, 1).reshape(64, D)
    g["bcat1"] = _bf(np.concatenate([bq, bk], axis=0))
    g["bcat2"] = _bf(np.concatenate([bo, bv], axis=0))
    g["starts"] = np.ascontiguousarray(
        np.asarray(inputs["L0_attn_lora_start_indices"]).astype(np.int32).reshape(1, 4))
    g.update(_host_constants())
    return g


def build_nc():
    nc = bass.Bass("TRN2", target_bir_lowering=False)
    P = Prog()

    x_d = nc.dram_tensor("x", [NSEQ * S, D], F32, kind="ExternalInput")
    out_d = nc.dram_tensor("out", [NSEQ * S, D], F32, kind="ExternalOutput")
    ins = {}
    for name, shp, dt in [
        ("wqT", [H, 128, D], F32), ("wkT", [H, 128, D], F32),
        ("wvT", [H, 128, D], F32), ("woT", [128, 16 * D], F32),
        ("w1T", [16, 128, D], F32), ("w3T", [16, 128, D], F32),
        ("w2T", [D, D], F32), ("wr_sb", [128, 128], F32),
        ("acat_q", [128, 1024], BF16), ("acat_k", [128, 1024], BF16),
        ("acat_v", [128, 1024], BF16), ("acat_o", [128, 1024], BF16),
        ("bcat1", [128, D], BF16), ("bcat2", [128, D], BF16),
        ("starts", [1, 4], I32),
        ("ident_f", [128, 128], F32), ("ident_b", [128, 128], BF16),
        ("tri_b", [128, 128], BF16),
        ("onescol_b", [128, 1], BF16), ("onescol_f", [128, 1], F32),
        ("onesrow_f", [1, 128], F32), ("iota128col", [128, 1], F32),
        ("ropeC", [128, S], F32), ("ropeS", [128, S], F32),
        ("bval", [1, 2], F32), ("c0m", [8, 1], F32), ("cvec", [128, 4], F32),
    ]:
        ins[name] = nc.dram_tensor(name, shp, dt, kind="ExternalInput")
    attT_d = nc.dram_tensor("attT", [H, 128, S], BF16,
                            kind="ExternalOutput" if "attT" in _DBG else "Internal")
    cc_in = nc.dram_tensor("cc_in", [8, D], F32)
    cc_out = nc.dram_tensor("cc_out", [8, D], F32)
    dbg = {}
    if "xnT" in _DBG:
        dbg["xnT"] = nc.dram_tensor("dbg_xnT", [128, NDC * S], BF16, kind="ExternalOutput")
    if "gates" in _DBG:
        dbg["logits"] = nc.dram_tensor("dbg_logits", [128, 128], F32, kind="ExternalOutput")
        dbg["gates"] = nc.dram_tensor("dbg_gates", [128, 128], BF16, kind="ExternalOutput")
    if "qkv" in _DBG:
        for n in ("qraw", "QT", "KT", "V"):
            dbg[n] = nc.dram_tensor("dbg_" + n, [128, S], BF16, kind="ExternalOutput")
        dbg["negB"] = nc.dram_tensor("dbg_negB", [128, 1], F32, kind="ExternalOutput")
    if "low" in _DBG:
        dbg["low1"] = nc.dram_tensor("dbg_low1", [128, S], BF16, kind="ExternalOutput")
        dbg["low2"] = nc.dram_tensor("dbg_low2", [128, S], BF16, kind="ExternalOutput")
    if "h2" in _DBG:
        dbg["h1"] = nc.dram_tensor("dbg_h1", [128, 16 * 1024], BF16, kind="ExternalOutput")
        dbg["h2"] = nc.dram_tensor("dbg_h2", [128, 16 * 1024], BF16, kind="ExternalOutput")
    if "G" in _DBG:
        dbg["G"] = nc.dram_tensor("dbg_G", [8, D], F32, kind="ExternalOutput")
    if "S" in _DBG:
        dbg["S"] = nc.dram_tensor("dbg_S", [8, D], F32, kind="ExternalOutput")
    if "snap" in _DBG:
        dbg["snap"] = nc.dram_tensor("dbg_snap", [128, 64], F32, kind="ExternalOutput")

    es = contextlib.ExitStack()
    with es:
        sems = {}
        for n in ("d_pre", "d_x0", "d_x1", "d_w0", "d_w1", "d_w2", "d_wo", "d_at0",
                  "d_at1", "d_as0", "d_as1", "d_out0", "d_out1", "d_cc", "d_pt",
                  "c_mm", "c_sc", "c_ve", "c_cc",
                  "f_x0", "f_x1", "f_w0", "f_w1", "f_at0", "f_at1"):
            sems[n] = Sem(es.enter_context(nc.semaphore(n)))
        d_pre = sems["d_pre"]
        d_x = [sems["d_x0"], sems["d_x1"]]
        d_w = [sems["d_w0"], sems["d_w1"], sems["d_w2"]]
        d_wo = sems["d_wo"]
        d_at = [sems["d_at0"], sems["d_at1"]]
        d_as = [sems["d_as0"], sems["d_as1"]]
        d_out = [sems["d_out0"], sems["d_out1"]]
        d_cc = sems["d_cc"]
        d_pt = sems["d_pt"]
        c_mm = sems["c_mm"]
        c_sc = sems["c_sc"]
        c_ve = sems["c_ve"]
        c_cc = sems["c_cc"]
        f_x = [sems["f_x0"], sems["f_x1"]]
        f_w = [sems["f_w0"], sems["f_w1"]]
        f_at = [sems["f_at0"], sems["f_at1"]]

        sb = lambda n, shp, dt: es.enter_context(nc.sbuf_tensor("s_" + n, shp, dt))
        identf = sb("identf", [128, 128], F32)
        identb = sb("identb", [128, 128], BF16)
        tri = sb("tri", [128, 128], BF16)
        onescol_b = sb("onescol_b", [128, 1], BF16)
        onescol_f = sb("onescol_f", [128, 1], F32)
        onesrow_f = sb("onesrow_f", [1, 128], F32)
        iota128 = sb("iota128", [128, 1], F32)
        wr_sb = sb("wr_sb", [128, 128], F32)
        acat = {p: sb("acat_" + p, [128, 1024], BF16) for p in "qkvo"}
        bcat1 = sb("bcat1", [128, D], BF16)
        bcat2 = sb("bcat2", [128, D], BF16)
        starts_sb = sb("starts_sb", [1, 4], I32)
        startsf = sb("startsf", [1, 4], F32)
        bval_sb = sb("bval_sb", [1, 2], F32)
        c0m_sb = sb("c0m_sb", [8, 1], F32)
        m64col = sb("m64col", [128, 1], F32)
        LOW1 = sb("LOW1", [128, S], BF16)
        LOW2 = sb("LOW2", [128, S], BF16)
        gates = sb("gates", [128, 128], BF16)
        logit_sb = sb("logit_sb", [128, 128], F32)
        Gsb = sb("Gsb", [8, D], F32)
        nrm = sb("nrm", [1, 8], F32)
        st1 = sb("st1", [128, 1], F32)
        st2 = sb("st2", [128, 1], F32)
        st3 = sb("st3", [128, 1], F32)
        st4 = sb("st4", [128, 1], F32)
        negB = sb("negB", [128, 1], F32)
        gsc = sb("gsc", [128, 96], F32)
        cst = sb("cst", [128, 4], F32)
        snap = sb("snap", [128, 64], F32) if "snap" in _DBG else None
        xnT = sb("xnT", [128, NDC * S], BF16)

        scA = es.enter_context(nc.psum_tensor("scA", [128, 1024], F32))
        scB = es.enter_context(nc.psum_tensor("scB", [128, 1024], F32))
        pb47 = [es.enter_context(nc.psum_tensor("pb%d" % i, [128, 512], F32))
                for i in range(4, 8)]

        def pbank(i):
            if i < 2:
                return scA[:, i * 512:(i + 1) * 512]
            if i < 4:
                return scB[:, (i - 2) * 512:(i - 1) * 512]
            return pb47[i - 4][:, :]
        pb = [pbank(i) for i in range(8)]
        _uid = [0]

        def usb(stack, n, shp, dt):
            _uid[0] += 1
            return stack.enter_context(
                nc.sbuf_tensor("u%d_%s" % (_uid[0], n), shp, dt))

        bank_state = {}
        buf_state = {}

        def bank_acq(i, eng="tensor"):
            if i in bank_state:
                sem, v = bank_state[i]
                P.wait(eng, sem, v)

        def bank_rel(i, sem, v):
            bank_state[i] = (sem, v)

        def buf_acq(name, eng):
            if name in buf_state:
                sem, v = buf_state[name]
                P.wait(eng, sem, v)

        def buf_rel(name, sem, v):
            buf_state[name] = (sem, v)

        def AP(t, off, dims):
            return bass.AP(t, off, dims)

        def phase_fence(engs=("gpsimd", "sync"), extra=()):
            for e in engs:
                P.wait(e, c_mm, c_mm.v)
                P.wait(e, c_ve, c_ve.v)
                P.wait(e, c_sc, c_sc.v)
                for sem, v in extra:
                    P.wait(e, sem, v)

        def vdrain():
            P.op("vector", "drain")

        def sdrain():
            P.op("scalar", "drain")

        n_pre = 0
        for name, dst in [("ident_f", identf),
                          ("ident_b", identb), ("tri_b", tri),
                          ("onescol_b", onescol_b), ("onescol_f", onescol_f),
                          ("onesrow_f", onesrow_f), ("iota128col", iota128),
                          ("wr_sb", wr_sb), ("bcat1", bcat1), ("bcat2", bcat2),
                          ("starts", starts_sb), ("bval", bval_sb), ("c0m", c0m_sb),
                          ("cvec", cst)]:
            P.op("sync", "dma_start", out=dst[:, :], in_=ins[name].ap(), inc=d_pre, inck=16)
            n_pre += 1
        for p in "qkvo":
            P.op("sync", "dma_start", out=acat[p][:, :], in_=ins["acat_" + p].ap(),
                 inc=d_pre, inck=16)
            n_pre += 1
        pre_v = d_pre.nxt(16 * n_pre)
        for e in ("vector", "scalar", "tensor", "gpsimd"):
            P.wait(e, d_pre, pre_v)

        P.op("vector", "tensor_copy", out=startsf[:, :], in_=starts_sb[:, :], inc=c_ve)
        v_startsf = c_ve.nxt()

        rdy = {}
        NH_LOADS = [0, 0, 0]
        x_loads = [0, 0]
        x_free_v = [None, None]
        xC_free_v = [None, None]
        at_loads = [0, 0]
        at_free_v = [None, None]
        as_st = [0, 0]
        out_st = [0, 0]
        w_free_v = [None, None, None]

        for s in range(NSEQ):
            row0 = s * S
            esA = contextlib.ExitStack()
            with esA:
                xsb = [usb(esA, "xsb%d" % i, [128, D], F32)
                       for i in range(2)]
                dsq = usb(esA, "dsq", [128, D], F32)
                xnt = [usb(esA, "xnt%d" % i, [128, D], F32)
                       for i in range(2)]
                xnf = [usb(esA, "xnf%d" % i, [128, 512], F32)
                       for i in range(2)]
                sumsq = usb(esA, "sumsq", [128, 1], F32)
                sumx = usb(esA, "sumx", [128, 1], F32)
                meanc = usb(esA, "meanc", [128, 1], F32)
                m2c = usb(esA, "m2c", [128, 1], F32)
                varc = usb(esA, "varc", [128, 1], F32)
                rstd = usb(esA, "rstd", [128, 1], F32)

                phase_fence(engs=("sync",),
                            extra=((d_out[0], 16 * out_st[0]),
                                   (d_out[1], 16 * out_st[1])))
                buf_acq("xnT", "vector")
                buf_acq("gates", "vector")
                buf_acq("logit_sb", "scalar")
                xnt_freeat = [0, 0]
                xnf_free_mm = [0, 0]
                xnf_free_ve = [0, 0]

                for tt in range(NTT):
                    p = tt % 2
                    if x_free_v[p] is not None:
                        _xf = {"ve": c_ve, "sc": c_sc}.get(x_free_v[p][0], c_mm)
                        P.wait("sync", _xf, x_free_v[p][1])
                    P.op("sync", "dma_start", out=xsb[p][:, :],
                         in_=x_d.ap()[row0 + tt * 128: row0 + (tt + 1) * 128, :],
                         inc=d_x[p], inck=16)
                    x_loads[p] += 1
                    xv = 16 * x_loads[p]
                    P.wait("vector", d_x[p], xv)
                    P.op("vector", "reduce_sum", sumx[:, :], xsb[p][:, :], axis=AX)
                    P.op("vector", "tensor_tensor", out=dsq[:, :], in0=xsb[p][:, :],
                         in1=xsb[p][:, :], op=OP.mult)
                    P.op("vector", "reduce_sum", sumsq[:, :], dsq[:, :], axis=AX)
                    vdrain()
                    P.op("vector", "tensor_scalar", out=meanc[:, :], in0=sumx[:, :],
                         scalar1=-1.0 / D, scalar2=None, op0=OP.mult)
                    P.op("vector", "tensor_scalar", out=varc[:, :], in0=sumsq[:, :],
                         scalar1=1.0 / D, scalar2=None, op0=OP.mult)
                    vdrain()
                    P.op("vector", "tensor_tensor", out=sumsq[:, :], in0=meanc[:, :],
                         in1=meanc[:, :], op=OP.mult)
                    vdrain()
                    P.op("vector", "tensor_tensor", out=varc[:, :], in0=varc[:, :],
                         in1=sumsq[:, :], op=OP.subtract, inc=c_ve)
                    v_var = c_ve.nxt()
                    P.wait("scalar", c_ve, v_var)
                    P.op("scalar", "activation", st1[:, :], varc[:, :], ACT.Ln,
                         bias=cst[:, 0:1])
                    sdrain()
                    P.op("scalar", "activation", rstd[:, :], st1[:, :], ACT.Exp,
                         scale=-0.5)
                    sdrain()
                    P.op("scalar", "activation", st1[:, :], meanc[:, :], ACT.Copy,
                         scale=rstd[:, :], bias=0.0)
                    sdrain()
                    P.wait("scalar", c_mm, xnt_freeat[p])
                    P.op("scalar", "activation", xnt[p][:, :], xsb[p][:, :],
                         ACT.Identity, scale=rstd[:, :], bias=st1[:, :],
                         inc=c_sc)
                    v_xn = c_sc.nxt()
                    x_free_v[p] = ("sc", v_xn)
                    if "snap" in _DBG and s == 0 and tt == 0:
                        P.op("vector", "tensor_copy", out=snap[:, 48:52],
                             in_=xnt[p][:, 0:4])
                        P.op("vector", "tensor_copy", out=snap[:, 52:53],
                             in_=rstd[:, :])
                        P.op("vector", "tensor_copy", out=snap[:, 53:54],
                             in_=varc[:, :])
                        P.op("vector", "tensor_copy", out=snap[:, 54:55],
                             in_=meanc[:, :])
                        P.op("vector", "tensor_copy", out=snap[:, 55:56],
                             in_=sumx[:, :])
                        P.op("vector", "tensor_copy", out=snap[:, 60:64],
                             in_=xsb[p][:, 0:4])
                    P.wait("tensor", c_sc, v_xn)
                    for g in range(4):
                        gp = g % 2
                        bank_acq(gp)
                        for k in range(4):
                            dc = 4 * g + k
                            P.op("tensor", "transpose", pb[gp][:, k * 128:(k + 1) * 128],
                                 xnt[p][:, dc * 128:(dc + 1) * 128], identf[:, :],
                                 inc=(c_mm if k == 3 else None))
                        v_tr = c_mm.nxt()
                        if g == 3:
                            xnt_freeat[p] = v_tr
                        P.wait("scalar", c_mm, v_tr)
                        P.wait("scalar", c_mm, xnf_free_mm[gp])
                        P.wait("scalar", c_ve, xnf_free_ve[gp])
                        P.op("scalar", "copy", xnf[gp][:, :], pb[gp][:, :], inc=c_sc)
                        v_ev = c_sc.nxt()
                        if "snap" in _DBG and s == 0 and tt == 0 and g == 0:
                            P.op("scalar", "copy", snap[:, 56:60], xnf[gp][:, 0:4])
                        bank_rel(gp, c_sc, v_ev)
                        P.wait("tensor", c_sc, v_ev)
                        if g == 0:
                            bank_acq(2)
                        for k in range(4):
                            dc = 4 * g + k
                            P.op("tensor", "matmul", pb[2][:, 0:8],
                                 xnf[gp][:, k * 128:(k + 1) * 128],
                                 wr_sb[:, dc * 8:(dc + 1) * 8],
                                 start=(dc == 0), stop=(dc == 15),
                                 inc=(c_mm if k == 3 else None))
                        v_rt = c_mm.nxt()
                        xnf_free_mm[gp] = v_rt
                        P.wait("vector", c_sc, v_ev)
                        P.op("vector", "tensor_copy",
                             out=AP(xnT, (4 * g) * S + tt * 128,
                                    [[NDC * S, 128], [S, 4], [1, 128]]),
                             in_=AP(xnf[gp], 0, [[512, 128], [128, 4], [1, 128]]),
                             inc=c_ve)
                        xnf_free_ve[gp] = c_ve.nxt()
                    P.wait("scalar", c_mm, c_mm.v)
                    P.op("scalar", "copy", logit_sb[:, tt * 8:(tt + 1) * 8],
                         pb[2][:, 0:8], inc=c_sc)
                    bank_rel(2, c_sc, c_sc.nxt())


                rdy[("xnT", s)] = c_ve.v
                v_logits = c_sc.v

                m1n = gsc[:, 0:16]
                m2n = gsc[:, 16:32]
                eb = usb(esA, "eb", [128, 128], F32)
                zb2 = usb(esA, "zb2", [128, 128], F32)
                indb = usb(esA, "indb", [128, 128], F32)
                l2b = usb(esA, "l2b", [128, 128], F32)
                rg = gsc[:, 48:64]
                P.wait("vector", c_sc, v_logits)
                for t2 in range(16):
                    P.op("vector", "reduce_max", m1n[:, t2:t2 + 1],
                         logit_sb[:, t2 * 8:(t2 + 1) * 8], axis=AX, negate=True)
                vdrain()
                for t2 in range(16):
                    P.op("vector", "tensor_scalar", out=zb2[:, t2 * 8:(t2 + 1) * 8],
                         in0=logit_sb[:, t2 * 8:(t2 + 1) * 8],
                         scalar1=m1n[:, t2:t2 + 1], scalar2=None, op0=OP.add)
                vdrain()
                P.op("vector", "tensor_single_scalar", indb[:, :], zb2[:, :], 0.0,
                     OP.is_equal)
                vdrain()
                P.op("vector", "scalar_tensor_tensor", out=l2b[:, :], in0=indb[:, :],
                     scalar=-1e30, in1=logit_sb[:, :], op0=OP.mult, op1=OP.add)
                vdrain()
                for t2 in range(16):
                    P.op("vector", "reduce_max", m2n[:, t2:t2 + 1],
                         l2b[:, t2 * 8:(t2 + 1) * 8], axis=AX, negate=True)
                vdrain()
                P.op("vector", "nop", inc=c_ve)
                v_g1 = c_ve.nxt()
                P.wait("scalar", c_ve, v_g1)
                P.op("scalar", "activation", eb[:, :], zb2[:, :], ACT.Exp,
                     inc=c_sc)
                v_g2 = c_sc.nxt()
                P.wait("vector", c_sc, v_g2)
                P.op("vector", "tensor_tensor", out=gsc[:, 64:80], in0=m1n,
                     in1=m2n, op=OP.subtract, inc=c_ve)
                v_g3 = c_ve.nxt()
                P.wait("scalar", c_ve, v_g3)
                P.op("scalar", "activation", gsc[:, 80:96], gsc[:, 64:80], ACT.Exp,
                     inc=c_sc)
                v_g4 = c_sc.nxt()
                P.wait("vector", c_sc, v_g4)
                P.op("vector", "tensor_scalar", out=gsc[:, 64:80],
                     in0=gsc[:, 80:96], scalar1=1.0, scalar2=None, op0=OP.add)
                vdrain()
                P.op("vector", "reciprocal", rg, gsc[:, 64:80])
                for t2 in range(16):
                    P.op("vector", "tensor_scalar", out=l2b[:, t2 * 8:(t2 + 1) * 8],
                         in0=logit_sb[:, t2 * 8:(t2 + 1) * 8],
                         scalar1=m2n[:, t2:t2 + 1], scalar2=None, op0=OP.add)
                vdrain()
                P.op("vector", "tensor_single_scalar", indb[:, :], l2b[:, :], 0.0,
                     OP.is_ge)
                vdrain()
                P.op("vector", "tensor_tensor", out=eb[:, :], in0=eb[:, :],
                     in1=indb[:, :], op=OP.mult)
                vdrain()
                for t2 in range(16):
                    P.op("vector", "tensor_scalar", out=gates[:, t2 * 8:(t2 + 1) * 8],
                         in0=eb[:, t2 * 8:(t2 + 1) * 8],
                         scalar1=rg[:, t2:t2 + 1], scalar2=None, op0=OP.mult,
                         inc=(c_ve if t2 == 15 else None))
                rdy[("gates", s)] = c_ve.nxt()
                if "gates" in _DBG and s == 0:
                    P.wait("sync", c_ve, rdy[("gates", s)])
                    P.op("sync", "dma_start", out=dbg["logits"].ap()[:, :],
                         in_=logit_sb[:, :], inc=d_pre, inck=16)
                    P.op("sync", "dma_start", out=dbg["gates"].ap()[:, :],
                         in_=gates[:, :], inc=d_pre, inck=16)
                    d_pre.nxt(32)
                if "xnT" in _DBG and s == 0:
                    P.wait("sync", c_ve, rdy[("xnT", s)])
                    P.op("sync", "dma_start", out=dbg["xnT"].ap()[:, :],
                         in_=xnT[:, :], inc=d_pre, inck=16)
                    d_pre.nxt(16)

            P.wait("vector", c_ve, v_startsf)
            buf_acq("LOW1", "vector")
            buf_acq("LOW2", "vector")
            P.op("vector", "tensor_scalar", out=gsc[0:1, 0:4],
                 in0=startsf[:, :], scalar1=bval_sb[0:1, s:s + 1], scalar2=None,
                 op0=OP.is_le)
            vdrain()
            P.op("vector", "reduce_sum", gsc[0:1, 4:5], gsc[0:1, 0:4], axis=AX)
            vdrain()
            P.op("vector", "tensor_scalar", out=gsc[0:1, 5:6], in0=gsc[0:1, 4:5],
                 scalar1=-1.0, scalar2=0.0, op0=OP.add, op1=OP.max)
            vdrain()
            P.op("vector", "tensor_scalar", out=gsc[0:1, 6:7], in0=gsc[0:1, 5:6],
                 scalar1=3.0, scalar2=None, op0=OP.min, inc=c_ve)
            v_ids = c_ve.nxt()
            P.wait("tensor", c_ve, v_ids)
            bank_acq(3)
            P.op("tensor", "matmul", pb[3][:, 0:1], onesrow_f[:, :], gsc[0:1, 6:7],
                 start=True, stop=True, inc=c_mm)
            v_idb = c_mm.nxt()
            P.wait("vector", c_mm, v_idb)
            P.op("vector", "tensor_copy", out=st2[:, :], in_=pb[3][:, 0:1], inc=c_ve)
            bank_rel(3, c_ve, c_ve.nxt())
            vdrain()
            P.op("vector", "tensor_scalar", out=m64col[:, :], in0=iota128[:, :],
                 scalar1=st2[:, :], scalar2=None, op0=OP.is_equal, inc=c_ve)
            v_m64 = c_ve.nxt()
            xnT_rdy = rdy[("xnT", s)]
            P.wait("tensor", c_ve, xnT_rdy)
            for pn, dstT, rlo in (("q", LOW1, 0), ("k", LOW1, 64), ("v", LOW2, 64)):
                for qc in range(NQC):
                    bi = 4 + (qc % 2)
                    bank = pb[bi]
                    bank_acq(bi)
                    for dc in range(NDC):
                        tp = (0, 64) if rlo == 64 else None
                        P.op("tensor", "matmul", bank[rlo:rlo + 64, 0:512],
                             acat[pn][:, dc * 64:(dc + 1) * 64],
                             xnT[:, dc * S + qc * 512: dc * S + qc * 512 + 512],
                             start=(dc == 0), stop=(dc == 15),
                             tile_position=tp,
                             inc=(c_mm if dc == 15 else None))
                    v_low = c_mm.nxt()
                    P.wait("vector", c_mm, v_low)
                    P.wait("vector", c_ve, v_m64)
                    P.op("vector", "tensor_scalar",
                         out=dstT[rlo:rlo + 64, qc * 512:(qc + 1) * 512],
                         in0=bank[rlo:rlo + 64, 0:512],
                         scalar1=m64col[rlo:rlo + 64, :], scalar2=None,
                         op0=OP.mult, inc=c_ve)
                    bank_rel(bi, c_ve, c_ve.nxt())
            rdy[("lows", s)] = c_ve.v
            if "low" in _DBG and s == 0:
                P.wait("sync", c_ve, rdy[("lows", s)])
                P.op("sync", "dma_start", out=dbg["low1"].ap()[:, :], in_=LOW1[:, :],
                     inc=d_pre, inck=16)
                d_pre.nxt(16)
            esB = contextlib.ExitStack()
            with esB:
                wsl = {pn: [usb(esB, "wsl_%s%d" % (pn, i), [128, D], BF16)
                    for i in range(2)] for pn in "qkv"}
                ropeC = usb(esB, "ropeC", [128, S], F32)
                ropeS = usb(esB, "ropeS", [128, S], F32)
                qraw = usb(esB, "qraw", [128, S], F32)
                kraw = usb(esB, "kraw", [128, S], F32)
                QT = usb(esB, "QT", [128, S], BF16)
                KT = usb(esB, "KT", [128, S], BF16)
                Vt = usb(esB, "Vt", [128, S], BF16)
                u1 = usb(esB, "u1", [128, S], F32)
                u2 = usb(esB, "u2", [128, S], F32)
                lowo_f = usb(esB, "lowo_f", [64, S], F32)
                stage = [usb(esB, "stage%d" % i, [128, 512], BF16) for i in range(2)]
                pt = [usb(esB, "pt%d" % i, [128, 1024], BF16) for i in range(2)]
                rb_sb = usb(esB, "rb_sb", [128, 512], F32)
                rcp_sb = usb(esB, "rcp_sb", [1, 512], F32)
                sq2 = [usb(esB, "sq2_%d" % i, [128, 512], F32) for i in range(2)]

                P.op("vector", "memset", lowo_f[:, :], 0.0, inc=c_ve)
                rdy[("lowo0", s)] = c_ve.nxt()
                phase_fence()
                P.op("sync", "dma_start", out=ropeC[:, :], in_=ins["ropeC"].ap(),
                     inc=d_pre, inck=16)
                P.op("sync", "dma_start", out=ropeS[:, :], in_=ins["ropeS"].ap(),
                     inc=d_pre, inck=16)
                v_rope_tabs = d_pre.nxt(32)
                P.wait("vector", d_pre, v_rope_tabs)
                wnames = {"q": "wqT", "k": "wkT", "v": "wvT"}
                rope_mask = []
                for i in range(16):
                    rope_mask += [2 * i + 1, 2 * i]

                pend = []

                def flush_pend():
                    while pend:
                        pend.pop(0)()

                for h in range(H):
                    hp = h % 2
                    if w_free_v[hp] is not None:
                        P.wait("gpsimd", c_mm, w_free_v[hp])
                    for pn in "qkv":
                        P.op("gpsimd", "dma_start", out=wsl[pn][hp][:, :],
                             in_=AP(ins[wnames[pn]], h * 128 * D,
                                    [[D, 128], [1, D]]),
                             inc=d_w[hp], inck=16)
                        NH_LOADS[hp] += 1
                    wv_ready = 16 * NH_LOADS[hp]

                    def proj(pn, b0, lorlo, bc, lowbuf):
                        P.wait("tensor", d_w[hp], wv_ready)
                        P.wait("tensor", c_ve, rdy[("lows", s)])
                        for qc in range(NQC):
                            bank_acq(b0 + qc)
                        for dc in range(NDC):
                            for qc in range(NQC):
                                P.op("tensor", "matmul", pb[b0 + qc],
                                     wsl[pn][hp][:, dc * 128:(dc + 1) * 128],
                                     xnT[:, dc * S + qc * 512: dc * S + qc * 512 + 512],
                                     start=(dc == 0), stop=False)
                        tp = (64, 0) if lorlo == 64 else None
                        for qc in range(NQC):
                            P.op("tensor", "matmul", pb[b0 + qc],
                                 bc[lorlo:lorlo + 64, h * 128:(h + 1) * 128],
                                 lowbuf[lorlo:lorlo + 64, qc * 512:(qc + 1) * 512],
                                 start=False, stop=True, tile_position=tp,
                                 inc=(c_mm if qc == 3 else None))
                        return c_mm.nxt()

                    def evac_norms(rawbuf, b0, ncol0, v_p):
                        for qc in range(NQC):
                            P.wait("scalar", c_mm, v_p)
                            buf_acq(rawbuf, "scalar")
                            P.op("scalar", "copy",
                                 locals_raw[rawbuf][:, qc * 512:(qc + 1) * 512],
                                 pb[b0 + qc])
                            buf_acq("sq%d" % (qc % 2), "scalar")
                            P.op("scalar", "activation", sq2[qc % 2][:, :],
                                 pb[b0 + qc], ACT.Square, inc=c_sc)
                            v_ev = c_sc.nxt()
                            P.wait("tensor", c_sc, v_ev)
                            P.op("tensor", "matmul", pb[b0 + qc][0:1, 0:512],
                                 onescol_f[:, :], sq2[qc % 2][:, :], start=True,
                                 stop=True, inc=c_mm)
                            v_nm = c_mm.nxt()
                            buf_rel("sq%d" % (qc % 2), c_mm, v_nm)
                            P.wait("vector", c_mm, v_nm)
                            P.op("vector", "reduce_max",
                                 nrm[0:1, ncol0 + qc:ncol0 + qc + 1],
                                 pb[b0 + qc][0:1, 0:512], axis=AX, inc=c_ve)
                            bank_rel(b0 + qc, c_ve, c_ve.nxt())
                        return c_sc.v

                    locals_raw = {"qraw": qraw, "kraw": kraw}

                    v_q = proj("q", 0, 0, bcat1, LOW1)
                    flush_pend()
                    v_qev = evac_norms("qraw", 0, 0, v_q)
                    v_k = proj("k", 4, 64, bcat1, LOW1)
                    v_kev = evac_norms("kraw", 4, 4, v_k)

                    v_rope = {}
                    for rawbuf, dstT, key, v_prod in ((qraw, QT, "q", v_qev),
                                                      (kraw, KT, "k", v_kev)):
                        P.wait("vector", c_sc, v_prod)
                        buf_acq("QT" if key == "q" else "KT", "vector")
                        P.op("vector", "tensor_tensor", out=u1[:, :], in0=rawbuf[:, :],
                             in1=ropeS[:, :], op=OP.mult)
                        P.op("vector", "stream_shuffle", out=u2[:, :], in_=u1[:, :],
                             mask=rope_mask)
                        P.op("vector", "tensor_tensor", out=u1[:, :], in0=rawbuf[:, :],
                             in1=ropeC[:, :], op=OP.mult)
                        P.op("vector", "tensor_tensor", out=dstT[:, :], in0=u2[:, :],
                             in1=u1[:, :], op=OP.add, inc=c_ve)
                        v_rope[key] = c_ve.nxt()
                    buf_rel("qraw", c_ve, v_rope["q"])
                    buf_rel("kraw", c_ve, v_rope["k"])

                    vdrain()
                    P.op("vector", "reduce_max", st1[0:1, 0:1], nrm[0:1, 0:4], axis=AX)
                    P.op("vector", "reduce_max", st2[0:1, 0:1], nrm[0:1, 4:8],
                         axis=AX, inc=c_ve)
                    v_nmx = c_ve.nxt()
                    P.wait("scalar", c_ve, v_nmx)
                    P.op("scalar", "activation", st3[0:1, 0:1], st1[0:1, 0:1],
                         ACT.Ln, bias=cst[0:1, 1:2])
                    P.op("scalar", "activation", st4[0:1, 0:1], st2[0:1, 0:1],
                         ACT.Ln, bias=cst[0:1, 1:2], inc=c_sc)
                    v_ln = c_sc.nxt()
                    P.wait("vector", c_sc, v_ln)
                    P.op("vector", "tensor_tensor", out=st3[0:1, 0:1],
                         in0=st3[0:1, 0:1], in1=st4[0:1, 0:1], op=OP.add, inc=c_ve)
                    v_add2 = c_ve.nxt()
                    P.wait("scalar", c_ve, v_add2)
                    sdrain()
                    P.op("scalar", "activation", st4[0:1, 0:1], st3[0:1, 0:1],
                         ACT.Exp, scale=0.5, bias=cst[0:1, 2:3], inc=c_sc)
                    v_B = c_sc.nxt()
                    P.wait("vector", c_sc, v_B)
                    vdrain()
                    P.op("vector", "tensor_scalar", out=st3[0:1, 0:1],
                         in0=st4[0:1, 0:1], scalar1=-1.0, scalar2=None, op0=OP.mult,
                         inc=c_ve)
                    v_nB = c_ve.nxt()

                    v_v = proj("v", 0, 64, bcat2, LOW2)
                    buf_acq("qraw", "scalar")
                    for qc in range(NQC):
                        P.wait("scalar", c_mm, v_v)
                        P.op("scalar", "copy", qraw[:, qc * 512:(qc + 1) * 512],
                             pb[qc], inc=c_sc)
                        bank_rel(qc, c_sc, c_sc.nxt())
                    v_vev = c_sc.v
                    buf_acq("Vt", "vector")
                    for g in range(4):
                        bi = 4 + (g % 2)
                        P.wait("tensor", c_sc, v_vev)
                        bank_acq(bi)
                        for k in range(4):
                            tc = 4 * g + k
                            P.op("tensor", "transpose",
                                 pb[bi][:, k * 128:(k + 1) * 128],
                                 qraw[:, tc * 128:(tc + 1) * 128], identf[:, :],
                                 inc=(c_mm if k == 3 else None))
                        v_tr = c_mm.nxt()
                        P.wait("vector", c_mm, v_tr)
                        P.op("vector", "tensor_copy",
                             out=Vt[:, g * 512:(g + 1) * 512],
                             in_=pb[bi][:, 0:512],
                             inc=c_ve)
                        bank_rel(bi, c_ve, c_ve.nxt())
                    v_vt = c_ve.v
                    buf_rel("qraw", c_mm, c_mm.v)
                    w_free_v[hp] = c_mm.v

                    P.wait("tensor", c_ve, v_nB)
                    bank_acq(6)
                    P.op("tensor", "matmul", pb[6][:, 0:1], onesrow_f[:, :],
                         st3[0:1, 0:1], start=True, stop=True, inc=c_mm)
                    v_nBb = c_mm.nxt()
                    P.wait("scalar", c_mm, v_nBb)
                    P.op("scalar", "copy", negB[:, :], pb[6][:, 0:1], inc=c_sc)
                    v_negB = c_sc.nxt()
                    bank_rel(6, c_sc, v_negB)

                    for qc in range(NQC):
                        jmax = 4 * qc + 4
                        ob_i = 4 + (qc % 2)
                        ob = pb[ob_i]
                        dbk = pb[6]
                        npair = jmax // 2

                        def emit_mm1(p_):
                            j0, j1 = 2 * p_, 2 * p_ + 1
                            ti = p_ % 2
                            T = scA if ti == 0 else scB
                            bank_acq(2 * ti)
                            bank_acq(2 * ti + 1)
                            P.wait("tensor", c_ve, v_rope["k"])
                            P.op("tensor", "matmul", T[:, 0:512],
                                 KT[:, j0 * 128:(j0 + 1) * 128],
                                 QT[:, qc * 512:(qc + 1) * 512],
                                 start=True, stop=True, inc=c_mm)
                            v_mm1a = c_mm.nxt()
                            P.op("tensor", "matmul", T[:, 512:1024],
                                 KT[:, j1 * 128:(j1 + 1) * 128],
                                 QT[:, qc * 512:(qc + 1) * 512],
                                 start=True, stop=True, inc=c_mm)
                            v_mm1b = c_mm.nxt()
                            P.wait("scalar", c_sc, v_negB)
                            buf_acq("pt%d" % ti, "scalar")
                            ve_h = []
                            vm_h = []
                            for hi, v_m in ((0, v_mm1a), (1, v_mm1b)):
                                half = hi * 512
                                P.wait("scalar", c_mm, v_m)
                                P.op("scalar", "activation",
                                     pt[ti][:, half:half + 512],
                                     T[:, half:half + 512], ACT.Exp,
                                     bias=negB[:, :], scale=SCL, inc=c_sc)
                                v_exp = c_sc.nxt()
                                bank_rel(2 * ti + hi, c_sc, v_exp)
                                ve_h.append(v_exp)
                                j = (j0, j1)[hi]
                                v_msk = 0
                                if j >= 4 * qc:
                                    m = j - 4 * qc
                                    P.wait("vector", c_sc, v_exp)
                                    if m > 0:
                                        P.op("vector", "memset",
                                             pt[ti][:, half:half + m * 128], 0.0)
                                    P.op("vector", "tensor_tensor",
                                         out=pt[ti][:, half + m * 128:half + (m + 1) * 128],
                                         in0=pt[ti][:, half + m * 128:half + (m + 1) * 128],
                                         in1=tri[:, :], op=OP.mult, inc=c_ve)
                                    v_msk = c_ve.nxt()
                                vm_h.append(v_msk)
                            return ve_h, vm_h

                        def emit_mm23(p_, ve_h, vm_h):
                            j0, j1 = 2 * p_, 2 * p_ + 1
                            ti = p_ % 2
                            P.wait("tensor", c_ve, v_vt)
                            for hi, j in ((0, j0), (1, j1)):
                                half = hi * 512
                                P.wait("tensor", c_sc, ve_h[hi])
                                if vm_h[hi]:
                                    P.wait("tensor", c_ve, vm_h[hi])
                                P.op("tensor", "matmul", ob,
                                     Vt[:, j * 128:(j + 1) * 128],
                                     pt[ti][:, half:half + 512],
                                     start=(j == 0), stop=(j == jmax - 1))
                                P.op("tensor", "matmul", dbk[0:1, 0:512],
                                     onescol_b[:, :], pt[ti][:, half:half + 512],
                                     start=(j == 0), stop=(j == jmax - 1),
                                     inc=(c_mm if hi == 1 else None))
                            buf_rel("pt%d" % ti, c_mm, c_mm.nxt())

                        prev = None
                        for p_ in range(npair):
                            e_m = emit_mm1(p_)
                            if p_ == 0:
                                flush_pend()
                                bank_acq(ob_i)
                                bank_acq(6)
                            if prev is not None:
                                emit_mm23(p_ - 1, *prev)
                            prev = e_m
                        emit_mm23(npair - 1, *prev)
                        v_pv = c_mm.v
                        sp = qc % 2
                        P.wait("vector", c_mm, v_pv)
                        buf_acq("rcp_sb", "vector")
                        P.op("vector", "reciprocal", rcp_sb[:, :],
                             dbk[0:1, 0:512], inc=c_ve)
                        v_rc = c_ve.nxt()
                        bank_rel(6, c_ve, v_rc)

                        def make_tail(qc=qc, sp=sp, ob=ob, ob_i=ob_i, v_rc=v_rc,
                                      h=h, s=s):
                            def tail():
                                P.wait("tensor", c_ve, v_rc)
                                bank_acq(7)
                                P.op("tensor", "matmul", pb[7], onesrow_f[:, :],
                                     rcp_sb[:, :], start=True, stop=True, inc=c_mm)
                                v_rbmm = c_mm.nxt()
                                buf_rel("rcp_sb", c_mm, v_rbmm)
                                P.wait("scalar", c_mm, v_rbmm)
                                buf_acq("rb_sb", "scalar")
                                P.op("scalar", "copy", rb_sb[:, :], pb[7], inc=c_sc)
                                v_rbev = c_sc.nxt()
                                bank_rel(7, c_sc, v_rbev)
                                P.wait("vector", c_sc, v_rbev)
                                P.wait("vector", d_as[sp], 16 * as_st[sp])
                                buf_acq("stage%d" % sp, "vector")
                                P.op("vector", "tensor_tensor", out=stage[sp][:, :],
                                     in0=ob, in1=rb_sb[:, :], op=OP.mult,
                                     inc=c_ve)
                                v_st = c_ve.nxt()
                                bank_rel(ob_i, c_ve, v_st)
                                buf_rel("rb_sb", c_ve, v_st)
                                P.wait("sync", c_ve, v_st)
                                P.op("sync", "dma_start",
                                     out=AP(attT_d, (4 * qc) * 128 * S + h * 128,
                                            [[S, 128], [128 * S, 4], [1, 128]]),
                                     in_=stage[sp][:, :], inc=d_as[sp], inck=16)
                                as_st[sp] += 1
                                P.wait("tensor", c_ve, v_st)
                                bank_acq(6)
                                P.op("tensor", "matmul", pb[6][0:64, 0:512],
                                     acat["o"][:, h * 64:(h + 1) * 64],
                                     stage[sp][:, :],
                                     start=True, stop=True, inc=c_mm)
                                v_ol = c_mm.nxt()
                                buf_rel("stage%d" % sp, c_mm, v_ol)
                                P.wait("vector", c_mm, v_ol)
                                P.wait("vector", c_ve, rdy[("lowo0", s)])
                                P.op("vector", "tensor_tensor",
                                     out=lowo_f[:, qc * 512:(qc + 1) * 512],
                                     in0=lowo_f[:, qc * 512:(qc + 1) * 512],
                                     in1=pb[6][0:64, 0:512], op=OP.add, inc=c_ve)
                                v_lo = c_ve.nxt()
                                bank_rel(6, c_ve, v_lo)
                            return tail
                        pend.append(make_tail())
                    buf_rel("QT", c_mm, c_mm.v)
                    buf_rel("KT", c_mm, c_mm.v)
                    buf_rel("Vt", c_mm, c_mm.v)
                flush_pend()
                P.wait("vector", c_mm, c_mm.v)
                P.op("vector", "tensor_scalar", out=LOW2[0:64, :],
                     in0=lowo_f[:, :], scalar1=m64col[0:64, :], scalar2=None,
                     op0=OP.mult, inc=c_ve)
                rdy[("lowo_bf", s)] = c_ve.nxt()
                buf_rel("lowo_f", c_ve, rdy[("lowo_bf", s)])
                if "low" in _DBG and s == 0:
                    P.wait("sync", c_ve, rdy[("lowo_bf", s)])
                    P.op("sync", "dma_start", out=dbg["low2"].ap()[:, :],
                         in_=LOW2[:, :], inc=d_pre, inck=16)
                    d_pre.nxt(16)

            esF = contextlib.ExitStack()
            with esF:
                w1sl = [usb(esF, "w1sl%d" % i, [128, D], BF16)
                        for i in range(2)]
                w1f = [usb(esF, "w1f%d" % i, [128, D], F32) for i in range(2)]
                h1b = usb(esF, "h1b", [128, 16 * 1024], BF16)
                h2b = usb(esF, "h2b", [128, 16 * 1024], BF16)
                h2tok = usb(esF, "h2tok", [128, 2048], BF16)
                phase_fence(extra=((d_as[0], 16 * as_st[0]), (d_as[1], 16 * as_st[1])))

                for hf in range(2):
                    tok0 = hf * 1024
                    for ic in range(16):
                        icp = ic % 2
                        P.wait("sync", c_ve, rdy.get(("wcast", icp), 0))
                        P.op("sync", "dma_start", out=w1f[icp][:, :],
                             in_=AP(ins["w1T"], ic * 128 * D,
                                    [[D, 128], [1, D]]),
                             inc=d_w[icp], inck=16)
                        NH_LOADS[icp] += 1
                        P.wait("vector", d_w[icp], 16 * NH_LOADS[icp])
                        if w_free_v[icp] is not None:
                            P.wait("vector", c_mm, w_free_v[icp])
                        P.op("vector", "tensor_copy", out=w1sl[icp][:, :],
                             in_=w1f[icp][:, :], inc=c_ve)
                        rdy[("wcast", icp)] = c_ve.nxt()
                        wv_r = rdy[("wcast", icp)]
                        P.wait("tensor", c_ve, wv_r)
                        bank_acq(6)
                        bank_acq(7)
                        for dc in range(NDC):
                            for qcc in range(2):
                                col0 = tok0 + qcc * 512
                                P.op("tensor", "matmul", pb[6 + qcc],
                                     w1sl[icp][:, dc * 128:(dc + 1) * 128],
                                     xnT[:, dc * S + col0: dc * S + col0 + 512],
                                     start=(dc == 0), stop=(dc == 15),
                                     inc=(c_mm if (dc == 15 and qcc == 1) else None))
                        v_h = c_mm.nxt()
                        for qcc in range(2):
                            P.wait("scalar", c_mm, v_h)
                            P.op("scalar", "activation",
                                 h1b[:, ic * 1024 + qcc * 512: ic * 1024 + qcc * 512 + 512],
                                 pb[6 + qcc], ACT.Silu, inc=c_sc)
                            bank_rel(6 + qcc, c_sc, c_sc.nxt())
                        w_free_v[icp] = c_mm.v
                    v_h1_done = c_sc.v
                    for ic2 in range(16):
                        icp = ic2 % 2
                        P.wait("sync", c_ve, rdy.get(("wcast", icp), 0))
                        P.op("sync", "dma_start", out=w1f[icp][:, :],
                             in_=AP(ins["w3T"], ic2 * 128 * D,
                                    [[D, 128], [1, D]]),
                             inc=d_w[icp], inck=16)
                        NH_LOADS[icp] += 1
                        P.wait("vector", d_w[icp], 16 * NH_LOADS[icp])
                        if w_free_v[icp] is not None:
                            P.wait("vector", c_mm, w_free_v[icp])
                        P.op("vector", "tensor_copy", out=w1sl[icp][:, :],
                             in_=w1f[icp][:, :], inc=c_ve)
                        rdy[("wcast", icp)] = c_ve.nxt()
                        wv_r = rdy[("wcast", icp)]
                        P.wait("tensor", c_ve, wv_r)
                        P.wait("tensor", c_sc, v_h1_done)
                        bank_acq(6)
                        bank_acq(7)
                        for i1 in range(NDC):
                            for qcc in range(2):
                                P.op("tensor", "matmul", pb[6 + qcc],
                                     w1sl[icp][:, i1 * 128:(i1 + 1) * 128],
                                     h1b[:, i1 * 1024 + qcc * 512: i1 * 1024 + qcc * 512 + 512],
                                     start=(i1 == 0), stop=(i1 == 15),
                                     inc=(c_mm if (i1 == 15 and qcc == 1) else None))
                        v_h = c_mm.nxt()
                        for qcc in range(2):
                            P.wait("scalar", c_mm, v_h)
                            P.op("scalar", "activation",
                                 h2b[:, ic2 * 1024 + qcc * 512: ic2 * 1024 + qcc * 512 + 512],
                                 pb[6 + qcc], ACT.Silu, inc=c_sc)
                            bank_rel(6 + qcc, c_sc, c_sc.nxt())
                        w_free_v[icp] = c_mm.v
                    v_h2_done = c_sc.v
                    if "h2" in _DBG and s == 0 and hf == 0:
                        P.wait("sync", c_sc, v_h2_done)
                        P.op("sync", "dma_start", out=dbg["h1"].ap()[:, :],
                             in_=h1b[:, :], inc=d_pre, inck=16)
                        P.op("sync", "dma_start", out=dbg["h2"].ap()[:, :],
                             in_=h2b[:, :], inc=d_pre, inck=16)
                        d_pre.nxt(32)
                    P.wait("tensor", c_sc, v_h2_done)
                    P.wait("tensor", c_ve, rdy[("gates", s)])
                    for tc in range(8):
                        for g in range(4):
                            bi = g % 2
                            bank_acq(bi)
                            buf_acq("h2tok_g%d" % g, "vector")
                            for k in range(4):
                                i2 = 4 * g + k
                                P.op("tensor", "transpose",
                                     pb[bi][:, k * 64:(k + 1) * 64].bitcast(BF16),
                                     h2b[:, i2 * 1024 + tc * 128: i2 * 1024 + tc * 128 + 128],
                                     identb[:, :],
                                     inc=(c_mm if k == 3 else None))
                            v_tr = c_mm.nxt()
                            P.wait("vector", c_mm, v_tr)
                            P.op("vector", "tensor_copy",
                                 out=h2tok[:, g * 512:(g + 1) * 512],
                                 in_=pb[bi][:, 0:256].bitcast(BF16),
                                 inc=c_ve)
                            bank_rel(bi, c_ve, c_ve.nxt())
                        v_ht = c_ve.v
                        P.wait("tensor", c_ve, v_ht)
                        tglob = hf * 8 + tc
                        if hf == 0 and tc == 0:
                            for gq in range(4):
                                bank_acq(2 + gq)
                        for gq in range(4):
                            P.op("tensor", "matmul", pb[2 + gq][0:8, 0:512],
                                 gates[:, tglob * 8:(tglob + 1) * 8],
                                 h2tok[:, gq * 512:(gq + 1) * 512],
                                 start=(hf == 0 and tc == 0),
                                 stop=(hf == 1 and tc == 7),
                                 inc=(c_mm if gq == 3 else None))
                        v_gmm = c_mm.nxt()
                        for g in range(4):
                            buf_rel("h2tok_g%d" % g, c_mm, v_gmm)
                v_gmm_all = c_mm.v
                P.wait("vector", c_mm, v_gmm_all)
                for gq in range(4):
                    if s == 0:
                        P.op("vector", "tensor_copy",
                             out=Gsb[0:8, gq * 512:(gq + 1) * 512],
                             in_=pb[2 + gq][0:8, 0:512],
                             inc=(c_ve if gq == 3 else None))
                    else:
                        P.op("vector", "tensor_tensor",
                             out=Gsb[0:8, gq * 512:(gq + 1) * 512],
                             in0=Gsb[0:8, gq * 512:(gq + 1) * 512],
                             in1=pb[2 + gq][0:8, 0:512], op=OP.add,
                             inc=(c_ve if gq == 3 else None))
                v_gsb = c_ve.nxt()
                rdy[("Gsb", s)] = v_gsb
                for gq in range(4):
                    bank_rel(2 + gq, c_ve, v_gsb)
                buf_rel("xnT", c_mm, c_mm.v)
                buf_rel("gates", c_mm, c_mm.v)
                buf_rel("logit_sb", c_ve, rdy[("gates", s)])

            esC = contextlib.ExitStack()
            with esC:
                wo_sb = usb(esC, "wo_sb", [128, NDC * D], BF16)
                attld = [usb(esC, "attld%d" % i, [128, 2048], BF16)
                         for i in range(2)]
                xsbC = [usb(esC, "xsbC%d" % i, [128, D], F32)
                        for i in range(2)]
                outb = [usb(esC, "outb%d" % i, [128, 512], F32)
                        for i in range(2)]
                wof = [usb(esC, "wof%d" % i, [128, D], F32) for i in range(2)]
                phase_fence()
                wo_cast = [0, 0]
                for dcg in range(16):
                    wp = dcg % 2
                    if wo_cast[wp]:
                        P.wait("sync", c_ve, wo_cast[wp])
                    P.op("sync", "dma_start",
                         out=wof[wp][:, :],
                         in_=AP(ins["woT"], dcg * D,
                                [[16 * D, 128], [1, D]]),
                         inc=d_wo, inck=16)
                    v_ld = d_wo.nxt(16)
                    P.wait("vector", d_wo, v_ld)
                    P.op("vector", "tensor_copy",
                         out=AP(wo_sb, dcg * D, [[NDC * D, 128], [1, D]]),
                         in_=wof[wp][:, :], inc=c_ve)
                    wo_cast[wp] = c_ve.nxt()
                v_wo = c_ve.v
                P.wait("tensor", c_ve, v_wo)
                P.wait("gpsimd", d_as[0], 16 * as_st[0])
                P.wait("gpsimd", d_as[1], 16 * as_st[1])
                for tt in range(NTT):
                    tp_ = tt % 2
                    if at_free_v[tp_] is not None:
                        P.wait("gpsimd", c_mm, at_free_v[tp_])
                    P.op("gpsimd", "dma_start",
                         out=attld[tp_][:, :],
                         in_=AP(attT_d, tt * 128 * S,
                                [[S, 128], [1, S]]),
                         inc=d_at[tp_], inck=16)
                    at_loads[tp_] += 1
                    v_at = 16 * at_loads[tp_]
                    if xC_free_v[tp_] is not None:
                        P.wait("gpsimd", c_ve, xC_free_v[tp_])
                    P.op("gpsimd", "dma_start", out=xsbC[tp_][:, :],
                         in_=x_d.ap()[row0 + tt * 128: row0 + (tt + 1) * 128, :],
                         inc=d_x[tp_], inck=16)
                    x_loads[tp_] += 1
                    v_x = 16 * x_loads[tp_]
                    bgrp_i = [4 * tp_ + i for i in range(4)]
                    P.wait("tensor", d_at[tp_], v_at)
                    P.wait("tensor", c_ve, rdy[("lowo_bf", s)])
                    for bi in bgrp_i:
                        bank_acq(bi)
                    for dc in range(NDC):
                        for oci, bi in enumerate(bgrp_i):
                            P.op("tensor", "matmul", pb[bi][:, 0:512],
                                 attld[tp_][:, dc * 128:(dc + 1) * 128],
                                 wo_sb[:, dc * D + oci * 512: dc * D + oci * 512 + 512],
                                 start=(dc == 0), stop=False)
                    for oci, bi in enumerate(bgrp_i):
                        P.op("tensor", "matmul", pb[bi][:, 0:512],
                             LOW2[0:64, tt * 128:(tt + 1) * 128],
                             bcat2[0:64, oci * 512:(oci + 1) * 512],
                             start=False, stop=True,
                             inc=(c_mm if oci == 3 else None))
                    v_ao = c_mm.nxt()
                    at_free_v[tp_] = v_ao
                    P.wait("vector", c_mm, v_ao)
                    P.wait("vector", d_x[tp_], v_x)
                    for oci, bi in enumerate(bgrp_i):
                        po = oci % 2
                        P.wait("vector", d_out[po], 16 * out_st[po])
                        P.op("vector", "tensor_tensor",
                             out=outb[po][:, :],
                             in0=pb[bi][:, 0:512],
                             in1=xsbC[tp_][:, oci * 512:(oci + 1) * 512],
                             op=OP.add, inc=c_ve)
                        v_epi = c_ve.nxt()
                        bank_rel(bi, c_ve, v_epi)
                        P.wait("sync", c_ve, v_epi)
                        P.op("sync", "dma_start",
                             out=out_d.ap()[row0 + tt * 128: row0 + (tt + 1) * 128,
                                            oci * 512:(oci + 1) * 512],
                             in_=outb[po][:, :], inc=d_out[po], inck=16)
                        out_st[po] += 1
                    v_epi = c_ve.v
                    rdy[("epi", s, tt)] = v_epi
                    xC_free_v[tp_] = v_epi
                buf_rel("LOW1", c_mm, c_mm.v)
                buf_rel("LOW2", c_mm, c_mm.v)

        esS = contextlib.ExitStack()
        with esS:
            GT_sb = usb(esS, "GT_sb", [128, 128], BF16)
            GT_lo = usb(esS, "GT_lo", [128, 128], BF16)
            w2f = [usb(esS, "w2f%d" % i, [128, D], F32) for i in range(2)]
            w2lo = [usb(esS, "w2lo%d" % i, [128, D], BF16) for i in range(2)]
            w2sl = [usb(esS, "w2sl%d" % i, [128, D], BF16)
                    for i in range(2)]
            patch = usb(esS, "patch", [8, D], F32)
            Ssum = usb(esS, "Ssum", [8, D], F32)
            phase_fence(extra=((d_out[0], 16 * out_st[0]),
                               (d_out[1], 16 * out_st[1])))
            P.wait("tensor", c_ve, rdy[("Gsb", 1)])
            for g in range(4):
                bi = 4 + (g % 2)
                bank_acq(bi)
                for k in range(4):
                    ic = 4 * g + k
                    P.op("tensor", "transpose", pb[bi][:, k * 8:(k + 1) * 8],
                         Gsb[0:8, ic * 128:(ic + 1) * 128], identf[0:8, 0:8],
                         inc=(c_mm if k == 3 else None))
                v_tr = c_mm.nxt()
                P.wait("vector", c_mm, v_tr)
                P.op("vector", "tensor_copy",
                     out=GT_sb[:, (4 * g) * 8:(4 * g + 4) * 8],
                     in_=pb[bi][:, 0:32])
                vdrain()
                P.op("vector", "tensor_tensor",
                     out=GT_lo[:, (4 * g) * 8:(4 * g + 4) * 8],
                     in0=pb[bi][:, 0:32],
                     in1=GT_sb[:, (4 * g) * 8:(4 * g + 4) * 8],
                     op=OP.subtract,
                     inc=c_ve)
                bank_rel(bi, c_ve, c_ve.nxt())
            v_gt = c_ve.v
            for ic in range(16):
                icp = ic % 2
                if w_free_v[icp] is not None:
                    P.wait("sync", c_mm, w_free_v[icp])
                P.wait("sync", c_ve, rdy.get(("w2lo", ic - 2), 0))
                P.op("sync", "dma_start", out=w2f[icp][:, :],
                     in_=ins["w2T"].ap()[ic * 128:(ic + 1) * 128, :],
                     inc=d_w[icp], inck=16)
                NH_LOADS[icp] += 1
                wv_r = 16 * NH_LOADS[icp]
                P.wait("vector", d_w[icp], wv_r)
                P.op("vector", "tensor_copy", out=w2sl[icp][:, :],
                     in_=w2f[icp][:, :])
                vdrain()
                P.op("vector", "tensor_tensor", out=w2lo[icp][:, :],
                     in0=w2f[icp][:, :], in1=w2sl[icp][:, :], op=OP.subtract,
                     inc=c_ve)
                rdy[("w2lo", ic)] = c_ve.nxt()
                P.wait("tensor", d_w[icp], wv_r)
                P.wait("tensor", c_ve, rdy[("w2lo", ic)])
                P.wait("tensor", c_ve, v_gt)
                if ic == 0:
                    for oc in range(4):
                        bank_acq(oc)
                for oc in range(4):
                    P.op("tensor", "matmul", pb[oc][0:8, 0:512],
                         GT_sb[:, ic * 8:(ic + 1) * 8],
                         w2sl[icp][:, oc * 512:(oc + 1) * 512],
                         start=(ic == 0), stop=False)
                    P.op("tensor", "matmul", pb[oc][0:8, 0:512],
                         GT_lo[:, ic * 8:(ic + 1) * 8],
                         w2sl[icp][:, oc * 512:(oc + 1) * 512],
                         start=False, stop=False)
                    P.op("tensor", "matmul", pb[oc][0:8, 0:512],
                         GT_sb[:, ic * 8:(ic + 1) * 8],
                         w2lo[icp][:, oc * 512:(oc + 1) * 512],
                         start=False, stop=(ic == 15),
                         inc=(c_mm if oc == 3 else None))
                c_mm.nxt()
                w_free_v[icp] = c_mm.v
            v_smm = c_mm.v
            P.wait("vector", c_mm, v_smm)
            for oc in range(4):
                P.op("vector", "tensor_copy",
                     out=Ssum[0:8, oc * 512:(oc + 1) * 512],
                     in_=pb[oc][0:8, 0:512], inc=(c_ve if oc == 3 else None))
            v_S = c_ve.nxt()
            if "G" in _DBG:
                P.wait("sync", c_ve, v_S)
                P.op("sync", "dma_start", out=dbg["G"].ap()[:, :], in_=Gsb[0:8, :],
                     inc=d_pre, inck=16)
                d_pre.nxt(16)
            P.wait("gpsimd", c_ve, v_S)
            P.op("gpsimd", "dma_start", out=cc_in.ap()[:, :], in_=Ssum[0:8, :],
                 inc=d_cc, inck=16)
            v_cc1 = d_cc.nxt(16)
            P.wait("gpsimd", d_cc, v_cc1)
            P.op("gpsimd", "collective_compute", "AllReduce", OP.add,
                 replica_groups=[list(range(NCORES))],
                 ins=[cc_in.ap().opt()], outs=[cc_out.ap().opt()], inc=c_cc)
            v_ccd = c_cc.nxt()
            P.wait("gpsimd", c_cc, v_ccd)
            P.op("gpsimd", "dma_start", out=Ssum[0:8, :], in_=cc_out.ap()[:, :],
                 inc=d_cc, inck=16)
            v_cc2 = d_cc.nxt(16)
            P.wait("vector", d_cc, v_cc2)
            P.op("vector", "tensor_scalar", out=Ssum[0:8, :], in0=Ssum[0:8, :],
                 scalar1=c0m_sb[:, :], scalar2=None, op0=OP.mult, inc=c_ve)
            v_Sm = c_ve.nxt()
            if "S" in _DBG:
                P.wait("sync", c_ve, v_Sm)
                P.op("sync", "dma_start", out=dbg["S"].ap()[:, :], in_=Ssum[0:8, :],
                     inc=d_pre, inck=16)
                d_pre.nxt(16)
            P.wait("sync", d_out[0], 16 * out_st[0])
            P.wait("sync", d_out[1], 16 * out_st[1])
            P.op("sync", "dma_start", out=patch[:, :], in_=out_d.ap()[0:8, :],
                 inc=d_pt, inck=16)
            v_cc3 = d_pt.nxt(16)
            P.wait("vector", d_pt, v_cc3)
            P.wait("vector", c_ve, v_Sm)
            P.op("vector", "tensor_tensor", out=patch[:, :], in0=patch[:, :],
                 in1=Ssum[0:8, :], op=OP.add, inc=c_ve)
            v_pt = c_ve.nxt()
            P.wait("sync", c_ve, v_pt)
            P.op("sync", "dma_start", out=out_d.ap()[0:8, :], in_=patch[:, :],
                 inc=d_cc, inck=16)
            v_cc4 = d_cc.nxt(16)
            if "snap" in _DBG:
                P.op("sync", "dma_start", out=dbg["snap"].ap()[:, :],
                     in_=snap[:, :], inc=d_pre, inck=16)
                d_pre.nxt(16)
            P.wait("sync", d_cc, v_cc4)
            P.wait("sync", d_pre, d_pre.v)

        block = es.enter_context(nc.Block())
        block.sync(lambda e: P.replay("sync", e))
        block.gpsimd(lambda e: P.replay("gpsimd", e))
        block.vector(lambda e: P.replay("vector", e))
        block.scalar(lambda e: P.replay("scalar", e))
        block.tensor(lambda e: P.replay("tensor", e))
    return nc


_nc_cache = {}


def kernel(**inputs):
    if "nc" not in _nc_cache:
        _nc_cache["nc"] = build_nc()
    nc = _nc_cache["nc"]
    g = _host_prep(inputs)
    x = np.ascontiguousarray(np.asarray(inputs["L0_attn_x"], dtype=np.float32))
    in_maps = []
    for c in range(NCORES):
        m = dict(g)
        m["x"] = np.ascontiguousarray(x[2 * c:2 * c + 2].reshape(NSEQ * S, D))
        m["bval"] = _f32(np.array([[2 * c, 2 * c + 1]]))
        m["c0m"] = _f32(np.full((8, 1), 1.0 if c == 0 else 0.0))
        in_maps.append(m)
    res = run_bass_kernel_spmd(nc, in_maps, core_ids=list(range(NCORES)))
    if _debug_capture is not None:
        _debug_capture.append(res)
    out = np.stack([np.asarray(res.results[c]["out"]).reshape(NSEQ, S, D)
                    for c in range(NCORES)])
    return np.ascontiguousarray(out.reshape(16, S, D).astype(np.float32))


# version 18 (final)
# speedup vs baseline: 1.2588x; 1.0273x over previous
import contextlib
import numpy as np
import ml_dtypes
import concourse.bass as bass
import concourse.mybir as mybir
from concourse.bass_utils import run_bass_kernel_spmd

F32 = mybir.dt.float32
BF16 = mybir.dt.bfloat16
I32 = mybir.dt.int32
OP = mybir.AluOpType
ACT = mybir.ActivationFunctionType
AX = mybir.AxisListType.X

NCORES = 8
NSEQ = 2
S = 2048
D = 2048
H = 16
DKV = 128
NTT = 16
NDC = 16
NQC = 4
SCL = float(1.0 / np.sqrt(np.float32(128.0)))
ROPE_THETA = 10000.0

_DBG = set()
_debug_capture = None


class Sem:
    def __init__(self, h):
        self.h = h
        self.v = 0

    def nxt(self, k=1):
        self.v += k
        return self.v


class Prog:
    ENGS = ("sync", "gpsimd", "vector", "scalar", "tensor")

    def __init__(self):
        self.s = {e: [] for e in self.ENGS}

    def op(self, eng, meth, *a, inc=None, inck=1, **kw):
        if (inc is not None and eng in ("vector", "scalar")
                and meth not in ("nop", "wait_ge", "drain")):
            self.s[eng].append((meth, a, kw, None, 0))
            self.s[eng].append(("drain", (), {}, None, 0))
            self.s[eng].append(("nop", (), {}, inc, inck))
        else:
            self.s[eng].append((meth, a, kw, inc, inck))

    def wait(self, eng, sem, v):
        if v > 0:
            self.s[eng].append(("wait_ge", (sem.h, v), {}, None, 0))

    def replay(self, eng_name, e):
        for meth, a, kw, inc, inck in self.s[eng_name]:
            inst = getattr(e, meth)(*a, **kw)
            if inc is not None:
                inst.then_inc(inc.h, inck)


def _bf(x):
    return np.ascontiguousarray(np.asarray(x).astype(ml_dtypes.bfloat16))


def _f32(x):
    return np.ascontiguousarray(np.asarray(x).astype(np.float32))


def _host_constants():
    c = {}
    c["ident_f"] = _f32(np.eye(128))
    c["ident_b"] = _bf(np.eye(128))
    r = np.arange(128)
    c["tri_b"] = _bf((r[None, :] >= r[:, None]).astype(np.float32))
    c["onescol_b"] = _bf(np.ones((128, 1)))
    c["onescol_f"] = _f32(np.ones((128, 1)))
    c["onesrow_f"] = _f32(np.ones((1, 128)))
    c["iota128col"] = _f32(((np.arange(128) % 64) // 16).reshape(128, 1))
    pos = np.arange(S, dtype=np.float64)
    freq = ROPE_THETA ** (-(np.arange(64, dtype=np.float64)) / 64.0)
    ang = pos[None, :] * freq[:, None]
    cosv = np.cos(ang)
    sinv = np.sin(ang)
    ctab = np.empty((128, S), dtype=np.float64)
    stab = np.empty((128, S), dtype=np.float64)
    ctab[0::2] = cosv
    ctab[1::2] = cosv
    stab[0::2] = sinv
    stab[1::2] = -sinv
    c["ropeC"] = _f32(ctab)
    c["ropeS"] = _f32(stab)
    cv = np.zeros((128, 4))
    cv[:, 0] = 1e-5
    cv[:, 1] = 1e-30
    cv[:, 2] = float(np.log(1.0 / np.sqrt(np.float32(128.0))) + np.log(1.03))
    cv[:, 3] = 1.0
    c["cvec"] = _f32(cv)
    return c


def _host_prep(inputs):
    g = {}
    def _tile_heads(WT):
        return _f32(WT.reshape(16, 128, H, DKV).transpose(2, 1, 0, 3)
                    .reshape(H, 128, D))

    g["wqT"] = _tile_heads(np.asarray(inputs["L0_attn_Wq"]).T)
    g["wkT"] = _tile_heads(np.asarray(inputs["L0_attn_Wk"]).T)
    g["wvT"] = _tile_heads(np.asarray(inputs["L0_attn_Wv"]).T)
    g["woT"] = _f32(np.asarray(inputs["L0_attn_Wo"]).T
                    .reshape(16, 128, D).transpose(1, 0, 2).reshape(128, 16 * D))
    g["w1T"] = _tile_heads(np.asarray(inputs["L0_ffn_w1"]).T)
    g["w3T"] = _tile_heads(np.asarray(inputs["L0_ffn_w3"]).T)
    g["w2T"] = _f32(np.asarray(inputs["L0_ffn_w2"]).T)
    g["wr_sb"] = _f32(np.asarray(inputs["L0_ffn_wr"]).T.reshape(16, 128, 8)
                      .transpose(1, 0, 2).reshape(128, 128))
    for p, W in (("q", "L0_attn_Aq"), ("k", "L0_attn_Ak"),
                 ("v", "L0_attn_Av"), ("o", "L0_attn_Ao")):
        acat = np.asarray(inputs[W]).transpose(2, 0, 1).reshape(D, 64)
        g["acat_" + p] = _bf(acat.reshape(16, 128, 64).transpose(1, 0, 2)
                             .reshape(128, 1024))
    bq = np.asarray(inputs["L0_attn_Bq"]).transpose(0, 2, 1).reshape(64, D)
    bk = np.asarray(inputs["L0_attn_Bk"]).transpose(0, 2, 1).reshape(64, D)
    bv = np.asarray(inputs["L0_attn_Bv"]).transpose(0, 2, 1).reshape(64, D)
    bo = np.asarray(inputs["L0_attn_Bo"]).transpose(0, 2, 1).reshape(64, D)
    g["bcat1"] = _bf(np.concatenate([bq, bk], axis=0))
    g["bcat2"] = _bf(np.concatenate([bo, bv], axis=0))
    g["starts"] = np.ascontiguousarray(
        np.asarray(inputs["L0_attn_lora_start_indices"]).astype(np.int32).reshape(1, 4))
    g.update(_host_constants())
    return g


def build_nc():
    nc = bass.Bass("TRN2", target_bir_lowering=False)
    P = Prog()

    x_d = nc.dram_tensor("x", [NSEQ * S, D], F32, kind="ExternalInput")
    out_d = nc.dram_tensor("out", [NSEQ * S, D], F32, kind="ExternalOutput")
    ins = {}
    for name, shp, dt in [
        ("wqT", [H, 128, D], F32), ("wkT", [H, 128, D], F32),
        ("wvT", [H, 128, D], F32), ("woT", [128, 16 * D], F32),
        ("w1T", [16, 128, D], F32), ("w3T", [16, 128, D], F32),
        ("w2T", [D, D], F32), ("wr_sb", [128, 128], F32),
        ("acat_q", [128, 1024], BF16), ("acat_k", [128, 1024], BF16),
        ("acat_v", [128, 1024], BF16), ("acat_o", [128, 1024], BF16),
        ("bcat1", [128, D], BF16), ("bcat2", [128, D], BF16),
        ("starts", [1, 4], I32),
        ("ident_f", [128, 128], F32), ("ident_b", [128, 128], BF16),
        ("tri_b", [128, 128], BF16),
        ("onescol_b", [128, 1], BF16), ("onescol_f", [128, 1], F32),
        ("onesrow_f", [1, 128], F32), ("iota128col", [128, 1], F32),
        ("ropeC", [128, S], F32), ("ropeS", [128, S], F32),
        ("bval", [1, 2], F32), ("c0m", [8, 1], F32), ("cvec", [128, 4], F32),
    ]:
        ins[name] = nc.dram_tensor(name, shp, dt, kind="ExternalInput")
    attT_d = nc.dram_tensor("attT", [H, 128, S], BF16,
                            kind="ExternalOutput" if "attT" in _DBG else "Internal")
    cc_in = nc.dram_tensor("cc_in", [8, D], F32)
    cc_out = nc.dram_tensor("cc_out", [8, D], F32)
    dbg = {}
    if "xnT" in _DBG:
        dbg["xnT"] = nc.dram_tensor("dbg_xnT", [128, NDC * S], BF16, kind="ExternalOutput")
    if "gates" in _DBG:
        dbg["logits"] = nc.dram_tensor("dbg_logits", [128, 128], F32, kind="ExternalOutput")
        dbg["gates"] = nc.dram_tensor("dbg_gates", [128, 128], BF16, kind="ExternalOutput")
    if "qkv" in _DBG:
        for n in ("qraw", "QT", "KT", "V"):
            dbg[n] = nc.dram_tensor("dbg_" + n, [128, S], BF16, kind="ExternalOutput")
        dbg["negB"] = nc.dram_tensor("dbg_negB", [128, 1], F32, kind="ExternalOutput")
    if "low" in _DBG:
        dbg["low1"] = nc.dram_tensor("dbg_low1", [128, S], BF16, kind="ExternalOutput")
        dbg["low2"] = nc.dram_tensor("dbg_low2", [128, S], BF16, kind="ExternalOutput")
    if "h2" in _DBG:
        dbg["h1"] = nc.dram_tensor("dbg_h1", [128, 16 * 1024], BF16, kind="ExternalOutput")
        dbg["h2"] = nc.dram_tensor("dbg_h2", [128, 16 * 1024], BF16, kind="ExternalOutput")
    if "G" in _DBG:
        dbg["G"] = nc.dram_tensor("dbg_G", [8, D], F32, kind="ExternalOutput")
    if "S" in _DBG:
        dbg["S"] = nc.dram_tensor("dbg_S", [8, D], F32, kind="ExternalOutput")
    if "snap" in _DBG:
        dbg["snap"] = nc.dram_tensor("dbg_snap", [128, 64], F32, kind="ExternalOutput")

    es = contextlib.ExitStack()
    with es:
        sems = {}
        for n in ("d_pre", "d_x0", "d_x1", "d_w0", "d_w1", "d_w2", "d_wo", "d_at0",
                  "d_at1", "d_as0", "d_as1", "d_out0", "d_out1", "d_cc", "d_pt",
                  "c_mm", "c_sc", "c_ve", "c_cc",
                  "f_x0", "f_x1", "f_w0", "f_w1", "f_at0", "f_at1"):
            sems[n] = Sem(es.enter_context(nc.semaphore(n)))
        d_pre = sems["d_pre"]
        d_x = [sems["d_x0"], sems["d_x1"]]
        d_w = [sems["d_w0"], sems["d_w1"], sems["d_w2"]]
        d_wo = sems["d_wo"]
        d_at = [sems["d_at0"], sems["d_at1"]]
        d_as = [sems["d_as0"], sems["d_as1"]]
        d_out = [sems["d_out0"], sems["d_out1"]]
        d_cc = sems["d_cc"]
        d_pt = sems["d_pt"]
        c_mm = sems["c_mm"]
        c_sc = sems["c_sc"]
        c_ve = sems["c_ve"]
        c_cc = sems["c_cc"]
        f_x = [sems["f_x0"], sems["f_x1"]]
        f_w = [sems["f_w0"], sems["f_w1"]]
        f_at = [sems["f_at0"], sems["f_at1"]]

        sb = lambda n, shp, dt: es.enter_context(nc.sbuf_tensor("s_" + n, shp, dt))
        identf = sb("identf", [128, 128], F32)
        identb = sb("identb", [128, 128], BF16)
        tri = sb("tri", [128, 128], BF16)
        onescol_b = sb("onescol_b", [128, 1], BF16)
        onescol_f = sb("onescol_f", [128, 1], F32)
        onesrow_f = sb("onesrow_f", [1, 128], F32)
        iota128 = sb("iota128", [128, 1], F32)
        wr_sb = sb("wr_sb", [128, 128], F32)
        acat = {p: sb("acat_" + p, [128, 1024], BF16) for p in "qkvo"}
        bcat1 = sb("bcat1", [128, D], BF16)
        bcat2 = sb("bcat2", [128, D], BF16)
        starts_sb = sb("starts_sb", [1, 4], I32)
        startsf = sb("startsf", [1, 4], F32)
        bval_sb = sb("bval_sb", [1, 2], F32)
        c0m_sb = sb("c0m_sb", [8, 1], F32)
        m64col = sb("m64col", [128, 1], F32)
        LOW1 = sb("LOW1", [128, S], BF16)
        LOW2 = sb("LOW2", [128, S], BF16)
        gates = sb("gates", [128, 128], BF16)
        logit_sb = sb("logit_sb", [128, 128], F32)
        Gsb = sb("Gsb", [8, D], F32)
        nrm = sb("nrm", [1, 8], F32)
        st1 = sb("st1", [128, 1], F32)
        st2 = sb("st2", [128, 1], F32)
        st3 = sb("st3", [128, 1], F32)
        st4 = sb("st4", [128, 1], F32)
        negB = sb("negB", [128, 1], F32)
        gsc = sb("gsc", [128, 96], F32)
        cst = sb("cst", [128, 4], F32)
        snap = sb("snap", [128, 64], F32) if "snap" in _DBG else None
        xnT = sb("xnT", [128, NDC * S], BF16)

        scA = es.enter_context(nc.psum_tensor("scA", [128, 1024], F32))
        scB = es.enter_context(nc.psum_tensor("scB", [128, 1024], F32))
        pb47 = [es.enter_context(nc.psum_tensor("pb%d" % i, [128, 512], F32))
                for i in range(4, 8)]

        def pbank(i):
            if i < 2:
                return scA[:, i * 512:(i + 1) * 512]
            if i < 4:
                return scB[:, (i - 2) * 512:(i - 1) * 512]
            return pb47[i - 4][:, :]
        pb = [pbank(i) for i in range(8)]
        _uid = [0]

        def usb(stack, n, shp, dt):
            _uid[0] += 1
            return stack.enter_context(
                nc.sbuf_tensor("u%d_%s" % (_uid[0], n), shp, dt))

        bank_state = {}
        buf_state = {}

        def bank_acq(i, eng="tensor"):
            if i in bank_state:
                sem, v = bank_state[i]
                P.wait(eng, sem, v)

        def bank_rel(i, sem, v):
            bank_state[i] = (sem, v)

        def buf_acq(name, eng):
            if name in buf_state:
                sem, v = buf_state[name]
                P.wait(eng, sem, v)

        def buf_rel(name, sem, v):
            buf_state[name] = (sem, v)

        def AP(t, off, dims):
            return bass.AP(t, off, dims)

        def phase_fence(engs=("gpsimd", "sync"), extra=()):
            for e in engs:
                P.wait(e, c_mm, c_mm.v)
                P.wait(e, c_ve, c_ve.v)
                P.wait(e, c_sc, c_sc.v)
                for sem, v in extra:
                    P.wait(e, sem, v)

        def vdrain():
            P.op("vector", "drain")

        def sdrain():
            P.op("scalar", "drain")

        n_pre = 0
        for name, dst in [("ident_f", identf),
                          ("ident_b", identb), ("tri_b", tri),
                          ("onescol_b", onescol_b), ("onescol_f", onescol_f),
                          ("onesrow_f", onesrow_f), ("iota128col", iota128),
                          ("wr_sb", wr_sb), ("bcat1", bcat1), ("bcat2", bcat2),
                          ("starts", starts_sb), ("bval", bval_sb), ("c0m", c0m_sb),
                          ("cvec", cst)]:
            P.op("sync", "dma_start", out=dst[:, :], in_=ins[name].ap(), inc=d_pre, inck=16)
            n_pre += 1
        for p in "qkvo":
            P.op("sync", "dma_start", out=acat[p][:, :], in_=ins["acat_" + p].ap(),
                 inc=d_pre, inck=16)
            n_pre += 1
        pre_v = d_pre.nxt(16 * n_pre)
        for e in ("vector", "scalar", "tensor", "gpsimd"):
            P.wait(e, d_pre, pre_v)

        P.op("vector", "tensor_copy", out=startsf[:, :], in_=starts_sb[:, :], inc=c_ve)
        v_startsf = c_ve.nxt()

        rdy = {}
        NH_LOADS = [0, 0, 0]
        x_loads = [0, 0]
        x_free_v = [None, None]
        xC_free_v = [None, None]
        at_loads = [0, 0]
        at_free_v = [None, None]
        as_st = [0, 0]
        out_st = [0, 0]
        w_free_v = [None, None, None]

        for s in range(NSEQ):
            row0 = s * S
            esA = contextlib.ExitStack()
            with esA:
                xsb = [usb(esA, "xsb%d" % i, [128, D], F32)
                       for i in range(2)]
                dsq = usb(esA, "dsq", [128, D], F32)
                xnt = [usb(esA, "xnt%d" % i, [128, D], F32)
                       for i in range(2)]
                xnf = [usb(esA, "xnf%d" % i, [128, 512], F32)
                       for i in range(2)]
                sumsq = usb(esA, "sumsq", [128, 1], F32)
                sumx = usb(esA, "sumx", [128, 1], F32)
                meanc = usb(esA, "meanc", [128, 1], F32)
                m2c = usb(esA, "m2c", [128, 1], F32)
                varc = usb(esA, "varc", [128, 1], F32)
                rstd = usb(esA, "rstd", [128, 1], F32)

                phase_fence(engs=("sync",),
                            extra=((d_out[0], 16 * out_st[0]),
                                   (d_out[1], 16 * out_st[1])))
                buf_acq("xnT", "vector")
                buf_acq("gates", "vector")
                buf_acq("logit_sb", "scalar")
                xnt_freeat = [0, 0]
                xnf_free_mm = [0, 0]
                xnf_free_ve = [0, 0]

                for tt in range(NTT):
                    p = tt % 2
                    if x_free_v[p] is not None:
                        _xf = {"ve": c_ve, "sc": c_sc}.get(x_free_v[p][0], c_mm)
                        P.wait("sync", _xf, x_free_v[p][1])
                    P.op("sync", "dma_start", out=xsb[p][:, :],
                         in_=x_d.ap()[row0 + tt * 128: row0 + (tt + 1) * 128, :],
                         inc=d_x[p], inck=16)
                    x_loads[p] += 1
                    xv = 16 * x_loads[p]
                    P.wait("vector", d_x[p], xv)
                    P.op("vector", "reduce_sum", sumx[:, :], xsb[p][:, :], axis=AX)
                    P.op("vector", "tensor_tensor", out=dsq[:, :], in0=xsb[p][:, :],
                         in1=xsb[p][:, :], op=OP.mult)
                    P.op("vector", "reduce_sum", sumsq[:, :], dsq[:, :], axis=AX)
                    vdrain()
                    P.op("vector", "tensor_scalar", out=meanc[:, :], in0=sumx[:, :],
                         scalar1=-1.0 / D, scalar2=None, op0=OP.mult)
                    P.op("vector", "tensor_scalar", out=varc[:, :], in0=sumsq[:, :],
                         scalar1=1.0 / D, scalar2=None, op0=OP.mult)
                    vdrain()
                    P.op("vector", "tensor_tensor", out=sumsq[:, :], in0=meanc[:, :],
                         in1=meanc[:, :], op=OP.mult)
                    vdrain()
                    P.op("vector", "tensor_tensor", out=varc[:, :], in0=varc[:, :],
                         in1=sumsq[:, :], op=OP.subtract, inc=c_ve)
                    v_var = c_ve.nxt()
                    P.wait("scalar", c_ve, v_var)
                    P.op("scalar", "activation", st1[:, :], varc[:, :], ACT.Ln,
                         bias=cst[:, 0:1])
                    sdrain()
                    P.op("scalar", "activation", rstd[:, :], st1[:, :], ACT.Exp,
                         scale=-0.5)
                    sdrain()
                    P.op("scalar", "activation", st1[:, :], meanc[:, :], ACT.Copy,
                         scale=rstd[:, :], bias=0.0)
                    sdrain()
                    P.wait("scalar", c_mm, xnt_freeat[p])
                    P.op("scalar", "activation", xnt[p][:, :], xsb[p][:, :],
                         ACT.Identity, scale=rstd[:, :], bias=st1[:, :],
                         inc=c_sc)
                    v_xn = c_sc.nxt()
                    x_free_v[p] = ("sc", v_xn)
                    if "snap" in _DBG and s == 0 and tt == 0:
                        P.op("vector", "tensor_copy", out=snap[:, 48:52],
                             in_=xnt[p][:, 0:4])
                        P.op("vector", "tensor_copy", out=snap[:, 52:53],
                             in_=rstd[:, :])
                        P.op("vector", "tensor_copy", out=snap[:, 53:54],
                             in_=varc[:, :])
                        P.op("vector", "tensor_copy", out=snap[:, 54:55],
                             in_=meanc[:, :])
                        P.op("vector", "tensor_copy", out=snap[:, 55:56],
                             in_=sumx[:, :])
                        P.op("vector", "tensor_copy", out=snap[:, 60:64],
                             in_=xsb[p][:, 0:4])
                    P.wait("tensor", c_sc, v_xn)
                    for g in range(4):
                        gp = g % 2
                        bank_acq(gp)
                        for k in range(4):
                            dc = 4 * g + k
                            P.op("tensor", "transpose", pb[gp][:, k * 128:(k + 1) * 128],
                                 xnt[p][:, dc * 128:(dc + 1) * 128], identf[:, :],
                                 inc=(c_mm if k == 3 else None))
                        v_tr = c_mm.nxt()
                        if g == 3:
                            xnt_freeat[p] = v_tr
                        P.wait("scalar", c_mm, v_tr)
                        P.wait("scalar", c_mm, xnf_free_mm[gp])
                        P.wait("scalar", c_ve, xnf_free_ve[gp])
                        P.op("scalar", "copy", xnf[gp][:, :], pb[gp][:, :], inc=c_sc)
                        v_ev = c_sc.nxt()
                        if "snap" in _DBG and s == 0 and tt == 0 and g == 0:
                            P.op("scalar", "copy", snap[:, 56:60], xnf[gp][:, 0:4])
                        bank_rel(gp, c_sc, v_ev)
                        P.wait("tensor", c_sc, v_ev)
                        if g == 0:
                            bank_acq(2)
                        for k in range(4):
                            dc = 4 * g + k
                            P.op("tensor", "matmul", pb[2][:, 0:8],
                                 xnf[gp][:, k * 128:(k + 1) * 128],
                                 wr_sb[:, dc * 8:(dc + 1) * 8],
                                 start=(dc == 0), stop=(dc == 15),
                                 inc=(c_mm if k == 3 else None))
                        v_rt = c_mm.nxt()
                        xnf_free_mm[gp] = v_rt
                        P.wait("vector", c_sc, v_ev)
                        P.op("vector", "tensor_copy",
                             out=AP(xnT, (4 * g) * S + tt * 128,
                                    [[NDC * S, 128], [S, 4], [1, 128]]),
                             in_=AP(xnf[gp], 0, [[512, 128], [128, 4], [1, 128]]),
                             inc=c_ve)
                        xnf_free_ve[gp] = c_ve.nxt()
                    P.wait("scalar", c_mm, c_mm.v)
                    P.op("scalar", "copy", logit_sb[:, tt * 8:(tt + 1) * 8],
                         pb[2][:, 0:8], inc=c_sc)
                    bank_rel(2, c_sc, c_sc.nxt())


                rdy[("xnT", s)] = c_ve.v
                v_logits = c_sc.v

                m1n = gsc[:, 0:16]
                m2n = gsc[:, 16:32]
                eb = usb(esA, "eb", [128, 128], F32)
                zb2 = usb(esA, "zb2", [128, 128], F32)
                indb = usb(esA, "indb", [128, 128], F32)
                l2b = usb(esA, "l2b", [128, 128], F32)
                rg = gsc[:, 48:64]
                P.wait("vector", c_sc, v_logits)
                for t2 in range(16):
                    P.op("vector", "reduce_max", m1n[:, t2:t2 + 1],
                         logit_sb[:, t2 * 8:(t2 + 1) * 8], axis=AX, negate=True)
                vdrain()
                for t2 in range(16):
                    P.op("vector", "tensor_scalar", out=zb2[:, t2 * 8:(t2 + 1) * 8],
                         in0=logit_sb[:, t2 * 8:(t2 + 1) * 8],
                         scalar1=m1n[:, t2:t2 + 1], scalar2=None, op0=OP.add)
                vdrain()
                P.op("vector", "tensor_single_scalar", indb[:, :], zb2[:, :], 0.0,
                     OP.is_equal)
                vdrain()
                P.op("vector", "scalar_tensor_tensor", out=l2b[:, :], in0=indb[:, :],
                     scalar=-1e30, in1=logit_sb[:, :], op0=OP.mult, op1=OP.add)
                vdrain()
                for t2 in range(16):
                    P.op("vector", "reduce_max", m2n[:, t2:t2 + 1],
                         l2b[:, t2 * 8:(t2 + 1) * 8], axis=AX, negate=True)
                vdrain()
                P.op("vector", "nop", inc=c_ve)
                v_g1 = c_ve.nxt()
                P.wait("scalar", c_ve, v_g1)
                P.op("scalar", "activation", eb[:, :], zb2[:, :], ACT.Exp,
                     inc=c_sc)
                v_g2 = c_sc.nxt()
                P.wait("vector", c_sc, v_g2)
                P.op("vector", "tensor_tensor", out=gsc[:, 64:80], in0=m1n,
                     in1=m2n, op=OP.subtract, inc=c_ve)
                v_g3 = c_ve.nxt()
                P.wait("scalar", c_ve, v_g3)
                P.op("scalar", "activation", gsc[:, 80:96], gsc[:, 64:80], ACT.Exp,
                     inc=c_sc)
                v_g4 = c_sc.nxt()
                P.wait("vector", c_sc, v_g4)
                P.op("vector", "tensor_scalar", out=gsc[:, 64:80],
                     in0=gsc[:, 80:96], scalar1=1.0, scalar2=None, op0=OP.add)
                vdrain()
                P.op("vector", "reciprocal", rg, gsc[:, 64:80])
                for t2 in range(16):
                    P.op("vector", "tensor_scalar", out=l2b[:, t2 * 8:(t2 + 1) * 8],
                         in0=logit_sb[:, t2 * 8:(t2 + 1) * 8],
                         scalar1=m2n[:, t2:t2 + 1], scalar2=None, op0=OP.add)
                vdrain()
                P.op("vector", "tensor_single_scalar", indb[:, :], l2b[:, :], 0.0,
                     OP.is_ge)
                vdrain()
                P.op("vector", "tensor_tensor", out=eb[:, :], in0=eb[:, :],
                     in1=indb[:, :], op=OP.mult)
                vdrain()
                for t2 in range(16):
                    P.op("vector", "tensor_scalar", out=gates[:, t2 * 8:(t2 + 1) * 8],
                         in0=eb[:, t2 * 8:(t2 + 1) * 8],
                         scalar1=rg[:, t2:t2 + 1], scalar2=None, op0=OP.mult,
                         inc=(c_ve if t2 == 15 else None))
                rdy[("gates", s)] = c_ve.nxt()
                if "gates" in _DBG and s == 0:
                    P.wait("sync", c_ve, rdy[("gates", s)])
                    P.op("sync", "dma_start", out=dbg["logits"].ap()[:, :],
                         in_=logit_sb[:, :], inc=d_pre, inck=16)
                    P.op("sync", "dma_start", out=dbg["gates"].ap()[:, :],
                         in_=gates[:, :], inc=d_pre, inck=16)
                    d_pre.nxt(32)
                if "xnT" in _DBG and s == 0:
                    P.wait("sync", c_ve, rdy[("xnT", s)])
                    P.op("sync", "dma_start", out=dbg["xnT"].ap()[:, :],
                         in_=xnT[:, :], inc=d_pre, inck=16)
                    d_pre.nxt(16)

            P.wait("vector", c_ve, v_startsf)
            buf_acq("LOW1", "vector")
            buf_acq("LOW2", "vector")
            P.op("vector", "tensor_scalar", out=gsc[0:1, 0:4],
                 in0=startsf[:, :], scalar1=bval_sb[0:1, s:s + 1], scalar2=None,
                 op0=OP.is_le)
            vdrain()
            P.op("vector", "reduce_sum", gsc[0:1, 4:5], gsc[0:1, 0:4], axis=AX)
            vdrain()
            P.op("vector", "tensor_scalar", out=gsc[0:1, 5:6], in0=gsc[0:1, 4:5],
                 scalar1=-1.0, scalar2=0.0, op0=OP.add, op1=OP.max)
            vdrain()
            P.op("vector", "tensor_scalar", out=gsc[0:1, 6:7], in0=gsc[0:1, 5:6],
                 scalar1=3.0, scalar2=None, op0=OP.min, inc=c_ve)
            v_ids = c_ve.nxt()
            P.wait("tensor", c_ve, v_ids)
            bank_acq(3)
            P.op("tensor", "matmul", pb[3][:, 0:1], onesrow_f[:, :], gsc[0:1, 6:7],
                 start=True, stop=True, inc=c_mm)
            v_idb = c_mm.nxt()
            P.wait("vector", c_mm, v_idb)
            P.op("vector", "tensor_copy", out=st2[:, :], in_=pb[3][:, 0:1], inc=c_ve)
            bank_rel(3, c_ve, c_ve.nxt())
            vdrain()
            P.op("vector", "tensor_scalar", out=m64col[:, :], in0=iota128[:, :],
                 scalar1=st2[:, :], scalar2=None, op0=OP.is_equal, inc=c_ve)
            v_m64 = c_ve.nxt()
            xnT_rdy = rdy[("xnT", s)]
            P.wait("tensor", c_ve, xnT_rdy)
            for pn, dstT, rlo in (("q", LOW1, 0), ("k", LOW1, 64), ("v", LOW2, 64)):
                for qc in range(NQC):
                    bi = 4 + (qc % 2)
                    bank = pb[bi]
                    bank_acq(bi)
                    for dc in range(NDC):
                        tp = (0, 64) if rlo == 64 else None
                        P.op("tensor", "matmul", bank[rlo:rlo + 64, 0:512],
                             acat[pn][:, dc * 64:(dc + 1) * 64],
                             xnT[:, dc * S + qc * 512: dc * S + qc * 512 + 512],
                             start=(dc == 0), stop=(dc == 15),
                             tile_position=tp,
                             inc=(c_mm if dc == 15 else None))
                    v_low = c_mm.nxt()
                    P.wait("vector", c_mm, v_low)
                    P.wait("vector", c_ve, v_m64)
                    P.op("vector", "tensor_scalar",
                         out=dstT[rlo:rlo + 64, qc * 512:(qc + 1) * 512],
                         in0=bank[rlo:rlo + 64, 0:512],
                         scalar1=m64col[rlo:rlo + 64, :], scalar2=None,
                         op0=OP.mult, inc=c_ve)
                    bank_rel(bi, c_ve, c_ve.nxt())
            rdy[("lows", s)] = c_ve.v
            if "low" in _DBG and s == 0:
                P.wait("sync", c_ve, rdy[("lows", s)])
                P.op("sync", "dma_start", out=dbg["low1"].ap()[:, :], in_=LOW1[:, :],
                     inc=d_pre, inck=16)
                d_pre.nxt(16)
            esB = contextlib.ExitStack()
            with esB:
                wsl = {pn: [usb(esB, "wsl_%s%d" % (pn, i), [128, D], BF16)
                    for i in range(2)] for pn in "qkv"}
                ropeC = usb(esB, "ropeC", [128, S], F32)
                ropeS = usb(esB, "ropeS", [128, S], F32)
                qraw = usb(esB, "qraw", [128, S], F32)
                kraw = usb(esB, "kraw", [128, S], F32)
                QT = usb(esB, "QT", [128, S], BF16)
                KT = usb(esB, "KT", [128, S], BF16)
                Vt = usb(esB, "Vt", [128, S], BF16)
                u1 = usb(esB, "u1", [128, S], F32)
                u2 = usb(esB, "u2", [128, S], F32)
                lowo_f = usb(esB, "lowo_f", [64, S], F32)
                stage = [usb(esB, "stage%d" % i, [128, 512], BF16) for i in range(2)]
                pt = [usb(esB, "pt%d" % i, [128, 1024], BF16) for i in range(2)]
                rb_sb = usb(esB, "rb_sb", [128, 512], F32)
                rcp_sb = usb(esB, "rcp_sb", [1, 512], F32)
                sq2 = [usb(esB, "sq2_%d" % i, [128, 512], F32) for i in range(2)]

                P.op("vector", "memset", lowo_f[:, :], 0.0, inc=c_ve)
                rdy[("lowo0", s)] = c_ve.nxt()
                phase_fence()
                P.op("sync", "dma_start", out=ropeC[:, :], in_=ins["ropeC"].ap(),
                     inc=d_pre, inck=16)
                P.op("sync", "dma_start", out=ropeS[:, :], in_=ins["ropeS"].ap(),
                     inc=d_pre, inck=16)
                v_rope_tabs = d_pre.nxt(32)
                P.wait("vector", d_pre, v_rope_tabs)
                wnames = {"q": "wqT", "k": "wkT", "v": "wvT"}
                rope_mask = []
                for i in range(16):
                    rope_mask += [2 * i + 1, 2 * i]

                pend = []
                pend2 = []

                def flush_pend1():
                    while pend:
                        pend.pop(0)()

                def flush_pend2():
                    while pend2:
                        pend2.pop(0)()

                def flush_pend():
                    flush_pend1()
                    flush_pend2()

                for h in range(H):
                    hp = h % 2
                    if w_free_v[hp] is not None:
                        P.wait("gpsimd", c_mm, w_free_v[hp])
                    for pn in "qkv":
                        P.op("gpsimd", "dma_start", out=wsl[pn][hp][:, :],
                             in_=AP(ins[wnames[pn]], h * 128 * D,
                                    [[D, 128], [1, D]]),
                             inc=d_w[hp], inck=16)
                        NH_LOADS[hp] += 1
                    wv_ready = 16 * NH_LOADS[hp]

                    def proj(pn, b0, lorlo, bc, lowbuf):
                        P.wait("tensor", d_w[hp], wv_ready)
                        P.wait("tensor", c_ve, rdy[("lows", s)])
                        for qc in range(NQC):
                            bank_acq(b0 + qc)
                        for dc in range(NDC):
                            for qc in range(NQC):
                                P.op("tensor", "matmul", pb[b0 + qc],
                                     wsl[pn][hp][:, dc * 128:(dc + 1) * 128],
                                     xnT[:, dc * S + qc * 512: dc * S + qc * 512 + 512],
                                     start=(dc == 0), stop=False)
                        tp = (64, 0) if lorlo == 64 else None
                        for qc in range(NQC):
                            P.op("tensor", "matmul", pb[b0 + qc],
                                 bc[lorlo:lorlo + 64, h * 128:(h + 1) * 128],
                                 lowbuf[lorlo:lorlo + 64, qc * 512:(qc + 1) * 512],
                                 start=False, stop=True, tile_position=tp,
                                 inc=(c_mm if qc == 3 else None))
                        return c_mm.nxt()

                    def evac_norms(rawbuf, b0, ncol0, v_p):
                        for qc in range(NQC):
                            P.wait("scalar", c_mm, v_p)
                            buf_acq(rawbuf, "scalar")
                            P.op("scalar", "copy",
                                 locals_raw[rawbuf][:, qc * 512:(qc + 1) * 512],
                                 pb[b0 + qc])
                            buf_acq("sq%d" % (qc % 2), "scalar")
                            P.op("scalar", "activation", sq2[qc % 2][:, :],
                                 pb[b0 + qc], ACT.Square, inc=c_sc)
                            v_ev = c_sc.nxt()
                            P.wait("tensor", c_sc, v_ev)
                            P.op("tensor", "matmul", pb[b0 + qc][0:1, 0:512],
                                 onescol_f[:, :], sq2[qc % 2][:, :], start=True,
                                 stop=True, inc=c_mm)
                            v_nm = c_mm.nxt()
                            buf_rel("sq%d" % (qc % 2), c_mm, v_nm)
                            P.wait("vector", c_mm, v_nm)
                            P.op("vector", "reduce_max",
                                 nrm[0:1, ncol0 + qc:ncol0 + qc + 1],
                                 pb[b0 + qc][0:1, 0:512], axis=AX, inc=c_ve)
                            bank_rel(b0 + qc, c_ve, c_ve.nxt())
                        return c_sc.v

                    locals_raw = {"qraw": qraw, "kraw": kraw}

                    v_q = proj("q", 0, 0, bcat1, LOW1)
                    flush_pend()
                    v_qev = evac_norms("qraw", 0, 0, v_q)
                    v_k = proj("k", 4, 64, bcat1, LOW1)
                    v_kev = evac_norms("kraw", 4, 4, v_k)

                    v_rope = {}
                    for rawbuf, dstT, key, v_prod in ((qraw, QT, "q", v_qev),
                                                      (kraw, KT, "k", v_kev)):
                        P.wait("vector", c_sc, v_prod)
                        buf_acq("QT" if key == "q" else "KT", "vector")
                        P.op("vector", "tensor_tensor", out=u1[:, :], in0=rawbuf[:, :],
                             in1=ropeS[:, :], op=OP.mult)
                        P.op("vector", "stream_shuffle", out=u2[:, :], in_=u1[:, :],
                             mask=rope_mask)
                        P.op("vector", "tensor_tensor", out=u1[:, :], in0=rawbuf[:, :],
                             in1=ropeC[:, :], op=OP.mult)
                        P.op("vector", "tensor_tensor", out=dstT[:, :], in0=u2[:, :],
                             in1=u1[:, :], op=OP.add, inc=c_ve)
                        v_rope[key] = c_ve.nxt()
                    buf_rel("qraw", c_ve, v_rope["q"])
                    buf_rel("kraw", c_ve, v_rope["k"])

                    vdrain()
                    P.op("vector", "reduce_max", st1[0:1, 0:1], nrm[0:1, 0:4], axis=AX)
                    P.op("vector", "reduce_max", st2[0:1, 0:1], nrm[0:1, 4:8],
                         axis=AX, inc=c_ve)
                    v_nmx = c_ve.nxt()
                    P.wait("scalar", c_ve, v_nmx)
                    P.op("scalar", "activation", st3[0:1, 0:1], st1[0:1, 0:1],
                         ACT.Ln, bias=cst[0:1, 1:2])
                    P.op("scalar", "activation", st4[0:1, 0:1], st2[0:1, 0:1],
                         ACT.Ln, bias=cst[0:1, 1:2], inc=c_sc)
                    v_ln = c_sc.nxt()
                    P.wait("vector", c_sc, v_ln)
                    P.op("vector", "tensor_tensor", out=st3[0:1, 0:1],
                         in0=st3[0:1, 0:1], in1=st4[0:1, 0:1], op=OP.add, inc=c_ve)
                    v_add2 = c_ve.nxt()
                    P.wait("scalar", c_ve, v_add2)
                    sdrain()
                    P.op("scalar", "activation", st4[0:1, 0:1], st3[0:1, 0:1],
                         ACT.Exp, scale=0.5, bias=cst[0:1, 2:3], inc=c_sc)
                    v_B = c_sc.nxt()
                    P.wait("vector", c_sc, v_B)
                    vdrain()
                    P.op("vector", "tensor_scalar", out=st3[0:1, 0:1],
                         in0=st4[0:1, 0:1], scalar1=-1.0, scalar2=None, op0=OP.mult,
                         inc=c_ve)
                    v_nB = c_ve.nxt()

                    v_v = proj("v", 0, 64, bcat2, LOW2)
                    buf_acq("qraw", "scalar")
                    for qc in range(NQC):
                        P.wait("scalar", c_mm, v_v)
                        P.op("scalar", "copy", qraw[:, qc * 512:(qc + 1) * 512],
                             pb[qc], inc=c_sc)
                        bank_rel(qc, c_sc, c_sc.nxt())
                    v_vev = c_sc.v
                    buf_acq("Vt", "vector")
                    for g in range(4):
                        bi = 4 + (g % 2)
                        P.wait("tensor", c_sc, v_vev)
                        bank_acq(bi)
                        for k in range(4):
                            tc = 4 * g + k
                            P.op("tensor", "transpose",
                                 pb[bi][:, k * 128:(k + 1) * 128],
                                 qraw[:, tc * 128:(tc + 1) * 128], identf[:, :],
                                 inc=(c_mm if k == 3 else None))
                        v_tr = c_mm.nxt()
                        P.wait("vector", c_mm, v_tr)
                        P.op("vector", "tensor_copy",
                             out=Vt[:, g * 512:(g + 1) * 512],
                             in_=pb[bi][:, 0:512],
                             inc=c_ve)
                        bank_rel(bi, c_ve, c_ve.nxt())
                    v_vt = c_ve.v
                    buf_rel("qraw", c_mm, c_mm.v)
                    w_free_v[hp] = c_mm.v

                    P.wait("tensor", c_ve, v_nB)
                    bank_acq(6)
                    P.op("tensor", "matmul", pb[6][:, 0:1], onesrow_f[:, :],
                         st3[0:1, 0:1], start=True, stop=True, inc=c_mm)
                    v_nBb = c_mm.nxt()
                    P.wait("scalar", c_mm, v_nBb)
                    P.op("scalar", "copy", negB[:, :], pb[6][:, 0:1], inc=c_sc)
                    v_negB = c_sc.nxt()
                    bank_rel(6, c_sc, v_negB)

                    for qc in range(NQC):
                        jmax = 4 * qc + 4
                        ob_i = 4 + (qc % 2)
                        ob = pb[ob_i]
                        dbk = pb[6]
                        npair = jmax // 2

                        def emit_mm1(p_):
                            j0, j1 = 2 * p_, 2 * p_ + 1
                            ti = p_ % 2
                            T = scA if ti == 0 else scB
                            bank_acq(2 * ti)
                            bank_acq(2 * ti + 1)
                            P.wait("tensor", c_ve, v_rope["k"])
                            P.op("tensor", "matmul", T[:, 0:512],
                                 KT[:, j0 * 128:(j0 + 1) * 128],
                                 QT[:, qc * 512:(qc + 1) * 512],
                                 start=True, stop=True, inc=c_mm)
                            v_mm1a = c_mm.nxt()
                            P.op("tensor", "matmul", T[:, 512:1024],
                                 KT[:, j1 * 128:(j1 + 1) * 128],
                                 QT[:, qc * 512:(qc + 1) * 512],
                                 start=True, stop=True, inc=c_mm)
                            v_mm1b = c_mm.nxt()
                            P.wait("scalar", c_sc, v_negB)
                            buf_acq("pt%d" % ti, "scalar")
                            ve_h = []
                            vm_h = []
                            for hi, v_m in ((0, v_mm1a), (1, v_mm1b)):
                                half = hi * 512
                                P.wait("scalar", c_mm, v_m)
                                P.op("scalar", "activation",
                                     pt[ti][:, half:half + 512],
                                     T[:, half:half + 512], ACT.Exp,
                                     bias=negB[:, :], scale=SCL, inc=c_sc)
                                v_exp = c_sc.nxt()
                                bank_rel(2 * ti + hi, c_sc, v_exp)
                                ve_h.append(v_exp)
                                j = (j0, j1)[hi]
                                v_msk = 0
                                if j >= 4 * qc:
                                    m = j - 4 * qc
                                    P.wait("vector", c_sc, v_exp)
                                    if m > 0:
                                        P.op("vector", "memset",
                                             pt[ti][:, half:half + m * 128], 0.0)
                                    P.op("vector", "tensor_tensor",
                                         out=pt[ti][:, half + m * 128:half + (m + 1) * 128],
                                         in0=pt[ti][:, half + m * 128:half + (m + 1) * 128],
                                         in1=tri[:, :], op=OP.mult, inc=c_ve)
                                    v_msk = c_ve.nxt()
                                vm_h.append(v_msk)
                            return ve_h, vm_h

                        def emit_mm23(p_, ve_h, vm_h):
                            j0, j1 = 2 * p_, 2 * p_ + 1
                            ti = p_ % 2
                            P.wait("tensor", c_ve, v_vt)
                            for hi, j in ((0, j0), (1, j1)):
                                half = hi * 512
                                P.wait("tensor", c_sc, ve_h[hi])
                                if vm_h[hi]:
                                    P.wait("tensor", c_ve, vm_h[hi])
                                P.op("tensor", "matmul", ob,
                                     Vt[:, j * 128:(j + 1) * 128],
                                     pt[ti][:, half:half + 512],
                                     start=(j == 0), stop=(j == jmax - 1))
                                P.op("tensor", "matmul", dbk[0:1, 0:512],
                                     onescol_b[:, :], pt[ti][:, half:half + 512],
                                     start=(j == 0), stop=(j == jmax - 1),
                                     inc=(c_mm if hi == 1 else None))
                            buf_rel("pt%d" % ti, c_mm, c_mm.nxt())

                        prev = None
                        for p_ in range(npair):
                            e_m = emit_mm1(p_)
                            if p_ == 0:
                                flush_pend1()
                                bank_acq(ob_i)
                                bank_acq(6)
                            if p_ == 1:
                                flush_pend2()
                            if prev is not None:
                                emit_mm23(p_ - 1, *prev)
                            prev = e_m
                        emit_mm23(npair - 1, *prev)
                        v_pv = c_mm.v
                        sp = qc % 2
                        P.wait("vector", c_mm, v_pv)
                        buf_acq("rcp_sb", "vector")
                        P.op("vector", "reciprocal", rcp_sb[:, :],
                             dbk[0:1, 0:512], inc=c_ve)
                        v_rc = c_ve.nxt()
                        bank_rel(6, c_ve, v_rc)

                        def make_tail(qc=qc, sp=sp, ob=ob, ob_i=ob_i, v_rc=v_rc,
                                      h=h, s=s):
                            cell = {}

                            def tail1():
                                P.wait("tensor", c_ve, v_rc)
                                bank_acq(7)
                                P.op("tensor", "matmul", pb[7], onesrow_f[:, :],
                                     rcp_sb[:, :], start=True, stop=True, inc=c_mm)
                                v_rbmm = c_mm.nxt()
                                buf_rel("rcp_sb", c_mm, v_rbmm)
                                P.wait("scalar", c_mm, v_rbmm)
                                buf_acq("rb_sb", "scalar")
                                P.op("scalar", "copy", rb_sb[:, :], pb[7], inc=c_sc)
                                v_rbev = c_sc.nxt()
                                bank_rel(7, c_sc, v_rbev)
                                P.wait("vector", c_sc, v_rbev)
                                P.wait("vector", d_as[sp], 16 * as_st[sp])
                                buf_acq("stage%d" % sp, "vector")
                                P.op("vector", "tensor_tensor", out=stage[sp][:, :],
                                     in0=ob, in1=rb_sb[:, :], op=OP.mult,
                                     inc=c_ve)
                                v_st = c_ve.nxt()
                                cell["v_st"] = v_st
                                bank_rel(ob_i, c_ve, v_st)
                                buf_rel("rb_sb", c_ve, v_st)
                                P.wait("sync", c_ve, v_st)
                                P.op("sync", "dma_start",
                                     out=AP(attT_d, (4 * qc) * 128 * S + h * 128,
                                            [[S, 128], [128 * S, 4], [1, 128]]),
                                     in_=stage[sp][:, :], inc=d_as[sp], inck=16)
                                as_st[sp] += 1

                            def tail2():
                                v_st = cell["v_st"]
                                P.wait("tensor", c_ve, v_st)
                                bank_acq(7)
                                P.op("tensor", "matmul", pb[7][0:64, 0:512],
                                     acat["o"][:, h * 64:(h + 1) * 64],
                                     stage[sp][:, :],
                                     start=True, stop=True, inc=c_mm)
                                v_ol = c_mm.nxt()
                                buf_rel("stage%d" % sp, c_mm, v_ol)
                                P.wait("vector", c_mm, v_ol)
                                P.wait("vector", c_ve, rdy[("lowo0", s)])
                                P.op("vector", "tensor_tensor",
                                     out=lowo_f[:, qc * 512:(qc + 1) * 512],
                                     in0=lowo_f[:, qc * 512:(qc + 1) * 512],
                                     in1=pb[7][0:64, 0:512], op=OP.add, inc=c_ve)
                                v_lo = c_ve.nxt()
                                bank_rel(7, c_ve, v_lo)
                            return tail1, tail2
                        t1_, t2_ = make_tail()
                        pend.append(t1_)
                        pend2.append(t2_)
                    buf_rel("QT", c_mm, c_mm.v)
                    buf_rel("KT", c_mm, c_mm.v)
                    buf_rel("Vt", c_mm, c_mm.v)
                flush_pend()
                P.wait("vector", c_mm, c_mm.v)
                P.op("vector", "tensor_scalar", out=LOW2[0:64, :],
                     in0=lowo_f[:, :], scalar1=m64col[0:64, :], scalar2=None,
                     op0=OP.mult, inc=c_ve)
                rdy[("lowo_bf", s)] = c_ve.nxt()
                buf_rel("lowo_f", c_ve, rdy[("lowo_bf", s)])
                if "low" in _DBG and s == 0:
                    P.wait("sync", c_ve, rdy[("lowo_bf", s)])
                    P.op("sync", "dma_start", out=dbg["low2"].ap()[:, :],
                         in_=LOW2[:, :], inc=d_pre, inck=16)
                    d_pre.nxt(16)

            esF = contextlib.ExitStack()
            with esF:
                w1sl = [usb(esF, "w1sl%d" % i, [128, D], BF16)
                        for i in range(2)]
                w1f = [usb(esF, "w1f%d" % i, [128, D], F32) for i in range(2)]
                h1b = usb(esF, "h1b", [128, 16 * 1024], BF16)
                h2b = usb(esF, "h2b", [128, 16 * 1024], BF16)
                h2tok = usb(esF, "h2tok", [128, 2048], BF16)
                phase_fence(extra=((d_as[0], 16 * as_st[0]), (d_as[1], 16 * as_st[1])))

                for hf in range(2):
                    tok0 = hf * 1024
                    for ic in range(16):
                        icp = ic % 2
                        P.wait("sync", c_ve, rdy.get(("wcast", icp), 0))
                        P.op("sync", "dma_start", out=w1f[icp][:, :],
                             in_=AP(ins["w1T"], ic * 128 * D,
                                    [[D, 128], [1, D]]),
                             inc=d_w[icp], inck=16)
                        NH_LOADS[icp] += 1
                        P.wait("vector", d_w[icp], 16 * NH_LOADS[icp])
                        if w_free_v[icp] is not None:
                            P.wait("vector", c_mm, w_free_v[icp])
                        P.op("vector", "tensor_copy", out=w1sl[icp][:, :],
                             in_=w1f[icp][:, :], inc=c_ve)
                        rdy[("wcast", icp)] = c_ve.nxt()
                        wv_r = rdy[("wcast", icp)]
                        P.wait("tensor", c_ve, wv_r)
                        bank_acq(6)
                        bank_acq(7)
                        for dc in range(NDC):
                            for qcc in range(2):
                                col0 = tok0 + qcc * 512
                                P.op("tensor", "matmul", pb[6 + qcc],
                                     w1sl[icp][:, dc * 128:(dc + 1) * 128],
                                     xnT[:, dc * S + col0: dc * S + col0 + 512],
                                     start=(dc == 0), stop=(dc == 15),
                                     inc=(c_mm if (dc == 15 and qcc == 1) else None))
                        v_h = c_mm.nxt()
                        for qcc in range(2):
                            P.wait("scalar", c_mm, v_h)
                            P.op("scalar", "activation",
                                 h1b[:, ic * 1024 + qcc * 512: ic * 1024 + qcc * 512 + 512],
                                 pb[6 + qcc], ACT.Silu, inc=c_sc)
                            bank_rel(6 + qcc, c_sc, c_sc.nxt())
                        w_free_v[icp] = c_mm.v
                    v_h1_done = c_sc.v
                    for ic2 in range(16):
                        icp = ic2 % 2
                        P.wait("sync", c_ve, rdy.get(("wcast", icp), 0))
                        P.op("sync", "dma_start", out=w1f[icp][:, :],
                             in_=AP(ins["w3T"], ic2 * 128 * D,
                                    [[D, 128], [1, D]]),
                             inc=d_w[icp], inck=16)
                        NH_LOADS[icp] += 1
                        P.wait("vector", d_w[icp], 16 * NH_LOADS[icp])
                        if w_free_v[icp] is not None:
                            P.wait("vector", c_mm, w_free_v[icp])
                        P.op("vector", "tensor_copy", out=w1sl[icp][:, :],
                             in_=w1f[icp][:, :], inc=c_ve)
                        rdy[("wcast", icp)] = c_ve.nxt()
                        wv_r = rdy[("wcast", icp)]
                        P.wait("tensor", c_ve, wv_r)
                        P.wait("tensor", c_sc, v_h1_done)
                        bank_acq(6)
                        bank_acq(7)
                        for i1 in range(NDC):
                            for qcc in range(2):
                                P.op("tensor", "matmul", pb[6 + qcc],
                                     w1sl[icp][:, i1 * 128:(i1 + 1) * 128],
                                     h1b[:, i1 * 1024 + qcc * 512: i1 * 1024 + qcc * 512 + 512],
                                     start=(i1 == 0), stop=(i1 == 15),
                                     inc=(c_mm if (i1 == 15 and qcc == 1) else None))
                        v_h = c_mm.nxt()
                        for qcc in range(2):
                            P.wait("scalar", c_mm, v_h)
                            P.op("scalar", "activation",
                                 h2b[:, ic2 * 1024 + qcc * 512: ic2 * 1024 + qcc * 512 + 512],
                                 pb[6 + qcc], ACT.Silu, inc=c_sc)
                            bank_rel(6 + qcc, c_sc, c_sc.nxt())
                        w_free_v[icp] = c_mm.v
                    v_h2_done = c_sc.v
                    if "h2" in _DBG and s == 0 and hf == 0:
                        P.wait("sync", c_sc, v_h2_done)
                        P.op("sync", "dma_start", out=dbg["h1"].ap()[:, :],
                             in_=h1b[:, :], inc=d_pre, inck=16)
                        P.op("sync", "dma_start", out=dbg["h2"].ap()[:, :],
                             in_=h2b[:, :], inc=d_pre, inck=16)
                        d_pre.nxt(32)
                    P.wait("tensor", c_sc, v_h2_done)
                    P.wait("tensor", c_ve, rdy[("gates", s)])
                    for tc in range(8):
                        for g in range(4):
                            bi = g % 2
                            bank_acq(bi)
                            buf_acq("h2tok_g%d" % g, "vector")
                            for k in range(4):
                                i2 = 4 * g + k
                                P.op("tensor", "transpose",
                                     pb[bi][:, k * 64:(k + 1) * 64].bitcast(BF16),
                                     h2b[:, i2 * 1024 + tc * 128: i2 * 1024 + tc * 128 + 128],
                                     identb[:, :],
                                     inc=(c_mm if k == 3 else None))
                            v_tr = c_mm.nxt()
                            P.wait("vector", c_mm, v_tr)
                            P.op("vector", "tensor_copy",
                                 out=h2tok[:, g * 512:(g + 1) * 512],
                                 in_=pb[bi][:, 0:256].bitcast(BF16),
                                 inc=c_ve)
                            bank_rel(bi, c_ve, c_ve.nxt())
                        v_ht = c_ve.v
                        P.wait("tensor", c_ve, v_ht)
                        tglob = hf * 8 + tc
                        if hf == 0 and tc == 0:
                            for gq in range(4):
                                bank_acq(2 + gq)
                        for gq in range(4):
                            P.op("tensor", "matmul", pb[2 + gq][0:8, 0:512],
                                 gates[:, tglob * 8:(tglob + 1) * 8],
                                 h2tok[:, gq * 512:(gq + 1) * 512],
                                 start=(hf == 0 and tc == 0),
                                 stop=(hf == 1 and tc == 7),
                                 inc=(c_mm if gq == 3 else None))
                        v_gmm = c_mm.nxt()
                        for g in range(4):
                            buf_rel("h2tok_g%d" % g, c_mm, v_gmm)
                v_gmm_all = c_mm.v
                P.wait("vector", c_mm, v_gmm_all)
                for gq in range(4):
                    if s == 0:
                        P.op("vector", "tensor_copy",
                             out=Gsb[0:8, gq * 512:(gq + 1) * 512],
                             in_=pb[2 + gq][0:8, 0:512],
                             inc=(c_ve if gq == 3 else None))
                    else:
                        P.op("vector", "tensor_tensor",
                             out=Gsb[0:8, gq * 512:(gq + 1) * 512],
                             in0=Gsb[0:8, gq * 512:(gq + 1) * 512],
                             in1=pb[2 + gq][0:8, 0:512], op=OP.add,
                             inc=(c_ve if gq == 3 else None))
                v_gsb = c_ve.nxt()
                rdy[("Gsb", s)] = v_gsb
                for gq in range(4):
                    bank_rel(2 + gq, c_ve, v_gsb)
                buf_rel("xnT", c_mm, c_mm.v)
                buf_rel("gates", c_mm, c_mm.v)
                buf_rel("logit_sb", c_ve, rdy[("gates", s)])

            esC = contextlib.ExitStack()
            with esC:
                wo_sb = usb(esC, "wo_sb", [128, NDC * D], BF16)
                attld = [usb(esC, "attld%d" % i, [128, 2048], BF16)
                         for i in range(2)]
                xsbC = [usb(esC, "xsbC%d" % i, [128, D], F32)
                        for i in range(2)]
                outb = [usb(esC, "outb%d" % i, [128, 512], F32)
                        for i in range(2)]
                wof = [usb(esC, "wof%d" % i, [128, D], F32) for i in range(2)]
                phase_fence()
                wo_cast = [0, 0]
                for dcg in range(16):
                    wp = dcg % 2
                    if wo_cast[wp]:
                        P.wait("sync", c_ve, wo_cast[wp])
                    P.op("sync", "dma_start",
                         out=wof[wp][:, :],
                         in_=AP(ins["woT"], dcg * D,
                                [[16 * D, 128], [1, D]]),
                         inc=d_wo, inck=16)
                    v_ld = d_wo.nxt(16)
                    P.wait("vector", d_wo, v_ld)
                    P.op("vector", "tensor_copy",
                         out=AP(wo_sb, dcg * D, [[NDC * D, 128], [1, D]]),
                         in_=wof[wp][:, :], inc=c_ve)
                    wo_cast[wp] = c_ve.nxt()
                v_wo = c_ve.v
                P.wait("tensor", c_ve, v_wo)
                P.wait("gpsimd", d_as[0], 16 * as_st[0])
                P.wait("gpsimd", d_as[1], 16 * as_st[1])
                for tt in range(NTT):
                    tp_ = tt % 2
                    if at_free_v[tp_] is not None:
                        P.wait("gpsimd", c_mm, at_free_v[tp_])
                    P.op("gpsimd", "dma_start",
                         out=attld[tp_][:, :],
                         in_=AP(attT_d, tt * 128 * S,
                                [[S, 128], [1, S]]),
                         inc=d_at[tp_], inck=16)
                    at_loads[tp_] += 1
                    v_at = 16 * at_loads[tp_]
                    if xC_free_v[tp_] is not None:
                        P.wait("gpsimd", c_ve, xC_free_v[tp_])
                    P.op("gpsimd", "dma_start", out=xsbC[tp_][:, :],
                         in_=x_d.ap()[row0 + tt * 128: row0 + (tt + 1) * 128, :],
                         inc=d_x[tp_], inck=16)
                    x_loads[tp_] += 1
                    v_x = 16 * x_loads[tp_]
                    bgrp_i = [4 * tp_ + i for i in range(4)]
                    P.wait("tensor", d_at[tp_], v_at)
                    P.wait("tensor", c_ve, rdy[("lowo_bf", s)])
                    for bi in bgrp_i:
                        bank_acq(bi)
                    for dc in range(NDC):
                        for oci, bi in enumerate(bgrp_i):
                            P.op("tensor", "matmul", pb[bi][:, 0:512],
                                 attld[tp_][:, dc * 128:(dc + 1) * 128],
                                 wo_sb[:, dc * D + oci * 512: dc * D + oci * 512 + 512],
                                 start=(dc == 0), stop=False)
                    for oci, bi in enumerate(bgrp_i):
                        P.op("tensor", "matmul", pb[bi][:, 0:512],
                             LOW2[0:64, tt * 128:(tt + 1) * 128],
                             bcat2[0:64, oci * 512:(oci + 1) * 512],
                             start=False, stop=True,
                             inc=(c_mm if oci == 3 else None))
                    v_ao = c_mm.nxt()
                    at_free_v[tp_] = v_ao
                    P.wait("vector", c_mm, v_ao)
                    P.wait("vector", d_x[tp_], v_x)
                    for oci, bi in enumerate(bgrp_i):
                        po = oci % 2
                        P.wait("vector", d_out[po], 16 * out_st[po])
                        P.op("vector", "tensor_tensor",
                             out=outb[po][:, :],
                             in0=pb[bi][:, 0:512],
                             in1=xsbC[tp_][:, oci * 512:(oci + 1) * 512],
                             op=OP.add, inc=c_ve)
                        v_epi = c_ve.nxt()
                        bank_rel(bi, c_ve, v_epi)
                        P.wait("sync", c_ve, v_epi)
                        P.op("sync", "dma_start",
                             out=out_d.ap()[row0 + tt * 128: row0 + (tt + 1) * 128,
                                            oci * 512:(oci + 1) * 512],
                             in_=outb[po][:, :], inc=d_out[po], inck=16)
                        out_st[po] += 1
                    v_epi = c_ve.v
                    rdy[("epi", s, tt)] = v_epi
                    xC_free_v[tp_] = v_epi
                buf_rel("LOW1", c_mm, c_mm.v)
                buf_rel("LOW2", c_mm, c_mm.v)

        esS = contextlib.ExitStack()
        with esS:
            GT_sb = usb(esS, "GT_sb", [128, 128], BF16)
            GT_lo = usb(esS, "GT_lo", [128, 128], BF16)
            w2f = [usb(esS, "w2f%d" % i, [128, D], F32) for i in range(2)]
            w2lo = [usb(esS, "w2lo%d" % i, [128, D], BF16) for i in range(2)]
            w2sl = [usb(esS, "w2sl%d" % i, [128, D], BF16)
                    for i in range(2)]
            patch = usb(esS, "patch", [8, D], F32)
            Ssum = usb(esS, "Ssum", [8, D], F32)
            phase_fence(extra=((d_out[0], 16 * out_st[0]),
                               (d_out[1], 16 * out_st[1])))
            P.wait("tensor", c_ve, rdy[("Gsb", 1)])
            for g in range(4):
                bi = 4 + (g % 2)
                bank_acq(bi)
                for k in range(4):
                    ic = 4 * g + k
                    P.op("tensor", "transpose", pb[bi][:, k * 8:(k + 1) * 8],
                         Gsb[0:8, ic * 128:(ic + 1) * 128], identf[0:8, 0:8],
                         inc=(c_mm if k == 3 else None))
                v_tr = c_mm.nxt()
                P.wait("vector", c_mm, v_tr)
                P.op("vector", "tensor_copy",
                     out=GT_sb[:, (4 * g) * 8:(4 * g + 4) * 8],
                     in_=pb[bi][:, 0:32])
                vdrain()
                P.op("vector", "tensor_tensor",
                     out=GT_lo[:, (4 * g) * 8:(4 * g + 4) * 8],
                     in0=pb[bi][:, 0:32],
                     in1=GT_sb[:, (4 * g) * 8:(4 * g + 4) * 8],
                     op=OP.subtract,
                     inc=c_ve)
                bank_rel(bi, c_ve, c_ve.nxt())
            v_gt = c_ve.v
            for ic in range(16):
                icp = ic % 2
                if w_free_v[icp] is not None:
                    P.wait("sync", c_mm, w_free_v[icp])
                P.wait("sync", c_ve, rdy.get(("w2lo", ic - 2), 0))
                P.op("sync", "dma_start", out=w2f[icp][:, :],
                     in_=ins["w2T"].ap()[ic * 128:(ic + 1) * 128, :],
                     inc=d_w[icp], inck=16)
                NH_LOADS[icp] += 1
                wv_r = 16 * NH_LOADS[icp]
                P.wait("vector", d_w[icp], wv_r)
                P.op("vector", "tensor_copy", out=w2sl[icp][:, :],
                     in_=w2f[icp][:, :])
                vdrain()
                P.op("vector", "tensor_tensor", out=w2lo[icp][:, :],
                     in0=w2f[icp][:, :], in1=w2sl[icp][:, :], op=OP.subtract,
                     inc=c_ve)
                rdy[("w2lo", ic)] = c_ve.nxt()
                P.wait("tensor", d_w[icp], wv_r)
                P.wait("tensor", c_ve, rdy[("w2lo", ic)])
                P.wait("tensor", c_ve, v_gt)
                if ic == 0:
                    for oc in range(4):
                        bank_acq(oc)
                for oc in range(4):
                    P.op("tensor", "matmul", pb[oc][0:8, 0:512],
                         GT_sb[:, ic * 8:(ic + 1) * 8],
                         w2sl[icp][:, oc * 512:(oc + 1) * 512],
                         start=(ic == 0), stop=False)
                    P.op("tensor", "matmul", pb[oc][0:8, 0:512],
                         GT_lo[:, ic * 8:(ic + 1) * 8],
                         w2sl[icp][:, oc * 512:(oc + 1) * 512],
                         start=False, stop=False)
                    P.op("tensor", "matmul", pb[oc][0:8, 0:512],
                         GT_sb[:, ic * 8:(ic + 1) * 8],
                         w2lo[icp][:, oc * 512:(oc + 1) * 512],
                         start=False, stop=(ic == 15),
                         inc=(c_mm if oc == 3 else None))
                c_mm.nxt()
                w_free_v[icp] = c_mm.v
            v_smm = c_mm.v
            P.wait("vector", c_mm, v_smm)
            for oc in range(4):
                P.op("vector", "tensor_copy",
                     out=Ssum[0:8, oc * 512:(oc + 1) * 512],
                     in_=pb[oc][0:8, 0:512], inc=(c_ve if oc == 3 else None))
            v_S = c_ve.nxt()
            if "G" in _DBG:
                P.wait("sync", c_ve, v_S)
                P.op("sync", "dma_start", out=dbg["G"].ap()[:, :], in_=Gsb[0:8, :],
                     inc=d_pre, inck=16)
                d_pre.nxt(16)
            P.wait("gpsimd", c_ve, v_S)
            P.op("gpsimd", "dma_start", out=cc_in.ap()[:, :], in_=Ssum[0:8, :],
                 inc=d_cc, inck=16)
            v_cc1 = d_cc.nxt(16)
            P.wait("gpsimd", d_cc, v_cc1)
            P.op("gpsimd", "collective_compute", "AllReduce", OP.add,
                 replica_groups=[list(range(NCORES))],
                 ins=[cc_in.ap().opt()], outs=[cc_out.ap().opt()], inc=c_cc)
            v_ccd = c_cc.nxt()
            P.wait("gpsimd", c_cc, v_ccd)
            P.op("gpsimd", "dma_start", out=Ssum[0:8, :], in_=cc_out.ap()[:, :],
                 inc=d_cc, inck=16)
            v_cc2 = d_cc.nxt(16)
            P.wait("vector", d_cc, v_cc2)
            P.op("vector", "tensor_scalar", out=Ssum[0:8, :], in0=Ssum[0:8, :],
                 scalar1=c0m_sb[:, :], scalar2=None, op0=OP.mult, inc=c_ve)
            v_Sm = c_ve.nxt()
            if "S" in _DBG:
                P.wait("sync", c_ve, v_Sm)
                P.op("sync", "dma_start", out=dbg["S"].ap()[:, :], in_=Ssum[0:8, :],
                     inc=d_pre, inck=16)
                d_pre.nxt(16)
            P.wait("sync", d_out[0], 16 * out_st[0])
            P.wait("sync", d_out[1], 16 * out_st[1])
            P.op("sync", "dma_start", out=patch[:, :], in_=out_d.ap()[0:8, :],
                 inc=d_pt, inck=16)
            v_cc3 = d_pt.nxt(16)
            P.wait("vector", d_pt, v_cc3)
            P.wait("vector", c_ve, v_Sm)
            P.op("vector", "tensor_tensor", out=patch[:, :], in0=patch[:, :],
                 in1=Ssum[0:8, :], op=OP.add, inc=c_ve)
            v_pt = c_ve.nxt()
            P.wait("sync", c_ve, v_pt)
            P.op("sync", "dma_start", out=out_d.ap()[0:8, :], in_=patch[:, :],
                 inc=d_cc, inck=16)
            v_cc4 = d_cc.nxt(16)
            if "snap" in _DBG:
                P.op("sync", "dma_start", out=dbg["snap"].ap()[:, :],
                     in_=snap[:, :], inc=d_pre, inck=16)
                d_pre.nxt(16)
            P.wait("sync", d_cc, v_cc4)
            P.wait("sync", d_pre, d_pre.v)

        block = es.enter_context(nc.Block())
        block.sync(lambda e: P.replay("sync", e))
        block.gpsimd(lambda e: P.replay("gpsimd", e))
        block.vector(lambda e: P.replay("vector", e))
        block.scalar(lambda e: P.replay("scalar", e))
        block.tensor(lambda e: P.replay("tensor", e))
    return nc


_nc_cache = {}


def kernel(**inputs):
    if "nc" not in _nc_cache:
        _nc_cache["nc"] = build_nc()
    nc = _nc_cache["nc"]
    g = _host_prep(inputs)
    x = np.ascontiguousarray(np.asarray(inputs["L0_attn_x"], dtype=np.float32))
    in_maps = []
    for c in range(NCORES):
        m = dict(g)
        m["x"] = np.ascontiguousarray(x[2 * c:2 * c + 2].reshape(NSEQ * S, D))
        m["bval"] = _f32(np.array([[2 * c, 2 * c + 1]]))
        m["c0m"] = _f32(np.full((8, 1), 1.0 if c == 0 else 0.0))
        in_maps.append(m)
    res = run_bass_kernel_spmd(nc, in_maps, core_ids=list(range(NCORES)))
    if _debug_capture is not None:
        _debug_capture.append(res)
    out = np.stack([np.asarray(res.results[c]["out"]).reshape(NSEQ, S, D)
                    for c in range(NCORES)])
    return np.ascontiguousarray(out.reshape(16, S, D).astype(np.float32))
